# version 03
# baseline (speedup 1.0000x reference)
import numpy as np
import ml_dtypes
from contextlib import ExitStack

import concourse.bass as bass
import concourse.mybir as mybir
from concourse.bass_utils import run_bass_kernel_spmd

F32 = mybir.dt.float32
BF16 = mybir.dt.float16
AF = mybir.ActivationFunctionType
ALU = mybir.AluOpType
AX = mybir.AxisListType.X

NCORES = 8
GROUP = 4
B = 2
S = 8192
D = 1024
SC = S // GROUP
NT = SC // 128
EPS = 1e-5

RG = [[0, 1, 2, 3], [4, 5, 6, 7]]

A_SEMS = ["aX", "aX2", "aS2", "aVR", "aRS", "aST", "aTR", "aLT", "aKL", "aKC",
          "aMM", "aEVQ", "aEVK", "aEVV", "aSP", "aSPV", "aCC", "aKV", "aSC", "aEX",
          "aPV", "aPVE", "aDN", "aDNE", "aQG", "aRD", "aOS", "aRES", "aEYE",
          "aONE", "aOUT", "aLX", "aDV"]


def make_sems(nc, ctx, names):
    return {n: ctx.enter_context(nc.semaphore(n)) for n in names}


class NS:
    def __init__(self, d):
        self.__dict__.update(d)


def build_stage_a(sem, bufs, io, store_out, dbg=False):
    qt = bufs.A_bf
    lnT = bufs.C_bf
    ktq = bufs.B_bf
    vq = bufs.C_bf
    pT = bufs.pT
    ker = bufs.W_bf
    R = bufs.R_f32
    xa = bufs.xa
    lnbf = bufs.lnbf
    sqscr = bufs.sqscr
    st = bufs.stats
    dn_sb = bufs.dn_sb
    rd_all = bufs.rd_all
    ones4 = bufs.ones4
    epsc = bufs.epsc
    eye = bufs.eye
    ps = bufs.ps
    psbf = bufs.psbf
    ps64 = bufs.ps64

    x_own, kern, kv_in, kv_g = io["x_own"], io["kern"], io["kv_in"], io["kv_g"]

    def stc(t, i):
        o = 16 * (t % 2) + i
        return st[:, o:o + 1]

    def sync_ops(e):
        xr = x_own.rearrange("(t p) e -> t p e", p=128)
        for t in range(NT):
            if t >= 1:
                e.wait_ge(sem["aX"], 16 * t)
            if t >= 2:
                e.wait_ge(sem["aST"], t - 1)
                e.wait_ge(sem["aS2"], t - 1)
            e.dma_start(out=xa[:, 1024 * (t % 2):1024 * (t % 2) + 1024],
                        in_=xr[t]).then_inc(sem["aX"], 16)
        kr = kern.rearrange("k (c p) e -> k p c e", p=128)
        Rst = R.rearrange("p (a e) -> p a e", e=1024)
        for i in range(3):
            e.wait_ge(sem["aKC"], i)
            e.dma_start(out=Rst[:, 0:8, :], in_=kr[i]).then_inc(sem["aKL"], 16)
        e.wait_ge(sem["aEVK"], 32)
        e.dma_start(out=kv_in[:, 0:16384], in_=ktq[:]).then_inc(sem["aSP"], 16)
        kvg_kt = kv_g.rearrange("(j p) n -> j p n", p=128)
        for kt8 in range(8):
            j, h = kt8 // 2, kt8 % 2
            if kt8 == 0:
                e.wait_ge(sem["aCC"], 1)
            if kt8 >= 2:
                e.wait_ge(sem["aSC"], 32 * (kt8 - 1))
                e.wait_ge(sem["aPV"], 32 * (kt8 - 1))
                e.wait_ge(sem["aDN"], 4 * (kt8 - 1))
            src = kvg_kt[j][:, 0:16384].rearrange("p (c s) -> p c s", c=8)
            if kt8 >= 1:
                e.wait_ge(sem["aKV"], 32 * kt8)
            e.dma_start(out=ktq[:, 8192 * (kt8 % 2):8192 * (kt8 % 2) + 8192]
                        .rearrange("p (c s) -> p c s", c=8),
                        in_=src[:, :, 1024 * h:1024 * h + 1024]
                        ).then_inc(sem["aKV"], 16)
            e.wait_ge(sem["aKV"], 32 * kt8 + 16)
            e.dma_start(out=vq[:, 8192 * (kt8 % 2):8192 * (kt8 % 2) + 8192],
                        in_=kvg_kt[j][:, 16384 + 8192 * h:16384 + 8192 * (h + 1)]
                        ).then_inc(sem["aKV"], 16)
        for t in range(NT):
            if t >= 1:
                e.wait_ge(sem["aX2"], 16 * t)
            if t >= 2:
                e.wait_ge(sem["aRES"], t - 1)
            e.dma_start(out=xa[:, 1024 * (t % 2):1024 * (t % 2) + 1024],
                        in_=xr[t]).then_inc(sem["aX2"], 16)

    def vector_ops(e):
        e.memset(ones4[:], 1.0)
        e.memset(epsc[:], float(EPS)).then_inc(sem["aONE"], 1)
        for t in range(NT):
            e.wait_ge(sem["aX"], 16 * (t + 1))
            xv = xa[:, 1024 * (t % 2):1024 * (t % 2) + 1024]
            e.reduce_sum(out=stc(t, 0), in_=xv, axis=AX)
            e.wait_ge(sem["aS2"], t + 1)
            e.tensor_add(out=stc(t, 2), in0=stc(t, 2), in1=stc(t, 9)) \
                .then_inc(sem["aDV"], 1)
            e.wait_ge(sem["aDV"], 2 * t + 1)
            e.tensor_scalar(out=stc(t, 1), in0=stc(t, 0), scalar1=1.0 / D,
                            scalar2=None, op0=ALU.mult)
            e.tensor_scalar(out=stc(t, 3), in0=stc(t, 2), scalar1=1.0 / D,
                            scalar2=None, op0=ALU.mult)
            e.scalar_tensor_tensor(out=stc(t, 4), in0=stc(t, 0),
                                   scalar=-1.0 / (D * D), in1=stc(t, 0),
                                   op0=ALU.mult, op1=ALU.mult) \
                .then_inc(sem["aDV"], 1)
            e.wait_ge(sem["aDV"], 2 * t + 2)
            e.tensor_add(out=stc(t, 5), in0=stc(t, 3), in1=stc(t, 4)) \
                .then_inc(sem["aVR"], 1)
            e.wait_ge(sem["aRS"], t + 1)
            e.tensor_scalar(out=lnbf[:, 1024 * (t % 2):1024 * (t % 2) + 1024],
                            in0=xv, scalar1=stc(t, 1), scalar2=stc(t, 7),
                            op0=ALU.subtract, op1=ALU.mult) \
                .then_inc(sem["aST"], 1)
        for i in range(3):
            e.wait_ge(sem["aKL"], 16 * (i + 1))
            if i >= 1:
                e.wait_ge(sem["aMM"], 32 * i)
            e.tensor_copy(out=ker[:, 0:8192], in_=R[:, 0:8192]) \
                .then_inc(sem["aKC"], 1)
        for kt8 in range(8):
            for qg in range(4):
                for qc in range(4):
                    for dh in range(2):
                        n = ((kt8 * 4 + qg) * 4 + qc) * 2 + dh
                        e.wait_ge(sem["aPV"], n + 1)
                        dst = R[:, (qg * 4 + qc) * 1024 + dh * 512:
                                (qg * 4 + qc) * 1024 + dh * 512 + 512]
                        pvp = ps[2 + (qc * 2 + dh) % 2]
                        if kt8 == 0:
                            e.tensor_copy(out=dst, in_=pvp[:]) \
                                .then_inc(sem["aPVE"], 1)
                        else:
                            e.tensor_add(out=dst, in0=dst, in1=pvp[:]) \
                                .then_inc(sem["aPVE"], 1)
                nqg = kt8 * 4 + qg
                e.wait_ge(sem["aDN"], nqg + 1)
                if kt8 == 0:
                    e.tensor_copy(out=dn_sb[:, 16 * qg:16 * qg + 16],
                                  in_=ps64[:, 0:16]).then_inc(sem["aDNE"], 1)
                else:
                    e.tensor_add(out=dn_sb[:, 16 * qg:16 * qg + 16],
                                 in0=dn_sb[:, 16 * qg:16 * qg + 16],
                                 in1=ps64[:, 0:16]).then_inc(sem["aDNE"], 1)
        dnv = dn_sb.rearrange("p (q f) -> p q f", f=4)
        e.wait_ge(sem["aDNE"], 32)
        e.reciprocal(out=rd_all[:], in_=dnv[:, :, 0]).then_inc(sem["aRD"], 1)
        e.wait_ge(sem["aRD"], 1)
        for t in range(NT):
            Ot = R[:, 1024 * t:1024 * t + 1024]
            e.tensor_scalar(out=Ot, in0=Ot, scalar1=rd_all[:, t:t + 1],
                            scalar2=None, op0=ALU.mult).then_inc(sem["aOS"], 1)
            e.wait_ge(sem["aOS"], t + 1)
            e.reduce_sum(out=stc(t, 0), in_=Ot, axis=AX)
            e.wait_ge(sem["aS2"], NT + t + 1)
            e.tensor_add(out=stc(t, 2), in0=stc(t, 2), in1=stc(t, 9)) \
                .then_inc(sem["aDV"], 1)
            e.wait_ge(sem["aDV"], 2 * NT + 3 * t + 1)
            e.tensor_scalar(out=stc(t, 1), in0=stc(t, 0), scalar1=1.0 / D,
                            scalar2=None, op0=ALU.mult)
            e.tensor_scalar(out=stc(t, 3), in0=stc(t, 2), scalar1=1.0 / D,
                            scalar2=None, op0=ALU.mult)
            e.scalar_tensor_tensor(out=stc(t, 4), in0=stc(t, 0),
                                   scalar=-1.0 / (D * D), in1=stc(t, 0),
                                   op0=ALU.mult, op1=ALU.mult) \
                .then_inc(sem["aDV"], 1)
            e.wait_ge(sem["aDV"], 2 * NT + 3 * t + 2)
            e.tensor_add(out=stc(t, 5), in0=stc(t, 3), in1=stc(t, 4)) \
                .then_inc(sem["aVR"], 1)
            e.wait_ge(sem["aRS"], NT + t + 1)
            e.tensor_scalar(out=Ot, in0=Ot, scalar1=stc(t, 1),
                            scalar2=stc(t, 7), op0=ALU.subtract, op1=ALU.mult) \
                .then_inc(sem["aDV"], 1)
            e.wait_ge(sem["aDV"], 2 * NT + 3 * t + 3)
            e.wait_ge(sem["aX2"], 16 * (t + 1))
            e.tensor_add(out=Ot, in0=Ot,
                         in1=xa[:, 1024 * (t % 2):1024 * (t % 2) + 1024]) \
                .then_inc(sem["aRES"], 1)

    def scalar_ops(e):
        e.wait_ge(sem["aONE"], 1)
        for t in range(NT):
            e.wait_ge(sem["aX"], 16 * (t + 1))
            xv = xa[:, 1024 * (t % 2):1024 * (t % 2) + 1024]
            e.activation(out=sqscr[:], in_=xv[:, 0:512], func=AF.Square,
                         accum_out=stc(t, 2))
            e.activation(out=sqscr[:], in_=xv[:, 512:1024], func=AF.Square,
                         accum_out=stc(t, 9)).then_inc(sem["aS2"], 1)
            e.wait_ge(sem["aVR"], t + 1)
            e.activation(out=stc(t, 6), in_=stc(t, 5), func=AF.Ln,
                         bias=epsc[:]).then_inc(sem["aLX"], 1)
            e.wait_ge(sem["aLX"], t + 1)
            e.activation(out=stc(t, 7), in_=stc(t, 6), func=AF.Exp,
                         scale=-0.5).then_inc(sem["aRS"], 1)
            e.wait_ge(sem["aTR"], t + 1)
            dst = lnT.rearrange("p (c s) -> p c s", c=8)[:, :, 128 * t:128 * t + 128]
            e.activation(out=dst, in_=psbf.rearrange("p (c s) -> p c s", c=8),
                         func=AF.Copy).then_inc(sem["aLT"], 1)
        for sc4 in range(4):
            for ec in range(8):
                n = sc4 * 8 + ec
                e.wait_ge(sem["aMM"], n + 1)
                e.activation(out=qt[:, ec * 2048 + 512 * sc4:ec * 2048 + 512 * sc4 + 512],
                             in_=ps[n % 2][:], func=AF.Copy).then_inc(sem["aEVQ"], 1)
        for sc4 in range(4):
            for ec in range(8):
                n = 32 + sc4 * 8 + ec
                e.wait_ge(sem["aMM"], n + 1)
                e.activation(out=ktq[:, ec * 2048 + 512 * sc4:ec * 2048 + 512 * sc4 + 512],
                             in_=ps[n % 2][:], func=AF.Copy).then_inc(sem["aEVK"], 1)
        for stt in range(16):
            for eh in range(2):
                n = 64 + stt * 2 + eh
                k = stt * 2 + eh
                e.wait_ge(sem["aMM"], n + 1)
                if k >= 4:
                    e.wait_ge(sem["aSPV"], 16 * (k - 3))
                e.activation(out=lnbf[:, (k % 4) * 512:(k % 4) * 512 + 512],
                             in_=ps[n % 2][:], func=AF.Copy).then_inc(sem["aEVV"], 1)
        for kt8 in range(8):
            for qg in range(4):
                for jc in range(8):
                    n = (kt8 * 4 + qg) * 8 + jc
                    e.wait_ge(sem["aSC"], n + 1)
                    pt_dst = pT[:, 4096 * (qg % 2) + 512 * jc:
                                4096 * (qg % 2) + 512 * jc + 512]
                    e.activation(out=pt_dst, in_=ps[jc % 2][:], func=AF.Exp,
                                 scale=1.0 / 32.0).then_inc(sem["aEX"], 1)
        for t in range(NT):
            e.wait_ge(sem["aOS"], t + 1)
            Ot = R[:, 1024 * t:1024 * t + 1024]
            e.activation(out=sqscr[:], in_=Ot[:, 0:512], func=AF.Square,
                         accum_out=stc(t, 2))
            e.activation(out=sqscr[:], in_=Ot[:, 512:1024], func=AF.Square,
                         accum_out=stc(t, 9)).then_inc(sem["aS2"], 1)
            e.wait_ge(sem["aVR"], NT + t + 1)
            e.activation(out=stc(t, 6), in_=stc(t, 5), func=AF.Ln,
                         bias=epsc[:]).then_inc(sem["aLX"], 1)
            e.wait_ge(sem["aLX"], NT + t + 1)
            e.activation(out=stc(t, 7), in_=stc(t, 6), func=AF.Exp,
                         scale=-0.5).then_inc(sem["aRS"], 1)

    def tensor_ops(e):
        e.wait_ge(sem["aEYE"], 16)
        for t in range(NT):
            e.wait_ge(sem["aST"], t + 1)
            if t >= 1:
                e.wait_ge(sem["aLT"], t)
            src = lnbf[:, 1024 * (t % 2):1024 * (t % 2) + 1024]
            last = None
            for dc in range(8):
                last = e.transpose(psbf[:, 128 * dc:128 * dc + 128],
                                   src[:, 128 * dc:128 * dc + 128], eye[:])
            last.then_inc(sem["aTR"], 1)
        e.wait_ge(sem["aLT"], NT)
        nmm = 0
        for kidx, evs in ((0, "aEVQ"), (1, "aEVK")):
            e.wait_ge(sem["aKC"], kidx + 1)
            kslice = ker[:, 0:8192]
            for sc4 in range(4):
                for ec in range(8):
                    prior = sc4 * 8 + ec
                    if prior >= 2:
                        e.wait_ge(sem[evs], prior - 1)
                    pst = ps[nmm % 2]
                    last = None
                    for dc in range(8):
                        last = e.matmul(
                            pst[:],
                            kslice[:, dc * 1024 + ec * 128:dc * 1024 + ec * 128 + 128],
                            lnT[:, dc * 2048 + 512 * sc4:dc * 2048 + 512 * sc4 + 512],
                            start=(dc == 0), stop=(dc == 7))
                    last.then_inc(sem["aMM"], 1)
                    nmm += 1
        e.wait_ge(sem["aKC"], 3)
        for stt in range(16):
            for eh in range(2):
                prior = stt * 2 + eh
                if prior >= 2:
                    e.wait_ge(sem["aEVV"], prior - 1)
                pst = ps[nmm % 2]
                last = None
                for dc in range(8):
                    last = e.matmul(
                        pst[:],
                        lnT[:, dc * 2048 + 128 * stt:dc * 2048 + 128 * stt + 128],
                        ker[:, dc * 1024 + eh * 512:dc * 1024 + eh * 512 + 512],
                        start=(dc == 0), stop=(dc == 7))
                last.then_inc(sem["aMM"], 1)
                nmm += 1
        e.wait_ge(sem["aONE"], 1)
        for kt8 in range(8):
            e.wait_ge(sem["aKV"], 32 * (kt8 + 1))
            ktb = ktq[:, 8192 * (kt8 % 2):8192 * (kt8 % 2) + 8192]
            vb = vq[:, 8192 * (kt8 % 2):8192 * (kt8 % 2) + 8192]
            for qg in range(4):
                nqg = kt8 * 4 + qg
                for jc in range(8):
                    n = nqg * 8 + jc
                    if n >= 2:
                        e.wait_ge(sem["aEX"], n - 1)
                    if jc == 0 and nqg >= 2:
                        e.wait_ge(sem["aPV"], (nqg - 1) * 8)
                        e.wait_ge(sem["aDN"], nqg - 1)
                    pst = ps[jc % 2]
                    last = None
                    for ec in range(8):
                        last = e.matmul(
                            pst[:],
                            ktb[:, ec * 1024 + jc * 128:ec * 1024 + jc * 128 + 128],
                            qt[:, ec * 2048 + qg * 512:ec * 2048 + qg * 512 + 512],
                            start=(ec == 0), stop=(ec == 7))
                    last.then_inc(sem["aSC"], 1)
                e.wait_ge(sem["aEX"], nqg * 8 + 8)
                pt_b = pT[:, 4096 * (qg % 2):4096 * (qg % 2) + 4096]
                for qc in range(4):
                    for dh in range(2):
                        n = (nqg * 4 + qc) * 2 + dh
                        pvp = ps[2 + (qc * 2 + dh) % 2]
                        if n >= 2:
                            e.wait_ge(sem["aPVE"], n - 1)
                        last = None
                        for jc in range(8):
                            last = e.matmul(
                                pvp[:],
                                pt_b[:, jc * 512 + qc * 128:jc * 512 + qc * 128 + 128],
                                vb[:, jc * 1024 + dh * 512:jc * 1024 + dh * 512 + 512],
                                start=(jc == 0), stop=(jc == 7))
                        last.then_inc(sem["aPV"], 1)
                    if nqg >= 1:
                        e.wait_ge(sem["aDNE"], nqg)
                    last = None
                    for jc in range(8):
                        last = e.matmul(
                            ps64[:, 4 * qc:4 * qc + 4],
                            pt_b[:, jc * 512 + qc * 128:jc * 512 + qc * 128 + 128],
                            ones4[:], start=(jc == 0), stop=(jc == 7),
                            skip_group_check=True)
                    if qc == 3:
                        last.then_inc(sem["aDN"], 1)

    def gpsimd_ops(e):
        for k in range(32):
            e.wait_ge(sem["aEVV"], k + 1)
            e.dma_start(out=kv_in[:, 16384 + 512 * k:16384 + 512 * k + 512],
                        in_=lnbf[:, (k % 4) * 512:(k % 4) * 512 + 512]) \
                .then_inc(sem["aSPV"], 16)
        e.wait_ge(sem["aSP"], 16)
        e.wait_ge(sem["aSPV"], 512)
        e.collective_compute("AllGather", ALU.bypass, replica_groups=RG,
                             ins=[kv_in.opt()], outs=[kv_g.opt()]) \
            .then_inc(sem["aCC"])
        if store_out:
            e.wait_ge(sem["aRES"], NT)
            nout = 0
            xo = io["x1_ext"].rearrange("(q p) e -> q p e", p=128)
            for t in range(NT):
                e.dma_start(out=xo[t], in_=R[:, 1024 * t:1024 * t + 1024]) \
                    .then_inc(sem["aOUT"], 16)
                nout += 1
            if dbg:
                e.dma_start(out=io["qt_d"], in_=qt[:]).then_inc(sem["aOUT"], 16)
                e.dma_start(out=io["kv_d"], in_=kv_in[:]).then_inc(sem["aOUT"], 16)
                e.dma_start(out=io["kvg_d"], in_=kv_g[:]).then_inc(sem["aOUT"], 16)
                e.dma_start(out=io["dn_d"], in_=dn_sb[:]).then_inc(sem["aOUT"], 16)
                nout += 4
            e.wait_ge(sem["aOUT"], 16 * nout)

    return dict(sync=sync_ops, vector=vector_ops, scalar=scalar_ops,
                tensor=tensor_ops, gpsimd=gpsimd_ops)


B_SEMS = ["bXR", "bS2", "bVR", "bRS", "bLX", "bDV", "bZ", "bTR", "bZT", "bCS",
          "bCC", "bSS", "bDM", "bHM", "bHE", "bYM", "bYE", "bRES", "bOUT"]


class TokenChain:

    def __init__(self, ss, dm):
        self.ss, self.dm = ss, dm
        self.k = 0
        self.m = 0
        self.steps = []
        self._last_cc = None
        self.marks = {}

    def mark(self, label):
        self.marks[label] = len(self.steps)

    def add(self, eng, fn, dma=False, cc=None):
        self.steps.append((eng, fn, self.k, self.m, dma, cc, self._last_cc))
        if cc is not None:
            self._last_cc = cc
        elif dma:
            self.m += 1
        else:
            self.k += 1

    def counts_at(self, pos):
        k = m = 0
        last_cc = None
        for eng, fn, k0, m0, dma, cc, pc in self.steps[:pos]:
            if cc is not None:
                last_cc = cc
            elif dma:
                m += 1
            else:
                k += 1
        return k, m, last_cc

    def emit(self, eng_name, e, limit=None):
        for i, (eng, fn, k0, m0, dma, cc, prior_cc) in enumerate(self.steps):
            if limit is not None and i >= limit:
                break
            if eng != eng_name:
                continue
            if k0 > 0:
                e.wait_ge(self.ss, k0)
            if m0 > 0:
                e.wait_ge(self.dm, 16 * m0)
            if prior_cc is not None:
                e.wait_ge(prior_cc, 1)
            ins = fn(e)
            if cc is not None:
                ins.then_inc(cc)
            elif dma:
                ins.then_inc(self.dm, 16)
            else:
                ins.then_inc(self.ss, 1)


def build_stage_b(sem, bufs, io, store_out, resident_in, gate_in=None):
    R = bufs.R_f32
    zT = bufs.B_bf
    af = bufs.A_bf.bitcast(F32)
    cf = bufs.C_bf.bitcast(F32)
    pTb = bufs.pT
    xa = bufs.xa
    lnbf = bufs.lnbf
    sqscr = bufs.sqscr
    st = bufs.stats
    epsc = bufs.epsc
    ps = bufs.ps
    psbf = bufs.psbf
    ps64 = bufs.ps64

    g_bc = af[:, 0:1024]
    b_bc = af[:, 1024:2048]
    wd_all = af[:, 2048:6144]
    wd_sel = af[:, 6144:6656]
    maskP = af[:, 6656:6664]
    gate_bc = af[:, 6664:6665]
    zbnT = af[:, 6672:6680]
    wr_sb = af[:, 6688:6752].rearrange("p (c e) -> p c e", c=8)
    ones1f = af[0:1, 6752:6880]
    onec = af[0:1, 6880:6881]
    mask_col = cf[32:40, 4160:4161]
    bdT = af[0:64, 6888:6889]
    wu_stage = cf[64:128, 1024:2048]
    wu_sel = cf[64:128, 2048:3072]
    bdown_sb = cf[32:40, 3072:3136]
    bup_sb = cf[32:40, 3136:4160]
    bnm = cf[0:1, 0:1024]
    bnv = cf[0:1, 1024:2048]
    bng = cf[0:1, 2048:3072]
    bnb = cf[0:1, 3072:4096]
    g_row = cf[0:1, 4096:5120]
    b_row = cf[0:1, 5120:6144]
    r0 = cf[0:1, 6144:7168]
    zsr = cf[0:1, 7168:8192]
    lvq = af[0:1, 6992:8016]
    br_row = af[0:1, 6912:6920]
    lg = af[0:1, 6928:6936]
    mx = af[0:1, 6936:6937]
    nmx = af[0:1, 6937:6938]
    el = af[0:1, 6944:6952]
    smr = af[0:1, 6952:6953]
    rsr = af[0:1, 6953:6954]
    probs = af[0:1, 6960:6968]
    pmax = af[0:1, 6968:6969]
    mask_row = af[0:1, 6976:6984]
    wd_sel_bf = pTb[:, 0:512]
    wu_sel_bf = pTb[0:64, 512:1536]
    hT = pTb[0:64, 1536:3584]
    bu_row_bf = pTb[0:1, 3584:4608]
    ones1_bf = pTb[0:1, 4608:4736]

    w = io["w"]

    def stc(t, i):
        o = 16 * (t % 2) + i
        return st[:, o:o + 1]

    tc = TokenChain(sem["bSS"], sem["bDM"])

    def _chain():
        tc.add("scalar", lambda e: e.activation(out=r0[:, 0:512], in_=ps[2][0:1, :],
                                                func=AF.Copy))
        tc.add("scalar", lambda e: e.activation(out=r0[:, 512:1024], in_=ps[3][0:1, :],
                                                func=AF.Copy))
        tc.add("gpsimd", lambda e: e.dma_start(out=io["zs_in"], in_=r0), dma=True)
        tc.add("gpsimd", lambda e: e.collective_compute(
            "AllReduce", ALU.add, replica_groups=RG,
            ins=[io["zs_in"].opt()], outs=[io["zs_out"].opt()]),
               cc=sem["bCC"])
        tc.add("gpsimd", lambda e: e.dma_start(out=zsr, in_=io["zs_out"]), dma=True)
        tc.add("vector", lambda e: e.tensor_scalar(out=zsr, in0=zsr, scalar1=1.0 / S,
                                                   scalar2=None, op0=ALU.mult))
        tc.add("vector", lambda e: e.tensor_sub(out=r0, in0=zsr, in1=bnm))
        tc.add("scalar", lambda e: e.activation(out=lvq, in_=bnv, func=AF.Ln,
                                                bias=epsc[0:1, :]))
        tc.add("scalar", lambda e: e.activation(out=lvq, in_=lvq, func=AF.Exp,
                                                scale=-0.5))
        tc.add("vector", lambda e: e.tensor_mul(out=r0, in0=r0, in1=lvq))
        tc.add("vector", lambda e: e.tensor_mul(out=r0, in0=r0, in1=bng))
        tc.add("vector", lambda e: e.tensor_add(out=r0, in0=r0, in1=bnb))

        def zbnt_mms(e):
            for dc in range(8):
                last = e.matmul(ps64[:, dc:dc + 1], r0[:, 128 * dc:128 * dc + 128],
                                onec, start=True, stop=True)
            return last
        tc.add("tensor", zbnt_mms)
        tc.add("scalar", lambda e: e.activation(out=zbnT, in_=ps64[:, 0:8],
                                                func=AF.Copy))

        def logit_mms(e):
            for dc in range(8):
                last = e.matmul(ps[0][0:1, 0:8], zbnT[:, dc:dc + 1],
                                wr_sb[:, dc, :], start=(dc == 0), stop=(dc == 7))
            return last
        tc.add("tensor", logit_mms)
        tc.add("scalar", lambda e: e.activation(out=lg, in_=ps[0][0:1, 0:8],
                                                func=AF.Copy))
        tc.add("vector", lambda e: e.tensor_add(out=lg, in0=lg, in1=br_row))
        tc.add("vector", lambda e: e.reduce_max(out=mx, in_=lg, axis=AX))
        tc.add("vector", lambda e: e.tensor_scalar(out=nmx, in0=mx, scalar1=-1.0,
                                                   scalar2=None, op0=ALU.mult))
        tc.add("scalar", lambda e: e.activation(out=el, in_=lg, func=AF.Exp,
                                                bias=nmx))
        tc.add("vector", lambda e: e.reduce_sum(out=smr, in_=el, axis=AX))
        tc.add("vector", lambda e: e.reciprocal(out=rsr, in_=smr))
        tc.add("vector", lambda e: e.tensor_scalar(out=probs, in0=el, scalar1=rsr,
                                                   scalar2=None, op0=ALU.mult))
        tc.add("vector", lambda e: e.reduce_max(out=pmax, in_=probs, axis=AX))
        tc.add("vector", lambda e: e.tensor_scalar(out=mask_row, in0=probs,
                                                   scalar1=pmax, scalar2=None,
                                                   op0=ALU.is_ge))
        tc.add("gpsimd", lambda e: e.dma_start(out=io["mrow_d"], in_=mask_row),
               dma=True)
        tc.add("gpsimd", lambda e: e.dma_start(
            out=mask_col, in_=io["mrow_d"].rearrange("o e -> e o")), dma=True)
        tc.add("tensor", lambda e: e.matmul(ps[1][:, 0:8], ones1f, mask_row,
                                            start=True, stop=True))
        tc.add("tensor", lambda e: e.matmul(ps[1][:, 8:9], ones1f, pmax,
                                            start=True, stop=True, skip_group_check=True))
        tc.add("scalar", lambda e: e.activation(out=maskP, in_=ps[1][:, 0:8],
                                                func=AF.Copy))
        tc.add("scalar", lambda e: e.activation(out=gate_bc, in_=ps[1][:, 8:9],
                                                func=AF.Copy))
        for ee in range(8):
            we = wd_all[:, 512 * ee:512 * ee + 512]
            if ee == 0:
                tc.add("vector", lambda e, we=we: e.tensor_scalar(
                    out=wd_sel, in0=we, scalar1=maskP[:, 0:1], scalar2=None,
                    op0=ALU.mult))
            else:
                tc.add("vector", lambda e, we=we, ee=ee: e.scalar_tensor_tensor(
                    out=wd_sel, in0=we, scalar=maskP[:, ee:ee + 1], in1=wd_sel,
                    op0=ALU.mult, op1=ALU.add))
        tc.add("vector", lambda e: e.tensor_copy(out=wd_sel_bf, in_=wd_sel))
        for ee in range(8):
            tc.add("gpsimd", lambda e, ee=ee: e.dma_start(
                out=wu_stage, in_=w["W_up"][ee]), dma=True)
            if ee == 0:
                tc.add("vector", lambda e: e.tensor_scalar(
                    out=wu_sel, in0=wu_stage, scalar1=maskP[64:128, 0:1],
                    scalar2=None, op0=ALU.mult))
            else:
                tc.add("vector", lambda e, ee=ee: e.scalar_tensor_tensor(
                    out=wu_sel, in0=wu_stage, scalar=maskP[64:128, ee:ee + 1],
                    in1=wu_sel, op0=ALU.mult, op1=ALU.add))
        tc.add("vector", lambda e: e.tensor_copy(out=wu_sel_bf, in_=wu_sel))
        tc.add("tensor", lambda e: e.matmul(ps[0][0:64, 0:1], bdown_sb, mask_col,
                                            start=True, stop=True))
        tc.add("scalar", lambda e: e.activation(out=bdT, in_=ps[0][0:64, 0:1],
                                                func=AF.Copy))
        tc.add("tensor", lambda e: e.matmul(ps[2][0:1, :], mask_col,
                                            bup_sb[:, 0:512], start=True, stop=True))
        tc.add("tensor", lambda e: e.matmul(ps[3][0:1, :], mask_col,
                                            bup_sb[:, 512:1024], start=True,
                                            stop=True))
        tc.add("scalar", lambda e: e.activation(out=bu_row_bf[:, 0:512],
                                                in_=ps[2][0:1, :], func=AF.Copy))
        tc.add("scalar", lambda e: e.activation(out=bu_row_bf[:, 512:1024],
                                                in_=ps[3][0:1, :], func=AF.Copy))
    _chain()
    TCK = tc.k

    def gate(e):
        if gate_in is not None:
            e.wait_ge(gate_in[0], gate_in[1])

    NW = 11

    def sync_ops2(e):
        gate(e)
        loads = [
            (g_row, w["ln_g"][None, :]),
            (b_row, w["ln_b"][None, :]),
            (bnm, w["bn_mean"][None, :]),
            (bnv, w["bn_var"][None, :]),
            (bng, w["bn_g"][None, :]),
            (bnb, w["bn_b"][None, :]),
            (br_row, w["br"][None, :]),
            (wr_sb, w["Wr"].rearrange("(c p) e -> p c e", p=128)),
            (bdown_sb, w["b_down"]),
            (bup_sb, w["b_up"]),
        ]
        nb = 0
        for dst, srcap in loads:
            if nb >= 1:
                e.wait_ge(sem["bXR"], 16 * nb)
            e.dma_start(out=dst, in_=srcap).then_inc(sem["bXR"], 16)
            nb += 1
        for ee in range(8):
            e.wait_ge(sem["bXR"], 16 * nb)
            e.dma_start(out=wd_all[:, 512 * ee:512 * ee + 512]
                        .rearrange("p (c h) -> p c h", c=8),
                        in_=w["W_down"][ee].rearrange("(c p) h -> p c h", p=128)) \
                .then_inc(sem["bXR"], 16)
            nb += 1
        if not resident_in:
            xr = io["x_in"].rearrange("(t p) e -> t p e", p=128)
            for t in range(NT):
                e.wait_ge(sem["bXR"], 16 * nb)
                e.dma_start(out=R[:, 1024 * t:1024 * t + 1024], in_=xr[t]) \
                    .then_inc(sem["bXR"], 16)
                nb += 1

    NW = 18

    def vector_ops(e):
        gate(e)
        e.memset(ones1f, 1.0)
        e.memset(onec, 1.0)
        e.memset(ones1_bf, 1.0)
        for t in range(NT):
            if not resident_in:
                e.wait_ge(sem["bXR"], 16 * (NW + t + 1))
            else:
                e.wait_ge(sem["bXR"], 16 * NW)
            Rt = R[:, 1024 * t:1024 * t + 1024]
            e.reduce_sum(out=stc(t, 0), in_=Rt, axis=AX)
            e.wait_ge(sem["bS2"], t + 1)
            e.tensor_add(out=stc(t, 2), in0=stc(t, 2), in1=stc(t, 9)) \
                .then_inc(sem["bDV"], 1)
            e.wait_ge(sem["bDV"], 5 * t + 1)
            e.tensor_scalar(out=stc(t, 1), in0=stc(t, 0), scalar1=1.0 / D,
                            scalar2=None, op0=ALU.mult)
            e.tensor_scalar(out=stc(t, 3), in0=stc(t, 2), scalar1=1.0 / D,
                            scalar2=None, op0=ALU.mult)
            e.scalar_tensor_tensor(out=stc(t, 4), in0=stc(t, 0),
                                   scalar=-1.0 / (D * D), in1=stc(t, 0),
                                   op0=ALU.mult, op1=ALU.mult) \
                .then_inc(sem["bDV"], 1)
            e.wait_ge(sem["bDV"], 5 * t + 2)
            e.tensor_add(out=stc(t, 5), in0=stc(t, 3), in1=stc(t, 4)) \
                .then_inc(sem["bVR"], 1)
            e.wait_ge(sem["bRS"], t + 1)
            za = xa[:, 1024 * (t % 2):1024 * (t % 2) + 1024]
            e.tensor_scalar(out=za, in0=Rt, scalar1=stc(t, 1),
                            scalar2=stc(t, 7), op0=ALU.subtract, op1=ALU.mult) \
                .then_inc(sem["bDV"], 1)
            e.wait_ge(sem["bDV"], 5 * t + 3)
            e.wait_ge(sem["bHE"], 4)
            e.tensor_mul(out=za, in0=za, in1=g_bc).then_inc(sem["bDV"], 1)
            e.wait_ge(sem["bDV"], 5 * t + 4)
            e.tensor_add(out=lnbf[:, 1024 * (t % 2):1024 * (t % 2) + 1024],
                         in0=za, in1=b_bc).then_inc(sem["bZ"], 1)
            e.nop().then_inc(sem["bDV"], 1)
        tc.emit("vector", e)
        for t in range(NT):
            for dh in range(2):
                n = 2 * t + dh
                e.wait_ge(sem["bYM"], n + 1)
                ya = xa[:, 1024 * (t % 2) + 512 * dh:
                        1024 * (t % 2) + 512 * dh + 512]
                e.tensor_scalar(out=ya, in0=ps[2 + n % 2][:], scalar1=gate_bc,
                                scalar2=None, op0=ALU.mult).then_inc(sem["bYE"], 1)
            yt = xa[:, 1024 * (t % 2):1024 * (t % 2) + 1024]
            e.wait_ge(sem["bYE"], 2 * t + 2)
            e.reduce_sum(out=stc(t, 0), in_=yt, axis=AX)
            e.wait_ge(sem["bS2"], NT + t + 1)
            e.tensor_add(out=stc(t, 2), in0=stc(t, 2), in1=stc(t, 9)) \
                .then_inc(sem["bDV"], 1)
            e.wait_ge(sem["bDV"], 5 * NT + 3 * t + 1)
            e.tensor_scalar(out=stc(t, 1), in0=stc(t, 0), scalar1=1.0 / D,
                            scalar2=None, op0=ALU.mult)
            e.tensor_scalar(out=stc(t, 3), in0=stc(t, 2), scalar1=1.0 / D,
                            scalar2=None, op0=ALU.mult)
            e.scalar_tensor_tensor(out=stc(t, 4), in0=stc(t, 0),
                                   scalar=-1.0 / (D * D), in1=stc(t, 0),
                                   op0=ALU.mult, op1=ALU.mult) \
                .then_inc(sem["bDV"], 1)
            e.wait_ge(sem["bDV"], 5 * NT + 3 * t + 2)
            e.tensor_add(out=stc(t, 5), in0=stc(t, 3), in1=stc(t, 4)) \
                .then_inc(sem["bVR"], 1)
            e.wait_ge(sem["bRS"], NT + t + 1)
            e.tensor_scalar(out=yt, in0=yt, scalar1=stc(t, 1),
                            scalar2=stc(t, 7), op0=ALU.subtract, op1=ALU.mult) \
                .then_inc(sem["bDV"], 1)
            e.wait_ge(sem["bDV"], 5 * NT + 3 * t + 3)
            Rt = R[:, 1024 * t:1024 * t + 1024]
            e.tensor_add(out=Rt, in0=Rt, in1=yt).then_inc(sem["bRES"], 1)

    def scalar_ops(e):
        gate(e)
        bdsts = [g_bc[:, 0:512], g_bc[:, 512:1024],
                 b_bc[:, 0:512], b_bc[:, 512:1024]]
        for k in range(4):
            e.wait_ge(sem["bHM"], k + 1)
            e.activation(out=bdsts[k], in_=ps[k % 2][:], func=AF.Copy) \
                .then_inc(sem["bHE"], 1)
        for t in range(NT):
            if not resident_in:
                e.wait_ge(sem["bXR"], 16 * (NW + t + 1))
            else:
                e.wait_ge(sem["bXR"], 16 * NW)
            Rt = R[:, 1024 * t:1024 * t + 1024]
            e.activation(out=sqscr[:], in_=Rt[:, 0:512], func=AF.Square,
                         accum_out=stc(t, 2))
            e.activation(out=sqscr[:], in_=Rt[:, 512:1024], func=AF.Square,
                         accum_out=stc(t, 9)).then_inc(sem["bS2"], 1)
            e.wait_ge(sem["bVR"], t + 1)
            e.activation(out=stc(t, 6), in_=stc(t, 5), func=AF.Ln,
                         bias=epsc[:]).then_inc(sem["bLX"], 1)
            e.wait_ge(sem["bLX"], t + 1)
            e.activation(out=stc(t, 7), in_=stc(t, 6), func=AF.Exp,
                         scale=-0.5).then_inc(sem["bRS"], 1)
            e.wait_ge(sem["bTR"], t + 1)
            dst = zT.rearrange("p (c s) -> p c s", c=8)[:, :, 128 * t:128 * t + 128]
            e.activation(out=dst, in_=psbf.rearrange("p (c s) -> p c s", c=8),
                         func=AF.Copy).then_inc(sem["bZT"], 1)
        tc.emit("scalar", e)
        for k in range(4):
            e.wait_ge(sem["bHM"], 4 + k + 1)
            e.activation(out=hT[:, 512 * k:512 * k + 512], in_=ps[k % 2][0:64, :],
                         func=AF.Relu, bias=bdT).then_inc(sem["bHE"], 1)
        for t in range(NT):
            e.wait_ge(sem["bYE"], 2 * t + 2)
            yt = xa[:, 1024 * (t % 2):1024 * (t % 2) + 1024]
            e.activation(out=sqscr[:], in_=yt[:, 0:512], func=AF.Square,
                         accum_out=stc(t, 2))
            e.activation(out=sqscr[:], in_=yt[:, 512:1024], func=AF.Square,
                         accum_out=stc(t, 9)).then_inc(sem["bS2"], 1)
            e.wait_ge(sem["bVR"], NT + t + 1)
            e.activation(out=stc(t, 6), in_=stc(t, 5), func=AF.Ln,
                         bias=epsc[:]).then_inc(sem["bLX"], 1)
            e.wait_ge(sem["bLX"], NT + t + 1)
            e.activation(out=stc(t, 7), in_=stc(t, 6), func=AF.Exp,
                         scale=-0.5).then_inc(sem["bRS"], 1)

    def tensor_ops(e):
        gate(e)
        e.wait_ge(sem["bXR"], 16 * NW)
        srcs = [g_row[:, 0:512], g_row[:, 512:1024],
                b_row[:, 0:512], b_row[:, 512:1024]]
        for k, srow in enumerate(srcs):
            if k >= 2:
                e.wait_ge(sem["bHE"], k - 1)
            e.matmul(ps[k % 2][:], ones1f, srow, start=True, stop=True) \
                .then_inc(sem["bHM"], 1)
        for t in range(NT):
            e.wait_ge(sem["bZ"], t + 1)
            if t >= 1:
                e.wait_ge(sem["bZT"], t)
            srcz = lnbf[:, 1024 * (t % 2):1024 * (t % 2) + 1024]
            last = None
            for dc in range(8):
                last = e.transpose(psbf[:, 128 * dc:128 * dc + 128],
                                   srcz[:, 128 * dc:128 * dc + 128], bufs.eye[:])
            last.then_inc(sem["bTR"], 1)
            for hh in range(2):
                last = e.matmul(ps[2 + hh][0:1, :], bufs.ones4[:, 0:1],
                                srcz[:, 512 * hh:512 * hh + 512],
                                start=(t == 0), stop=(t == NT - 1),
                                skip_group_check=True)
            if t == NT - 1:
                last.then_inc(sem["bCS"], 1)
        tc.emit("tensor", e)
        e.wait_ge(sem["bSS"], TCK)
        e.wait_ge(sem["bZT"], NT)
        for k in range(4):
            if k >= 2:
                e.wait_ge(sem["bHE"], 4 + k - 1)
            last = None
            for dc in range(8):
                last = e.matmul(ps[k % 2][0:64, :],
                                wd_sel_bf[:, 64 * dc:64 * dc + 64],
                                zT[:, 2048 * dc + 512 * k:2048 * dc + 512 * k + 512],
                                start=(dc == 0), stop=(dc == 7))
            last.then_inc(sem["bHM"], 1)
        e.wait_ge(sem["bHE"], 8)
        for t in range(NT):
            for dh in range(2):
                n = 2 * t + dh
                if n >= 2:
                    e.wait_ge(sem["bYE"], n - 1)
                pst = ps[2 + n % 2]
                e.matmul(pst[:], hT[:, 128 * t:128 * t + 128],
                         wu_sel_bf[:, 512 * dh:512 * dh + 512],
                         start=True, stop=False)
                e.matmul(pst[:], ones1_bf, bu_row_bf[:, 512 * dh:512 * dh + 512],
                         start=False, stop=True).then_inc(sem["bYM"], 1)

    def gpsimd_ops(e):
        gate(e)
        tc.emit("gpsimd", e)
        if store_out:
            e.wait_ge(sem["bRES"], NT)
            nout = 0
            xo = io["x_out"].rearrange("(q p) e -> q p e", p=128)
            for t in range(NT):
                e.dma_start(out=xo[t], in_=R[:, 1024 * t:1024 * t + 1024]) \
                    .then_inc(sem["bOUT"], 16)
                nout += 1
            if "af_d" in io:
                e.dma_start(out=io["af_d"], in_=bufs.A_bf[:]).then_inc(sem["bOUT"], 16)
                e.dma_start(out=io["cf_d"], in_=bufs.C_bf[:]).then_inc(sem["bOUT"], 16)
                e.dma_start(out=io["pt_d"], in_=pTb[:]).then_inc(sem["bOUT"], 16)
                e.dma_start(out=io["zt_d"], in_=zT[:]).then_inc(sem["bOUT"], 16)
                nout += 4
            e.wait_ge(sem["bOUT"], 16 * nout)

    return dict(sync=sync_ops2, vector=vector_ops, scalar=scalar_ops,
                tensor=tensor_ops, gpsimd=gpsimd_ops)


def alloc_common(nc, ctx):
    b = {}
    b["R_f32"] = ctx.enter_context(nc.sbuf_tensor("R_f32", [128, 16384], F32)).ap()
    b["A_bf"] = ctx.enter_context(nc.sbuf_tensor("A_bf", [128, 16384], BF16)).ap()
    b["B_bf"] = ctx.enter_context(nc.sbuf_tensor("B_bf", [128, 16384], BF16)).ap()
    b["C_bf"] = ctx.enter_context(nc.sbuf_tensor("C_bf", [128, 16384], BF16)).ap()
    b["W_bf"] = ctx.enter_context(nc.sbuf_tensor("W_bf", [128, 8192], BF16)).ap()
    b["pT"] = ctx.enter_context(nc.sbuf_tensor("pT", [128, 8192], BF16)).ap()
    b["lnbf"] = ctx.enter_context(nc.sbuf_tensor("lnbf", [128, 2048], BF16)).ap()
    b["xa"] = ctx.enter_context(nc.sbuf_tensor("xa", [128, 2048], F32)).ap()
    b["sqscr"] = ctx.enter_context(nc.sbuf_tensor("sqscr", [128, 512], F32)).ap()
    b["stats"] = ctx.enter_context(nc.sbuf_tensor("stats", [128, 32], F32)).ap()
    b["dn_sb"] = ctx.enter_context(nc.sbuf_tensor("dn_sb", [128, 64], F32)).ap()
    b["rd_all"] = ctx.enter_context(nc.sbuf_tensor("rd_all", [128, 16], F32)).ap()
    b["ones4"] = ctx.enter_context(nc.sbuf_tensor("ones4", [128, 4], BF16)).ap()
    b["epsc"] = ctx.enter_context(nc.sbuf_tensor("epsc", [128, 1], F32)).ap()
    b["eye"] = ctx.enter_context(nc.sbuf_tensor("eye", [128, 128], BF16)).ap()
    b["wbf"] = ctx.enter_context(nc.sbuf_tensor("wbf", [128, 256], BF16)).ap()
    b["ccs"] = ctx.enter_context(nc.sbuf_tensor("ccs", [128, 32], F32)).ap()
    b["mask_lt"] = ctx.enter_context(nc.sbuf_tensor("mask_lt", [128, 4], F32)).ap()
    b["mask_prev"] = ctx.enter_context(nc.sbuf_tensor("mask_prev", [128, 4], F32)).ap()
    b["first_m"] = ctx.enter_context(nc.sbuf_tensor("first_m", [128, 1], F32)).ap()
    b["ps"] = [ctx.enter_context(nc.psum_tensor(f"ps{i}", [128, 512], F32)).ap()
               for i in range(4)]
    b["psbf"] = ctx.enter_context(nc.psum_tensor("psbf", [128, 1024], BF16)).ap()
    b["ps64"] = ctx.enter_context(nc.psum_tensor("ps64", [128, 64], F32)).ap()
    return NS(b)


_NC_CACHE = {}


def build_launch_a(dbg=False):
    key = ("a", dbg)
    if key in _NC_CACHE:
        return _NC_CACHE[key]
    nc = bass.Bass("TRN2", target_bir_lowering=False, debug=False)
    io = {}
    io["x_own"] = nc.dram_tensor("x_own", [SC, D], F32, kind="ExternalInput").ap()
    io["kern"] = nc.dram_tensor("kern", [3, D, D], F32, kind="ExternalInput").ap()
    eye_in = nc.dram_tensor("eye_in", [128, 128], BF16, kind="ExternalInput").ap()
    io["kv_in"] = nc.dram_tensor("kv_in", [128, 32768], BF16, kind="Internal").ap()
    io["kv_g"] = nc.dram_tensor("kv_g", [512, 32768], BF16, kind="Internal").ap()
    io["x1_ext"] = nc.dram_tensor("x1", [SC, D], F32, kind="ExternalOutput").ap()
    if dbg:
        io["qt_d"] = nc.dram_tensor("qt_d", [128, 16384], BF16, kind="ExternalOutput").ap()
        io["kv_d"] = nc.dram_tensor("kv_d", [128, 32768], BF16, kind="ExternalOutput").ap()
        io["kvg_d"] = nc.dram_tensor("kvg_d", [512, 32768], BF16, kind="ExternalOutput").ap()
        io["dn_d"] = nc.dram_tensor("dn_d", [128, 64], F32, kind="ExternalOutput").ap()
    ctx = ExitStack()
    with ctx:
        bufs = alloc_common(nc, ctx)
        sem = make_sems(nc, ctx, A_SEMS)
        ops = build_stage_a(sem, bufs, io, store_out=True, dbg=dbg)
        block = ctx.enter_context(nc.Block())

        @block.sync
        def _(e):
            e.dma_start(out=bufs.eye, in_=eye_in).then_inc(sem["aEYE"], 16)
            ops["sync"](e)

        @block.vector
        def _(e):
            ops["vector"](e)

        @block.scalar
        def _(e):
            ops["scalar"](e)

        @block.tensor
        def _(e):
            ops["tensor"](e)

        @block.gpsimd
        def _(e):
            ops["gpsimd"](e)
    return nc


C_SEMS = ["cXR", "cS2", "cVR", "cRS", "cLX", "cDV", "cZ", "cTR", "cLT",
          "cWL", "cWC", "cWT", "cWE", "cGM", "cGE", "cSS", "cDM", "cBP",
          "cYM", "cYE", "cX2", "cRES", "cOUT"] \
    + [f"cCA{i}" for i in range(8)] + [f"cCB{i}" for i in range(8)]

C2 = 0.02 / 32.0
C3 = 0.02 / 32.0


def build_stage_c(sem, bufs, io, store_out, resident_in, gate_in=None, trunc=None):
    R = bufs.R_f32
    ln2T = bufs.A_bf
    OT = bufs.B_bf
    GT = bufs.C_bf
    WQ = bufs.W_bf
    WK = bufs.pT
    xa = bufs.xa
    lnbf = bufs.lnbf
    sqscr = bufs.sqscr
    st = bufs.stats
    epsc = bufs.epsc
    eye = bufs.eye
    ps = bufs.ps
    psbf = bufs.psbf
    wbf = bufs.wbf

    w = io["w"]
    mask_lt = bufs.mask_lt
    mask_prev = bufs.mask_prev
    first_m = bufs.first_m

    pay = xa[:, 0:5]
    pay_g = bufs.ccs[:, 0:20].rearrange("p (j f) -> p j f", f=5)
    carryK = xa[:, 32:33]
    carryKV = xa[:, 33:34]
    vprev = xa[:, 34:35]
    qprev = xa[:, 35:36]
    gprev = xa[:, 36:37]
    t_a = xa[:, 38:39]
    t_b = xa[:, 39:40]
    oarprev = xa[:, 40:41]
    pay2 = xa[:, 42:43]
    pay2_g = bufs.ccs[:, 24:28]
    carry2 = xa[:, 48:49]
    bq1T = xa[:, 56:64]
    bk1T = xa[:, 64:72]
    bk2T = xa[:, 72:80]
    bq1c = xa[:, 80:88]
    bk2pc = xa[:, 88:96]
    bprow = xa[0:1, 96:1120]
    wstg = xa[:, 1152:1408]
    bprow_bf = ln2T[0:1, 0:1024]
    ones1_bf = ln2T[0:1, 1024:1152]

    def slot(i):
        return R[:, 2048 * i:2048 * i + 2048]

    def stc(t, i):
        o = 16 * (t % 2) + i
        return st[:, o:o + 1]

    def gate(e):
        if gate_in is not None:
            e.wait_ge(gate_in[0], gate_in[1])

    NWC = 7
    WMATS = ["Wk2", "Wq1", "Wk1", "Wproj"]
    WDST = [WQ, WQ, WK, WQ]

    tc = TokenChain(sem["cSS"], sem["cDM"])

    def _chain():
        tc.add("vector", lambda e: e.tensor_scalar(
            out=bq1c, in0=bq1T, scalar1=0.5, scalar2=None, op0=ALU.mult))
        tc.add("vector", lambda e: e.tensor_scalar(
            out=bk2pc, in0=bk2T, scalar1=C2 / 2.0, scalar2=None, op0=ALU.mult))
        sK, sKV, sQ, sKc, sKVc, sS, sOar, sE = [slot(i) for i in range(8)]
        for dc in range(8):
            for wmat, outslot, act in (
                    (WK, 0, "exp"), (WQ, 2, "tanh")):
                for s4 in range(4):
                    def mm(e, wmat=wmat, dc=dc, s4=s4):
                        last = None
                        for di in range(8):
                            last = e.matmul(
                                ps[s4 % 2][:],
                                wmat[:, di * 1024 + dc * 128:di * 1024 + dc * 128 + 128],
                                ln2T[:, di * 2048 + 512 * s4:di * 2048 + 512 * s4 + 512],
                                start=(di == 0), stop=(di == 7))
                        return last
                    tc.add("tensor", mm)
                    tc.add("vector", lambda e: e.nop())
                    if act == "exp":
                        tc.add("scalar", lambda e, dc=dc, s4=s4, o=outslot:
                               e.activation(out=slot(o)[:, 512 * s4:512 * s4 + 512],
                                            in_=ps[s4 % 2][:], func=AF.Exp,
                                            bias=bk1T[:, dc:dc + 1]))
                    else:
                        tc.add("scalar", lambda e, dc=dc, s4=s4, o=outslot:
                               e.activation(out=slot(o)[:, 512 * s4:512 * s4 + 512],
                                            in_=ps[s4 % 2][:], func=AF.Tanh,
                                            scale=0.5, bias=bq1c[:, dc:dc + 1]))
            tc.add("vector", lambda e: e.tensor_scalar(
                out=sQ, in0=sQ, scalar1=0.5, scalar2=0.5, op0=ALU.mult,
                op1=ALU.add))
            tc.add("vector", lambda e, dc=dc: e.tensor_mul(
                out=sKV, in0=sK, in1=ln2T[:, dc * 2048:dc * 2048 + 2048]))
            tc.add("vector", lambda e: e.tensor_tensor_scan(
                out=sKc, data0=sK, data1=sK, initial=0.0, op0=ALU.add,
                op1=ALU.bypass))
            tc.add("vector", lambda e: e.tensor_tensor_scan(
                out=sKVc, data0=sKV, data1=sKV, initial=0.0, op0=ALU.add,
                op1=ALU.bypass))
            tc.mark(f"scans{dc}")
            tc.add("gpsimd", lambda e, dc=dc: e.dma_start(
                out=io["cc1_in"][dc][:, 0:1], in_=sKc[:, 2047:2048],
                allow_slow_non_contiguous=True), dma=True)
            tc.add("gpsimd", lambda e, dc=dc: e.dma_start(
                out=io["cc1_in"][dc][:, 1:2], in_=sKVc[:, 2047:2048],
                allow_slow_non_contiguous=True), dma=True)
            tc.add("vector", lambda e, dc=dc: e.tensor_copy(
                out=pay[:, 2:3], in_=ln2T[:, dc * 2048 + 2047:dc * 2048 + 2048]))
            tc.add("vector", lambda e, dc=dc: e.tensor_copy(
                out=pay[:, 4:5], in_=GT[:, dc * 2048 + 2047:dc * 2048 + 2048]))
            tc.add("gpsimd", lambda e, dc=dc: e.dma_start(
                out=io["cc1_in"][dc][:, 3:4], in_=sQ[:, 2047:2048],
                allow_slow_non_contiguous=True), dma=True)
            tc.add("gpsimd", lambda e, dc=dc: e.dma_start(
                out=io["cc1_in"][dc][:, 2:3], in_=pay[:, 2:3],
                allow_slow_non_contiguous=True), dma=True)
            tc.add("gpsimd", lambda e, dc=dc: e.dma_start(
                out=io["cc1_in"][dc][:, 4:5], in_=pay[:, 4:5],
                allow_slow_non_contiguous=True), dma=True)
            tc.add("gpsimd", lambda e, dc=dc: e.collective_compute(
                "AllGather", ALU.bypass, replica_groups=RG,
                ins=[io["cc1_in"][dc].opt()], outs=[io["cc1_out"][dc].opt()]),
                   cc=sem[f"cCA{dc}"])
            tc.add("gpsimd", lambda e, dc=dc: e.dma_start(
                out=pay_g, in_=io["cc1_out"][dc].rearrange("(j p) f -> p j f",
                                                           p=128)), dma=True)
            for j in range(4):
                def acc1(e, j=j):
                    if j == 0:
                        e.tensor_scalar(out=carryK, in0=pay_g[:, 0, 0:1],
                                        scalar1=mask_lt[:, 0:1], scalar2=None,
                                        op0=ALU.mult)
                        e.tensor_scalar(out=carryKV, in0=pay_g[:, 0, 1:2],
                                        scalar1=mask_lt[:, 0:1], scalar2=None,
                                        op0=ALU.mult)
                        e.tensor_scalar(out=vprev, in0=pay_g[:, 0, 2:3],
                                        scalar1=mask_prev[:, 0:1], scalar2=None,
                                        op0=ALU.mult)
                        e.tensor_scalar(out=qprev, in0=pay_g[:, 0, 3:4],
                                        scalar1=mask_prev[:, 0:1], scalar2=None,
                                        op0=ALU.mult)
                        return e.tensor_scalar(out=gprev, in0=pay_g[:, 0, 4:5],
                                               scalar1=mask_prev[:, 0:1],
                                               scalar2=None, op0=ALU.mult)
                    e.scalar_tensor_tensor(out=carryK, in0=pay_g[:, j, 0:1],
                                           scalar=mask_lt[:, j:j + 1], in1=carryK,
                                           op0=ALU.mult, op1=ALU.add)
                    e.scalar_tensor_tensor(out=carryKV, in0=pay_g[:, j, 1:2],
                                           scalar=mask_lt[:, j:j + 1], in1=carryKV,
                                           op0=ALU.mult, op1=ALU.add)
                    e.scalar_tensor_tensor(out=vprev, in0=pay_g[:, j, 2:3],
                                           scalar=mask_prev[:, j:j + 1], in1=vprev,
                                           op0=ALU.mult, op1=ALU.add)
                    e.scalar_tensor_tensor(out=qprev, in0=pay_g[:, j, 3:4],
                                           scalar=mask_prev[:, j:j + 1], in1=qprev,
                                           op0=ALU.mult, op1=ALU.add)
                    return e.scalar_tensor_tensor(out=gprev, in0=pay_g[:, j, 4:5],
                                                  scalar=mask_prev[:, j:j + 1],
                                                  in1=gprev, op0=ALU.mult,
                                                  op1=ALU.add)
                tc.add("vector", acc1)
            tc.add("vector", lambda e: e.tensor_scalar(
                out=t_a, in0=carryK, scalar1=1e-6, scalar2=None, op0=ALU.add))
            tc.add("vector", lambda e: e.reciprocal(out=t_b, in_=t_a))
            tc.add("vector", lambda e: e.tensor_mul(out=t_a, in0=carryKV,
                                                    in1=t_b))
            tc.add("vector", lambda e: e.tensor_mul(out=oarprev, in0=t_a,
                                                    in1=qprev))
            tc.add("vector", lambda e: e.tensor_scalar(
                out=sKc, in0=sKc, scalar1=carryK, scalar2=1e-6, op0=ALU.add,
                op1=ALU.add))
            tc.add("vector", lambda e: e.tensor_scalar(
                out=sKVc, in0=sKVc, scalar1=carryKV, scalar2=None, op0=ALU.add))
            tc.add("vector", lambda e: e.reciprocal(out=sS, in_=sKc))
            tc.add("vector", lambda e: e.tensor_mul(out=sKV, in0=sKVc, in1=sS))
            tc.add("vector", lambda e: e.tensor_mul(out=sOar, in0=sKV, in1=sQ))
            tc.add("vector", lambda e, dc=dc: e.tensor_sub(
                out=sE[:, 0:1], in_=None, in0=ln2T[:, dc * 2048:dc * 2048 + 1],
                in1=oarprev) if False else e.tensor_sub(
                out=sE[:, 0:1], in0=ln2T[:, dc * 2048:dc * 2048 + 1],
                in1=oarprev))
            tc.add("vector", lambda e: e.tensor_scalar(
                out=sE[:, 0:1], in0=sE[:, 0:1], scalar1=first_m[:, 0:1],
                scalar2=None, op0=ALU.mult))
            tc.add("vector", lambda e, dc=dc: e.tensor_sub(
                out=sE[:, 1:2048], in0=ln2T[:, dc * 2048 + 1:dc * 2048 + 2048],
                in1=sOar[:, 0:2047]))
            tc.add("scalar", lambda e, dc=dc: e.activation(
                out=sS[:, 0:1], in_=gprev, func=AF.Tanh, scale=C2 / 2.0,
                bias=bk2pc[:, dc:dc + 1]))
            tc.add("scalar", lambda e, dc=dc: e.activation(
                out=sS[:, 1:2048], in_=GT[:, dc * 2048:dc * 2048 + 2047],
                func=AF.Tanh, scale=C2 / 2.0, bias=bk2pc[:, dc:dc + 1]))
            tc.add("vector", lambda e: e.tensor_scalar(
                out=sS, in0=sS, scalar1=0.5, scalar2=0.5, op0=ALU.mult,
                op1=ALU.add))
            tc.add("vector", lambda e: e.tensor_mul(out=sE, in0=sS, in1=sE))
            tc.add("vector", lambda e: e.tensor_tensor_scan(
                out=sK, data0=sE, data1=sE, initial=0.0, op0=ALU.add,
                op1=ALU.bypass))
            tc.add("gpsimd", lambda e, dc=dc: e.dma_start(
                out=io["cc2_in"][dc], in_=sK[:, 2047:2048],
                allow_slow_non_contiguous=True), dma=True)
            tc.add("gpsimd", lambda e, dc=dc: e.collective_compute(
                "AllGather", ALU.bypass, replica_groups=RG,
                ins=[io["cc2_in"][dc].opt()], outs=[io["cc2_out"][dc].opt()]),
                   cc=sem[f"cCB{dc}"])
            tc.add("gpsimd", lambda e, dc=dc: e.dma_start(
                out=pay2_g, in_=io["cc2_out"][dc].rearrange("(j p) f -> p (j f)",
                                                            p=128),
                allow_slow_non_contiguous=True), dma=True)
            for j in range(4):
                def acc3(e, j=j):
                    if j == 0:
                        return e.tensor_scalar(out=carry2, in0=pay2_g[:, 0:1],
                                               scalar1=mask_lt[:, 0:1],
                                               scalar2=None, op0=ALU.mult)
                    return e.scalar_tensor_tensor(out=carry2,
                                                  in0=pay2_g[:, j:j + 1],
                                                  scalar=mask_lt[:, j:j + 1],
                                                  in1=carry2, op0=ALU.mult,
                                                  op1=ALU.add)
                tc.add("vector", acc3)
            tc.add("vector", lambda e: e.tensor_scalar(
                out=sK, in0=sK, scalar1=carry2, scalar2=None, op0=ALU.add))
            tc.add("vector", lambda e: e.scalar_tensor_tensor(
                out=sS[:, 0:1], in0=sK[:, 0:1], scalar=C3, in1=qprev,
                op0=ALU.mult, op1=ALU.mult))
            tc.add("vector", lambda e: e.scalar_tensor_tensor(
                out=sS[:, 1:2048], in0=sK[:, 1:2048], scalar=C3,
                in1=sQ[:, 0:2047], op0=ALU.mult, op1=ALU.mult))
            tc.add("vector", lambda e, dc=dc: e.tensor_add(
                out=OT[:, dc * 2048:dc * 2048 + 2048], in0=sOar, in1=sS))
            tc.mark(f"done{dc}")
    _chain()
    TCK = tc.k

    def sync_ops(e):
        gate(e)
        e.dma_start(out=bq1T, in_=w["bq1"].rearrange("(c p) -> p c", p=128),
                    allow_slow_non_contiguous=True).then_inc(sem["cXR"], 16)
        e.wait_ge(sem["cXR"], 16)
        e.dma_start(out=bk1T, in_=w["bk1"].rearrange("(c p) -> p c", p=128),
                    allow_slow_non_contiguous=True).then_inc(sem["cXR"], 16)
        e.wait_ge(sem["cXR"], 32)
        e.dma_start(out=bk2T, in_=w["bk2"].rearrange("(c p) -> p c", p=128),
                    allow_slow_non_contiguous=True).then_inc(sem["cXR"], 16)
        e.wait_ge(sem["cXR"], 48)
        e.dma_start(out=bprow, in_=w["bproj"][None, :]).then_inc(sem["cXR"], 16)
        e.wait_ge(sem["cXR"], 64)
        e.dma_start(out=mask_lt, in_=io["mask_lt"]).then_inc(sem["cXR"], 16)
        e.wait_ge(sem["cXR"], 80)
        e.dma_start(out=mask_prev, in_=io["mask_prev"]).then_inc(sem["cXR"], 16)
        e.wait_ge(sem["cXR"], 96)
        e.dma_start(out=first_m, in_=io["first_m"]).then_inc(sem["cXR"], 16)
        if not resident_in:
            xr = io["x_in"].rearrange("(t p) e -> t p e", p=128)
            for t in range(NT):
                e.wait_ge(sem["cXR"], 16 * (NWC + t))
                e.dma_start(out=R[:, 1024 * t:1024 * t + 1024], in_=xr[t]) \
                    .then_inc(sem["cXR"], 16)
        else:
            e.wait_ge(sem["cZ"], NT)
            e.wait_ge(sem["cS2"], NT)
            xo = io["x2_sp"].rearrange("(t p) e -> t p e", p=128)
            for t in range(NT):
                e.dma_start(out=xo[t], in_=R[:, 1024 * t:1024 * t + 1024]) \
                    .then_inc(sem["cX2"], 16)
            e.wait_ge(sem["cX2"], 16 * NT)
        if trunc == -1:
            return
        nmat = 4 if trunc is None else (1 if trunc == -4 else 3)
        for m in range(nmat):
            wm = w[WMATS[m]]
            for i in range(64):
                eo, do = i // 8, i % 8
                n = m * 64 + i
                if n >= 1:
                    e.wait_ge(sem["cWL"], 16 * n)
                if n >= 2:
                    e.wait_ge(sem["cWC"], n - 1)
                e.dma_start(out=wstg[:, 128 * (n % 2):128 * (n % 2) + 128],
                            in_=wm[128 * eo:128 * eo + 128, 128 * do:128 * do + 128]) \
                    .then_inc(sem["cWL"], 16)
        if trunc is not None:
            return
        xsrc = io["x_in"] if not resident_in else io["x2_sp"]
        xr2 = xsrc.rearrange("(t p) e -> t p e", p=128)
        e.wait_ge(sem["cSS"], TCK)
        e.wait_ge(sem["cWC"], 256)
        e.wait_ge(sem["cBP"], 1)
        base = 16 * NT if resident_in else 0
        for t in range(NT):
            if t >= 1:
                e.wait_ge(sem["cX2"], base + 16 * t)
            if t >= 2:
                e.wait_ge(sem["cRES"], t - 1)
            e.dma_start(out=xa[:, 1024 * (t % 2):1024 * (t % 2) + 1024],
                        in_=xr2[t]).then_inc(sem["cX2"], 16)

    def vector_ops(e):
        gate(e)
        for t in range(NT):
            if not resident_in:
                e.wait_ge(sem["cXR"], 16 * (NWC + t + 1))
            else:
                e.wait_ge(sem["cXR"], 16 * NWC)
            Rt = R[:, 1024 * t:1024 * t + 1024]
            e.reduce_sum(out=stc(t, 0), in_=Rt, axis=AX)
            e.wait_ge(sem["cS2"], t + 1)
            e.tensor_add(out=stc(t, 2), in0=stc(t, 2), in1=stc(t, 9)) \
                .then_inc(sem["cDV"], 1)
            e.wait_ge(sem["cDV"], 2 * t + 1)
            e.tensor_scalar(out=stc(t, 1), in0=stc(t, 0), scalar1=1.0 / D,
                            scalar2=None, op0=ALU.mult)
            e.tensor_scalar(out=stc(t, 3), in0=stc(t, 2), scalar1=1.0 / D,
                            scalar2=None, op0=ALU.mult)
            e.scalar_tensor_tensor(out=stc(t, 4), in0=stc(t, 0),
                                   scalar=-1.0 / (D * D), in1=stc(t, 0),
                                   op0=ALU.mult, op1=ALU.mult) \
                .then_inc(sem["cDV"], 1)
            e.wait_ge(sem["cDV"], 2 * t + 2)
            e.tensor_add(out=stc(t, 5), in0=stc(t, 3), in1=stc(t, 4)) \
                .then_inc(sem["cVR"], 1)
            e.wait_ge(sem["cRS"], t + 1)
            e.tensor_scalar(out=lnbf[:, 1024 * (t % 2):1024 * (t % 2) + 1024],
                            in0=Rt, scalar1=stc(t, 1), scalar2=stc(t, 7),
                            op0=ALU.subtract, op1=ALU.mult) \
                .then_inc(sem["cZ"], 1)
        ncast_pre = {None: 192, -1: 0, -3: 256, -4: 64}.get(trunc, 192)
        for n in range(ncast_pre):
            e.wait_ge(sem["cWL"], 16 * (n + 1))
            if n >= 2 and trunc != -3:
                e.wait_ge(sem["cWT"], n - 1)
            e.tensor_copy(out=wbf[:, 128 * (n % 2):128 * (n % 2) + 128],
                          in_=wstg[:, 128 * (n % 2):128 * (n % 2) + 128]) \
                .then_inc(sem["cWC"], 1)
        e.wait_ge(sem["cXR"], 16 * NWC)
        if trunc is None or trunc >= 0:
            e.wait_ge(sem["cGE"], 32)
        if resident_in:
            e.wait_ge(sem["cX2"], 16 * NT)
        tc.emit("vector", e, limit=max(trunc, 0) if trunc is not None else None)
        if trunc is not None:
            return
        for n in range(192, 256):
            e.wait_ge(sem["cWL"], 16 * (n + 1))
            e.wait_ge(sem["cWT"], n - 1)
            e.tensor_copy(out=wbf[:, 128 * (n % 2):128 * (n % 2) + 128],
                          in_=wstg[:, 128 * (n % 2):128 * (n % 2) + 128]) \
                .then_inc(sem["cWC"], 1)
        e.tensor_copy(out=bprow_bf, in_=bprow)
        e.memset(ones1_bf, 1.0).then_inc(sem["cBP"], 1)
        for t in range(NT):
            Rt = R[:, 1024 * t:1024 * t + 1024]
            e.wait_ge(sem["cYE"], 2 * t + 2)
            e.reduce_sum(out=stc(t, 0), in_=Rt, axis=AX)
            e.wait_ge(sem["cS2"], NT + t + 1)
            e.tensor_add(out=stc(t, 2), in0=stc(t, 2), in1=stc(t, 9)) \
                .then_inc(sem["cDV"], 1)
            e.wait_ge(sem["cDV"], 2 * NT + 3 * t + 1)
            e.tensor_scalar(out=stc(t, 1), in0=stc(t, 0), scalar1=1.0 / D,
                            scalar2=None, op0=ALU.mult)
            e.tensor_scalar(out=stc(t, 3), in0=stc(t, 2), scalar1=1.0 / D,
                            scalar2=None, op0=ALU.mult)
            e.scalar_tensor_tensor(out=stc(t, 4), in0=stc(t, 0),
                                   scalar=-1.0 / (D * D), in1=stc(t, 0),
                                   op0=ALU.mult, op1=ALU.mult) \
                .then_inc(sem["cDV"], 1)
            e.wait_ge(sem["cDV"], 2 * NT + 3 * t + 2)
            e.tensor_add(out=stc(t, 5), in0=stc(t, 3), in1=stc(t, 4)) \
                .then_inc(sem["cVR"], 1)
            e.wait_ge(sem["cRS"], NT + t + 1)
            e.tensor_scalar(out=Rt, in0=Rt, scalar1=stc(t, 1),
                            scalar2=stc(t, 7), op0=ALU.subtract, op1=ALU.mult) \
                .then_inc(sem["cDV"], 1)
            e.wait_ge(sem["cDV"], 2 * NT + 3 * t + 3)
            base = 16 * NT if resident_in else 0
            e.wait_ge(sem["cX2"], base + 16 * (t + 1))
            e.tensor_add(out=Rt, in0=Rt,
                         in1=xa[:, 1024 * (t % 2):1024 * (t % 2) + 1024]) \
                .then_inc(sem["cRES"], 1)

    def scalar_ops(e):
        gate(e)
        for t in range(NT):
            if not resident_in:
                e.wait_ge(sem["cXR"], 16 * (NWC + t + 1))
            else:
                e.wait_ge(sem["cXR"], 16 * NWC)
            Rt = R[:, 1024 * t:1024 * t + 1024]
            e.activation(out=sqscr[:], in_=Rt[:, 0:512], func=AF.Square,
                         accum_out=stc(t, 2))
            e.activation(out=sqscr[:], in_=Rt[:, 512:1024], func=AF.Square,
                         accum_out=stc(t, 9)).then_inc(sem["cS2"], 1)
            e.wait_ge(sem["cVR"], t + 1)
            e.activation(out=stc(t, 6), in_=stc(t, 5), func=AF.Ln,
                         bias=epsc[:]).then_inc(sem["cLX"], 1)
            e.wait_ge(sem["cLX"], t + 1)
            e.activation(out=stc(t, 7), in_=stc(t, 6), func=AF.Exp,
                         scale=-0.5).then_inc(sem["cRS"], 1)
            e.wait_ge(sem["cTR"], t + 1)
            dst = ln2T.rearrange("p (c s) -> p c s", c=8)[:, :, 128 * t:128 * t + 128]
            e.activation(out=dst, in_=psbf.rearrange("p (c s) -> p c s", c=8),
                         func=AF.Copy).then_inc(sem["cLT"], 1)
        for n in range(0 if trunc in (-1, -3) else 64):
            e.wait_ge(sem["cWT"], n + 1)
            eo, do = n // 8, n % 8
            e.activation(out=WQ[:, do * 1024 + eo * 128:do * 1024 + eo * 128 + 128],
                         in_=psbf[:, 0:128], func=AF.Copy).then_inc(sem["cWE"], 1)
        for k in range(0 if (trunc is not None and trunc < 0) else 32):
            e.wait_ge(sem["cGM"], k + 1)
            dc, s4 = k // 4, k % 4
            e.activation(out=GT[:, dc * 2048 + 512 * s4:dc * 2048 + 512 * s4 + 512],
                         in_=ps[k % 2][:], func=AF.Copy).then_inc(sem["cGE"], 1)
        for m in ([] if trunc in (-1, -3, -4) else range(1, 3)):
            for i in range(64):
                n = m * 64 + i
                eo, do = i // 8, i % 8
                if m == 1 and i == 0:
                    e.wait_ge(sem["cGM"], 32)
                e.wait_ge(sem["cWT"], n + 1)
                e.activation(out=WDST[m][:, do * 1024 + eo * 128:do * 1024 + eo * 128 + 128],
                             in_=psbf[:, 0:128], func=AF.Copy).then_inc(sem["cWE"], 1)
        if resident_in:
            e.wait_ge(sem["cX2"], 16 * NT)
        tc.emit("scalar", e, limit=max(trunc, 0) if trunc is not None else None)
        if trunc is not None:
            return
        for i in range(64):
            n = 192 + i
            eo, do = i // 8, i % 8
            e.wait_ge(sem["cWT"], n + 1)
            e.activation(out=WQ[:, do * 1024 + eo * 128:do * 1024 + eo * 128 + 128],
                         in_=psbf[:, 0:128], func=AF.Copy).then_inc(sem["cWE"], 1)
        for t in range(NT):
            for hh in range(2):
                n = 2 * t + hh
                e.wait_ge(sem["cYM"], n + 1)
                e.activation(out=R[:, 1024 * t + 512 * hh:1024 * t + 512 * hh + 512],
                             in_=ps[2 + n % 2][:], func=AF.Copy) \
                    .then_inc(sem["cYE"], 1)
            e.wait_ge(sem["cYE"], 2 * t + 2)
            Rt = R[:, 1024 * t:1024 * t + 1024]
            e.activation(out=sqscr[:], in_=Rt[:, 0:512], func=AF.Square,
                         accum_out=stc(t, 2))
            e.activation(out=sqscr[:], in_=Rt[:, 512:1024], func=AF.Square,
                         accum_out=stc(t, 9)).then_inc(sem["cS2"], 1)
            e.wait_ge(sem["cVR"], NT + t + 1)
            e.activation(out=stc(t, 6), in_=stc(t, 5), func=AF.Ln,
                         bias=epsc[:]).then_inc(sem["cLX"], 1)
            e.wait_ge(sem["cLX"], NT + t + 1)
            e.activation(out=stc(t, 7), in_=stc(t, 6), func=AF.Exp,
                         scale=-0.5).then_inc(sem["cRS"], 1)

    def tensor_ops(e):
        gate(e)
        for t in range(NT):
            e.wait_ge(sem["cZ"], t + 1)
            if t >= 1:
                e.wait_ge(sem["cLT"], t)
            srcz = lnbf[:, 1024 * (t % 2):1024 * (t % 2) + 1024]
            last = None
            for dc in range(8):
                last = e.transpose(psbf[:, 128 * dc:128 * dc + 128],
                                   srcz[:, 128 * dc:128 * dc + 128], eye[:])
            last.then_inc(sem["cTR"], 1)
        e.wait_ge(sem["cLT"], NT)
        for n in range(0 if trunc in (-1, -3) else 64):
            e.wait_ge(sem["cWC"], n + 1)
            if n >= 1:
                e.wait_ge(sem["cWE"], n)
            e.transpose(psbf[:, 0:128], wbf[:, 128 * (n % 2):128 * (n % 2) + 128],
                        eye[:]).then_inc(sem["cWT"], 1)
        if trunc is None or trunc >= 0:
            e.wait_ge(sem["cWE"], 64)
        for k in range(0 if (trunc is not None and trunc < 0) else 32):
            dc, s4 = k // 4, k % 4
            if k >= 2:
                e.wait_ge(sem["cGE"], k - 1)
            last = None
            for di in range(8):
                last = e.matmul(
                    ps[k % 2][:],
                    WQ[:, di * 1024 + dc * 128:di * 1024 + dc * 128 + 128],
                    ln2T[:, di * 2048 + 512 * s4:di * 2048 + 512 * s4 + 512],
                    start=(di == 0), stop=(di == 7))
            last.then_inc(sem["cGM"], 1)
        for n in ([] if trunc in (-1, -3, -4) else range(64, 192)):
            e.wait_ge(sem["cWC"], n + 1)
            e.wait_ge(sem["cWE"], n)
            e.transpose(psbf[:, 0:128], wbf[:, 128 * (n % 2):128 * (n % 2) + 128],
                        eye[:]).then_inc(sem["cWT"], 1)
        if trunc is None or trunc > 0:
            e.wait_ge(sem["cWE"], 192)
        tc.emit("tensor", e, limit=max(trunc, 0) if trunc is not None else None)
        if trunc is not None:
            return
        for n in range(192, 256):
            e.wait_ge(sem["cWC"], n + 1)
            e.wait_ge(sem["cWE"], n)
            e.transpose(psbf[:, 0:128], wbf[:, 128 * (n % 2):128 * (n % 2) + 128],
                        eye[:]).then_inc(sem["cWT"], 1)
        e.wait_ge(sem["cSS"], TCK)
        e.wait_ge(sem["cWE"], 256)
        e.wait_ge(sem["cBP"], 1)
        for t in range(NT):
            for hh in range(2):
                n = 2 * t + hh
                if n >= 2:
                    e.wait_ge(sem["cYE"], n - 1)
                pst = ps[2 + n % 2]
                last = None
                for di in range(8):
                    last = e.matmul(
                        pst[:], OT[:, di * 2048 + 128 * t:di * 2048 + 128 * t + 128],
                        WQ[:, di * 1024 + 512 * hh:di * 1024 + 512 * hh + 512],
                        start=(di == 0), stop=False)
                e.matmul(pst[:], ones1_bf,
                         bprow_bf[:, 512 * hh:512 * hh + 512],
                         start=False, stop=True).then_inc(sem["cYM"], 1)

    def gpsimd_ops(e):
        gate(e)
        tc.emit("gpsimd", e, limit=max(trunc, 0) if trunc is not None else None)
        if trunc is not None:
            kk, mm, lcc = tc.counts_at(max(trunc, 0))
            if kk > 0:
                e.wait_ge(sem["cSS"], kk)
            if mm > 0:
                e.wait_ge(sem["cDM"], 16 * mm)
            if lcc is not None:
                e.wait_ge(lcc, 1)
            e.dma_start(out=io["r_d"], in_=R[:]).then_inc(sem["cOUT"], 16)
            e.dma_start(out=io["ot_d"], in_=OT[:]).then_inc(sem["cOUT"], 16)
            e.dma_start(out=io["xa_d"], in_=xa[:]).then_inc(sem["cOUT"], 16)
            for dcq in range(8):
                e.dma_start(out=io["cc_d"][:, 5 * dcq:5 * dcq + 5],
                            in_=io["cc1_out"][dcq]).then_inc(sem["cOUT"], 16)
                e.dma_start(out=io["cc_d"][:, 40 + dcq:40 + dcq + 1],
                            in_=io["cc2_out"][dcq],
                            allow_slow_non_contiguous=True).then_inc(sem["cOUT"], 16)
            e.wait_ge(sem["cOUT"], 48 + 256)
            return
        if store_out:
            e.wait_ge(sem["cRES"], NT)
            xo = io["x_out"].rearrange("(q p) e -> q p e", p=128)
            for t in range(NT):
                e.dma_start(out=xo[t], in_=R[:, 1024 * t:1024 * t + 1024]) \
                    .then_inc(sem["cOUT"], 16)
            e.wait_ge(sem["cOUT"], 16 * NT)

    return dict(sync=sync_ops, vector=vector_ops, scalar=scalar_ops,
                tensor=tensor_ops, gpsimd=gpsimd_ops, TCK=TCK, tc=tc,
                marks=tc.marks)


D_SEMS = ["dXR", "dWL", "dWC", "dS2", "dVR", "dRS", "dLX", "dDV", "dZ",
          "dTR", "dLT", "dPM", "dPEv", "dHM", "dGA", "dGQ", "dGW", "dGT",
          "dHE", "dRM", "dRE", "dYM", "dYE", "dRES", "dOUT", "dONE"]

GELU_C = float(np.sqrt(2.0 / np.pi))


def build_stage_d(sem, bufs, io, store_out, resident_in, gate_in=None):
    R = bufs.R_f32
    ln3T = bufs.A_bf
    HT = bufs.B_bf
    u2s = bufs.C_bf
    v1s = bufs.W_bf
    pTb = bufs.pT
    xa = bufs.xa
    lnbf = bufs.lnbf
    sqscr = bufs.sqscr
    st = bufs.stats
    epsc = bufs.epsc
    eye = bufs.eye
    ps = bufs.ps
    psbf = bufs.psbf

    u1s = pTb[:, 0:2048]
    v2s = pTb[:, 2048:4096]
    b2row = pTb[0:1, 4096:5120]
    PTs = [pTb[:, 5120:6144], pTb[:, 6144:7168]]
    RTs = pTb[:, 7168:8192]
    ones1_bf = lnbf[0:1, 0:128]
    b1T = bufs.dn_sb[:, 0:32]

    w = io["w"]

    def stc(t, i):
        o = 16 * (t % 2) + i
        return st[:, o:o + 1]

    def gate(e):
        if gate_in is not None:
            e.wait_ge(gate_in[0], gate_in[1])

    def sync_ops(e):
        gate(e)
        e.dma_start(out=b1T, in_=w["b1"].rearrange("(c p) -> p c", p=128),
                    allow_slow_non_contiguous=True).then_inc(sem["dXR"], 16)
        if not resident_in:
            xr = io["x_in"].rearrange("(t p) e -> t p e", p=128)
            for t in range(NT):
                e.wait_ge(sem["dXR"], 16 * (t + 1))
                e.dma_start(out=R[:, 1024 * t:1024 * t + 1024], in_=xr[t]) \
                    .then_inc(sem["dXR"], 16)
        loads = []
        loads.append(w["U1"].rearrange("(c p) r -> p c r", p=128))
        for rc in range(2):
            for hh in range(2):
                loads.append(w["V1"][128 * rc:128 * rc + 128,
                                     2048 * hh:2048 * hh + 2048])
        for c0 in range(4):
            loads.append(w["U2"][1024 * c0:1024 * (c0 + 1)]
                         .rearrange("(c p) r -> p c r", p=128))
        loads.append(w["V2"].rearrange("(c p) e -> p c e", p=128))
        loads.append(w["b2"][None, :])
        for k, srcap in enumerate(loads):
            if k >= 1:
                e.wait_ge(sem["dWC"], k)
            if k == 10:
                dst = xa[0:1, 0:1024]
            elif len(srcap.shape) == 3:
                dst = xa[:, 0:srcap.shape[1] * srcap.shape[2]] \
                    .rearrange("p (c r) -> p c r", c=srcap.shape[1])
            else:
                dst = xa[:, 0:srcap.shape[1]]
            e.dma_start(out=dst, in_=srcap).then_inc(sem["dWL"], 16)

    def vector_ops(e):
        gate(e)
        for t in range(NT):
            if not resident_in:
                e.wait_ge(sem["dXR"], 16 * (t + 2))
            else:
                e.wait_ge(sem["dXR"], 16)
            Rt = R[:, 1024 * t:1024 * t + 1024]
            e.reduce_sum(out=stc(t, 0), in_=Rt, axis=AX)
            e.wait_ge(sem["dS2"], t + 1)
            e.tensor_add(out=stc(t, 2), in0=stc(t, 2), in1=stc(t, 9)) \
                .then_inc(sem["dDV"], 1)
            e.wait_ge(sem["dDV"], 2 * t + 1)
            e.tensor_scalar(out=stc(t, 1), in0=stc(t, 0), scalar1=1.0 / D,
                            scalar2=None, op0=ALU.mult)
            e.tensor_scalar(out=stc(t, 3), in0=stc(t, 2), scalar1=1.0 / D,
                            scalar2=None, op0=ALU.mult)
            e.scalar_tensor_tensor(out=stc(t, 4), in0=stc(t, 0),
                                   scalar=-1.0 / (D * D), in1=stc(t, 0),
                                   op0=ALU.mult, op1=ALU.mult) \
                .then_inc(sem["dDV"], 1)
            e.wait_ge(sem["dDV"], 2 * t + 2)
            e.tensor_add(out=stc(t, 5), in0=stc(t, 3), in1=stc(t, 4)) \
                .then_inc(sem["dVR"], 1)
            e.wait_ge(sem["dRS"], t + 1)
            e.tensor_scalar(out=lnbf[:, 1024 * (t % 2):1024 * (t % 2) + 1024],
                            in0=Rt, scalar1=stc(t, 1), scalar2=stc(t, 7),
                            op0=ALU.subtract, op1=ALU.mult) \
                .then_inc(sem["dZ"], 1)
        cdsts = [u1s]
        for rc in range(2):
            for hh in range(2):
                cdsts.append(v1s[:, 4096 * rc + 2048 * hh:4096 * rc + 2048 * hh + 2048])
        for c0 in range(4):
            cdsts.append(u2s[:, 2048 * c0:2048 * (c0 + 1)])
        cdsts.append(v2s)
        cdsts.append(b2row)
        for k, dst in enumerate(cdsts):
            e.wait_ge(sem["dWL"], 16 * (k + 1))
            nsrc = 1024 if k == 10 else 2048
            srcv = xa[0:1, 0:1024] if k == 10 else xa[:, 0:2048]
            e.tensor_copy(out=dst, in_=srcv).then_inc(sem["dWC"], 1)
        e.wait_ge(sem["dTR"], NT)
        e.memset(ones1_bf, 1.0).then_inc(sem["dONE"], 1)
        for s4 in range(4):
            for fc in range(32):
                k = s4 * 32 + fc
                u = xa[:, 512 * (fc % 2):512 * (fc % 2) + 512]
                q1 = xa[:, 1024 + 512 * (fc % 2):1024 + 512 * (fc % 2) + 512]
                e.wait_ge(sem["dGA"], k + 1)
                e.tensor_mul(out=q1, in0=u, in1=q1).then_inc(sem["dGQ"], 1)
                e.wait_ge(sem["dGQ"], k + 1)
                e.scalar_tensor_tensor(out=q1, in0=q1, scalar=0.044715, in1=u,
                                       op0=ALU.mult, op1=ALU.add) \
                    .then_inc(sem["dGW"], 1)
                e.wait_ge(sem["dGT"], k + 1)
                e.scalar_tensor_tensor(out=HT[:, 512 * fc:512 * fc + 512],
                                       in0=sqscr[:], scalar=1.0, in1=u,
                                       op0=ALU.add, op1=ALU.mult) \
                    .then_inc(sem["dHE"], 1)
            for sc in range(4):
                t = s4 * 4 + sc
                for dh in range(2):
                    n = 2 * t + dh
                    e.wait_ge(sem["dYM"], n + 1)
                    ya = xa[:, 1024 * (sc % 2) + 512 * dh:
                            1024 * (sc % 2) + 512 * dh + 512]
                    e.tensor_copy(out=ya, in_=ps[n % 2][:]).then_inc(sem["dYE"], 1)
                yt = xa[:, 1024 * (sc % 2):1024 * (sc % 2) + 1024]
                e.wait_ge(sem["dYE"], 2 * t + 2)
                e.reduce_sum(out=stc(t, 0), in_=yt, axis=AX)
                e.wait_ge(sem["dS2"], NT + t + 1)
                e.tensor_add(out=stc(t, 2), in0=stc(t, 2), in1=stc(t, 9)) \
                    .then_inc(sem["dDV"], 1)
                e.wait_ge(sem["dDV"], 2 * NT + 3 * t + 1)
                e.tensor_scalar(out=stc(t, 1), in0=stc(t, 0), scalar1=1.0 / D,
                                scalar2=None, op0=ALU.mult)
                e.tensor_scalar(out=stc(t, 3), in0=stc(t, 2), scalar1=1.0 / D,
                                scalar2=None, op0=ALU.mult)
                e.scalar_tensor_tensor(out=stc(t, 4), in0=stc(t, 0),
                                       scalar=-1.0 / (D * D), in1=stc(t, 0),
                                       op0=ALU.mult, op1=ALU.mult) \
                    .then_inc(sem["dDV"], 1)
                e.wait_ge(sem["dDV"], 2 * NT + 3 * t + 2)
                e.tensor_add(out=stc(t, 5), in0=stc(t, 3), in1=stc(t, 4)) \
                    .then_inc(sem["dVR"], 1)
                e.wait_ge(sem["dRS"], NT + t + 1)
                e.tensor_scalar(out=yt, in0=yt, scalar1=stc(t, 1),
                                scalar2=stc(t, 7), op0=ALU.subtract,
                                op1=ALU.mult).then_inc(sem["dDV"], 1)
                e.wait_ge(sem["dDV"], 2 * NT + 3 * t + 3)
                Rt = R[:, 1024 * t:1024 * t + 1024]
                e.tensor_add(out=Rt, in0=Rt, in1=yt).then_inc(sem["dRES"], 1)

    def scalar_ops(e):
        gate(e)
        for t in range(NT):
            if not resident_in:
                e.wait_ge(sem["dXR"], 16 * (t + 2))
            else:
                e.wait_ge(sem["dXR"], 16)
            Rt = R[:, 1024 * t:1024 * t + 1024]
            e.activation(out=sqscr[:], in_=Rt[:, 0:512], func=AF.Square,
                         accum_out=stc(t, 2))
            e.activation(out=sqscr[:], in_=Rt[:, 512:1024], func=AF.Square,
                         accum_out=stc(t, 9)).then_inc(sem["dS2"], 1)
            e.wait_ge(sem["dVR"], t + 1)
            e.activation(out=stc(t, 6), in_=stc(t, 5), func=AF.Ln,
                         bias=epsc[:]).then_inc(sem["dLX"], 1)
            e.wait_ge(sem["dLX"], t + 1)
            e.activation(out=stc(t, 7), in_=stc(t, 6), func=AF.Exp,
                         scale=-0.5).then_inc(sem["dRS"], 1)
            e.wait_ge(sem["dTR"], t + 1)
            dst = ln3T.rearrange("p (c s) -> p c s", c=8)[:, :, 128 * t:128 * t + 128]
            e.activation(out=dst, in_=psbf.rearrange("p (c s) -> p c s", c=8),
                         func=AF.Copy).then_inc(sem["dLT"], 1)
        for s4 in range(4):
            for rc in range(2):
                n = 2 * s4 + rc
                e.wait_ge(sem["dPM"], n + 1)
                e.activation(out=PTs[s4 % 2][:, 512 * rc:512 * rc + 512],
                             in_=ps[rc][:], func=AF.Copy).then_inc(sem["dPEv"], 1)
            for fc in range(32):
                k = s4 * 32 + fc
                u = xa[:, 512 * (fc % 2):512 * (fc % 2) + 512]
                q1 = xa[:, 1024 + 512 * (fc % 2):1024 + 512 * (fc % 2) + 512]
                e.wait_ge(sem["dHM"], k + 1)
                if k >= 2:
                    e.wait_ge(sem["dGQ"], k - 1)
                e.activation(out=u, in_=ps[fc % 2][:], func=AF.Identity,
                             bias=b1T[:, fc:fc + 1])
                e.activation(out=q1, in_=ps[fc % 2][:], func=AF.Square,
                             bias=b1T[:, fc:fc + 1]).then_inc(sem["dGA"], 1)
                e.wait_ge(sem["dGW"], k + 1)
                if k >= 1:
                    e.wait_ge(sem["dHE"], k)
                e.activation(out=sqscr[:], in_=q1, func=AF.Tanh,
                             scale=GELU_C).then_inc(sem["dGT"], 1)
            for rc in range(2):
                n = 2 * s4 + rc
                e.wait_ge(sem["dRM"], n + 1)
                e.activation(out=RTs[:, 512 * rc:512 * rc + 512],
                             in_=ps[2 + rc][:], func=AF.Copy, scale=0.5) \
                    .then_inc(sem["dRE"], 1)
            for sc in range(4):
                t = s4 * 4 + sc
                yt = xa[:, 1024 * (sc % 2):1024 * (sc % 2) + 1024]
                e.wait_ge(sem["dYE"], 2 * t + 2)
                e.activation(out=sqscr[:], in_=yt[:, 0:512], func=AF.Square,
                             accum_out=stc(t, 2))
                e.activation(out=sqscr[:], in_=yt[:, 512:1024], func=AF.Square,
                             accum_out=stc(t, 9)).then_inc(sem["dS2"], 1)
                e.wait_ge(sem["dVR"], NT + t + 1)
                e.activation(out=stc(t, 6), in_=stc(t, 5), func=AF.Ln,
                             bias=epsc[:]).then_inc(sem["dLX"], 1)
                e.wait_ge(sem["dLX"], NT + t + 1)
                e.activation(out=stc(t, 7), in_=stc(t, 6), func=AF.Exp,
                             scale=-0.5).then_inc(sem["dRS"], 1)

    def tensor_ops(e):
        gate(e)
        for t in range(NT):
            e.wait_ge(sem["dZ"], t + 1)
            if t >= 1:
                e.wait_ge(sem["dLT"], t)
            srcz = lnbf[:, 1024 * (t % 2):1024 * (t % 2) + 1024]
            last = None
            for dc in range(8):
                last = e.transpose(psbf[:, 128 * dc:128 * dc + 128],
                                   srcz[:, 128 * dc:128 * dc + 128], eye[:])
            last.then_inc(sem["dTR"], 1)
        e.wait_ge(sem["dLT"], NT)
        e.wait_ge(sem["dONE"], 1)
        for s4 in range(4):
            e.wait_ge(sem["dWC"], 1)
            if s4 >= 1:
                e.wait_ge(sem["dYE"], 8 * s4)
                e.wait_ge(sem["dPEv"], 2 * s4)
            for rc in range(2):
                last = None
                for dcc in range(8):
                    last = e.matmul(
                        ps[rc][:],
                        u1s[:, dcc * 256 + rc * 128:dcc * 256 + rc * 128 + 128],
                        ln3T[:, dcc * 2048 + 512 * s4:dcc * 2048 + 512 * s4 + 512],
                        start=(dcc == 0), stop=(dcc == 7))
                last.then_inc(sem["dPM"], 1)
            e.wait_ge(sem["dWC"], 5)
            e.wait_ge(sem["dPEv"], 2 * (s4 + 1))
            for fc in range(32):
                k = s4 * 32 + fc
                if k >= 2:
                    e.wait_ge(sem["dGA"], k - 1)
                last = None
                for rc in range(2):
                    last = e.matmul(
                        ps[fc % 2][:],
                        v1s[:, rc * 4096 + fc * 128:rc * 4096 + fc * 128 + 128],
                        PTs[s4 % 2][:, rc * 512:rc * 512 + 512],
                        start=(rc == 0), stop=(rc == 1))
                last.then_inc(sem["dHM"], 1)
            e.wait_ge(sem["dWC"], 9)
            e.wait_ge(sem["dHE"], 32 * (s4 + 1))
            if s4 >= 1:
                e.wait_ge(sem["dRE"], 2 * s4)
            for rc in range(2):
                last = None
                for fc in range(32):
                    last = e.matmul(
                        ps[2 + rc][:],
                        u2s[:, fc * 256 + rc * 128:fc * 256 + rc * 128 + 128],
                        HT[:, fc * 512:fc * 512 + 512],
                        start=(fc == 0), stop=(fc == 31))
                last.then_inc(sem["dRM"], 1)
            e.wait_ge(sem["dWC"], 11)
            e.wait_ge(sem["dRE"], 2 * (s4 + 1))
            for sc in range(4):
                t = s4 * 4 + sc
                for dh in range(2):
                    n = 2 * t + dh
                    if n >= 2:
                        e.wait_ge(sem["dYE"], n - 1)
                    pst = ps[n % 2]
                    for rc in range(2):
                        e.matmul(pst[:],
                                 RTs[:, rc * 512 + sc * 128:rc * 512 + sc * 128 + 128],
                                 v2s[:, rc * 1024 + dh * 512:rc * 1024 + dh * 512 + 512],
                                 start=(rc == 0), stop=False)
                    e.matmul(pst[:], ones1_bf, b2row[:, 512 * dh:512 * dh + 512],
                             start=False, stop=True).then_inc(sem["dYM"], 1)

    def gpsimd_ops(e):
        gate(e)
        if store_out:
            xo = io["y_out"].rearrange("(q p) e -> q p e", p=128)
            for t in range(NT):
                e.wait_ge(sem["dRES"], t + 1)
                e.dma_start(out=xo[t], in_=R[:, 1024 * t:1024 * t + 1024]) \
                    .then_inc(sem["dOUT"], 16)
            e.wait_ge(sem["dOUT"], 16 * NT)

    return dict(sync=sync_ops, vector=vector_ops, scalar=scalar_ops,
                tensor=tensor_ops, gpsimd=gpsimd_ops)


FFN1_KEYS = ["U1", "V1", "b1", "U2", "V2", "b2"]


def build_launch_d():
    key = ("d",)
    if key in _NC_CACHE:
        return _NC_CACHE[key]
    nc = bass.Bass("TRN2", target_bir_lowering=False, debug=False)
    io = {}
    io["x_in"] = nc.dram_tensor("x_in", [SC, D], F32, kind="ExternalInput").ap()
    w = {}
    shapes = {"U1": [D, 256], "V1": [256, 4096], "b1": [4096],
              "U2": [4096, 256], "V2": [256, D], "b2": [D]}
    for k in FFN1_KEYS:
        w[k] = nc.dram_tensor("w_" + k, shapes[k], F32, kind="ExternalInput").ap()
    io["w"] = w
    eye_in = nc.dram_tensor("eye_in", [128, 128], BF16, kind="ExternalInput").ap()
    io["y_out"] = nc.dram_tensor("y_out", [SC, D], F32, kind="ExternalOutput").ap()
    ctx = ExitStack()
    with ctx:
        bufs = alloc_common(nc, ctx)
        sem = make_sems(nc, ctx, D_SEMS + ["dEYE", "dONE2"])
        ops = build_stage_d(sem, bufs, io, store_out=True, resident_in=False)
        block = ctx.enter_context(nc.Block())

        @block.sync
        def _(e):
            e.dma_start(out=bufs.eye, in_=eye_in).then_inc(sem["dEYE"], 16)
            ops["sync"](e)

        @block.vector
        def _(e):
            e.memset(bufs.epsc[:], float(EPS)).then_inc(sem["dONE2"], 1)
            ops["vector"](e)

        @block.scalar
        def _(e):
            e.wait_ge(sem["dONE2"], 1)
            ops["scalar"](e)

        @block.tensor
        def _(e):
            e.wait_ge(sem["dEYE"], 16)
            ops["tensor"](e)

        @block.gpsimd
        def _(e):
            ops["gpsimd"](e)
    _NC_CACHE[key] = nc
    return nc


def run_stage_d(x3_full, weights):
    nc = build_launch_d()
    eye = np.eye(128, dtype=np.float16)
    in_maps = []
    for c in range(NCORES):
        b, r = c // GROUP, c % GROUP
        m = {"x_in": np.ascontiguousarray(x3_full[b, r * SC:(r + 1) * SC]),
             "eye_in": eye}
        for k in FFN1_KEYS:
            m["w_" + k] = np.ascontiguousarray(weights[k])
        in_maps.append(m)
    res = run_bass_kernel_spmd(nc, in_maps, core_ids=list(range(NCORES)))
    out = np.zeros((B, S, D), np.float32)
    for c in range(NCORES):
        b, r = c // GROUP, c % GROUP
        out[b, r * SC:(r + 1) * SC] = res.results[c]["y_out"]
    return out


FFN0_KEYS = ["ln_g", "ln_b", "bn_g", "bn_b", "bn_mean", "bn_var", "Wr", "br",
             "W_down", "b_down", "W_up", "b_up"]


def build_launch_b():
    key = ("b",)
    if key in _NC_CACHE:
        return _NC_CACHE[key]
    nc = bass.Bass("TRN2", target_bir_lowering=False, debug=False)
    io = {}
    io["x_in"] = nc.dram_tensor("x_in", [SC, D], F32, kind="ExternalInput").ap()
    w = {}
    shapes = {"ln_g": [D], "ln_b": [D], "bn_g": [D], "bn_b": [D],
              "bn_mean": [D], "bn_var": [D], "Wr": [D, 8], "br": [8],
              "W_down": [8, D, 64], "b_down": [8, 64], "W_up": [8, 64, D],
              "b_up": [8, D]}
    for k in FFN0_KEYS:
        w[k] = nc.dram_tensor("w_" + k, shapes[k], F32, kind="ExternalInput").ap()
    io["w"] = w
    eye_in = nc.dram_tensor("eye_in", [128, 128], BF16, kind="ExternalInput").ap()
    io["zs_in"] = nc.dram_tensor("zs_in", [1, D], F32, kind="Internal").ap()
    io["zs_out"] = nc.dram_tensor("zs_out", [1, D], F32, kind="Internal").ap()
    io["mrow_d"] = nc.dram_tensor("mrow_d", [1, 8], F32, kind="Internal").ap()
    io["x_out"] = nc.dram_tensor("x_out", [SC, D], F32, kind="ExternalOutput").ap()
    io["af_d"] = nc.dram_tensor("af_d", [128, 16384], BF16, kind="ExternalOutput").ap()
    io["cf_d"] = nc.dram_tensor("cf_d", [128, 16384], BF16, kind="ExternalOutput").ap()
    io["pt_d"] = nc.dram_tensor("pt_d", [128, 8192], BF16, kind="ExternalOutput").ap()
    io["zt_d"] = nc.dram_tensor("zt_d", [128, 16384], BF16, kind="ExternalOutput").ap()
    ctx = ExitStack()
    with ctx:
        bufs = alloc_common(nc, ctx)
        sem = make_sems(nc, ctx, B_SEMS + ["bEYE", "bONE"])
        ops = build_stage_b(sem, bufs, io, store_out=True, resident_in=False)
        block = ctx.enter_context(nc.Block())

        @block.sync
        def _(e):
            e.dma_start(out=bufs.eye, in_=eye_in).then_inc(sem["bEYE"], 16)
            ops["sync"](e)

        @block.vector
        def _(e):
            e.memset(bufs.ones4[:], 1.0)
            e.memset(bufs.epsc[:], float(EPS)).then_inc(sem["bONE"], 1)
            ops["vector"](e)

        @block.scalar
        def _(e):
            e.wait_ge(sem["bONE"], 1)
            ops["scalar"](e)

        @block.tensor
        def _(e):
            e.wait_ge(sem["bEYE"], 16)
            e.wait_ge(sem["bONE"], 1)
            ops["tensor"](e)

        @block.gpsimd
        def _(e):
            ops["gpsimd"](e)
    _NC_CACHE[key] = nc
    return nc


def run_stage_b(x1_full, weights):
    nc = build_launch_b()
    eye = np.eye(128, dtype=np.float16)
    in_maps = []
    for c in range(NCORES):
        b, r = c // GROUP, c % GROUP
        m = {"x_in": np.ascontiguousarray(x1_full[b, r * SC:(r + 1) * SC]),
             "eye_in": eye}
        for k in FFN0_KEYS:
            m["w_" + k] = np.ascontiguousarray(weights[k])
        in_maps.append(m)
    res = run_bass_kernel_spmd(nc, in_maps, core_ids=list(range(NCORES)))
    out = np.zeros((B, S, D), np.float32)
    for c in range(NCORES):
        b, r = c // GROUP, c % GROUP
        out[b, r * SC:(r + 1) * SC] = res.results[c]["x_out"]
    return out


def run_stage_a(x_full, kernel_w):
    nc = build_launch_a()
    eye = np.eye(128, dtype=np.float16)
    in_maps = []
    for c in range(NCORES):
        b, r = c // GROUP, c % GROUP
        in_maps.append({
            "x_own": np.ascontiguousarray(x_full[b, r * SC:(r + 1) * SC]),
            "kern": np.ascontiguousarray(kernel_w),
            "eye_in": eye,
        })
    res = run_bass_kernel_spmd(nc, in_maps, core_ids=list(range(NCORES)))
    out = np.zeros((B, S, D), np.float32)
    for c in range(NCORES):
        b, r = c // GROUP, c % GROUP
        out[b, r * SC:(r + 1) * SC] = res.results[c]["x1"]
    return out


def kernel(**inputs):
    ins = {k: np.asarray(v) for k, v in inputs.items()}
    x1 = run_stage_a(ins["L0_attn_x"], ins["L0_attn_kernel"])
    wb = {k: ins["L0_ffn_" + k] for k in FFN0_KEYS}
    x2 = run_stage_b(x1, wb)
    wc = {k: ins["L1_attn_" + k] for k in L1_KEYS}
    x3 = run_stage_c(x2, wc)
    wd = {k: ins["L1_ffn_" + k] for k in FFN1_KEYS}
    x4 = run_stage_d(x3, wd)
    return x4


L1_KEYS = ["Wq1", "bq1", "Wk1", "bk1", "Wk2", "bk2", "Wproj", "bproj"]


def make_masks(core):
    r = core % GROUP
    mask_lt = np.zeros((128, 4), np.float32)
    mask_prev = np.zeros((128, 4), np.float32)
    first_m = np.zeros((128, 1), np.float32)
    mask_lt[:, :r] = 1.0
    if r > 0:
        mask_prev[:, r - 1] = 1.0
        first_m[:] = 1.0
    return mask_lt, mask_prev, first_m


def build_launch_c(trunc=None):
    key = ("c", trunc)
    if key in _NC_CACHE:
        return _NC_CACHE[key]
    nc = bass.Bass("TRN2", target_bir_lowering=False, debug=False)
    io = {}
    io["x_in"] = nc.dram_tensor("x_in", [SC, D], F32, kind="ExternalInput").ap()
    w = {}
    shapes = {"Wq1": [D, D], "bq1": [D], "Wk1": [D, D], "bk1": [D],
              "Wk2": [D, D], "bk2": [D], "Wproj": [D, D], "bproj": [D]}
    for k in L1_KEYS:
        w[k] = nc.dram_tensor("w_" + k, shapes[k], F32, kind="ExternalInput").ap()
    io["w"] = w
    eye_in = nc.dram_tensor("eye_in", [128, 128], BF16, kind="ExternalInput").ap()
    io["mask_lt"] = nc.dram_tensor("mask_lt_in", [128, 4], F32, kind="ExternalInput").ap()
    io["mask_prev"] = nc.dram_tensor("mask_prev_in", [128, 4], F32, kind="ExternalInput").ap()
    io["first_m"] = nc.dram_tensor("first_m_in", [128, 1], F32, kind="ExternalInput").ap()
    io["cc1_in"] = [nc.dram_tensor(f"cc1_in{i}", [128, 5], F32, kind="Internal").ap()
                    for i in range(8)]
    io["cc1_out"] = [nc.dram_tensor(f"cc1_out{i}", [512, 5], F32, kind="Internal").ap()
                     for i in range(8)]
    io["cc2_in"] = [nc.dram_tensor(f"cc2_in{i}", [128, 1], F32, kind="Internal").ap()
                    for i in range(8)]
    io["cc2_out"] = [nc.dram_tensor(f"cc2_out{i}", [512, 1], F32, kind="Internal").ap()
                     for i in range(8)]
    io["x_out"] = nc.dram_tensor("x_out", [SC, D], F32, kind="ExternalOutput").ap()
    if trunc is not None:
        io["r_d"] = nc.dram_tensor("r_d", [128, 16384], F32, kind="ExternalOutput").ap()
        io["ot_d"] = nc.dram_tensor("ot_d", [128, 16384], BF16, kind="ExternalOutput").ap()
        io["xa_d"] = nc.dram_tensor("xa_d", [128, 2048], F32, kind="ExternalOutput").ap()
        io["cc_d"] = nc.dram_tensor("cc_d", [512, 48], F32, kind="ExternalOutput").ap()
    ctx = ExitStack()
    with ctx:
        bufs = alloc_common(nc, ctx)
        sem = make_sems(nc, ctx, C_SEMS + ["cEYE", "cONE"])
        ops = build_stage_c(sem, bufs, io, store_out=True, resident_in=False,
                            trunc=trunc)
        block = ctx.enter_context(nc.Block())

        @block.sync
        def _(e):
            e.dma_start(out=bufs.eye, in_=eye_in).then_inc(sem["cEYE"], 16)
            ops["sync"](e)

        @block.vector
        def _(e):
            e.memset(bufs.epsc[:], float(EPS)).then_inc(sem["cONE"], 1)
            ops["vector"](e)

        @block.scalar
        def _(e):
            e.wait_ge(sem["cONE"], 1)
            ops["scalar"](e)

        @block.tensor
        def _(e):
            e.wait_ge(sem["cEYE"], 16)
            ops["tensor"](e)

        @block.gpsimd
        def _(e):
            ops["gpsimd"](e)
    _NC_CACHE[key] = nc
    return nc


def run_stage_c(x2_full, weights):
    nc = build_launch_c()
    eye = np.eye(128, dtype=np.float16)
    in_maps = []
    for c in range(NCORES):
        b, r = c // GROUP, c % GROUP
        ml, mp, fm = make_masks(c)
        m = {"x_in": np.ascontiguousarray(x2_full[b, r * SC:(r + 1) * SC]),
             "eye_in": eye, "mask_lt_in": ml, "mask_prev_in": mp,
             "first_m_in": fm}
        for k in L1_KEYS:
            m["w_" + k] = np.ascontiguousarray(weights[k])
        in_maps.append(m)
    res = run_bass_kernel_spmd(nc, in_maps, core_ids=list(range(NCORES)))
    out = np.zeros((B, S, D), np.float32)
    for c in range(NCORES):
        b, r = c // GROUP, c % GROUP
        out[b, r * SC:(r + 1) * SC] = res.results[c]["x_out"]
    return out


# version 04
# speedup vs baseline: 1.0498x; 1.0498x over previous
import numpy as np
import ml_dtypes
from contextlib import ExitStack

import concourse.bass as bass
import concourse.mybir as mybir
from concourse.bass_utils import run_bass_kernel_spmd

F32 = mybir.dt.float32
BF16 = mybir.dt.float16
AF = mybir.ActivationFunctionType
ALU = mybir.AluOpType
AX = mybir.AxisListType.X

NCORES = 8
GROUP = 4
B = 2
S = 8192
D = 1024
SC = S // GROUP
NT = SC // 128
EPS = 1e-5

RG = [[0, 1, 2, 3], [4, 5, 6, 7]]

A_SEMS = ["aX", "aX2", "aS2", "aVR", "aRS", "aST", "aTR", "aLT", "aKL", "aKC",
          "aMM", "aEVQ", "aEVK", "aEVV", "aSP", "aSPV", "aCC", "aKV", "aSC", "aEX",
          "aPV", "aPVE", "aDN", "aDNE", "aQG", "aRD", "aOS", "aRES", "aEYE",
          "aONE", "aOUT", "aLX", "aDV"]


def make_sems(nc, ctx, names):
    return {n: ctx.enter_context(nc.semaphore(n)) for n in names}


class NS:
    def __init__(self, d):
        self.__dict__.update(d)


def build_stage_a(sem, bufs, io, store_out, dbg=False):
    qt = bufs.A_bf
    lnT = bufs.C_bf
    ktq = bufs.B_bf
    vq = bufs.C_bf
    pT = bufs.pT
    ker = bufs.W_bf
    R = bufs.R_f32
    xa = bufs.xa
    lnbf = bufs.lnbf
    sqscr = bufs.sqscr
    st = bufs.stats
    dn_sb = bufs.dn_sb
    rd_all = bufs.rd_all
    ones4 = bufs.ones4
    epsc = bufs.epsc
    eye = bufs.eye
    ps = bufs.ps
    psbf = bufs.psbf
    ps64 = bufs.ps64

    x_own, kern, kv_in, kv_g = io["x_own"], io["kern"], io["kv_in"], io["kv_g"]

    def stc(t, i):
        o = 16 * (t % 2) + i
        return st[:, o:o + 1]

    def sync_ops(e):
        xr = x_own.rearrange("(t p) e -> t p e", p=128)
        for t in range(NT):
            if t >= 1:
                e.wait_ge(sem["aX"], 16 * t)
            if t >= 2:
                e.wait_ge(sem["aST"], t - 1)
                e.wait_ge(sem["aS2"], t - 1)
            e.dma_start(out=xa[:, 1024 * (t % 2):1024 * (t % 2) + 1024],
                        in_=xr[t]).then_inc(sem["aX"], 16)
        kr = kern.rearrange("k (c p) e -> k p c e", p=128)
        Rst = R.rearrange("p (a e) -> p a e", e=1024)
        for i in range(3):
            e.wait_ge(sem["aKC"], i)
            e.dma_start(out=Rst[:, 0:8, :], in_=kr[i]).then_inc(sem["aKL"], 16)
        e.wait_ge(sem["aEVK"], 32)
        e.dma_start(out=kv_in[:, 0:16384], in_=ktq[:]).then_inc(sem["aSP"], 16)
        kvg_kt = kv_g.rearrange("(j p) n -> j p n", p=128)
        for kt8 in range(8):
            j, h = kt8 // 2, kt8 % 2
            if kt8 == 0:
                e.wait_ge(sem["aCC"], 1)
            if kt8 >= 2:
                e.wait_ge(sem["aSC"], 32 * (kt8 - 1))
                e.wait_ge(sem["aPV"], 32 * (kt8 - 1))
                e.wait_ge(sem["aDN"], 4 * (kt8 - 1))
            src = kvg_kt[j][:, 0:16384].rearrange("p (c s) -> p c s", c=8)
            if kt8 >= 1:
                e.wait_ge(sem["aKV"], 32 * kt8)
            e.dma_start(out=ktq[:, 8192 * (kt8 % 2):8192 * (kt8 % 2) + 8192]
                        .rearrange("p (c s) -> p c s", c=8),
                        in_=src[:, :, 1024 * h:1024 * h + 1024]
                        ).then_inc(sem["aKV"], 16)
            e.wait_ge(sem["aKV"], 32 * kt8 + 16)
            e.dma_start(out=vq[:, 8192 * (kt8 % 2):8192 * (kt8 % 2) + 8192],
                        in_=kvg_kt[j][:, 16384 + 8192 * h:16384 + 8192 * (h + 1)]
                        ).then_inc(sem["aKV"], 16)
        for t in range(NT):
            if t >= 1:
                e.wait_ge(sem["aX2"], 16 * t)
            if t >= 2:
                e.wait_ge(sem["aRES"], t - 1)
            e.dma_start(out=xa[:, 1024 * (t % 2):1024 * (t % 2) + 1024],
                        in_=xr[t]).then_inc(sem["aX2"], 16)

    def vector_ops(e):
        e.memset(ones4[:], 1.0)
        e.memset(epsc[:], float(EPS)).then_inc(sem["aONE"], 1)
        for t in range(NT):
            e.wait_ge(sem["aX"], 16 * (t + 1))
            xv = xa[:, 1024 * (t % 2):1024 * (t % 2) + 1024]
            e.reduce_sum(out=stc(t, 0), in_=xv, axis=AX)
            e.wait_ge(sem["aS2"], t + 1)
            e.tensor_add(out=stc(t, 2), in0=stc(t, 2), in1=stc(t, 9)) \
                .then_inc(sem["aDV"], 1)
            e.wait_ge(sem["aDV"], 2 * t + 1)
            e.tensor_scalar(out=stc(t, 1), in0=stc(t, 0), scalar1=1.0 / D,
                            scalar2=None, op0=ALU.mult)
            e.tensor_scalar(out=stc(t, 3), in0=stc(t, 2), scalar1=1.0 / D,
                            scalar2=None, op0=ALU.mult)
            e.scalar_tensor_tensor(out=stc(t, 4), in0=stc(t, 0),
                                   scalar=-1.0 / (D * D), in1=stc(t, 0),
                                   op0=ALU.mult, op1=ALU.mult) \
                .then_inc(sem["aDV"], 1)
            e.wait_ge(sem["aDV"], 2 * t + 2)
            e.tensor_add(out=stc(t, 5), in0=stc(t, 3), in1=stc(t, 4)) \
                .then_inc(sem["aVR"], 1)
            e.wait_ge(sem["aRS"], t + 1)
            e.tensor_scalar(out=lnbf[:, 1024 * (t % 2):1024 * (t % 2) + 1024],
                            in0=xv, scalar1=stc(t, 1), scalar2=stc(t, 7),
                            op0=ALU.subtract, op1=ALU.mult) \
                .then_inc(sem["aST"], 1)
        for i in range(3):
            e.wait_ge(sem["aKL"], 16 * (i + 1))
            if i >= 1:
                e.wait_ge(sem["aMM"], 32 * i)
            e.tensor_copy(out=ker[:, 0:8192], in_=R[:, 0:8192]) \
                .then_inc(sem["aKC"], 1)
        for kt8 in range(8):
            for qg in range(4):
                for qc in range(4):
                    for dh in range(2):
                        n = ((kt8 * 4 + qg) * 4 + qc) * 2 + dh
                        e.wait_ge(sem["aPV"], n + 1)
                        dst = R[:, (qg * 4 + qc) * 1024 + dh * 512:
                                (qg * 4 + qc) * 1024 + dh * 512 + 512]
                        pvp = ps[2 + (qc * 2 + dh) % 2]
                        if kt8 == 0:
                            e.tensor_copy(out=dst, in_=pvp[:]) \
                                .then_inc(sem["aPVE"], 1)
                        else:
                            e.tensor_add(out=dst, in0=dst, in1=pvp[:]) \
                                .then_inc(sem["aPVE"], 1)
                nqg = kt8 * 4 + qg
                e.wait_ge(sem["aDN"], nqg + 1)
                if kt8 == 0:
                    e.tensor_copy(out=dn_sb[:, 16 * qg:16 * qg + 16],
                                  in_=ps64[:, 0:16]).then_inc(sem["aDNE"], 1)
                else:
                    e.tensor_add(out=dn_sb[:, 16 * qg:16 * qg + 16],
                                 in0=dn_sb[:, 16 * qg:16 * qg + 16],
                                 in1=ps64[:, 0:16]).then_inc(sem["aDNE"], 1)
        dnv = dn_sb.rearrange("p (q f) -> p q f", f=4)
        e.wait_ge(sem["aDNE"], 32)
        e.reciprocal(out=rd_all[:], in_=dnv[:, :, 0]).then_inc(sem["aRD"], 1)
        e.wait_ge(sem["aRD"], 1)
        for t in range(NT):
            Ot = R[:, 1024 * t:1024 * t + 1024]
            e.tensor_scalar(out=Ot, in0=Ot, scalar1=rd_all[:, t:t + 1],
                            scalar2=None, op0=ALU.mult).then_inc(sem["aOS"], 1)
            e.wait_ge(sem["aOS"], t + 1)
            e.reduce_sum(out=stc(t, 0), in_=Ot, axis=AX)
            e.wait_ge(sem["aS2"], NT + t + 1)
            e.tensor_add(out=stc(t, 2), in0=stc(t, 2), in1=stc(t, 9)) \
                .then_inc(sem["aDV"], 1)
            e.wait_ge(sem["aDV"], 2 * NT + 3 * t + 1)
            e.tensor_scalar(out=stc(t, 1), in0=stc(t, 0), scalar1=1.0 / D,
                            scalar2=None, op0=ALU.mult)
            e.tensor_scalar(out=stc(t, 3), in0=stc(t, 2), scalar1=1.0 / D,
                            scalar2=None, op0=ALU.mult)
            e.scalar_tensor_tensor(out=stc(t, 4), in0=stc(t, 0),
                                   scalar=-1.0 / (D * D), in1=stc(t, 0),
                                   op0=ALU.mult, op1=ALU.mult) \
                .then_inc(sem["aDV"], 1)
            e.wait_ge(sem["aDV"], 2 * NT + 3 * t + 2)
            e.tensor_add(out=stc(t, 5), in0=stc(t, 3), in1=stc(t, 4)) \
                .then_inc(sem["aVR"], 1)
            e.wait_ge(sem["aRS"], NT + t + 1)
            e.tensor_scalar(out=Ot, in0=Ot, scalar1=stc(t, 1),
                            scalar2=stc(t, 7), op0=ALU.subtract, op1=ALU.mult) \
                .then_inc(sem["aDV"], 1)
            e.wait_ge(sem["aDV"], 2 * NT + 3 * t + 3)
            e.wait_ge(sem["aX2"], 16 * (t + 1))
            e.tensor_add(out=Ot, in0=Ot,
                         in1=xa[:, 1024 * (t % 2):1024 * (t % 2) + 1024]) \
                .then_inc(sem["aRES"], 1)

    def scalar_ops(e):
        e.wait_ge(sem["aONE"], 1)
        for t in range(NT):
            e.wait_ge(sem["aX"], 16 * (t + 1))
            xv = xa[:, 1024 * (t % 2):1024 * (t % 2) + 1024]
            e.activation(out=sqscr[:], in_=xv[:, 0:512], func=AF.Square,
                         accum_out=stc(t, 2))
            e.activation(out=sqscr[:], in_=xv[:, 512:1024], func=AF.Square,
                         accum_out=stc(t, 9)).then_inc(sem["aS2"], 1)
            e.wait_ge(sem["aVR"], t + 1)
            e.activation(out=stc(t, 6), in_=stc(t, 5), func=AF.Ln,
                         bias=epsc[:]).then_inc(sem["aLX"], 1)
            e.wait_ge(sem["aLX"], t + 1)
            e.activation(out=stc(t, 7), in_=stc(t, 6), func=AF.Exp,
                         scale=-0.5).then_inc(sem["aRS"], 1)
            e.wait_ge(sem["aTR"], t + 1)
            dst = lnT.rearrange("p (c s) -> p c s", c=8)[:, :, 128 * t:128 * t + 128]
            e.activation(out=dst, in_=psbf.rearrange("p (c s) -> p c s", c=8),
                         func=AF.Copy).then_inc(sem["aLT"], 1)
        for sc4 in range(4):
            for ec in range(8):
                n = sc4 * 8 + ec
                e.wait_ge(sem["aMM"], n + 1)
                e.activation(out=qt[:, ec * 2048 + 512 * sc4:ec * 2048 + 512 * sc4 + 512],
                             in_=ps[n % 2][:], func=AF.Copy).then_inc(sem["aEVQ"], 1)
        for sc4 in range(4):
            for ec in range(8):
                n = 32 + sc4 * 8 + ec
                e.wait_ge(sem["aMM"], n + 1)
                e.activation(out=ktq[:, ec * 2048 + 512 * sc4:ec * 2048 + 512 * sc4 + 512],
                             in_=ps[n % 2][:], func=AF.Copy).then_inc(sem["aEVK"], 1)
        for stt in range(16):
            for eh in range(2):
                n = 64 + stt * 2 + eh
                k = stt * 2 + eh
                e.wait_ge(sem["aMM"], n + 1)
                if k >= 4:
                    e.wait_ge(sem["aSPV"], 16 * (k - 3))
                e.activation(out=lnbf[:, (k % 4) * 512:(k % 4) * 512 + 512],
                             in_=ps[n % 2][:], func=AF.Copy).then_inc(sem["aEVV"], 1)
        for kt8 in range(8):
            for qg in range(4):
                for jc in range(8):
                    n = (kt8 * 4 + qg) * 8 + jc
                    e.wait_ge(sem["aSC"], n + 1)
                    pt_dst = pT[:, 4096 * (qg % 2) + 512 * jc:
                                4096 * (qg % 2) + 512 * jc + 512]
                    e.activation(out=pt_dst, in_=ps[jc % 2][:], func=AF.Exp,
                                 scale=1.0 / 32.0).then_inc(sem["aEX"], 1)
        for t in range(NT):
            e.wait_ge(sem["aOS"], t + 1)
            Ot = R[:, 1024 * t:1024 * t + 1024]
            e.activation(out=sqscr[:], in_=Ot[:, 0:512], func=AF.Square,
                         accum_out=stc(t, 2))
            e.activation(out=sqscr[:], in_=Ot[:, 512:1024], func=AF.Square,
                         accum_out=stc(t, 9)).then_inc(sem["aS2"], 1)
            e.wait_ge(sem["aVR"], NT + t + 1)
            e.activation(out=stc(t, 6), in_=stc(t, 5), func=AF.Ln,
                         bias=epsc[:]).then_inc(sem["aLX"], 1)
            e.wait_ge(sem["aLX"], NT + t + 1)
            e.activation(out=stc(t, 7), in_=stc(t, 6), func=AF.Exp,
                         scale=-0.5).then_inc(sem["aRS"], 1)

    def tensor_ops(e):
        e.wait_ge(sem["aEYE"] if "aEYE" in sem else sem["gEYE"], 16)
        for t in range(NT):
            e.wait_ge(sem["aST"], t + 1)
            if t >= 1:
                e.wait_ge(sem["aLT"], t)
            src = lnbf[:, 1024 * (t % 2):1024 * (t % 2) + 1024]
            last = None
            for dc in range(8):
                last = e.transpose(psbf[:, 128 * dc:128 * dc + 128],
                                   src[:, 128 * dc:128 * dc + 128], eye[:])
            last.then_inc(sem["aTR"], 1)
        e.wait_ge(sem["aLT"], NT)
        nmm = 0
        for kidx, evs in ((0, "aEVQ"), (1, "aEVK")):
            e.wait_ge(sem["aKC"], kidx + 1)
            kslice = ker[:, 0:8192]
            for sc4 in range(4):
                for ec in range(8):
                    prior = sc4 * 8 + ec
                    if prior >= 2:
                        e.wait_ge(sem[evs], prior - 1)
                    pst = ps[nmm % 2]
                    last = None
                    for dc in range(8):
                        last = e.matmul(
                            pst[:],
                            kslice[:, dc * 1024 + ec * 128:dc * 1024 + ec * 128 + 128],
                            lnT[:, dc * 2048 + 512 * sc4:dc * 2048 + 512 * sc4 + 512],
                            start=(dc == 0), stop=(dc == 7))
                    last.then_inc(sem["aMM"], 1)
                    nmm += 1
        e.wait_ge(sem["aKC"], 3)
        for stt in range(16):
            for eh in range(2):
                prior = stt * 2 + eh
                if prior >= 2:
                    e.wait_ge(sem["aEVV"], prior - 1)
                pst = ps[nmm % 2]
                last = None
                for dc in range(8):
                    last = e.matmul(
                        pst[:],
                        lnT[:, dc * 2048 + 128 * stt:dc * 2048 + 128 * stt + 128],
                        ker[:, dc * 1024 + eh * 512:dc * 1024 + eh * 512 + 512],
                        start=(dc == 0), stop=(dc == 7))
                last.then_inc(sem["aMM"], 1)
                nmm += 1
        e.wait_ge(sem["aONE"], 1)
        for kt8 in range(8):
            e.wait_ge(sem["aKV"], 32 * (kt8 + 1))
            ktb = ktq[:, 8192 * (kt8 % 2):8192 * (kt8 % 2) + 8192]
            vb = vq[:, 8192 * (kt8 % 2):8192 * (kt8 % 2) + 8192]
            for qg in range(4):
                nqg = kt8 * 4 + qg
                for jc in range(8):
                    n = nqg * 8 + jc
                    if n >= 2:
                        e.wait_ge(sem["aEX"], n - 1)
                    if jc == 0 and nqg >= 2:
                        e.wait_ge(sem["aPV"], (nqg - 1) * 8)
                        e.wait_ge(sem["aDN"], nqg - 1)
                    pst = ps[jc % 2]
                    last = None
                    for ec in range(8):
                        last = e.matmul(
                            pst[:],
                            ktb[:, ec * 1024 + jc * 128:ec * 1024 + jc * 128 + 128],
                            qt[:, ec * 2048 + qg * 512:ec * 2048 + qg * 512 + 512],
                            start=(ec == 0), stop=(ec == 7))
                    last.then_inc(sem["aSC"], 1)
                e.wait_ge(sem["aEX"], nqg * 8 + 8)
                pt_b = pT[:, 4096 * (qg % 2):4096 * (qg % 2) + 4096]
                for qc in range(4):
                    for dh in range(2):
                        n = (nqg * 4 + qc) * 2 + dh
                        pvp = ps[2 + (qc * 2 + dh) % 2]
                        if n >= 2:
                            e.wait_ge(sem["aPVE"], n - 1)
                        last = None
                        for jc in range(8):
                            last = e.matmul(
                                pvp[:],
                                pt_b[:, jc * 512 + qc * 128:jc * 512 + qc * 128 + 128],
                                vb[:, jc * 1024 + dh * 512:jc * 1024 + dh * 512 + 512],
                                start=(jc == 0), stop=(jc == 7))
                        last.then_inc(sem["aPV"], 1)
                    if nqg >= 1:
                        e.wait_ge(sem["aDNE"], nqg)
                    last = None
                    for jc in range(8):
                        last = e.matmul(
                            ps64[:, 4 * qc:4 * qc + 4],
                            pt_b[:, jc * 512 + qc * 128:jc * 512 + qc * 128 + 128],
                            ones4[:], start=(jc == 0), stop=(jc == 7),
                            skip_group_check=True)
                    if qc == 3:
                        last.then_inc(sem["aDN"], 1)

    def gpsimd_ops(e):
        for k in range(32):
            e.wait_ge(sem["aEVV"], k + 1)
            e.dma_start(out=kv_in[:, 16384 + 512 * k:16384 + 512 * k + 512],
                        in_=lnbf[:, (k % 4) * 512:(k % 4) * 512 + 512]) \
                .then_inc(sem["aSPV"], 16)
        e.wait_ge(sem["aSP"], 16)
        e.wait_ge(sem["aSPV"], 512)
        e.collective_compute("AllGather", ALU.bypass, replica_groups=RG,
                             ins=[kv_in.opt()], outs=[kv_g.opt()]) \
            .then_inc(sem["aCC"])
        if store_out:
            e.wait_ge(sem["aRES"], NT)
            nout = 0
            xo = io["x1_ext"].rearrange("(q p) e -> q p e", p=128)
            for t in range(NT):
                e.dma_start(out=xo[t], in_=R[:, 1024 * t:1024 * t + 1024]) \
                    .then_inc(sem["aOUT"], 16)
                nout += 1
            if dbg:
                e.dma_start(out=io["qt_d"], in_=qt[:]).then_inc(sem["aOUT"], 16)
                e.dma_start(out=io["kv_d"], in_=kv_in[:]).then_inc(sem["aOUT"], 16)
                e.dma_start(out=io["kvg_d"], in_=kv_g[:]).then_inc(sem["aOUT"], 16)
                e.dma_start(out=io["dn_d"], in_=dn_sb[:]).then_inc(sem["aOUT"], 16)
                nout += 4
            e.wait_ge(sem["aOUT"], 16 * nout)

    return dict(sync=sync_ops, vector=vector_ops, scalar=scalar_ops,
                tensor=tensor_ops, gpsimd=gpsimd_ops)


B_SEMS = ["bXR", "bS2", "bVR", "bRS", "bLX", "bDV", "bZ", "bTR", "bZT", "bCS",
          "bCC", "bSS", "bDM", "bHM", "bHE", "bYM", "bYE", "bRES", "bOUT"]


class TokenChain:

    def __init__(self, ss, dm):
        self.ss, self.dm = ss, dm
        self.k = 0
        self.m = 0
        self.steps = []
        self._last_cc = None
        self.marks = {}

    def mark(self, label):
        self.marks[label] = len(self.steps)

    def add(self, eng, fn, dma=False, cc=None):
        self.steps.append((eng, fn, self.k, self.m, dma, cc, self._last_cc))
        if cc is not None:
            self._last_cc = cc
        elif dma:
            self.m += 1
        else:
            self.k += 1

    def counts_at(self, pos):
        k = m = 0
        last_cc = None
        for eng, fn, k0, m0, dma, cc, pc in self.steps[:pos]:
            if cc is not None:
                last_cc = cc
            elif dma:
                m += 1
            else:
                k += 1
        return k, m, last_cc

    def emit(self, eng_name, e, limit=None):
        for i, (eng, fn, k0, m0, dma, cc, prior_cc) in enumerate(self.steps):
            if limit is not None and i >= limit:
                break
            if eng != eng_name:
                continue
            if k0 > 0:
                e.wait_ge(self.ss, k0)
            if m0 > 0:
                e.wait_ge(self.dm, 16 * m0)
            if prior_cc is not None:
                e.wait_ge(prior_cc, 1)
            ins = fn(e)
            if cc is not None:
                ins.then_inc(cc)
            elif dma:
                ins.then_inc(self.dm, 16)
            else:
                ins.then_inc(self.ss, 1)


def build_stage_b(sem, bufs, io, store_out, resident_in, gate_in=None):
    R = bufs.R_f32
    zT = bufs.B_bf
    af = bufs.A_bf.bitcast(F32)
    cf = bufs.C_bf.bitcast(F32)
    pTb = bufs.pT
    xa = bufs.xa
    lnbf = bufs.lnbf
    sqscr = bufs.sqscr
    st = bufs.stats
    epsc = bufs.epsc
    ps = bufs.ps
    psbf = bufs.psbf
    ps64 = bufs.ps64

    g_bc = af[:, 0:1024]
    b_bc = af[:, 1024:2048]
    wd_all = af[:, 2048:6144]
    wd_sel = af[:, 6144:6656]
    maskP = af[:, 6656:6664]
    gate_bc = af[:, 6664:6665]
    zbnT = af[:, 6672:6680]
    wr_sb = af[:, 6688:6752].rearrange("p (c e) -> p c e", c=8)
    ones1f = af[0:1, 6752:6880]
    onec = af[0:1, 6880:6881]
    mask_col = cf[32:40, 4160:4161]
    bdT = af[0:64, 6888:6889]
    wu_stage = cf[64:128, 1024:2048]
    wu_sel = cf[64:128, 2048:3072]
    bdown_sb = cf[32:40, 3072:3136]
    bup_sb = cf[32:40, 3136:4160]
    bnm = cf[0:1, 0:1024]
    bnv = cf[0:1, 1024:2048]
    bng = cf[0:1, 2048:3072]
    bnb = cf[0:1, 3072:4096]
    g_row = cf[0:1, 4096:5120]
    b_row = cf[0:1, 5120:6144]
    r0 = cf[0:1, 6144:7168]
    zsr = cf[0:1, 7168:8192]
    lvq = af[0:1, 6992:8016]
    br_row = af[0:1, 6912:6920]
    lg = af[0:1, 6928:6936]
    mx = af[0:1, 6936:6937]
    nmx = af[0:1, 6937:6938]
    el = af[0:1, 6944:6952]
    smr = af[0:1, 6952:6953]
    rsr = af[0:1, 6953:6954]
    probs = af[0:1, 6960:6968]
    pmax = af[0:1, 6968:6969]
    mask_row = af[0:1, 6976:6984]
    wd_sel_bf = pTb[:, 0:512]
    wu_sel_bf = pTb[0:64, 512:1536]
    hT = pTb[0:64, 1536:3584]
    bu_row_bf = pTb[0:1, 3584:4608]
    ones1_bf = pTb[0:1, 4608:4736]

    w = io["w"]

    def stc(t, i):
        o = 16 * (t % 2) + i
        return st[:, o:o + 1]

    tc = TokenChain(sem["bSS"], sem["bDM"])

    def _chain():
        tc.add("scalar", lambda e: e.activation(out=r0[:, 0:512], in_=ps[2][0:1, :],
                                                func=AF.Copy))
        tc.add("scalar", lambda e: e.activation(out=r0[:, 512:1024], in_=ps[3][0:1, :],
                                                func=AF.Copy))
        tc.add("gpsimd", lambda e: e.dma_start(out=io["zs_in"], in_=r0), dma=True)
        tc.add("gpsimd", lambda e: e.collective_compute(
            "AllReduce", ALU.add, replica_groups=RG,
            ins=[io["zs_in"].opt()], outs=[io["zs_out"].opt()]))
        tc.add("gpsimd", lambda e: e.dma_start(out=zsr, in_=io["zs_out"]), dma=True)
        tc.add("vector", lambda e: e.tensor_scalar(out=zsr, in0=zsr, scalar1=1.0 / S,
                                                   scalar2=None, op0=ALU.mult))
        tc.add("vector", lambda e: e.tensor_sub(out=r0, in0=zsr, in1=bnm))
        tc.add("scalar", lambda e: e.activation(out=lvq, in_=bnv, func=AF.Ln,
                                                bias=epsc[0:1, :]))
        tc.add("scalar", lambda e: e.activation(out=lvq, in_=lvq, func=AF.Exp,
                                                scale=-0.5))
        tc.add("vector", lambda e: e.tensor_mul(out=r0, in0=r0, in1=lvq))
        tc.add("vector", lambda e: e.tensor_mul(out=r0, in0=r0, in1=bng))
        tc.add("vector", lambda e: e.tensor_add(out=r0, in0=r0, in1=bnb))

        def zbnt_mms(e):
            for dc in range(8):
                last = e.matmul(ps64[:, dc:dc + 1], r0[:, 128 * dc:128 * dc + 128],
                                onec, start=True, stop=True)
            return last
        tc.add("tensor", zbnt_mms)
        tc.add("scalar", lambda e: e.activation(out=zbnT, in_=ps64[:, 0:8],
                                                func=AF.Copy))

        def logit_mms(e):
            for dc in range(8):
                last = e.matmul(ps[0][0:1, 0:8], zbnT[:, dc:dc + 1],
                                wr_sb[:, dc, :], start=(dc == 0), stop=(dc == 7))
            return last
        tc.add("tensor", logit_mms)
        tc.add("scalar", lambda e: e.activation(out=lg, in_=ps[0][0:1, 0:8],
                                                func=AF.Copy))
        tc.add("vector", lambda e: e.tensor_add(out=lg, in0=lg, in1=br_row))
        tc.add("vector", lambda e: e.reduce_max(out=mx, in_=lg, axis=AX))
        tc.add("vector", lambda e: e.tensor_scalar(out=nmx, in0=mx, scalar1=-1.0,
                                                   scalar2=None, op0=ALU.mult))
        tc.add("scalar", lambda e: e.activation(out=el, in_=lg, func=AF.Exp,
                                                bias=nmx))
        tc.add("vector", lambda e: e.reduce_sum(out=smr, in_=el, axis=AX))
        tc.add("vector", lambda e: e.reciprocal(out=rsr, in_=smr))
        tc.add("vector", lambda e: e.tensor_scalar(out=probs, in0=el, scalar1=rsr,
                                                   scalar2=None, op0=ALU.mult))
        tc.add("vector", lambda e: e.reduce_max(out=pmax, in_=probs, axis=AX))
        tc.add("vector", lambda e: e.tensor_scalar(out=mask_row, in0=probs,
                                                   scalar1=pmax, scalar2=None,
                                                   op0=ALU.is_ge))
        tc.add("gpsimd", lambda e: e.dma_start(out=io["mrow_d"], in_=mask_row),
               dma=True)
        tc.add("gpsimd", lambda e: e.dma_start(
            out=mask_col, in_=io["mrow_d"].rearrange("o e -> e o")), dma=True)
        tc.add("tensor", lambda e: e.matmul(ps[1][:, 0:8], ones1f, mask_row,
                                            start=True, stop=True))
        tc.add("tensor", lambda e: e.matmul(ps[1][:, 8:9], ones1f, pmax,
                                            start=True, stop=True, skip_group_check=True))
        tc.add("scalar", lambda e: e.activation(out=maskP, in_=ps[1][:, 0:8],
                                                func=AF.Copy))
        tc.add("scalar", lambda e: e.activation(out=gate_bc, in_=ps[1][:, 8:9],
                                                func=AF.Copy))
        for ee in range(8):
            we = wd_all[:, 512 * ee:512 * ee + 512]
            if ee == 0:
                tc.add("vector", lambda e, we=we: e.tensor_scalar(
                    out=wd_sel, in0=we, scalar1=maskP[:, 0:1], scalar2=None,
                    op0=ALU.mult))
            else:
                tc.add("vector", lambda e, we=we, ee=ee: e.scalar_tensor_tensor(
                    out=wd_sel, in0=we, scalar=maskP[:, ee:ee + 1], in1=wd_sel,
                    op0=ALU.mult, op1=ALU.add))
        tc.add("vector", lambda e: e.tensor_copy(out=wd_sel_bf, in_=wd_sel))
        for ee in range(8):
            tc.add("gpsimd", lambda e, ee=ee: e.dma_start(
                out=wu_stage, in_=w["W_up"][ee]), dma=True)
            if ee == 0:
                tc.add("vector", lambda e: e.tensor_scalar(
                    out=wu_sel, in0=wu_stage, scalar1=maskP[64:128, 0:1],
                    scalar2=None, op0=ALU.mult))
            else:
                tc.add("vector", lambda e, ee=ee: e.scalar_tensor_tensor(
                    out=wu_sel, in0=wu_stage, scalar=maskP[64:128, ee:ee + 1],
                    in1=wu_sel, op0=ALU.mult, op1=ALU.add))
        tc.add("vector", lambda e: e.tensor_copy(out=wu_sel_bf, in_=wu_sel))
        tc.add("tensor", lambda e: e.matmul(ps[0][0:64, 0:1], bdown_sb, mask_col,
                                            start=True, stop=True))
        tc.add("scalar", lambda e: e.activation(out=bdT, in_=ps[0][0:64, 0:1],
                                                func=AF.Copy))
        tc.add("tensor", lambda e: e.matmul(ps[2][0:1, :], mask_col,
                                            bup_sb[:, 0:512], start=True, stop=True))
        tc.add("tensor", lambda e: e.matmul(ps[3][0:1, :], mask_col,
                                            bup_sb[:, 512:1024], start=True,
                                            stop=True))
        tc.add("scalar", lambda e: e.activation(out=bu_row_bf[:, 0:512],
                                                in_=ps[2][0:1, :], func=AF.Copy))
        tc.add("scalar", lambda e: e.activation(out=bu_row_bf[:, 512:1024],
                                                in_=ps[3][0:1, :], func=AF.Copy))
    _chain()
    TCK = tc.k

    def gate(e):
        if gate_in is not None:
            e.wait_ge(gate_in[0], gate_in[1])

    NW = 11

    def sync_ops2(e):
        gate(e)
        loads = [
            (g_row, w["ln_g"][None, :]),
            (b_row, w["ln_b"][None, :]),
            (bnm, w["bn_mean"][None, :]),
            (bnv, w["bn_var"][None, :]),
            (bng, w["bn_g"][None, :]),
            (bnb, w["bn_b"][None, :]),
            (br_row, w["br"][None, :]),
            (wr_sb, w["Wr"].rearrange("(c p) e -> p c e", p=128)),
            (bdown_sb, w["b_down"]),
            (bup_sb, w["b_up"]),
        ]
        nb = 0
        for dst, srcap in loads:
            if nb >= 1:
                e.wait_ge(sem["bXR"], 16 * nb)
            e.dma_start(out=dst, in_=srcap).then_inc(sem["bXR"], 16)
            nb += 1
        for ee in range(8):
            e.wait_ge(sem["bXR"], 16 * nb)
            e.dma_start(out=wd_all[:, 512 * ee:512 * ee + 512]
                        .rearrange("p (c h) -> p c h", c=8),
                        in_=w["W_down"][ee].rearrange("(c p) h -> p c h", p=128)) \
                .then_inc(sem["bXR"], 16)
            nb += 1
        if not resident_in:
            xr = io["x_in"].rearrange("(t p) e -> t p e", p=128)
            for t in range(NT):
                e.wait_ge(sem["bXR"], 16 * nb)
                e.dma_start(out=R[:, 1024 * t:1024 * t + 1024], in_=xr[t]) \
                    .then_inc(sem["bXR"], 16)
                nb += 1

    NW = 18

    def vector_ops(e):
        gate(e)
        e.memset(ones1f, 1.0)
        e.memset(onec, 1.0)
        e.memset(ones1_bf, 1.0)
        for t in range(NT):
            if not resident_in:
                e.wait_ge(sem["bXR"], 16 * (NW + t + 1))
            else:
                e.wait_ge(sem["bXR"], 16 * NW)
            Rt = R[:, 1024 * t:1024 * t + 1024]
            e.reduce_sum(out=stc(t, 0), in_=Rt, axis=AX)
            e.wait_ge(sem["bS2"], t + 1)
            e.tensor_add(out=stc(t, 2), in0=stc(t, 2), in1=stc(t, 9)) \
                .then_inc(sem["bDV"], 1)
            e.wait_ge(sem["bDV"], 5 * t + 1)
            e.tensor_scalar(out=stc(t, 1), in0=stc(t, 0), scalar1=1.0 / D,
                            scalar2=None, op0=ALU.mult)
            e.tensor_scalar(out=stc(t, 3), in0=stc(t, 2), scalar1=1.0 / D,
                            scalar2=None, op0=ALU.mult)
            e.scalar_tensor_tensor(out=stc(t, 4), in0=stc(t, 0),
                                   scalar=-1.0 / (D * D), in1=stc(t, 0),
                                   op0=ALU.mult, op1=ALU.mult) \
                .then_inc(sem["bDV"], 1)
            e.wait_ge(sem["bDV"], 5 * t + 2)
            e.tensor_add(out=stc(t, 5), in0=stc(t, 3), in1=stc(t, 4)) \
                .then_inc(sem["bVR"], 1)
            e.wait_ge(sem["bRS"], t + 1)
            za = xa[:, 1024 * (t % 2):1024 * (t % 2) + 1024]
            e.tensor_scalar(out=za, in0=Rt, scalar1=stc(t, 1),
                            scalar2=stc(t, 7), op0=ALU.subtract, op1=ALU.mult) \
                .then_inc(sem["bDV"], 1)
            e.wait_ge(sem["bDV"], 5 * t + 3)
            e.wait_ge(sem["bHE"], 4)
            e.tensor_mul(out=za, in0=za, in1=g_bc).then_inc(sem["bDV"], 1)
            e.wait_ge(sem["bDV"], 5 * t + 4)
            e.tensor_add(out=lnbf[:, 1024 * (t % 2):1024 * (t % 2) + 1024],
                         in0=za, in1=b_bc).then_inc(sem["bZ"], 1)
            e.nop().then_inc(sem["bDV"], 1)
        tc.emit("vector", e)
        for t in range(NT):
            for dh in range(2):
                n = 2 * t + dh
                e.wait_ge(sem["bYM"], n + 1)
                ya = xa[:, 1024 * (t % 2) + 512 * dh:
                        1024 * (t % 2) + 512 * dh + 512]
                e.tensor_scalar(out=ya, in0=ps[2 + n % 2][:], scalar1=gate_bc,
                                scalar2=None, op0=ALU.mult).then_inc(sem["bYE"], 1)
            yt = xa[:, 1024 * (t % 2):1024 * (t % 2) + 1024]
            e.wait_ge(sem["bYE"], 2 * t + 2)
            e.reduce_sum(out=stc(t, 0), in_=yt, axis=AX)
            e.wait_ge(sem["bS2"], NT + t + 1)
            e.tensor_add(out=stc(t, 2), in0=stc(t, 2), in1=stc(t, 9)) \
                .then_inc(sem["bDV"], 1)
            e.wait_ge(sem["bDV"], 5 * NT + 3 * t + 1)
            e.tensor_scalar(out=stc(t, 1), in0=stc(t, 0), scalar1=1.0 / D,
                            scalar2=None, op0=ALU.mult)
            e.tensor_scalar(out=stc(t, 3), in0=stc(t, 2), scalar1=1.0 / D,
                            scalar2=None, op0=ALU.mult)
            e.scalar_tensor_tensor(out=stc(t, 4), in0=stc(t, 0),
                                   scalar=-1.0 / (D * D), in1=stc(t, 0),
                                   op0=ALU.mult, op1=ALU.mult) \
                .then_inc(sem["bDV"], 1)
            e.wait_ge(sem["bDV"], 5 * NT + 3 * t + 2)
            e.tensor_add(out=stc(t, 5), in0=stc(t, 3), in1=stc(t, 4)) \
                .then_inc(sem["bVR"], 1)
            e.wait_ge(sem["bRS"], NT + t + 1)
            e.tensor_scalar(out=yt, in0=yt, scalar1=stc(t, 1),
                            scalar2=stc(t, 7), op0=ALU.subtract, op1=ALU.mult) \
                .then_inc(sem["bDV"], 1)
            e.wait_ge(sem["bDV"], 5 * NT + 3 * t + 3)
            Rt = R[:, 1024 * t:1024 * t + 1024]
            e.tensor_add(out=Rt, in0=Rt, in1=yt).then_inc(sem["bRES"], 1)

    def scalar_ops(e):
        gate(e)
        bdsts = [g_bc[:, 0:512], g_bc[:, 512:1024],
                 b_bc[:, 0:512], b_bc[:, 512:1024]]
        for k in range(4):
            e.wait_ge(sem["bHM"], k + 1)
            e.activation(out=bdsts[k], in_=ps[k % 2][:], func=AF.Copy) \
                .then_inc(sem["bHE"], 1)
        for t in range(NT):
            if not resident_in:
                e.wait_ge(sem["bXR"], 16 * (NW + t + 1))
            else:
                e.wait_ge(sem["bXR"], 16 * NW)
            Rt = R[:, 1024 * t:1024 * t + 1024]
            e.activation(out=sqscr[:], in_=Rt[:, 0:512], func=AF.Square,
                         accum_out=stc(t, 2))
            e.activation(out=sqscr[:], in_=Rt[:, 512:1024], func=AF.Square,
                         accum_out=stc(t, 9)).then_inc(sem["bS2"], 1)
            e.wait_ge(sem["bVR"], t + 1)
            e.activation(out=stc(t, 6), in_=stc(t, 5), func=AF.Ln,
                         bias=epsc[:]).then_inc(sem["bLX"], 1)
            e.wait_ge(sem["bLX"], t + 1)
            e.activation(out=stc(t, 7), in_=stc(t, 6), func=AF.Exp,
                         scale=-0.5).then_inc(sem["bRS"], 1)
            e.wait_ge(sem["bTR"], t + 1)
            dst = zT.rearrange("p (c s) -> p c s", c=8)[:, :, 128 * t:128 * t + 128]
            e.activation(out=dst, in_=psbf.rearrange("p (c s) -> p c s", c=8),
                         func=AF.Copy).then_inc(sem["bZT"], 1)
        tc.emit("scalar", e)
        for k in range(4):
            e.wait_ge(sem["bHM"], 4 + k + 1)
            e.activation(out=hT[:, 512 * k:512 * k + 512], in_=ps[k % 2][0:64, :],
                         func=AF.Relu, bias=bdT).then_inc(sem["bHE"], 1)
        for t in range(NT):
            e.wait_ge(sem["bYE"], 2 * t + 2)
            yt = xa[:, 1024 * (t % 2):1024 * (t % 2) + 1024]
            e.activation(out=sqscr[:], in_=yt[:, 0:512], func=AF.Square,
                         accum_out=stc(t, 2))
            e.activation(out=sqscr[:], in_=yt[:, 512:1024], func=AF.Square,
                         accum_out=stc(t, 9)).then_inc(sem["bS2"], 1)
            e.wait_ge(sem["bVR"], NT + t + 1)
            e.activation(out=stc(t, 6), in_=stc(t, 5), func=AF.Ln,
                         bias=epsc[:]).then_inc(sem["bLX"], 1)
            e.wait_ge(sem["bLX"], NT + t + 1)
            e.activation(out=stc(t, 7), in_=stc(t, 6), func=AF.Exp,
                         scale=-0.5).then_inc(sem["bRS"], 1)

    def tensor_ops(e):
        gate(e)
        e.wait_ge(sem["bXR"], 16 * NW)
        srcs = [g_row[:, 0:512], g_row[:, 512:1024],
                b_row[:, 0:512], b_row[:, 512:1024]]
        for k, srow in enumerate(srcs):
            if k >= 2:
                e.wait_ge(sem["bHE"], k - 1)
            e.matmul(ps[k % 2][:], ones1f, srow, start=True, stop=True) \
                .then_inc(sem["bHM"], 1)
        for t in range(NT):
            e.wait_ge(sem["bZ"], t + 1)
            if t >= 1:
                e.wait_ge(sem["bZT"], t)
            srcz = lnbf[:, 1024 * (t % 2):1024 * (t % 2) + 1024]
            last = None
            for dc in range(8):
                last = e.transpose(psbf[:, 128 * dc:128 * dc + 128],
                                   srcz[:, 128 * dc:128 * dc + 128], bufs.eye[:])
            last.then_inc(sem["bTR"], 1)
            for hh in range(2):
                last = e.matmul(ps[2 + hh][0:1, :], bufs.ones4[:, 0:1],
                                srcz[:, 512 * hh:512 * hh + 512],
                                start=(t == 0), stop=(t == NT - 1),
                                skip_group_check=True)
            if t == NT - 1:
                last.then_inc(sem["bCS"], 1)
        tc.emit("tensor", e)
        e.wait_ge(sem["bSS"], TCK)
        e.wait_ge(sem["bZT"], NT)
        for k in range(4):
            if k >= 2:
                e.wait_ge(sem["bHE"], 4 + k - 1)
            last = None
            for dc in range(8):
                last = e.matmul(ps[k % 2][0:64, :],
                                wd_sel_bf[:, 64 * dc:64 * dc + 64],
                                zT[:, 2048 * dc + 512 * k:2048 * dc + 512 * k + 512],
                                start=(dc == 0), stop=(dc == 7))
            last.then_inc(sem["bHM"], 1)
        e.wait_ge(sem["bHE"], 8)
        for t in range(NT):
            for dh in range(2):
                n = 2 * t + dh
                if n >= 2:
                    e.wait_ge(sem["bYE"], n - 1)
                pst = ps[2 + n % 2]
                e.matmul(pst[:], hT[:, 128 * t:128 * t + 128],
                         wu_sel_bf[:, 512 * dh:512 * dh + 512],
                         start=True, stop=False)
                e.matmul(pst[:], ones1_bf, bu_row_bf[:, 512 * dh:512 * dh + 512],
                         start=False, stop=True).then_inc(sem["bYM"], 1)

    def gpsimd_ops(e):
        gate(e)
        tc.emit("gpsimd", e)
        if store_out:
            e.wait_ge(sem["bRES"], NT)
            nout = 0
            xo = io["x_out"].rearrange("(q p) e -> q p e", p=128)
            for t in range(NT):
                e.dma_start(out=xo[t], in_=R[:, 1024 * t:1024 * t + 1024]) \
                    .then_inc(sem["bOUT"], 16)
                nout += 1
            if "af_d" in io:
                e.dma_start(out=io["af_d"], in_=bufs.A_bf[:]).then_inc(sem["bOUT"], 16)
                e.dma_start(out=io["cf_d"], in_=bufs.C_bf[:]).then_inc(sem["bOUT"], 16)
                e.dma_start(out=io["pt_d"], in_=pTb[:]).then_inc(sem["bOUT"], 16)
                e.dma_start(out=io["zt_d"], in_=zT[:]).then_inc(sem["bOUT"], 16)
                nout += 4
            e.wait_ge(sem["bOUT"], 16 * nout)

    return dict(sync=sync_ops2, vector=vector_ops, scalar=scalar_ops,
                tensor=tensor_ops, gpsimd=gpsimd_ops)


def alloc_common(nc, ctx):
    b = {}
    b["R_f32"] = ctx.enter_context(nc.sbuf_tensor("R_f32", [128, 16384], F32)).ap()
    b["A_bf"] = ctx.enter_context(nc.sbuf_tensor("A_bf", [128, 16384], BF16)).ap()
    b["B_bf"] = ctx.enter_context(nc.sbuf_tensor("B_bf", [128, 16384], BF16)).ap()
    b["C_bf"] = ctx.enter_context(nc.sbuf_tensor("C_bf", [128, 16384], BF16)).ap()
    b["W_bf"] = ctx.enter_context(nc.sbuf_tensor("W_bf", [128, 8192], BF16)).ap()
    b["pT"] = ctx.enter_context(nc.sbuf_tensor("pT", [128, 8192], BF16)).ap()
    b["lnbf"] = ctx.enter_context(nc.sbuf_tensor("lnbf", [128, 2048], BF16)).ap()
    b["xa"] = ctx.enter_context(nc.sbuf_tensor("xa", [128, 2048], F32)).ap()
    b["sqscr"] = ctx.enter_context(nc.sbuf_tensor("sqscr", [128, 512], F32)).ap()
    b["stats"] = ctx.enter_context(nc.sbuf_tensor("stats", [128, 32], F32)).ap()
    b["dn_sb"] = ctx.enter_context(nc.sbuf_tensor("dn_sb", [128, 64], F32)).ap()
    b["rd_all"] = ctx.enter_context(nc.sbuf_tensor("rd_all", [128, 16], F32)).ap()
    b["ones4"] = ctx.enter_context(nc.sbuf_tensor("ones4", [128, 4], BF16)).ap()
    b["epsc"] = ctx.enter_context(nc.sbuf_tensor("epsc", [128, 1], F32)).ap()
    b["eye"] = ctx.enter_context(nc.sbuf_tensor("eye", [128, 128], BF16)).ap()
    b["wbf"] = ctx.enter_context(nc.sbuf_tensor("wbf", [128, 256], BF16)).ap()
    b["ccs"] = ctx.enter_context(nc.sbuf_tensor("ccs", [128, 32], F32)).ap()
    b["mask_lt"] = ctx.enter_context(nc.sbuf_tensor("mask_lt", [128, 4], F32)).ap()
    b["mask_prev"] = ctx.enter_context(nc.sbuf_tensor("mask_prev", [128, 4], F32)).ap()
    b["first_m"] = ctx.enter_context(nc.sbuf_tensor("first_m", [128, 1], F32)).ap()
    b["ps"] = [ctx.enter_context(nc.psum_tensor(f"ps{i}", [128, 512], F32)).ap()
               for i in range(4)]
    b["psbf"] = ctx.enter_context(nc.psum_tensor("psbf", [128, 1024], BF16)).ap()
    b["ps64"] = ctx.enter_context(nc.psum_tensor("ps64", [128, 64], F32)).ap()
    return NS(b)


_NC_CACHE = {}


def build_launch_a(dbg=False):
    key = ("a", dbg)
    if key in _NC_CACHE:
        return _NC_CACHE[key]
    nc = bass.Bass("TRN2", target_bir_lowering=False, debug=False)
    io = {}
    io["x_own"] = nc.dram_tensor("x_own", [SC, D], F32, kind="ExternalInput").ap()
    io["kern"] = nc.dram_tensor("kern", [3, D, D], F32, kind="ExternalInput").ap()
    eye_in = nc.dram_tensor("eye_in", [128, 128], BF16, kind="ExternalInput").ap()
    io["kv_in"] = nc.dram_tensor("kv_in", [128, 32768], BF16, kind="Internal").ap()
    io["kv_g"] = nc.dram_tensor("kv_g", [512, 32768], BF16, kind="Internal").ap()
    io["x1_ext"] = nc.dram_tensor("x1", [SC, D], F32, kind="ExternalOutput").ap()
    if dbg:
        io["qt_d"] = nc.dram_tensor("qt_d", [128, 16384], BF16, kind="ExternalOutput").ap()
        io["kv_d"] = nc.dram_tensor("kv_d", [128, 32768], BF16, kind="ExternalOutput").ap()
        io["kvg_d"] = nc.dram_tensor("kvg_d", [512, 32768], BF16, kind="ExternalOutput").ap()
        io["dn_d"] = nc.dram_tensor("dn_d", [128, 64], F32, kind="ExternalOutput").ap()
    ctx = ExitStack()
    with ctx:
        bufs = alloc_common(nc, ctx)
        sem = make_sems(nc, ctx, A_SEMS)
        ops = build_stage_a(sem, bufs, io, store_out=True, dbg=dbg)
        block = ctx.enter_context(nc.Block())

        @block.sync
        def _(e):
            e.dma_start(out=bufs.eye, in_=eye_in).then_inc(sem["aEYE"], 16)
            ops["sync"](e)

        @block.vector
        def _(e):
            ops["vector"](e)

        @block.scalar
        def _(e):
            ops["scalar"](e)

        @block.tensor
        def _(e):
            ops["tensor"](e)

        @block.gpsimd
        def _(e):
            ops["gpsimd"](e)
    return nc


C_SEMS = ["cXR", "cS2", "cVR", "cRS", "cLX", "cDV", "cZ", "cTR", "cLT",
          "cWL", "cWC", "cWT", "cWE", "cGM", "cGE", "cSS", "cDM", "cBP",
          "cYM", "cYE", "cX2", "cRES", "cOUT"]

C2 = 0.02 / 32.0
C3 = 0.02 / 32.0


def build_stage_c(sem, bufs, io, store_out, resident_in, gate_in=None, trunc=None):
    R = bufs.R_f32
    ln2T = bufs.A_bf
    OT = bufs.B_bf
    GT = bufs.C_bf
    WQ = bufs.W_bf
    WK = bufs.pT
    xa = bufs.xa
    lnbf = bufs.lnbf
    sqscr = bufs.sqscr
    st = bufs.stats
    epsc = bufs.epsc
    eye = bufs.eye
    ps = bufs.ps
    psbf = bufs.psbf
    wbf = bufs.wbf

    w = io["w"]
    mask_lt = bufs.mask_lt
    mask_prev = bufs.mask_prev
    first_m = bufs.first_m

    pay = xa[:, 0:5]
    pay_g = bufs.ccs[:, 0:20].rearrange("p (j f) -> p j f", f=5)
    carryK = xa[:, 32:33]
    carryKV = xa[:, 33:34]
    vprev = xa[:, 34:35]
    qprev = xa[:, 35:36]
    gprev = xa[:, 36:37]
    t_a = xa[:, 38:39]
    t_b = xa[:, 39:40]
    oarprev = xa[:, 40:41]
    pay2 = xa[:, 42:43]
    pay2_g = bufs.ccs[:, 24:28]
    carry2 = xa[:, 48:49]
    bq1T = xa[:, 56:64]
    bk1T = xa[:, 64:72]
    bk2T = xa[:, 72:80]
    bq1c = xa[:, 80:88]
    bk2pc = xa[:, 88:96]
    bprow = xa[0:1, 96:1120]
    wstg = xa[:, 1152:1408]
    bprow_bf = ln2T[0:1, 0:1024]
    ones1_bf = ln2T[0:1, 1024:1152]

    def slot(i):
        return R[:, 2048 * i:2048 * i + 2048]

    def stc(t, i):
        o = 16 * (t % 2) + i
        return st[:, o:o + 1]

    def gate(e):
        if gate_in is not None:
            e.wait_ge(gate_in[0], gate_in[1])

    NWC = 7
    WMATS = ["Wk2", "Wq1", "Wk1", "Wproj"]
    WDST = [WQ, WQ, WK, WQ]

    tc = TokenChain(sem["cSS"], sem["cDM"])

    def _chain():
        tc.add("vector", lambda e: e.tensor_scalar(
            out=bq1c, in0=bq1T, scalar1=0.5, scalar2=None, op0=ALU.mult))
        tc.add("vector", lambda e: e.tensor_scalar(
            out=bk2pc, in0=bk2T, scalar1=C2 / 2.0, scalar2=None, op0=ALU.mult))
        sK, sKV, sQ, sKc, sKVc, sS, sOar, sE = [slot(i) for i in range(8)]
        for dc in range(8):
            for wmat, outslot, act in (
                    (WK, 0, "exp"), (WQ, 2, "tanh")):
                for s4 in range(4):
                    def mm(e, wmat=wmat, dc=dc, s4=s4):
                        last = None
                        for di in range(8):
                            last = e.matmul(
                                ps[s4 % 2][:],
                                wmat[:, di * 1024 + dc * 128:di * 1024 + dc * 128 + 128],
                                ln2T[:, di * 2048 + 512 * s4:di * 2048 + 512 * s4 + 512],
                                start=(di == 0), stop=(di == 7))
                        return last
                    tc.add("tensor", mm)
                    tc.add("vector", lambda e: e.nop())
                    if act == "exp":
                        tc.add("scalar", lambda e, dc=dc, s4=s4, o=outslot:
                               e.activation(out=slot(o)[:, 512 * s4:512 * s4 + 512],
                                            in_=ps[s4 % 2][:], func=AF.Exp,
                                            bias=bk1T[:, dc:dc + 1]))
                    else:
                        tc.add("scalar", lambda e, dc=dc, s4=s4, o=outslot:
                               e.activation(out=slot(o)[:, 512 * s4:512 * s4 + 512],
                                            in_=ps[s4 % 2][:], func=AF.Tanh,
                                            scale=0.5, bias=bq1c[:, dc:dc + 1]))
            tc.add("vector", lambda e: e.tensor_scalar(
                out=sQ, in0=sQ, scalar1=0.5, scalar2=0.5, op0=ALU.mult,
                op1=ALU.add))
            tc.add("vector", lambda e, dc=dc: e.tensor_mul(
                out=sKV, in0=sK, in1=ln2T[:, dc * 2048:dc * 2048 + 2048]))
            tc.add("vector", lambda e: e.tensor_tensor_scan(
                out=sKc, data0=sK, data1=sK, initial=0.0, op0=ALU.add,
                op1=ALU.bypass))
            tc.add("vector", lambda e: e.tensor_tensor_scan(
                out=sKVc, data0=sKV, data1=sKV, initial=0.0, op0=ALU.add,
                op1=ALU.bypass))
            tc.mark(f"scans{dc}")
            tc.add("gpsimd", lambda e, dc=dc: e.dma_start(
                out=io["cc1_in"][dc][:, 0:1], in_=sKc[:, 2047:2048],
                allow_slow_non_contiguous=True), dma=True)
            tc.add("gpsimd", lambda e, dc=dc: e.dma_start(
                out=io["cc1_in"][dc][:, 1:2], in_=sKVc[:, 2047:2048],
                allow_slow_non_contiguous=True), dma=True)
            tc.add("vector", lambda e, dc=dc: e.tensor_copy(
                out=pay[:, 2:3], in_=ln2T[:, dc * 2048 + 2047:dc * 2048 + 2048]))
            tc.add("vector", lambda e, dc=dc: e.tensor_copy(
                out=pay[:, 4:5], in_=GT[:, dc * 2048 + 2047:dc * 2048 + 2048]))
            tc.add("gpsimd", lambda e, dc=dc: e.dma_start(
                out=io["cc1_in"][dc][:, 3:4], in_=sQ[:, 2047:2048],
                allow_slow_non_contiguous=True), dma=True)
            tc.add("gpsimd", lambda e, dc=dc: e.dma_start(
                out=io["cc1_in"][dc][:, 2:3], in_=pay[:, 2:3],
                allow_slow_non_contiguous=True), dma=True)
            tc.add("gpsimd", lambda e, dc=dc: e.dma_start(
                out=io["cc1_in"][dc][:, 4:5], in_=pay[:, 4:5],
                allow_slow_non_contiguous=True), dma=True)
            tc.add("gpsimd", lambda e, dc=dc: e.collective_compute(
                "AllGather", ALU.bypass, replica_groups=RG,
                ins=[io["cc1_in"][dc].opt()], outs=[io["cc1_out"][dc].opt()]))
            tc.add("gpsimd", lambda e, dc=dc: e.dma_start(
                out=pay_g, in_=io["cc1_out"][dc].rearrange("(j p) f -> p j f",
                                                           p=128)), dma=True)
            for j in range(4):
                def acc1(e, j=j):
                    if j == 0:
                        e.tensor_scalar(out=carryK, in0=pay_g[:, 0, 0:1],
                                        scalar1=mask_lt[:, 0:1], scalar2=None,
                                        op0=ALU.mult)
                        e.tensor_scalar(out=carryKV, in0=pay_g[:, 0, 1:2],
                                        scalar1=mask_lt[:, 0:1], scalar2=None,
                                        op0=ALU.mult)
                        e.tensor_scalar(out=vprev, in0=pay_g[:, 0, 2:3],
                                        scalar1=mask_prev[:, 0:1], scalar2=None,
                                        op0=ALU.mult)
                        e.tensor_scalar(out=qprev, in0=pay_g[:, 0, 3:4],
                                        scalar1=mask_prev[:, 0:1], scalar2=None,
                                        op0=ALU.mult)
                        return e.tensor_scalar(out=gprev, in0=pay_g[:, 0, 4:5],
                                               scalar1=mask_prev[:, 0:1],
                                               scalar2=None, op0=ALU.mult)
                    e.scalar_tensor_tensor(out=carryK, in0=pay_g[:, j, 0:1],
                                           scalar=mask_lt[:, j:j + 1], in1=carryK,
                                           op0=ALU.mult, op1=ALU.add)
                    e.scalar_tensor_tensor(out=carryKV, in0=pay_g[:, j, 1:2],
                                           scalar=mask_lt[:, j:j + 1], in1=carryKV,
                                           op0=ALU.mult, op1=ALU.add)
                    e.scalar_tensor_tensor(out=vprev, in0=pay_g[:, j, 2:3],
                                           scalar=mask_prev[:, j:j + 1], in1=vprev,
                                           op0=ALU.mult, op1=ALU.add)
                    e.scalar_tensor_tensor(out=qprev, in0=pay_g[:, j, 3:4],
                                           scalar=mask_prev[:, j:j + 1], in1=qprev,
                                           op0=ALU.mult, op1=ALU.add)
                    return e.scalar_tensor_tensor(out=gprev, in0=pay_g[:, j, 4:5],
                                                  scalar=mask_prev[:, j:j + 1],
                                                  in1=gprev, op0=ALU.mult,
                                                  op1=ALU.add)
                tc.add("vector", acc1)
            tc.add("vector", lambda e: e.tensor_scalar(
                out=t_a, in0=carryK, scalar1=1e-6, scalar2=None, op0=ALU.add))
            tc.add("vector", lambda e: e.reciprocal(out=t_b, in_=t_a))
            tc.add("vector", lambda e: e.tensor_mul(out=t_a, in0=carryKV,
                                                    in1=t_b))
            tc.add("vector", lambda e: e.tensor_mul(out=oarprev, in0=t_a,
                                                    in1=qprev))
            tc.add("vector", lambda e: e.tensor_scalar(
                out=sKc, in0=sKc, scalar1=carryK, scalar2=1e-6, op0=ALU.add,
                op1=ALU.add))
            tc.add("vector", lambda e: e.tensor_scalar(
                out=sKVc, in0=sKVc, scalar1=carryKV, scalar2=None, op0=ALU.add))
            tc.add("vector", lambda e: e.reciprocal(out=sS, in_=sKc))
            tc.add("vector", lambda e: e.tensor_mul(out=sKV, in0=sKVc, in1=sS))
            tc.add("vector", lambda e: e.tensor_mul(out=sOar, in0=sKV, in1=sQ))
            tc.add("vector", lambda e, dc=dc: e.tensor_sub(
                out=sE[:, 0:1], in_=None, in0=ln2T[:, dc * 2048:dc * 2048 + 1],
                in1=oarprev) if False else e.tensor_sub(
                out=sE[:, 0:1], in0=ln2T[:, dc * 2048:dc * 2048 + 1],
                in1=oarprev))
            tc.add("vector", lambda e: e.tensor_scalar(
                out=sE[:, 0:1], in0=sE[:, 0:1], scalar1=first_m[:, 0:1],
                scalar2=None, op0=ALU.mult))
            tc.add("vector", lambda e, dc=dc: e.tensor_sub(
                out=sE[:, 1:2048], in0=ln2T[:, dc * 2048 + 1:dc * 2048 + 2048],
                in1=sOar[:, 0:2047]))
            tc.add("scalar", lambda e, dc=dc: e.activation(
                out=sS[:, 0:1], in_=gprev, func=AF.Tanh, scale=C2 / 2.0,
                bias=bk2pc[:, dc:dc + 1]))
            tc.add("scalar", lambda e, dc=dc: e.activation(
                out=sS[:, 1:2048], in_=GT[:, dc * 2048:dc * 2048 + 2047],
                func=AF.Tanh, scale=C2 / 2.0, bias=bk2pc[:, dc:dc + 1]))
            tc.add("vector", lambda e: e.tensor_scalar(
                out=sS, in0=sS, scalar1=0.5, scalar2=0.5, op0=ALU.mult,
                op1=ALU.add))
            tc.add("vector", lambda e: e.tensor_mul(out=sE, in0=sS, in1=sE))
            tc.add("vector", lambda e: e.tensor_tensor_scan(
                out=sK, data0=sE, data1=sE, initial=0.0, op0=ALU.add,
                op1=ALU.bypass))
            tc.add("gpsimd", lambda e, dc=dc: e.dma_start(
                out=io["cc2_in"][dc], in_=sK[:, 2047:2048],
                allow_slow_non_contiguous=True), dma=True)
            tc.add("gpsimd", lambda e, dc=dc: e.collective_compute(
                "AllGather", ALU.bypass, replica_groups=RG,
                ins=[io["cc2_in"][dc].opt()], outs=[io["cc2_out"][dc].opt()]))
            tc.add("gpsimd", lambda e, dc=dc: e.dma_start(
                out=pay2_g, in_=io["cc2_out"][dc].rearrange("(j p) f -> p (j f)",
                                                            p=128),
                allow_slow_non_contiguous=True), dma=True)
            for j in range(4):
                def acc3(e, j=j):
                    if j == 0:
                        return e.tensor_scalar(out=carry2, in0=pay2_g[:, 0:1],
                                               scalar1=mask_lt[:, 0:1],
                                               scalar2=None, op0=ALU.mult)
                    return e.scalar_tensor_tensor(out=carry2,
                                                  in0=pay2_g[:, j:j + 1],
                                                  scalar=mask_lt[:, j:j + 1],
                                                  in1=carry2, op0=ALU.mult,
                                                  op1=ALU.add)
                tc.add("vector", acc3)
            tc.add("vector", lambda e: e.tensor_scalar(
                out=sK, in0=sK, scalar1=carry2, scalar2=None, op0=ALU.add))
            tc.add("vector", lambda e: e.scalar_tensor_tensor(
                out=sS[:, 0:1], in0=sK[:, 0:1], scalar=C3, in1=qprev,
                op0=ALU.mult, op1=ALU.mult))
            tc.add("vector", lambda e: e.scalar_tensor_tensor(
                out=sS[:, 1:2048], in0=sK[:, 1:2048], scalar=C3,
                in1=sQ[:, 0:2047], op0=ALU.mult, op1=ALU.mult))
            tc.add("vector", lambda e, dc=dc: e.tensor_add(
                out=OT[:, dc * 2048:dc * 2048 + 2048], in0=sOar, in1=sS))
            tc.mark(f"done{dc}")
    _chain()
    TCK = tc.k

    def sync_ops(e):
        gate(e)
        e.dma_start(out=bq1T, in_=w["bq1"].rearrange("(c p) -> p c", p=128),
                    allow_slow_non_contiguous=True).then_inc(sem["cXR"], 16)
        e.wait_ge(sem["cXR"], 16)
        e.dma_start(out=bk1T, in_=w["bk1"].rearrange("(c p) -> p c", p=128),
                    allow_slow_non_contiguous=True).then_inc(sem["cXR"], 16)
        e.wait_ge(sem["cXR"], 32)
        e.dma_start(out=bk2T, in_=w["bk2"].rearrange("(c p) -> p c", p=128),
                    allow_slow_non_contiguous=True).then_inc(sem["cXR"], 16)
        e.wait_ge(sem["cXR"], 48)
        e.dma_start(out=bprow, in_=w["bproj"][None, :]).then_inc(sem["cXR"], 16)
        e.wait_ge(sem["cXR"], 64)
        e.dma_start(out=mask_lt, in_=io["mask_lt"]).then_inc(sem["cXR"], 16)
        e.wait_ge(sem["cXR"], 80)
        e.dma_start(out=mask_prev, in_=io["mask_prev"]).then_inc(sem["cXR"], 16)
        e.wait_ge(sem["cXR"], 96)
        e.dma_start(out=first_m, in_=io["first_m"]).then_inc(sem["cXR"], 16)
        if not resident_in:
            xr = io["x_in"].rearrange("(t p) e -> t p e", p=128)
            for t in range(NT):
                e.wait_ge(sem["cXR"], 16 * (NWC + t))
                e.dma_start(out=R[:, 1024 * t:1024 * t + 1024], in_=xr[t]) \
                    .then_inc(sem["cXR"], 16)
        else:
            e.wait_ge(sem["cZ"], NT)
            e.wait_ge(sem["cS2"], NT)
            xo = io["x2_sp"].rearrange("(t p) e -> t p e", p=128)
            for t in range(NT):
                e.dma_start(out=xo[t], in_=R[:, 1024 * t:1024 * t + 1024]) \
                    .then_inc(sem["cX2"], 16)
            e.wait_ge(sem["cX2"], 16 * NT)
        if trunc == -1:
            return
        nmat = 4 if trunc is None else (1 if trunc == -4 else 3)
        for m in range(nmat):
            wm = w[WMATS[m]]
            for i in range(64):
                eo, do = i // 8, i % 8
                n = m * 64 + i
                if n >= 1:
                    e.wait_ge(sem["cWL"], 16 * n)
                if n >= 2:
                    e.wait_ge(sem["cWC"], n - 1)
                e.dma_start(out=wstg[:, 128 * (n % 2):128 * (n % 2) + 128],
                            in_=wm[128 * eo:128 * eo + 128, 128 * do:128 * do + 128]) \
                    .then_inc(sem["cWL"], 16)
        if trunc is not None:
            return
        xsrc = io["x_in"] if not resident_in else io["x2_sp"]
        xr2 = xsrc.rearrange("(t p) e -> t p e", p=128)
        e.wait_ge(sem["cSS"], TCK)
        e.wait_ge(sem["cWC"], 256)
        e.wait_ge(sem["cBP"], 1)
        base = 16 * NT if resident_in else 0
        for t in range(NT):
            if t >= 1:
                e.wait_ge(sem["cX2"], base + 16 * t)
            if t >= 2:
                e.wait_ge(sem["cRES"], t - 1)
            e.dma_start(out=xa[:, 1024 * (t % 2):1024 * (t % 2) + 1024],
                        in_=xr2[t]).then_inc(sem["cX2"], 16)

    def vector_ops(e):
        gate(e)
        for t in range(NT):
            if not resident_in:
                e.wait_ge(sem["cXR"], 16 * (NWC + t + 1))
            else:
                e.wait_ge(sem["cXR"], 16 * NWC)
            Rt = R[:, 1024 * t:1024 * t + 1024]
            e.reduce_sum(out=stc(t, 0), in_=Rt, axis=AX)
            e.wait_ge(sem["cS2"], t + 1)
            e.tensor_add(out=stc(t, 2), in0=stc(t, 2), in1=stc(t, 9)) \
                .then_inc(sem["cDV"], 1)
            e.wait_ge(sem["cDV"], 2 * t + 1)
            e.tensor_scalar(out=stc(t, 1), in0=stc(t, 0), scalar1=1.0 / D,
                            scalar2=None, op0=ALU.mult)
            e.tensor_scalar(out=stc(t, 3), in0=stc(t, 2), scalar1=1.0 / D,
                            scalar2=None, op0=ALU.mult)
            e.scalar_tensor_tensor(out=stc(t, 4), in0=stc(t, 0),
                                   scalar=-1.0 / (D * D), in1=stc(t, 0),
                                   op0=ALU.mult, op1=ALU.mult) \
                .then_inc(sem["cDV"], 1)
            e.wait_ge(sem["cDV"], 2 * t + 2)
            e.tensor_add(out=stc(t, 5), in0=stc(t, 3), in1=stc(t, 4)) \
                .then_inc(sem["cVR"], 1)
            e.wait_ge(sem["cRS"], t + 1)
            e.tensor_scalar(out=lnbf[:, 1024 * (t % 2):1024 * (t % 2) + 1024],
                            in0=Rt, scalar1=stc(t, 1), scalar2=stc(t, 7),
                            op0=ALU.subtract, op1=ALU.mult) \
                .then_inc(sem["cZ"], 1)
        ncast_pre = {None: 192, -1: 0, -3: 256, -4: 64}.get(trunc, 192)
        for n in range(ncast_pre):
            e.wait_ge(sem["cWL"], 16 * (n + 1))
            if n >= 2 and trunc != -3:
                e.wait_ge(sem["cWT"], n - 1)
            e.tensor_copy(out=wbf[:, 128 * (n % 2):128 * (n % 2) + 128],
                          in_=wstg[:, 128 * (n % 2):128 * (n % 2) + 128]) \
                .then_inc(sem["cWC"], 1)
        e.wait_ge(sem["cXR"], 16 * NWC)
        if trunc is None or trunc >= 0:
            e.wait_ge(sem["cGE"], 32)
        if resident_in:
            e.wait_ge(sem["cX2"], 16 * NT)
        tc.emit("vector", e, limit=max(trunc, 0) if trunc is not None else None)
        if trunc is not None:
            return
        for n in range(192, 256):
            e.wait_ge(sem["cWL"], 16 * (n + 1))
            e.wait_ge(sem["cWT"], n - 1)
            e.tensor_copy(out=wbf[:, 128 * (n % 2):128 * (n % 2) + 128],
                          in_=wstg[:, 128 * (n % 2):128 * (n % 2) + 128]) \
                .then_inc(sem["cWC"], 1)
        e.tensor_copy(out=bprow_bf, in_=bprow)
        e.memset(ones1_bf, 1.0).then_inc(sem["cBP"], 1)
        for t in range(NT):
            Rt = R[:, 1024 * t:1024 * t + 1024]
            e.wait_ge(sem["cYE"], 2 * t + 2)
            e.reduce_sum(out=stc(t, 0), in_=Rt, axis=AX)
            e.wait_ge(sem["cS2"], NT + t + 1)
            e.tensor_add(out=stc(t, 2), in0=stc(t, 2), in1=stc(t, 9)) \
                .then_inc(sem["cDV"], 1)
            e.wait_ge(sem["cDV"], 2 * NT + 3 * t + 1)
            e.tensor_scalar(out=stc(t, 1), in0=stc(t, 0), scalar1=1.0 / D,
                            scalar2=None, op0=ALU.mult)
            e.tensor_scalar(out=stc(t, 3), in0=stc(t, 2), scalar1=1.0 / D,
                            scalar2=None, op0=ALU.mult)
            e.scalar_tensor_tensor(out=stc(t, 4), in0=stc(t, 0),
                                   scalar=-1.0 / (D * D), in1=stc(t, 0),
                                   op0=ALU.mult, op1=ALU.mult) \
                .then_inc(sem["cDV"], 1)
            e.wait_ge(sem["cDV"], 2 * NT + 3 * t + 2)
            e.tensor_add(out=stc(t, 5), in0=stc(t, 3), in1=stc(t, 4)) \
                .then_inc(sem["cVR"], 1)
            e.wait_ge(sem["cRS"], NT + t + 1)
            e.tensor_scalar(out=Rt, in0=Rt, scalar1=stc(t, 1),
                            scalar2=stc(t, 7), op0=ALU.subtract, op1=ALU.mult) \
                .then_inc(sem["cDV"], 1)
            e.wait_ge(sem["cDV"], 2 * NT + 3 * t + 3)
            base = 16 * NT if resident_in else 0
            e.wait_ge(sem["cX2"], base + 16 * (t + 1))
            e.tensor_add(out=Rt, in0=Rt,
                         in1=xa[:, 1024 * (t % 2):1024 * (t % 2) + 1024]) \
                .then_inc(sem["cRES"], 1)

    def scalar_ops(e):
        gate(e)
        for t in range(NT):
            if not resident_in:
                e.wait_ge(sem["cXR"], 16 * (NWC + t + 1))
            else:
                e.wait_ge(sem["cXR"], 16 * NWC)
            Rt = R[:, 1024 * t:1024 * t + 1024]
            e.activation(out=sqscr[:], in_=Rt[:, 0:512], func=AF.Square,
                         accum_out=stc(t, 2))
            e.activation(out=sqscr[:], in_=Rt[:, 512:1024], func=AF.Square,
                         accum_out=stc(t, 9)).then_inc(sem["cS2"], 1)
            e.wait_ge(sem["cVR"], t + 1)
            e.activation(out=stc(t, 6), in_=stc(t, 5), func=AF.Ln,
                         bias=epsc[:]).then_inc(sem["cLX"], 1)
            e.wait_ge(sem["cLX"], t + 1)
            e.activation(out=stc(t, 7), in_=stc(t, 6), func=AF.Exp,
                         scale=-0.5).then_inc(sem["cRS"], 1)
            e.wait_ge(sem["cTR"], t + 1)
            dst = ln2T.rearrange("p (c s) -> p c s", c=8)[:, :, 128 * t:128 * t + 128]
            e.activation(out=dst, in_=psbf.rearrange("p (c s) -> p c s", c=8),
                         func=AF.Copy).then_inc(sem["cLT"], 1)
        for n in range(0 if trunc in (-1, -3) else 64):
            e.wait_ge(sem["cWT"], n + 1)
            eo, do = n // 8, n % 8
            e.activation(out=WQ[:, do * 1024 + eo * 128:do * 1024 + eo * 128 + 128],
                         in_=psbf[:, 0:128], func=AF.Copy).then_inc(sem["cWE"], 1)
        for k in range(0 if (trunc is not None and trunc < 0) else 32):
            e.wait_ge(sem["cGM"], k + 1)
            dc, s4 = k // 4, k % 4
            e.activation(out=GT[:, dc * 2048 + 512 * s4:dc * 2048 + 512 * s4 + 512],
                         in_=ps[k % 2][:], func=AF.Copy).then_inc(sem["cGE"], 1)
        for m in ([] if trunc in (-1, -3, -4) else range(1, 3)):
            for i in range(64):
                n = m * 64 + i
                eo, do = i // 8, i % 8
                if m == 1 and i == 0:
                    e.wait_ge(sem["cGM"], 32)
                e.wait_ge(sem["cWT"], n + 1)
                e.activation(out=WDST[m][:, do * 1024 + eo * 128:do * 1024 + eo * 128 + 128],
                             in_=psbf[:, 0:128], func=AF.Copy).then_inc(sem["cWE"], 1)
        if resident_in:
            e.wait_ge(sem["cX2"], 16 * NT)
        tc.emit("scalar", e, limit=max(trunc, 0) if trunc is not None else None)
        if trunc is not None:
            return
        for i in range(64):
            n = 192 + i
            eo, do = i // 8, i % 8
            e.wait_ge(sem["cWT"], n + 1)
            e.activation(out=WQ[:, do * 1024 + eo * 128:do * 1024 + eo * 128 + 128],
                         in_=psbf[:, 0:128], func=AF.Copy).then_inc(sem["cWE"], 1)
        for t in range(NT):
            for hh in range(2):
                n = 2 * t + hh
                e.wait_ge(sem["cYM"], n + 1)
                e.activation(out=R[:, 1024 * t + 512 * hh:1024 * t + 512 * hh + 512],
                             in_=ps[2 + n % 2][:], func=AF.Copy) \
                    .then_inc(sem["cYE"], 1)
            e.wait_ge(sem["cYE"], 2 * t + 2)
            Rt = R[:, 1024 * t:1024 * t + 1024]
            e.activation(out=sqscr[:], in_=Rt[:, 0:512], func=AF.Square,
                         accum_out=stc(t, 2))
            e.activation(out=sqscr[:], in_=Rt[:, 512:1024], func=AF.Square,
                         accum_out=stc(t, 9)).then_inc(sem["cS2"], 1)
            e.wait_ge(sem["cVR"], NT + t + 1)
            e.activation(out=stc(t, 6), in_=stc(t, 5), func=AF.Ln,
                         bias=epsc[:]).then_inc(sem["cLX"], 1)
            e.wait_ge(sem["cLX"], NT + t + 1)
            e.activation(out=stc(t, 7), in_=stc(t, 6), func=AF.Exp,
                         scale=-0.5).then_inc(sem["cRS"], 1)

    def tensor_ops(e):
        gate(e)
        for t in range(NT):
            e.wait_ge(sem["cZ"], t + 1)
            if t >= 1:
                e.wait_ge(sem["cLT"], t)
            srcz = lnbf[:, 1024 * (t % 2):1024 * (t % 2) + 1024]
            last = None
            for dc in range(8):
                last = e.transpose(psbf[:, 128 * dc:128 * dc + 128],
                                   srcz[:, 128 * dc:128 * dc + 128], eye[:])
            last.then_inc(sem["cTR"], 1)
        e.wait_ge(sem["cLT"], NT)
        for n in range(0 if trunc in (-1, -3) else 64):
            e.wait_ge(sem["cWC"], n + 1)
            if n >= 1:
                e.wait_ge(sem["cWE"], n)
            e.transpose(psbf[:, 0:128], wbf[:, 128 * (n % 2):128 * (n % 2) + 128],
                        eye[:]).then_inc(sem["cWT"], 1)
        if trunc is None or trunc >= 0:
            e.wait_ge(sem["cWE"], 64)
        for k in range(0 if (trunc is not None and trunc < 0) else 32):
            dc, s4 = k // 4, k % 4
            if k >= 2:
                e.wait_ge(sem["cGE"], k - 1)
            last = None
            for di in range(8):
                last = e.matmul(
                    ps[k % 2][:],
                    WQ[:, di * 1024 + dc * 128:di * 1024 + dc * 128 + 128],
                    ln2T[:, di * 2048 + 512 * s4:di * 2048 + 512 * s4 + 512],
                    start=(di == 0), stop=(di == 7))
            last.then_inc(sem["cGM"], 1)
        for n in ([] if trunc in (-1, -3, -4) else range(64, 192)):
            e.wait_ge(sem["cWC"], n + 1)
            e.wait_ge(sem["cWE"], n)
            e.transpose(psbf[:, 0:128], wbf[:, 128 * (n % 2):128 * (n % 2) + 128],
                        eye[:]).then_inc(sem["cWT"], 1)
        if trunc is None or trunc > 0:
            e.wait_ge(sem["cWE"], 192)
        tc.emit("tensor", e, limit=max(trunc, 0) if trunc is not None else None)
        if trunc is not None:
            return
        for n in range(192, 256):
            e.wait_ge(sem["cWC"], n + 1)
            e.wait_ge(sem["cWE"], n)
            e.transpose(psbf[:, 0:128], wbf[:, 128 * (n % 2):128 * (n % 2) + 128],
                        eye[:]).then_inc(sem["cWT"], 1)
        e.wait_ge(sem["cSS"], TCK)
        e.wait_ge(sem["cWE"], 256)
        e.wait_ge(sem["cBP"], 1)
        for t in range(NT):
            for hh in range(2):
                n = 2 * t + hh
                if n >= 2:
                    e.wait_ge(sem["cYE"], n - 1)
                pst = ps[2 + n % 2]
                last = None
                for di in range(8):
                    last = e.matmul(
                        pst[:], OT[:, di * 2048 + 128 * t:di * 2048 + 128 * t + 128],
                        WQ[:, di * 1024 + 512 * hh:di * 1024 + 512 * hh + 512],
                        start=(di == 0), stop=False)
                e.matmul(pst[:], ones1_bf,
                         bprow_bf[:, 512 * hh:512 * hh + 512],
                         start=False, stop=True).then_inc(sem["cYM"], 1)

    def gpsimd_ops(e):
        gate(e)
        tc.emit("gpsimd", e, limit=max(trunc, 0) if trunc is not None else None)
        if trunc is not None:
            kk, mm, lcc = tc.counts_at(max(trunc, 0))
            if kk > 0:
                e.wait_ge(sem["cSS"], kk)
            if mm > 0:
                e.wait_ge(sem["cDM"], 16 * mm)
            if lcc is not None:
                e.wait_ge(lcc, 1)
            e.dma_start(out=io["r_d"], in_=R[:]).then_inc(sem["cOUT"], 16)
            e.dma_start(out=io["ot_d"], in_=OT[:]).then_inc(sem["cOUT"], 16)
            e.dma_start(out=io["xa_d"], in_=xa[:]).then_inc(sem["cOUT"], 16)
            for dcq in range(8):
                e.dma_start(out=io["cc_d"][:, 5 * dcq:5 * dcq + 5],
                            in_=io["cc1_out"][dcq]).then_inc(sem["cOUT"], 16)
                e.dma_start(out=io["cc_d"][:, 40 + dcq:40 + dcq + 1],
                            in_=io["cc2_out"][dcq],
                            allow_slow_non_contiguous=True).then_inc(sem["cOUT"], 16)
            e.wait_ge(sem["cOUT"], 48 + 256)
            return
        if store_out:
            e.wait_ge(sem["cRES"], NT)
            xo = io["x_out"].rearrange("(q p) e -> q p e", p=128)
            for t in range(NT):
                e.dma_start(out=xo[t], in_=R[:, 1024 * t:1024 * t + 1024]) \
                    .then_inc(sem["cOUT"], 16)
            e.wait_ge(sem["cOUT"], 16 * NT)

    return dict(sync=sync_ops, vector=vector_ops, scalar=scalar_ops,
                tensor=tensor_ops, gpsimd=gpsimd_ops, TCK=TCK, tc=tc,
                marks=tc.marks)


D_SEMS = ["dXR", "dWL", "dWC", "dS2", "dVR", "dRS", "dLX", "dDV", "dZ",
          "dTR", "dLT", "dPM", "dPEv", "dHM", "dGA", "dGQ", "dGW", "dGT",
          "dHE", "dRM", "dRE", "dYM", "dYE", "dRES", "dOUT", "dONE"]

GELU_C = float(np.sqrt(2.0 / np.pi))


def build_stage_d(sem, bufs, io, store_out, resident_in, gate_in=None):
    R = bufs.R_f32
    ln3T = bufs.A_bf
    HT = bufs.B_bf
    u2s = bufs.C_bf
    v1s = bufs.W_bf
    pTb = bufs.pT
    xa = bufs.xa
    lnbf = bufs.lnbf
    sqscr = bufs.sqscr
    st = bufs.stats
    epsc = bufs.epsc
    eye = bufs.eye
    ps = bufs.ps
    psbf = bufs.psbf

    u1s = pTb[:, 0:2048]
    v2s = pTb[:, 2048:4096]
    b2row = pTb[0:1, 4096:5120]
    PTs = [pTb[:, 5120:6144], pTb[:, 6144:7168]]
    RTs = pTb[:, 7168:8192]
    ones1_bf = lnbf[0:1, 0:128]
    b1T = bufs.dn_sb[:, 0:32]

    w = io["w"]

    def stc(t, i):
        o = 16 * (t % 2) + i
        return st[:, o:o + 1]

    def gate(e):
        if gate_in is not None:
            e.wait_ge(gate_in[0], gate_in[1])

    def sync_ops(e):
        gate(e)
        e.dma_start(out=b1T, in_=w["b1"].rearrange("(c p) -> p c", p=128),
                    allow_slow_non_contiguous=True).then_inc(sem["dXR"], 16)
        if not resident_in:
            xr = io["x_in"].rearrange("(t p) e -> t p e", p=128)
            for t in range(NT):
                e.wait_ge(sem["dXR"], 16 * (t + 1))
                e.dma_start(out=R[:, 1024 * t:1024 * t + 1024], in_=xr[t]) \
                    .then_inc(sem["dXR"], 16)
        loads = []
        loads.append(w["U1"].rearrange("(c p) r -> p c r", p=128))
        for rc in range(2):
            for hh in range(2):
                loads.append(w["V1"][128 * rc:128 * rc + 128,
                                     2048 * hh:2048 * hh + 2048])
        for c0 in range(4):
            loads.append(w["U2"][1024 * c0:1024 * (c0 + 1)]
                         .rearrange("(c p) r -> p c r", p=128))
        loads.append(w["V2"].rearrange("(c p) e -> p c e", p=128))
        loads.append(w["b2"][None, :])
        for k, srcap in enumerate(loads):
            if k >= 1:
                e.wait_ge(sem["dWC"], k)
            if k == 10:
                dst = xa[0:1, 0:1024]
            elif len(srcap.shape) == 3:
                dst = xa[:, 0:srcap.shape[1] * srcap.shape[2]] \
                    .rearrange("p (c r) -> p c r", c=srcap.shape[1])
            else:
                dst = xa[:, 0:srcap.shape[1]]
            e.dma_start(out=dst, in_=srcap).then_inc(sem["dWL"], 16)

    def vector_ops(e):
        gate(e)
        for t in range(NT):
            if not resident_in:
                e.wait_ge(sem["dXR"], 16 * (t + 2))
            else:
                e.wait_ge(sem["dXR"], 16)
            Rt = R[:, 1024 * t:1024 * t + 1024]
            e.reduce_sum(out=stc(t, 0), in_=Rt, axis=AX)
            e.wait_ge(sem["dS2"], t + 1)
            e.tensor_add(out=stc(t, 2), in0=stc(t, 2), in1=stc(t, 9)) \
                .then_inc(sem["dDV"], 1)
            e.wait_ge(sem["dDV"], 2 * t + 1)
            e.tensor_scalar(out=stc(t, 1), in0=stc(t, 0), scalar1=1.0 / D,
                            scalar2=None, op0=ALU.mult)
            e.tensor_scalar(out=stc(t, 3), in0=stc(t, 2), scalar1=1.0 / D,
                            scalar2=None, op0=ALU.mult)
            e.scalar_tensor_tensor(out=stc(t, 4), in0=stc(t, 0),
                                   scalar=-1.0 / (D * D), in1=stc(t, 0),
                                   op0=ALU.mult, op1=ALU.mult) \
                .then_inc(sem["dDV"], 1)
            e.wait_ge(sem["dDV"], 2 * t + 2)
            e.tensor_add(out=stc(t, 5), in0=stc(t, 3), in1=stc(t, 4)) \
                .then_inc(sem["dVR"], 1)
            e.wait_ge(sem["dRS"], t + 1)
            e.tensor_scalar(out=lnbf[:, 1024 * (t % 2):1024 * (t % 2) + 1024],
                            in0=Rt, scalar1=stc(t, 1), scalar2=stc(t, 7),
                            op0=ALU.subtract, op1=ALU.mult) \
                .then_inc(sem["dZ"], 1)
        cdsts = [u1s]
        for rc in range(2):
            for hh in range(2):
                cdsts.append(v1s[:, 4096 * rc + 2048 * hh:4096 * rc + 2048 * hh + 2048])
        for c0 in range(4):
            cdsts.append(u2s[:, 2048 * c0:2048 * (c0 + 1)])
        cdsts.append(v2s)
        cdsts.append(b2row)
        for k, dst in enumerate(cdsts):
            e.wait_ge(sem["dWL"], 16 * (k + 1))
            nsrc = 1024 if k == 10 else 2048
            srcv = xa[0:1, 0:1024] if k == 10 else xa[:, 0:2048]
            e.tensor_copy(out=dst, in_=srcv).then_inc(sem["dWC"], 1)
        e.wait_ge(sem["dTR"], NT)
        e.memset(ones1_bf, 1.0).then_inc(sem["dONE"], 1)
        for s4 in range(4):
            for fc in range(32):
                k = s4 * 32 + fc
                u = xa[:, 512 * (fc % 2):512 * (fc % 2) + 512]
                q1 = xa[:, 1024 + 512 * (fc % 2):1024 + 512 * (fc % 2) + 512]
                e.wait_ge(sem["dGA"], k + 1)
                e.tensor_mul(out=q1, in0=u, in1=q1).then_inc(sem["dGQ"], 1)
                e.wait_ge(sem["dGQ"], k + 1)
                e.scalar_tensor_tensor(out=q1, in0=q1, scalar=0.044715, in1=u,
                                       op0=ALU.mult, op1=ALU.add) \
                    .then_inc(sem["dGW"], 1)
                e.wait_ge(sem["dGT"], k + 1)
                e.scalar_tensor_tensor(out=HT[:, 512 * fc:512 * fc + 512],
                                       in0=sqscr[:], scalar=1.0, in1=u,
                                       op0=ALU.add, op1=ALU.mult) \
                    .then_inc(sem["dHE"], 1)
            for sc in range(4):
                t = s4 * 4 + sc
                for dh in range(2):
                    n = 2 * t + dh
                    e.wait_ge(sem["dYM"], n + 1)
                    ya = xa[:, 1024 * (sc % 2) + 512 * dh:
                            1024 * (sc % 2) + 512 * dh + 512]
                    e.tensor_copy(out=ya, in_=ps[n % 2][:]).then_inc(sem["dYE"], 1)
                yt = xa[:, 1024 * (sc % 2):1024 * (sc % 2) + 1024]
                e.wait_ge(sem["dYE"], 2 * t + 2)
                e.reduce_sum(out=stc(t, 0), in_=yt, axis=AX)
                e.wait_ge(sem["dS2"], NT + t + 1)
                e.tensor_add(out=stc(t, 2), in0=stc(t, 2), in1=stc(t, 9)) \
                    .then_inc(sem["dDV"], 1)
                e.wait_ge(sem["dDV"], 2 * NT + 3 * t + 1)
                e.tensor_scalar(out=stc(t, 1), in0=stc(t, 0), scalar1=1.0 / D,
                                scalar2=None, op0=ALU.mult)
                e.tensor_scalar(out=stc(t, 3), in0=stc(t, 2), scalar1=1.0 / D,
                                scalar2=None, op0=ALU.mult)
                e.scalar_tensor_tensor(out=stc(t, 4), in0=stc(t, 0),
                                       scalar=-1.0 / (D * D), in1=stc(t, 0),
                                       op0=ALU.mult, op1=ALU.mult) \
                    .then_inc(sem["dDV"], 1)
                e.wait_ge(sem["dDV"], 2 * NT + 3 * t + 2)
                e.tensor_add(out=stc(t, 5), in0=stc(t, 3), in1=stc(t, 4)) \
                    .then_inc(sem["dVR"], 1)
                e.wait_ge(sem["dRS"], NT + t + 1)
                e.tensor_scalar(out=yt, in0=yt, scalar1=stc(t, 1),
                                scalar2=stc(t, 7), op0=ALU.subtract,
                                op1=ALU.mult).then_inc(sem["dDV"], 1)
                e.wait_ge(sem["dDV"], 2 * NT + 3 * t + 3)
                Rt = R[:, 1024 * t:1024 * t + 1024]
                e.tensor_add(out=Rt, in0=Rt, in1=yt).then_inc(sem["dRES"], 1)

    def scalar_ops(e):
        gate(e)
        for t in range(NT):
            if not resident_in:
                e.wait_ge(sem["dXR"], 16 * (t + 2))
            else:
                e.wait_ge(sem["dXR"], 16)
            Rt = R[:, 1024 * t:1024 * t + 1024]
            e.activation(out=sqscr[:], in_=Rt[:, 0:512], func=AF.Square,
                         accum_out=stc(t, 2))
            e.activation(out=sqscr[:], in_=Rt[:, 512:1024], func=AF.Square,
                         accum_out=stc(t, 9)).then_inc(sem["dS2"], 1)
            e.wait_ge(sem["dVR"], t + 1)
            e.activation(out=stc(t, 6), in_=stc(t, 5), func=AF.Ln,
                         bias=epsc[:]).then_inc(sem["dLX"], 1)
            e.wait_ge(sem["dLX"], t + 1)
            e.activation(out=stc(t, 7), in_=stc(t, 6), func=AF.Exp,
                         scale=-0.5).then_inc(sem["dRS"], 1)
            e.wait_ge(sem["dTR"], t + 1)
            dst = ln3T.rearrange("p (c s) -> p c s", c=8)[:, :, 128 * t:128 * t + 128]
            e.activation(out=dst, in_=psbf.rearrange("p (c s) -> p c s", c=8),
                         func=AF.Copy).then_inc(sem["dLT"], 1)
        for s4 in range(4):
            for rc in range(2):
                n = 2 * s4 + rc
                e.wait_ge(sem["dPM"], n + 1)
                e.activation(out=PTs[s4 % 2][:, 512 * rc:512 * rc + 512],
                             in_=ps[rc][:], func=AF.Copy).then_inc(sem["dPEv"], 1)
            for fc in range(32):
                k = s4 * 32 + fc
                u = xa[:, 512 * (fc % 2):512 * (fc % 2) + 512]
                q1 = xa[:, 1024 + 512 * (fc % 2):1024 + 512 * (fc % 2) + 512]
                e.wait_ge(sem["dHM"], k + 1)
                if k >= 2:
                    e.wait_ge(sem["dGQ"], k - 1)
                e.activation(out=u, in_=ps[fc % 2][:], func=AF.Identity,
                             bias=b1T[:, fc:fc + 1])
                e.activation(out=q1, in_=ps[fc % 2][:], func=AF.Square,
                             bias=b1T[:, fc:fc + 1]).then_inc(sem["dGA"], 1)
                e.wait_ge(sem["dGW"], k + 1)
                if k >= 1:
                    e.wait_ge(sem["dHE"], k)
                e.activation(out=sqscr[:], in_=q1, func=AF.Tanh,
                             scale=GELU_C).then_inc(sem["dGT"], 1)
            for rc in range(2):
                n = 2 * s4 + rc
                e.wait_ge(sem["dRM"], n + 1)
                e.activation(out=RTs[:, 512 * rc:512 * rc + 512],
                             in_=ps[2 + rc][:], func=AF.Copy, scale=0.5) \
                    .then_inc(sem["dRE"], 1)
            for sc in range(4):
                t = s4 * 4 + sc
                yt = xa[:, 1024 * (sc % 2):1024 * (sc % 2) + 1024]
                e.wait_ge(sem["dYE"], 2 * t + 2)
                e.activation(out=sqscr[:], in_=yt[:, 0:512], func=AF.Square,
                             accum_out=stc(t, 2))
                e.activation(out=sqscr[:], in_=yt[:, 512:1024], func=AF.Square,
                             accum_out=stc(t, 9)).then_inc(sem["dS2"], 1)
                e.wait_ge(sem["dVR"], NT + t + 1)
                e.activation(out=stc(t, 6), in_=stc(t, 5), func=AF.Ln,
                             bias=epsc[:]).then_inc(sem["dLX"], 1)
                e.wait_ge(sem["dLX"], NT + t + 1)
                e.activation(out=stc(t, 7), in_=stc(t, 6), func=AF.Exp,
                             scale=-0.5).then_inc(sem["dRS"], 1)

    def tensor_ops(e):
        gate(e)
        for t in range(NT):
            e.wait_ge(sem["dZ"], t + 1)
            if t >= 1:
                e.wait_ge(sem["dLT"], t)
            srcz = lnbf[:, 1024 * (t % 2):1024 * (t % 2) + 1024]
            last = None
            for dc in range(8):
                last = e.transpose(psbf[:, 128 * dc:128 * dc + 128],
                                   srcz[:, 128 * dc:128 * dc + 128], eye[:])
            last.then_inc(sem["dTR"], 1)
        e.wait_ge(sem["dLT"], NT)
        e.wait_ge(sem["dONE"], 1)
        for s4 in range(4):
            e.wait_ge(sem["dWC"], 1)
            if s4 >= 1:
                e.wait_ge(sem["dYE"], 8 * s4)
                e.wait_ge(sem["dPEv"], 2 * s4)
            for rc in range(2):
                last = None
                for dcc in range(8):
                    last = e.matmul(
                        ps[rc][:],
                        u1s[:, dcc * 256 + rc * 128:dcc * 256 + rc * 128 + 128],
                        ln3T[:, dcc * 2048 + 512 * s4:dcc * 2048 + 512 * s4 + 512],
                        start=(dcc == 0), stop=(dcc == 7))
                last.then_inc(sem["dPM"], 1)
            e.wait_ge(sem["dWC"], 5)
            e.wait_ge(sem["dPEv"], 2 * (s4 + 1))
            for fc in range(32):
                k = s4 * 32 + fc
                if k >= 2:
                    e.wait_ge(sem["dGA"], k - 1)
                last = None
                for rc in range(2):
                    last = e.matmul(
                        ps[fc % 2][:],
                        v1s[:, rc * 4096 + fc * 128:rc * 4096 + fc * 128 + 128],
                        PTs[s4 % 2][:, rc * 512:rc * 512 + 512],
                        start=(rc == 0), stop=(rc == 1))
                last.then_inc(sem["dHM"], 1)
            e.wait_ge(sem["dWC"], 9)
            e.wait_ge(sem["dHE"], 32 * (s4 + 1))
            if s4 >= 1:
                e.wait_ge(sem["dRE"], 2 * s4)
            for rc in range(2):
                last = None
                for fc in range(32):
                    last = e.matmul(
                        ps[2 + rc][:],
                        u2s[:, fc * 256 + rc * 128:fc * 256 + rc * 128 + 128],
                        HT[:, fc * 512:fc * 512 + 512],
                        start=(fc == 0), stop=(fc == 31))
                last.then_inc(sem["dRM"], 1)
            e.wait_ge(sem["dWC"], 11)
            e.wait_ge(sem["dRE"], 2 * (s4 + 1))
            for sc in range(4):
                t = s4 * 4 + sc
                for dh in range(2):
                    n = 2 * t + dh
                    if n >= 2:
                        e.wait_ge(sem["dYE"], n - 1)
                    pst = ps[n % 2]
                    for rc in range(2):
                        e.matmul(pst[:],
                                 RTs[:, rc * 512 + sc * 128:rc * 512 + sc * 128 + 128],
                                 v2s[:, rc * 1024 + dh * 512:rc * 1024 + dh * 512 + 512],
                                 start=(rc == 0), stop=False)
                    e.matmul(pst[:], ones1_bf, b2row[:, 512 * dh:512 * dh + 512],
                             start=False, stop=True).then_inc(sem["dYM"], 1)

    def gpsimd_ops(e):
        gate(e)
        if store_out:
            xo = io["y_out"].rearrange("(q p) e -> q p e", p=128)
            for t in range(NT):
                e.wait_ge(sem["dRES"], t + 1)
                e.dma_start(out=xo[t], in_=R[:, 1024 * t:1024 * t + 1024]) \
                    .then_inc(sem["dOUT"], 16)
            e.wait_ge(sem["dOUT"], 16 * NT)

    return dict(sync=sync_ops, vector=vector_ops, scalar=scalar_ops,
                tensor=tensor_ops, gpsimd=gpsimd_ops)


FFN1_KEYS = ["U1", "V1", "b1", "U2", "V2", "b2"]


def build_launch_d():
    key = ("d",)
    if key in _NC_CACHE:
        return _NC_CACHE[key]
    nc = bass.Bass("TRN2", target_bir_lowering=False, debug=False)
    io = {}
    io["x_in"] = nc.dram_tensor("x_in", [SC, D], F32, kind="ExternalInput").ap()
    w = {}
    shapes = {"U1": [D, 256], "V1": [256, 4096], "b1": [4096],
              "U2": [4096, 256], "V2": [256, D], "b2": [D]}
    for k in FFN1_KEYS:
        w[k] = nc.dram_tensor("w_" + k, shapes[k], F32, kind="ExternalInput").ap()
    io["w"] = w
    eye_in = nc.dram_tensor("eye_in", [128, 128], BF16, kind="ExternalInput").ap()
    io["y_out"] = nc.dram_tensor("y_out", [SC, D], F32, kind="ExternalOutput").ap()
    ctx = ExitStack()
    with ctx:
        bufs = alloc_common(nc, ctx)
        sem = make_sems(nc, ctx, D_SEMS + ["dEYE", "dONE2"])
        ops = build_stage_d(sem, bufs, io, store_out=True, resident_in=False)
        block = ctx.enter_context(nc.Block())

        @block.sync
        def _(e):
            e.dma_start(out=bufs.eye, in_=eye_in).then_inc(sem["dEYE"], 16)
            ops["sync"](e)

        @block.vector
        def _(e):
            e.memset(bufs.epsc[:], float(EPS)).then_inc(sem["dONE2"], 1)
            ops["vector"](e)

        @block.scalar
        def _(e):
            e.wait_ge(sem["dONE2"], 1)
            ops["scalar"](e)

        @block.tensor
        def _(e):
            e.wait_ge(sem["dEYE"], 16)
            ops["tensor"](e)

        @block.gpsimd
        def _(e):
            ops["gpsimd"](e)
    _NC_CACHE[key] = nc
    return nc


def run_stage_d(x3_full, weights):
    nc = build_launch_d()
    eye = np.eye(128, dtype=np.float16)
    in_maps = []
    for c in range(NCORES):
        b, r = c // GROUP, c % GROUP
        m = {"x_in": np.ascontiguousarray(x3_full[b, r * SC:(r + 1) * SC]),
             "eye_in": eye}
        for k in FFN1_KEYS:
            m["w_" + k] = np.ascontiguousarray(weights[k])
        in_maps.append(m)
    res = run_bass_kernel_spmd(nc, in_maps, core_ids=list(range(NCORES)))
    out = np.zeros((B, S, D), np.float32)
    for c in range(NCORES):
        b, r = c // GROUP, c % GROUP
        out[b, r * SC:(r + 1) * SC] = res.results[c]["y_out"]
    return out


FFN0_KEYS = ["ln_g", "ln_b", "bn_g", "bn_b", "bn_mean", "bn_var", "Wr", "br",
             "W_down", "b_down", "W_up", "b_up"]


def build_launch_b():
    key = ("b",)
    if key in _NC_CACHE:
        return _NC_CACHE[key]
    nc = bass.Bass("TRN2", target_bir_lowering=False, debug=False)
    io = {}
    io["x_in"] = nc.dram_tensor("x_in", [SC, D], F32, kind="ExternalInput").ap()
    w = {}
    shapes = {"ln_g": [D], "ln_b": [D], "bn_g": [D], "bn_b": [D],
              "bn_mean": [D], "bn_var": [D], "Wr": [D, 8], "br": [8],
              "W_down": [8, D, 64], "b_down": [8, 64], "W_up": [8, 64, D],
              "b_up": [8, D]}
    for k in FFN0_KEYS:
        w[k] = nc.dram_tensor("w_" + k, shapes[k], F32, kind="ExternalInput").ap()
    io["w"] = w
    eye_in = nc.dram_tensor("eye_in", [128, 128], BF16, kind="ExternalInput").ap()
    io["zs_in"] = nc.dram_tensor("zs_in", [1, D], F32, kind="Internal").ap()
    io["zs_out"] = nc.dram_tensor("zs_out", [1, D], F32, kind="Internal").ap()
    io["mrow_d"] = nc.dram_tensor("mrow_d", [1, 8], F32, kind="Internal").ap()
    io["x_out"] = nc.dram_tensor("x_out", [SC, D], F32, kind="ExternalOutput").ap()
    io["af_d"] = nc.dram_tensor("af_d", [128, 16384], BF16, kind="ExternalOutput").ap()
    io["cf_d"] = nc.dram_tensor("cf_d", [128, 16384], BF16, kind="ExternalOutput").ap()
    io["pt_d"] = nc.dram_tensor("pt_d", [128, 8192], BF16, kind="ExternalOutput").ap()
    io["zt_d"] = nc.dram_tensor("zt_d", [128, 16384], BF16, kind="ExternalOutput").ap()
    ctx = ExitStack()
    with ctx:
        bufs = alloc_common(nc, ctx)
        sem = make_sems(nc, ctx, B_SEMS + ["bEYE", "bONE"])
        ops = build_stage_b(sem, bufs, io, store_out=True, resident_in=False)
        block = ctx.enter_context(nc.Block())

        @block.sync
        def _(e):
            e.dma_start(out=bufs.eye, in_=eye_in).then_inc(sem["bEYE"], 16)
            ops["sync"](e)

        @block.vector
        def _(e):
            e.memset(bufs.ones4[:], 1.0)
            e.memset(bufs.epsc[:], float(EPS)).then_inc(sem["bONE"], 1)
            ops["vector"](e)

        @block.scalar
        def _(e):
            e.wait_ge(sem["bONE"], 1)
            ops["scalar"](e)

        @block.tensor
        def _(e):
            e.wait_ge(sem["bEYE"], 16)
            e.wait_ge(sem["bONE"], 1)
            ops["tensor"](e)

        @block.gpsimd
        def _(e):
            ops["gpsimd"](e)
    _NC_CACHE[key] = nc
    return nc


def run_stage_b(x1_full, weights):
    nc = build_launch_b()
    eye = np.eye(128, dtype=np.float16)
    in_maps = []
    for c in range(NCORES):
        b, r = c // GROUP, c % GROUP
        m = {"x_in": np.ascontiguousarray(x1_full[b, r * SC:(r + 1) * SC]),
             "eye_in": eye}
        for k in FFN0_KEYS:
            m["w_" + k] = np.ascontiguousarray(weights[k])
        in_maps.append(m)
    res = run_bass_kernel_spmd(nc, in_maps, core_ids=list(range(NCORES)))
    out = np.zeros((B, S, D), np.float32)
    for c in range(NCORES):
        b, r = c // GROUP, c % GROUP
        out[b, r * SC:(r + 1) * SC] = res.results[c]["x_out"]
    return out


def run_stage_a(x_full, kernel_w):
    nc = build_launch_a()
    eye = np.eye(128, dtype=np.float16)
    in_maps = []
    for c in range(NCORES):
        b, r = c // GROUP, c % GROUP
        in_maps.append({
            "x_own": np.ascontiguousarray(x_full[b, r * SC:(r + 1) * SC]),
            "kern": np.ascontiguousarray(kernel_w),
            "eye_in": eye,
        })
    res = run_bass_kernel_spmd(nc, in_maps, core_ids=list(range(NCORES)))
    out = np.zeros((B, S, D), np.float32)
    for c in range(NCORES):
        b, r = c // GROUP, c % GROUP
        out[b, r * SC:(r + 1) * SC] = res.results[c]["x1"]
    return out


def kernel(**inputs):
    return kernel_fused(**inputs)


L1_KEYS = ["Wq1", "bq1", "Wk1", "bk1", "Wk2", "bk2", "Wproj", "bproj"]


def make_masks(core):
    r = core % GROUP
    mask_lt = np.zeros((128, 4), np.float32)
    mask_prev = np.zeros((128, 4), np.float32)
    first_m = np.zeros((128, 1), np.float32)
    mask_lt[:, :r] = 1.0
    if r > 0:
        mask_prev[:, r - 1] = 1.0
        first_m[:] = 1.0
    return mask_lt, mask_prev, first_m


def build_launch_c(trunc=None):
    key = ("c", trunc)
    if key in _NC_CACHE:
        return _NC_CACHE[key]
    nc = bass.Bass("TRN2", target_bir_lowering=False, debug=False)
    io = {}
    io["x_in"] = nc.dram_tensor("x_in", [SC, D], F32, kind="ExternalInput").ap()
    w = {}
    shapes = {"Wq1": [D, D], "bq1": [D], "Wk1": [D, D], "bk1": [D],
              "Wk2": [D, D], "bk2": [D], "Wproj": [D, D], "bproj": [D]}
    for k in L1_KEYS:
        w[k] = nc.dram_tensor("w_" + k, shapes[k], F32, kind="ExternalInput").ap()
    io["w"] = w
    eye_in = nc.dram_tensor("eye_in", [128, 128], BF16, kind="ExternalInput").ap()
    io["mask_lt"] = nc.dram_tensor("mask_lt_in", [128, 4], F32, kind="ExternalInput").ap()
    io["mask_prev"] = nc.dram_tensor("mask_prev_in", [128, 4], F32, kind="ExternalInput").ap()
    io["first_m"] = nc.dram_tensor("first_m_in", [128, 1], F32, kind="ExternalInput").ap()
    io["cc1_in"] = [nc.dram_tensor(f"cc1_in{i}", [128, 5], F32, kind="Internal").ap()
                    for i in range(8)]
    io["cc1_out"] = [nc.dram_tensor(f"cc1_out{i}", [512, 5], F32, kind="Internal").ap()
                     for i in range(8)]
    io["cc2_in"] = [nc.dram_tensor(f"cc2_in{i}", [128, 1], F32, kind="Internal").ap()
                    for i in range(8)]
    io["cc2_out"] = [nc.dram_tensor(f"cc2_out{i}", [512, 1], F32, kind="Internal").ap()
                     for i in range(8)]
    io["x_out"] = nc.dram_tensor("x_out", [SC, D], F32, kind="ExternalOutput").ap()
    if trunc is not None:
        io["r_d"] = nc.dram_tensor("r_d", [128, 16384], F32, kind="ExternalOutput").ap()
        io["ot_d"] = nc.dram_tensor("ot_d", [128, 16384], BF16, kind="ExternalOutput").ap()
        io["xa_d"] = nc.dram_tensor("xa_d", [128, 2048], F32, kind="ExternalOutput").ap()
        io["cc_d"] = nc.dram_tensor("cc_d", [512, 48], F32, kind="ExternalOutput").ap()
    ctx = ExitStack()
    with ctx:
        bufs = alloc_common(nc, ctx)
        sem = make_sems(nc, ctx, C_SEMS + ["cEYE", "cONE"])
        ops = build_stage_c(sem, bufs, io, store_out=True, resident_in=False,
                            trunc=trunc)
        block = ctx.enter_context(nc.Block())

        @block.sync
        def _(e):
            e.dma_start(out=bufs.eye, in_=eye_in).then_inc(sem["cEYE"], 16)
            ops["sync"](e)

        @block.vector
        def _(e):
            e.memset(bufs.epsc[:], float(EPS)).then_inc(sem["cONE"], 1)
            ops["vector"](e)

        @block.scalar
        def _(e):
            e.wait_ge(sem["cONE"], 1)
            ops["scalar"](e)

        @block.tensor
        def _(e):
            e.wait_ge(sem["cEYE"], 16)
            ops["tensor"](e)

        @block.gpsimd
        def _(e):
            ops["gpsimd"](e)
    _NC_CACHE[key] = nc
    return nc


def run_stage_c(x2_full, weights):
    nc = build_launch_c()
    eye = np.eye(128, dtype=np.float16)
    in_maps = []
    for c in range(NCORES):
        b, r = c // GROUP, c % GROUP
        ml, mp, fm = make_masks(c)
        m = {"x_in": np.ascontiguousarray(x2_full[b, r * SC:(r + 1) * SC]),
             "eye_in": eye, "mask_lt_in": ml, "mask_prev_in": mp,
             "first_m_in": fm}
        for k in L1_KEYS:
            m["w_" + k] = np.ascontiguousarray(weights[k])
        in_maps.append(m)
    res = run_bass_kernel_spmd(nc, in_maps, core_ids=list(range(NCORES)))
    out = np.zeros((B, S, D), np.float32)
    for c in range(NCORES):
        b, r = c // GROUP, c % GROUP
        out[b, r * SC:(r + 1) * SC] = res.results[c]["x_out"]
    return out


def build_fused():
    key = ("fused",)
    if key in _NC_CACHE:
        return _NC_CACHE[key]
    nc = bass.Bass("TRN2", target_bir_lowering=False, debug=False)
    ioa = {}
    ioa["x_own"] = nc.dram_tensor("x_own", [SC, D], F32, kind="ExternalInput").ap()
    ioa["kern"] = nc.dram_tensor("kern", [3, D, D], F32, kind="ExternalInput").ap()
    eye_in = nc.dram_tensor("eye_in", [128, 128], BF16, kind="ExternalInput").ap()
    ioa["kv_in"] = nc.dram_tensor("kv_in", [128, 32768], BF16, kind="Internal").ap()
    ioa["kv_g"] = nc.dram_tensor("kv_g", [512, 32768], BF16, kind="Internal").ap()

    iob = {}
    wb = {}
    shapes_b = {"ln_g": [D], "ln_b": [D], "bn_g": [D], "bn_b": [D],
                "bn_mean": [D], "bn_var": [D], "Wr": [D, 8], "br": [8],
                "W_down": [8, D, 64], "b_down": [8, 64], "W_up": [8, 64, D],
                "b_up": [8, D]}
    for k in FFN0_KEYS:
        wb[k] = nc.dram_tensor("wb_" + k, shapes_b[k], F32, kind="ExternalInput").ap()
    iob["w"] = wb
    iob["zs_in"] = nc.dram_tensor("zs_in", [1, D], F32, kind="Internal").ap()
    iob["zs_out"] = nc.dram_tensor("zs_out", [1, D], F32, kind="Internal").ap()
    iob["mrow_d"] = nc.dram_tensor("mrow_d", [1, 8], F32, kind="Internal").ap()

    ioc = {}
    wc = {}
    shapes_c = {"Wq1": [D, D], "bq1": [D], "Wk1": [D, D], "bk1": [D],
                "Wk2": [D, D], "bk2": [D], "Wproj": [D, D], "bproj": [D]}
    for k in L1_KEYS:
        wc[k] = nc.dram_tensor("wc_" + k, shapes_c[k], F32, kind="ExternalInput").ap()
    ioc["w"] = wc
    ioc["mask_lt"] = nc.dram_tensor("mask_lt_in", [128, 4], F32, kind="ExternalInput").ap()
    ioc["mask_prev"] = nc.dram_tensor("mask_prev_in", [128, 4], F32, kind="ExternalInput").ap()
    ioc["first_m"] = nc.dram_tensor("first_m_in", [128, 1], F32, kind="ExternalInput").ap()
    ioc["cc1_in"] = [nc.dram_tensor(f"cc1_in{i}", [128, 5], F32, kind="Internal").ap()
                     for i in range(8)]
    ioc["cc1_out"] = [nc.dram_tensor(f"cc1_out{i}", [512, 5], F32, kind="Internal").ap()
                      for i in range(8)]
    ioc["cc2_in"] = [nc.dram_tensor(f"cc2_in{i}", [128, 1], F32, kind="Internal").ap()
                     for i in range(8)]
    ioc["cc2_out"] = [nc.dram_tensor(f"cc2_out{i}", [512, 1], F32, kind="Internal").ap()
                      for i in range(8)]
    ioc["x2_sp"] = nc.dram_tensor("x2_sp", [SC, D], F32, kind="Internal").ap()

    iod = {}
    wd = {}
    shapes_d = {"U1": [D, 256], "V1": [256, 4096], "b1": [4096],
                "U2": [4096, 256], "V2": [256, D], "b2": [D]}
    for k in FFN1_KEYS:
        wd[k] = nc.dram_tensor("wd_" + k, shapes_d[k], F32, kind="ExternalInput").ap()
    iod["w"] = wd
    iod["y_out"] = nc.dram_tensor("y_out", [SC, D], F32, kind="ExternalOutput").ap()

    ctx = ExitStack()
    with ctx:
        bufs = alloc_common(nc, ctx)
        fa = [s for s in A_SEMS if s not in ("aEYE", "aOUT", "aQG")]
        fb = [s for s in B_SEMS if s not in ("bOUT", "bCC")]
        fc = [s for s in C_SEMS if s not in ("cOUT",)]
        fd = list(D_SEMS)
        sem = make_sems(nc, ctx, fa + fb + fc + fd + ["gEYE", "gONE"])
        opsa = build_stage_a(sem, bufs, ioa, store_out=False)
        opsb = build_stage_b(sem, bufs, iob, store_out=False, resident_in=True,
                             gate_in=(sem["aRES"], NT))
        opsc = build_stage_c(sem, bufs, ioc, store_out=False, resident_in=True,
                             gate_in=(sem["bRES"], NT))
        opsd = build_stage_d(sem, bufs, iod, store_out=True, resident_in=True,
                             gate_in=(sem["cRES"], NT))
        block = ctx.enter_context(nc.Block())

        @block.sync
        def _(e):
            e.dma_start(out=bufs.eye, in_=eye_in).then_inc(sem["gEYE"], 16)
            for name in ("aEYE",):
                pass
            opsa["sync"](e)
            opsb["sync"](e)
            opsc["sync"](e)
            opsd["sync"](e)

        @block.vector
        def _(e):
            e.memset(bufs.epsc[:], float(EPS)).then_inc(sem["gONE"], 1)
            opsa["vector"](e)
            opsb["vector"](e)
            opsc["vector"](e)
            opsd["vector"](e)

        @block.scalar
        def _(e):
            e.wait_ge(sem["gONE"], 1)
            opsa["scalar"](e)
            opsb["scalar"](e)
            opsc["scalar"](e)
            opsd["scalar"](e)

        @block.tensor
        def _(e):
            e.wait_ge(sem["gEYE"], 16)
            opsa["tensor"](e)
            opsb["tensor"](e)
            opsc["tensor"](e)
            opsd["tensor"](e)

        @block.gpsimd
        def _(e):
            opsa["gpsimd"](e)
            opsb["gpsimd"](e)
            opsc["gpsimd"](e)
            opsd["gpsimd"](e)
    _NC_CACHE[key] = nc
    return nc


def kernel_fused(**inputs):
    ins = {k: np.asarray(v) for k, v in inputs.items()}
    nc = build_fused()
    eye = np.eye(128, dtype=np.float16)
    x = ins["L0_attn_x"]
    in_maps = []
    for c in range(NCORES):
        b, r = c // GROUP, c % GROUP
        ml_, mp, fm = make_masks(c)
        m = {"x_own": np.ascontiguousarray(x[b, r * SC:(r + 1) * SC]),
             "kern": np.ascontiguousarray(ins["L0_attn_kernel"]),
             "eye_in": eye, "mask_lt_in": ml_, "mask_prev_in": mp,
             "first_m_in": fm}
        for k in FFN0_KEYS:
            m["wb_" + k] = np.ascontiguousarray(ins["L0_ffn_" + k])
        for k in L1_KEYS:
            m["wc_" + k] = np.ascontiguousarray(ins["L1_attn_" + k])
        for k in FFN1_KEYS:
            m["wd_" + k] = np.ascontiguousarray(ins["L1_ffn_" + k])
        in_maps.append(m)
    res = run_bass_kernel_spmd(nc, in_maps, core_ids=list(range(NCORES)))
    out = np.zeros((B, S, D), np.float32)
    for c in range(NCORES):
        b, r = c // GROUP, c % GROUP
        out[b, r * SC:(r + 1) * SC] = res.results[c]["y_out"]
    return out


# version 05
# speedup vs baseline: 1.2351x; 1.1766x over previous
import numpy as np
import ml_dtypes
from contextlib import ExitStack

import concourse.bass as bass
import concourse.mybir as mybir
from concourse.bass_utils import run_bass_kernel_spmd

F32 = mybir.dt.float32
BF16 = mybir.dt.float16
AF = mybir.ActivationFunctionType
ALU = mybir.AluOpType
AX = mybir.AxisListType.X

NCORES = 8
GROUP = 4
B = 2
S = 8192
D = 1024
SC = S // GROUP
NT = SC // 128
EPS = 1e-5

RG = [[0, 1, 2, 3], [4, 5, 6, 7]]

A_SEMS = ["aX", "aX2", "aS2", "aVR", "aRS", "aST", "aTR", "aLT", "aKL", "aKC",
          "aMM", "aEVQ", "aEVK", "aEVV", "aSP", "aSPV", "aCC", "aKV", "aSC", "aEX",
          "aPV", "aPVE", "aDN", "aDNE", "aQG", "aRD", "aOS", "aRES", "aEYE",
          "aONE", "aOUT", "aLX", "aDV"]


def make_sems(nc, ctx, names):
    return {n: ctx.enter_context(nc.semaphore(n)) for n in names}


class NS:
    def __init__(self, d):
        self.__dict__.update(d)


def build_stage_a(sem, bufs, io, store_out, dbg=False):
    qt = bufs.A_bf
    lnT = bufs.C_bf
    ktq = bufs.B_bf
    vq = bufs.C_bf
    pT = bufs.pT
    ker = bufs.W_bf
    R = bufs.R_f32
    xa = bufs.xa
    lnbf = bufs.lnbf
    sqscr = bufs.sqscr
    st = bufs.stats
    dn_sb = bufs.dn_sb
    rd_all = bufs.rd_all
    ones4 = bufs.ones4
    epsc = bufs.epsc
    eye = bufs.eye
    ps = bufs.ps
    psbf = bufs.psbf
    ps64 = bufs.ps64

    x_own, kern, kv_in, kv_g = io["x_own"], io["kern"], io["kv_in"], io["kv_g"]

    def stc(t, i):
        o = 16 * (t % 2) + i
        return st[:, o:o + 1]

    def sync_ops(e):
        xr = x_own.rearrange("(t p) e -> t p e", p=128)
        for t in range(NT):
            if t >= 1:
                e.wait_ge(sem["aX"], 16 * t)
            if t >= 2:
                e.wait_ge(sem["aST"], t - 1)
                e.wait_ge(sem["aS2"], t - 1)
            e.dma_start(out=xa[:, 1024 * (t % 2):1024 * (t % 2) + 1024],
                        in_=xr[t]).then_inc(sem["aX"], 16)
        kr = kern.rearrange("k (c p) e -> k p c e", p=128)
        Rst = R.rearrange("p (a e) -> p a e", e=1024)
        for i, ki in enumerate((1, 2, 0)):
            e.wait_ge(sem["aKC"], i)
            e.dma_start(out=Rst[:, 0:8, :], in_=kr[ki]).then_inc(sem["aKL"], 16)
        e.wait_ge(sem["aEVK"], 32)
        e.dma_start(out=kv_in[:, 0:16384], in_=ktq[:]).then_inc(sem["aSP"], 16)
        kvg_kt = kv_g.rearrange("(j p) n -> j p n", p=128)
        for kt8 in range(8):
            j, h = kt8 // 2, kt8 % 2
            if kt8 == 0:
                e.wait_ge(sem["aCC"], 1)
            if kt8 >= 2:
                e.wait_ge(sem["aSC"], 32 * (kt8 - 1))
                e.wait_ge(sem["aPV"], 32 * (kt8 - 1))
                e.wait_ge(sem["aDN"], 4 * (kt8 - 1))
            src = kvg_kt[j][:, 0:16384].rearrange("p (c s) -> p c s", c=8)
            if kt8 >= 1:
                e.wait_ge(sem["aKV"], 32 * kt8)
            e.dma_start(out=ktq[:, 8192 * (kt8 % 2):8192 * (kt8 % 2) + 8192]
                        .rearrange("p (c s) -> p c s", c=8),
                        in_=src[:, :, 1024 * h:1024 * h + 1024]
                        ).then_inc(sem["aKV"], 16)
            e.wait_ge(sem["aKV"], 32 * kt8 + 16)
            e.dma_start(out=vq[:, 8192 * (kt8 % 2):8192 * (kt8 % 2) + 8192],
                        in_=kvg_kt[j][:, 16384 + 8192 * h:16384 + 8192 * (h + 1)]
                        ).then_inc(sem["aKV"], 16)
        for t in range(NT):
            if t >= 1:
                e.wait_ge(sem["aX2"], 16 * t)
            if t >= 2:
                e.wait_ge(sem["aRES"], t - 1)
            e.dma_start(out=xa[:, 1024 * (t % 2):1024 * (t % 2) + 1024],
                        in_=xr[t]).then_inc(sem["aX2"], 16)

    def vector_ops(e):
        e.memset(ones4[:], 1.0)
        e.memset(epsc[:], float(EPS)).then_inc(sem["aONE"], 1)
        for t in range(NT):
            e.wait_ge(sem["aX"], 16 * (t + 1))
            xv = xa[:, 1024 * (t % 2):1024 * (t % 2) + 1024]
            e.reduce_sum(out=stc(t, 0), in_=xv, axis=AX)
            e.wait_ge(sem["aS2"], t + 1)
            e.tensor_add(out=stc(t, 2), in0=stc(t, 2), in1=stc(t, 9)) \
                .then_inc(sem["aDV"], 1)
            e.wait_ge(sem["aDV"], 2 * t + 1)
            e.tensor_scalar(out=stc(t, 1), in0=stc(t, 0), scalar1=1.0 / D,
                            scalar2=None, op0=ALU.mult)
            e.tensor_scalar(out=stc(t, 3), in0=stc(t, 2), scalar1=1.0 / D,
                            scalar2=None, op0=ALU.mult)
            e.scalar_tensor_tensor(out=stc(t, 4), in0=stc(t, 0),
                                   scalar=-1.0 / (D * D), in1=stc(t, 0),
                                   op0=ALU.mult, op1=ALU.mult) \
                .then_inc(sem["aDV"], 1)
            e.wait_ge(sem["aDV"], 2 * t + 2)
            e.tensor_add(out=stc(t, 5), in0=stc(t, 3), in1=stc(t, 4)) \
                .then_inc(sem["aVR"], 1)
            e.wait_ge(sem["aRS"], t + 1)
            e.tensor_scalar(out=lnbf[:, 1024 * (t % 2):1024 * (t % 2) + 1024],
                            in0=xv, scalar1=stc(t, 1), scalar2=stc(t, 7),
                            op0=ALU.subtract, op1=ALU.mult) \
                .then_inc(sem["aST"], 1)
        for i in range(3):
            e.wait_ge(sem["aKL"], 16 * (i + 1))
            if i >= 1:
                e.wait_ge(sem["aMM"], 32 * i)
            e.tensor_copy(out=ker[:, 0:8192], in_=R[:, 0:8192]) \
                .then_inc(sem["aKC"], 1)
        for kt8 in range(8):
            for qg in range(4):
                for qc in range(4):
                    for dh in range(2):
                        n = ((kt8 * 4 + qg) * 4 + qc) * 2 + dh
                        e.wait_ge(sem["aPV"], n + 1)
                        dst = R[:, (qg * 4 + qc) * 1024 + dh * 512:
                                (qg * 4 + qc) * 1024 + dh * 512 + 512]
                        pvp = ps[2 + (qc * 2 + dh) % 2]
                        if kt8 == 0:
                            e.tensor_copy(out=dst, in_=pvp[:]) \
                                .then_inc(sem["aPVE"], 1)
                        else:
                            e.tensor_add(out=dst, in0=dst, in1=pvp[:]) \
                                .then_inc(sem["aPVE"], 1)
                nqg = kt8 * 4 + qg
                e.wait_ge(sem["aDN"], nqg + 1)
                if kt8 == 0:
                    e.tensor_copy(out=dn_sb[:, 16 * qg:16 * qg + 16],
                                  in_=ps64[:, 0:16]).then_inc(sem["aDNE"], 1)
                else:
                    e.tensor_add(out=dn_sb[:, 16 * qg:16 * qg + 16],
                                 in0=dn_sb[:, 16 * qg:16 * qg + 16],
                                 in1=ps64[:, 0:16]).then_inc(sem["aDNE"], 1)
        dnv = dn_sb.rearrange("p (q f) -> p q f", f=4)
        e.wait_ge(sem["aDNE"], 32)
        e.reciprocal(out=rd_all[:], in_=dnv[:, :, 0]).then_inc(sem["aRD"], 1)
        e.wait_ge(sem["aRD"], 1)
        for t in range(NT):
            Ot = R[:, 1024 * t:1024 * t + 1024]
            e.tensor_scalar(out=Ot, in0=Ot, scalar1=rd_all[:, t:t + 1],
                            scalar2=None, op0=ALU.mult).then_inc(sem["aOS"], 1)
            e.wait_ge(sem["aOS"], t + 1)
            e.reduce_sum(out=stc(t, 0), in_=Ot, axis=AX)
            e.wait_ge(sem["aS2"], NT + t + 1)
            e.tensor_add(out=stc(t, 2), in0=stc(t, 2), in1=stc(t, 9)) \
                .then_inc(sem["aDV"], 1)
            e.wait_ge(sem["aDV"], 2 * NT + 3 * t + 1)
            e.tensor_scalar(out=stc(t, 1), in0=stc(t, 0), scalar1=1.0 / D,
                            scalar2=None, op0=ALU.mult)
            e.tensor_scalar(out=stc(t, 3), in0=stc(t, 2), scalar1=1.0 / D,
                            scalar2=None, op0=ALU.mult)
            e.scalar_tensor_tensor(out=stc(t, 4), in0=stc(t, 0),
                                   scalar=-1.0 / (D * D), in1=stc(t, 0),
                                   op0=ALU.mult, op1=ALU.mult) \
                .then_inc(sem["aDV"], 1)
            e.wait_ge(sem["aDV"], 2 * NT + 3 * t + 2)
            e.tensor_add(out=stc(t, 5), in0=stc(t, 3), in1=stc(t, 4)) \
                .then_inc(sem["aVR"], 1)
            e.wait_ge(sem["aRS"], NT + t + 1)
            e.tensor_scalar(out=Ot, in0=Ot, scalar1=stc(t, 1),
                            scalar2=stc(t, 7), op0=ALU.subtract, op1=ALU.mult) \
                .then_inc(sem["aDV"], 1)
            e.wait_ge(sem["aDV"], 2 * NT + 3 * t + 3)
            e.wait_ge(sem["aX2"], 16 * (t + 1))
            e.tensor_add(out=Ot, in0=Ot,
                         in1=xa[:, 1024 * (t % 2):1024 * (t % 2) + 1024]) \
                .then_inc(sem["aRES"], 1)

    def scalar_ops(e):
        e.wait_ge(sem["aONE"], 1)
        for t in range(NT):
            e.wait_ge(sem["aX"], 16 * (t + 1))
            xv = xa[:, 1024 * (t % 2):1024 * (t % 2) + 1024]
            e.activation(out=sqscr[:], in_=xv[:, 0:512], func=AF.Square,
                         accum_out=stc(t, 2))
            e.activation(out=sqscr[:], in_=xv[:, 512:1024], func=AF.Square,
                         accum_out=stc(t, 9)).then_inc(sem["aS2"], 1)
            e.wait_ge(sem["aVR"], t + 1)
            e.activation(out=stc(t, 6), in_=stc(t, 5), func=AF.Ln,
                         bias=epsc[:]).then_inc(sem["aLX"], 1)
            e.wait_ge(sem["aLX"], t + 1)
            e.activation(out=stc(t, 7), in_=stc(t, 6), func=AF.Exp,
                         scale=-0.5).then_inc(sem["aRS"], 1)
            e.wait_ge(sem["aTR"], t + 1)
            dst = lnT.rearrange("p (c s) -> p c s", c=8)[:, :, 128 * t:128 * t + 128]
            e.activation(out=dst, in_=psbf.rearrange("p (c s) -> p c s", c=8),
                         func=AF.Copy).then_inc(sem["aLT"], 1)
        for sc4 in range(4):
            for ec in range(8):
                n = sc4 * 8 + ec
                e.wait_ge(sem["aMM"], n + 1)
                e.activation(out=ktq[:, ec * 2048 + 512 * sc4:ec * 2048 + 512 * sc4 + 512],
                             in_=ps[n % 2][:], func=AF.Copy).then_inc(sem["aEVK"], 1)
        for stt in range(16):
            for eh in range(2):
                n = 32 + stt * 2 + eh
                k = stt * 2 + eh
                e.wait_ge(sem["aMM"], n + 1)
                if k >= 4:
                    e.wait_ge(sem["aSPV"], 16 * (k - 3))
                e.activation(out=lnbf[:, (k % 4) * 512:(k % 4) * 512 + 512],
                             in_=ps[n % 2][:], func=AF.Copy).then_inc(sem["aEVV"], 1)
        for sc4 in range(4):
            for ec in range(8):
                n = 64 + sc4 * 8 + ec
                e.wait_ge(sem["aMM"], n + 1)
                e.activation(out=qt[:, ec * 2048 + 512 * sc4:ec * 2048 + 512 * sc4 + 512],
                             in_=ps[n % 2][:], func=AF.Copy).then_inc(sem["aEVQ"], 1)
        for kt8 in range(8):
            for qg in range(4):
                for jc in range(8):
                    n = (kt8 * 4 + qg) * 8 + jc
                    e.wait_ge(sem["aSC"], n + 1)
                    pt_dst = pT[:, 4096 * (qg % 2) + 512 * jc:
                                4096 * (qg % 2) + 512 * jc + 512]
                    e.activation(out=pt_dst, in_=ps[jc % 2][:], func=AF.Exp,
                                 scale=1.0 / 32.0).then_inc(sem["aEX"], 1)
        for t in range(NT):
            e.wait_ge(sem["aOS"], t + 1)
            Ot = R[:, 1024 * t:1024 * t + 1024]
            e.activation(out=sqscr[:], in_=Ot[:, 0:512], func=AF.Square,
                         accum_out=stc(t, 2))
            e.activation(out=sqscr[:], in_=Ot[:, 512:1024], func=AF.Square,
                         accum_out=stc(t, 9)).then_inc(sem["aS2"], 1)
            e.wait_ge(sem["aVR"], NT + t + 1)
            e.activation(out=stc(t, 6), in_=stc(t, 5), func=AF.Ln,
                         bias=epsc[:]).then_inc(sem["aLX"], 1)
            e.wait_ge(sem["aLX"], NT + t + 1)
            e.activation(out=stc(t, 7), in_=stc(t, 6), func=AF.Exp,
                         scale=-0.5).then_inc(sem["aRS"], 1)

    def tensor_ops(e):
        e.wait_ge(sem["aEYE"] if "aEYE" in sem else sem["gEYE"], 16)
        for t in range(NT):
            e.wait_ge(sem["aST"], t + 1)
            if t >= 1:
                e.wait_ge(sem["aLT"], t)
            src = lnbf[:, 1024 * (t % 2):1024 * (t % 2) + 1024]
            last = None
            for dc in range(8):
                last = e.transpose(psbf[:, 128 * dc:128 * dc + 128],
                                   src[:, 128 * dc:128 * dc + 128], eye[:])
            last.then_inc(sem["aTR"], 1)
        e.wait_ge(sem["aLT"], NT)
        nmm = 0
        e.wait_ge(sem["aKC"], 1)
        kslice = ker[:, 0:8192]
        for sc4 in range(4):
            for ec in range(8):
                prior = sc4 * 8 + ec
                if prior >= 2:
                    e.wait_ge(sem["aEVK"], prior - 1)
                pst = ps[nmm % 2]
                last = None
                for dc in range(8):
                    last = e.matmul(
                        pst[:],
                        kslice[:, dc * 1024 + ec * 128:dc * 1024 + ec * 128 + 128],
                        lnT[:, dc * 2048 + 512 * sc4:dc * 2048 + 512 * sc4 + 512],
                        start=(dc == 0), stop=(dc == 7))
                last.then_inc(sem["aMM"], 1)
                nmm += 1
        e.wait_ge(sem["aKC"], 2)
        for stt in range(16):
            for eh in range(2):
                prior = stt * 2 + eh
                if prior >= 2:
                    e.wait_ge(sem["aEVV"], prior - 1)
                pst = ps[nmm % 2]
                last = None
                for dc in range(8):
                    last = e.matmul(
                        pst[:],
                        lnT[:, dc * 2048 + 128 * stt:dc * 2048 + 128 * stt + 128],
                        ker[:, dc * 1024 + eh * 512:dc * 1024 + eh * 512 + 512],
                        start=(dc == 0), stop=(dc == 7))
                last.then_inc(sem["aMM"], 1)
                nmm += 1
        e.wait_ge(sem["aKC"], 3)
        for sc4 in range(4):
            for ec in range(8):
                prior = sc4 * 8 + ec
                if prior >= 2:
                    e.wait_ge(sem["aEVQ"], prior - 1)
                pst = ps[nmm % 2]
                last = None
                for dc in range(8):
                    last = e.matmul(
                        pst[:],
                        kslice[:, dc * 1024 + ec * 128:dc * 1024 + ec * 128 + 128],
                        lnT[:, dc * 2048 + 512 * sc4:dc * 2048 + 512 * sc4 + 512],
                        start=(dc == 0), stop=(dc == 7))
                last.then_inc(sem["aMM"], 1)
                nmm += 1
        e.wait_ge(sem["aONE"], 1)
        e.wait_ge(sem["aEVQ"], 32)
        for kt8 in range(8):
            e.wait_ge(sem["aKV"], 32 * (kt8 + 1))
            ktb = ktq[:, 8192 * (kt8 % 2):8192 * (kt8 % 2) + 8192]
            vb = vq[:, 8192 * (kt8 % 2):8192 * (kt8 % 2) + 8192]
            for qg in range(4):
                nqg = kt8 * 4 + qg
                for jc in range(8):
                    n = nqg * 8 + jc
                    if n >= 2:
                        e.wait_ge(sem["aEX"], n - 1)
                    if jc == 0 and nqg >= 2:
                        e.wait_ge(sem["aPV"], (nqg - 1) * 8)
                        e.wait_ge(sem["aDN"], nqg - 1)
                    pst = ps[jc % 2]
                    last = None
                    for ec in range(8):
                        last = e.matmul(
                            pst[:],
                            ktb[:, ec * 1024 + jc * 128:ec * 1024 + jc * 128 + 128],
                            qt[:, ec * 2048 + qg * 512:ec * 2048 + qg * 512 + 512],
                            start=(ec == 0), stop=(ec == 7))
                    last.then_inc(sem["aSC"], 1)
                e.wait_ge(sem["aEX"], nqg * 8 + 8)
                pt_b = pT[:, 4096 * (qg % 2):4096 * (qg % 2) + 4096]
                for qc in range(4):
                    for dh in range(2):
                        n = (nqg * 4 + qc) * 2 + dh
                        pvp = ps[2 + (qc * 2 + dh) % 2]
                        if n >= 2:
                            e.wait_ge(sem["aPVE"], n - 1)
                        last = None
                        for jc in range(8):
                            last = e.matmul(
                                pvp[:],
                                pt_b[:, jc * 512 + qc * 128:jc * 512 + qc * 128 + 128],
                                vb[:, jc * 1024 + dh * 512:jc * 1024 + dh * 512 + 512],
                                start=(jc == 0), stop=(jc == 7))
                        last.then_inc(sem["aPV"], 1)
                    if nqg >= 1:
                        e.wait_ge(sem["aDNE"], nqg)
                    last = None
                    for jc in range(8):
                        last = e.matmul(
                            ps64[:, 4 * qc:4 * qc + 4],
                            pt_b[:, jc * 512 + qc * 128:jc * 512 + qc * 128 + 128],
                            ones4[:], start=(jc == 0), stop=(jc == 7),
                            skip_group_check=True)
                    if qc == 3:
                        last.then_inc(sem["aDN"], 1)

    def gpsimd_ops(e):
        for k in range(32):
            e.wait_ge(sem["aEVV"], k + 1)
            e.dma_start(out=kv_in[:, 16384 + 512 * k:16384 + 512 * k + 512],
                        in_=lnbf[:, (k % 4) * 512:(k % 4) * 512 + 512]) \
                .then_inc(sem["aSPV"], 16)
        e.wait_ge(sem["aSP"], 16)
        e.wait_ge(sem["aSPV"], 512)
        e.collective_compute("AllGather", ALU.bypass, replica_groups=RG,
                             ins=[kv_in.opt()], outs=[kv_g.opt()]) \
            .then_inc(sem["aCC"])
        if store_out:
            e.wait_ge(sem["aRES"], NT)
            nout = 0
            xo = io["x1_ext"].rearrange("(q p) e -> q p e", p=128)
            for t in range(NT):
                e.dma_start(out=xo[t], in_=R[:, 1024 * t:1024 * t + 1024]) \
                    .then_inc(sem["aOUT"], 16)
                nout += 1
            if dbg:
                e.dma_start(out=io["qt_d"], in_=qt[:]).then_inc(sem["aOUT"], 16)
                e.dma_start(out=io["kv_d"], in_=kv_in[:]).then_inc(sem["aOUT"], 16)
                e.dma_start(out=io["kvg_d"], in_=kv_g[:]).then_inc(sem["aOUT"], 16)
                e.dma_start(out=io["dn_d"], in_=dn_sb[:]).then_inc(sem["aOUT"], 16)
                nout += 4
            e.wait_ge(sem["aOUT"], 16 * nout)

    return dict(sync=sync_ops, vector=vector_ops, scalar=scalar_ops,
                tensor=tensor_ops, gpsimd=gpsimd_ops)


B_SEMS = ["bXR", "bS2", "bVR", "bRS", "bLX", "bDV", "bZ", "bTR", "bZT", "bCS",
          "bCC", "bSS", "bDM", "bHM", "bHE", "bYM", "bYE", "bRES", "bOUT"]


class TokenChain:

    def __init__(self, ss, dm):
        self.ss, self.dm = ss, dm
        self.k = 0
        self.m = 0
        self.steps = []
        self._last_cc = None
        self.marks = {}

    def mark(self, label):
        self.marks[label] = len(self.steps)

    def add(self, eng, fn, dma=False, cc=None):
        self.steps.append((eng, fn, self.k, self.m, dma, cc, self._last_cc))
        if cc is not None:
            self._last_cc = cc
        elif dma:
            self.m += 1
        else:
            self.k += 1

    def counts_at(self, pos):
        k = m = 0
        last_cc = None
        for eng, fn, k0, m0, dma, cc, pc in self.steps[:pos]:
            if cc is not None:
                last_cc = cc
            elif dma:
                m += 1
            else:
                k += 1
        return k, m, last_cc

    def emit(self, eng_name, e, limit=None):
        for i, (eng, fn, k0, m0, dma, cc, prior_cc) in enumerate(self.steps):
            if limit is not None and i >= limit:
                break
            if eng != eng_name:
                continue
            if k0 > 0:
                e.wait_ge(self.ss, k0)
            if m0 > 0:
                e.wait_ge(self.dm, 16 * m0)
            if prior_cc is not None:
                e.wait_ge(prior_cc, 1)
            ins = fn(e)
            if cc is not None:
                ins.then_inc(cc)
            elif dma:
                ins.then_inc(self.dm, 16)
            else:
                ins.then_inc(self.ss, 1)


def build_stage_b(sem, bufs, io, store_out, resident_in, gate_in=None):
    R = bufs.R_f32
    zT = bufs.B_bf
    af = bufs.A_bf.bitcast(F32)
    cf = bufs.C_bf.bitcast(F32)
    pTb = bufs.pT
    xa = bufs.xa
    lnbf = bufs.lnbf
    sqscr = bufs.sqscr
    st = bufs.stats
    epsc = bufs.epsc
    ps = bufs.ps
    psbf = bufs.psbf
    ps64 = bufs.ps64

    g_bc = af[:, 0:1024]
    b_bc = af[:, 1024:2048]
    wd_all = af[:, 2048:6144]
    wd_sel = af[:, 6144:6656]
    maskP = af[:, 6656:6664]
    gate_bc = af[:, 6664:6665]
    zbnT = af[:, 6672:6680]
    wr_sb = af[:, 6688:6752].rearrange("p (c e) -> p c e", c=8)
    ones1f = af[0:1, 6752:6880]
    onec = af[0:1, 6880:6881]
    mask_col = cf[32:40, 4160:4161]
    bdT = af[0:64, 6888:6889]
    wu_stage = cf[64:128, 1024:2048]
    wu_sel = cf[64:128, 2048:3072]
    bdown_sb = cf[32:40, 3072:3136]
    bup_sb = cf[32:40, 3136:4160]
    bnm = cf[0:1, 0:1024]
    bnv = cf[0:1, 1024:2048]
    bng = cf[0:1, 2048:3072]
    bnb = cf[0:1, 3072:4096]
    g_row = cf[0:1, 4096:5120]
    b_row = cf[0:1, 5120:6144]
    r0 = cf[0:1, 6144:7168]
    zsr = cf[0:1, 7168:8192]
    lvq = af[0:1, 6992:8016]
    br_row = af[0:1, 6912:6920]
    lg = af[0:1, 6928:6936]
    mx = af[0:1, 6936:6937]
    nmx = af[0:1, 6937:6938]
    el = af[0:1, 6944:6952]
    smr = af[0:1, 6952:6953]
    rsr = af[0:1, 6953:6954]
    probs = af[0:1, 6960:6968]
    pmax = af[0:1, 6968:6969]
    mask_row = af[0:1, 6976:6984]
    wd_sel_bf = pTb[:, 0:512]
    wu_sel_bf = pTb[0:64, 512:1536]
    hT = pTb[0:64, 1536:3584]
    bu_row_bf = pTb[0:1, 3584:4608]
    ones1_bf = pTb[0:1, 4608:4736]

    w = io["w"]

    def stc(t, i):
        o = 16 * (t % 2) + i
        return st[:, o:o + 1]

    tc = TokenChain(sem["bSS"], sem["bDM"])

    def _chain():
        tc.add("scalar", lambda e: e.activation(out=r0[:, 0:512], in_=ps[2][0:1, :],
                                                func=AF.Copy))
        tc.add("scalar", lambda e: e.activation(out=r0[:, 512:1024], in_=ps[3][0:1, :],
                                                func=AF.Copy))
        tc.add("gpsimd", lambda e: e.dma_start(out=io["zs_in"], in_=r0), dma=True)
        tc.add("gpsimd", lambda e: e.collective_compute(
            "AllReduce", ALU.add, replica_groups=RG,
            ins=[io["zs_in"].opt()], outs=[io["zs_out"].opt()]))
        tc.add("gpsimd", lambda e: e.dma_start(out=zsr, in_=io["zs_out"]), dma=True)
        tc.add("vector", lambda e: e.tensor_scalar(out=zsr, in0=zsr, scalar1=1.0 / S,
                                                   scalar2=None, op0=ALU.mult))
        tc.add("vector", lambda e: e.tensor_sub(out=r0, in0=zsr, in1=bnm))
        tc.add("scalar", lambda e: e.activation(out=lvq, in_=bnv, func=AF.Ln,
                                                bias=epsc[0:1, :]))
        tc.add("scalar", lambda e: e.activation(out=lvq, in_=lvq, func=AF.Exp,
                                                scale=-0.5))
        tc.add("vector", lambda e: e.tensor_mul(out=r0, in0=r0, in1=lvq))
        tc.add("vector", lambda e: e.tensor_mul(out=r0, in0=r0, in1=bng))
        tc.add("vector", lambda e: e.tensor_add(out=r0, in0=r0, in1=bnb))

        def zbnt_mms(e):
            for dc in range(8):
                last = e.matmul(ps64[:, dc:dc + 1], r0[:, 128 * dc:128 * dc + 128],
                                onec, start=True, stop=True)
            return last
        tc.add("tensor", zbnt_mms)
        tc.add("scalar", lambda e: e.activation(out=zbnT, in_=ps64[:, 0:8],
                                                func=AF.Copy))

        def logit_mms(e):
            for dc in range(8):
                last = e.matmul(ps[0][0:1, 0:8], zbnT[:, dc:dc + 1],
                                wr_sb[:, dc, :], start=(dc == 0), stop=(dc == 7))
            return last
        tc.add("tensor", logit_mms)
        tc.add("scalar", lambda e: e.activation(out=lg, in_=ps[0][0:1, 0:8],
                                                func=AF.Copy))
        tc.add("vector", lambda e: e.tensor_add(out=lg, in0=lg, in1=br_row))
        tc.add("vector", lambda e: e.reduce_max(out=mx, in_=lg, axis=AX))
        tc.add("vector", lambda e: e.tensor_scalar(out=nmx, in0=mx, scalar1=-1.0,
                                                   scalar2=None, op0=ALU.mult))
        tc.add("scalar", lambda e: e.activation(out=el, in_=lg, func=AF.Exp,
                                                bias=nmx))
        tc.add("vector", lambda e: e.reduce_sum(out=smr, in_=el, axis=AX))
        tc.add("vector", lambda e: e.reciprocal(out=rsr, in_=smr))
        tc.add("vector", lambda e: e.tensor_scalar(out=probs, in0=el, scalar1=rsr,
                                                   scalar2=None, op0=ALU.mult))
        tc.add("vector", lambda e: e.reduce_max(out=pmax, in_=probs, axis=AX))
        tc.add("vector", lambda e: e.tensor_scalar(out=mask_row, in0=probs,
                                                   scalar1=pmax, scalar2=None,
                                                   op0=ALU.is_ge))
        tc.add("gpsimd", lambda e: e.dma_start(out=io["mrow_d"], in_=mask_row),
               dma=True)
        tc.add("gpsimd", lambda e: e.dma_start(
            out=mask_col, in_=io["mrow_d"].rearrange("o e -> e o")), dma=True)
        tc.add("tensor", lambda e: e.matmul(ps[1][:, 0:8], ones1f, mask_row,
                                            start=True, stop=True))
        tc.add("tensor", lambda e: e.matmul(ps[1][:, 8:9], ones1f, pmax,
                                            start=True, stop=True, skip_group_check=True))
        tc.add("scalar", lambda e: e.activation(out=maskP, in_=ps[1][:, 0:8],
                                                func=AF.Copy))
        tc.add("scalar", lambda e: e.activation(out=gate_bc, in_=ps[1][:, 8:9],
                                                func=AF.Copy))
        for ee in range(8):
            we = wd_all[:, 512 * ee:512 * ee + 512]
            if ee == 0:
                tc.add("vector", lambda e, we=we: e.tensor_scalar(
                    out=wd_sel, in0=we, scalar1=maskP[:, 0:1], scalar2=None,
                    op0=ALU.mult))
            else:
                tc.add("vector", lambda e, we=we, ee=ee: e.scalar_tensor_tensor(
                    out=wd_sel, in0=we, scalar=maskP[:, ee:ee + 1], in1=wd_sel,
                    op0=ALU.mult, op1=ALU.add))
        tc.add("vector", lambda e: e.tensor_copy(out=wd_sel_bf, in_=wd_sel))
        for ee in range(8):
            tc.add("gpsimd", lambda e, ee=ee: e.dma_start(
                out=wu_stage, in_=w["W_up"][ee]), dma=True)
            if ee == 0:
                tc.add("vector", lambda e: e.tensor_scalar(
                    out=wu_sel, in0=wu_stage, scalar1=maskP[64:128, 0:1],
                    scalar2=None, op0=ALU.mult))
            else:
                tc.add("vector", lambda e, ee=ee: e.scalar_tensor_tensor(
                    out=wu_sel, in0=wu_stage, scalar=maskP[64:128, ee:ee + 1],
                    in1=wu_sel, op0=ALU.mult, op1=ALU.add))
        tc.add("vector", lambda e: e.tensor_copy(out=wu_sel_bf, in_=wu_sel))
        tc.add("tensor", lambda e: e.matmul(ps[0][0:64, 0:1], bdown_sb, mask_col,
                                            start=True, stop=True))
        tc.add("scalar", lambda e: e.activation(out=bdT, in_=ps[0][0:64, 0:1],
                                                func=AF.Copy))
        tc.add("tensor", lambda e: e.matmul(ps[2][0:1, :], mask_col,
                                            bup_sb[:, 0:512], start=True, stop=True))
        tc.add("tensor", lambda e: e.matmul(ps[3][0:1, :], mask_col,
                                            bup_sb[:, 512:1024], start=True,
                                            stop=True))
        tc.add("scalar", lambda e: e.activation(out=bu_row_bf[:, 0:512],
                                                in_=ps[2][0:1, :], func=AF.Copy))
        tc.add("scalar", lambda e: e.activation(out=bu_row_bf[:, 512:1024],
                                                in_=ps[3][0:1, :], func=AF.Copy))
    _chain()
    TCK = tc.k

    def gate(e):
        if gate_in is not None:
            e.wait_ge(gate_in[0], gate_in[1])

    NW = 11

    def sync_ops2(e):
        gate(e)
        loads = [
            (g_row, w["ln_g"][None, :]),
            (b_row, w["ln_b"][None, :]),
            (bnm, w["bn_mean"][None, :]),
            (bnv, w["bn_var"][None, :]),
            (bng, w["bn_g"][None, :]),
            (bnb, w["bn_b"][None, :]),
            (br_row, w["br"][None, :]),
            (wr_sb, w["Wr"].rearrange("(c p) e -> p c e", p=128)),
            (bdown_sb, w["b_down"]),
            (bup_sb, w["b_up"]),
        ]
        nb = 0
        for dst, srcap in loads:
            if nb >= 1:
                e.wait_ge(sem["bXR"], 16 * nb)
            e.dma_start(out=dst, in_=srcap).then_inc(sem["bXR"], 16)
            nb += 1
        for ee in range(8):
            e.wait_ge(sem["bXR"], 16 * nb)
            e.dma_start(out=wd_all[:, 512 * ee:512 * ee + 512]
                        .rearrange("p (c h) -> p c h", c=8),
                        in_=w["W_down"][ee].rearrange("(c p) h -> p c h", p=128)) \
                .then_inc(sem["bXR"], 16)
            nb += 1
        if not resident_in:
            xr = io["x_in"].rearrange("(t p) e -> t p e", p=128)
            for t in range(NT):
                e.wait_ge(sem["bXR"], 16 * nb)
                e.dma_start(out=R[:, 1024 * t:1024 * t + 1024], in_=xr[t]) \
                    .then_inc(sem["bXR"], 16)
                nb += 1

    NW = 18

    def vector_ops(e):
        gate(e)
        e.memset(ones1f, 1.0)
        e.memset(onec, 1.0)
        e.memset(ones1_bf, 1.0)
        for t in range(NT):
            if not resident_in:
                e.wait_ge(sem["bXR"], 16 * (NW + t + 1))
            else:
                e.wait_ge(sem["bXR"], 16 * NW)
            Rt = R[:, 1024 * t:1024 * t + 1024]
            e.reduce_sum(out=stc(t, 0), in_=Rt, axis=AX)
            e.wait_ge(sem["bS2"], t + 1)
            e.tensor_add(out=stc(t, 2), in0=stc(t, 2), in1=stc(t, 9)) \
                .then_inc(sem["bDV"], 1)
            e.wait_ge(sem["bDV"], 5 * t + 1)
            e.tensor_scalar(out=stc(t, 1), in0=stc(t, 0), scalar1=1.0 / D,
                            scalar2=None, op0=ALU.mult)
            e.tensor_scalar(out=stc(t, 3), in0=stc(t, 2), scalar1=1.0 / D,
                            scalar2=None, op0=ALU.mult)
            e.scalar_tensor_tensor(out=stc(t, 4), in0=stc(t, 0),
                                   scalar=-1.0 / (D * D), in1=stc(t, 0),
                                   op0=ALU.mult, op1=ALU.mult) \
                .then_inc(sem["bDV"], 1)
            e.wait_ge(sem["bDV"], 5 * t + 2)
            e.tensor_add(out=stc(t, 5), in0=stc(t, 3), in1=stc(t, 4)) \
                .then_inc(sem["bVR"], 1)
            e.wait_ge(sem["bRS"], t + 1)
            za = xa[:, 1024 * (t % 2):1024 * (t % 2) + 1024]
            e.tensor_scalar(out=za, in0=Rt, scalar1=stc(t, 1),
                            scalar2=stc(t, 7), op0=ALU.subtract, op1=ALU.mult) \
                .then_inc(sem["bDV"], 1)
            e.wait_ge(sem["bDV"], 5 * t + 3)
            e.wait_ge(sem["bHE"], 4)
            e.tensor_mul(out=za, in0=za, in1=g_bc).then_inc(sem["bDV"], 1)
            e.wait_ge(sem["bDV"], 5 * t + 4)
            e.tensor_add(out=lnbf[:, 1024 * (t % 2):1024 * (t % 2) + 1024],
                         in0=za, in1=b_bc).then_inc(sem["bZ"], 1)
            e.nop().then_inc(sem["bDV"], 1)
        tc.emit("vector", e)
        for t in range(NT):
            for dh in range(2):
                n = 2 * t + dh
                e.wait_ge(sem["bYM"], n + 1)
                ya = xa[:, 1024 * (t % 2) + 512 * dh:
                        1024 * (t % 2) + 512 * dh + 512]
                e.tensor_scalar(out=ya, in0=ps[2 + n % 2][:], scalar1=gate_bc,
                                scalar2=None, op0=ALU.mult).then_inc(sem["bYE"], 1)
            yt = xa[:, 1024 * (t % 2):1024 * (t % 2) + 1024]
            e.wait_ge(sem["bYE"], 2 * t + 2)
            e.reduce_sum(out=stc(t, 0), in_=yt, axis=AX)
            e.wait_ge(sem["bS2"], NT + t + 1)
            e.tensor_add(out=stc(t, 2), in0=stc(t, 2), in1=stc(t, 9)) \
                .then_inc(sem["bDV"], 1)
            e.wait_ge(sem["bDV"], 5 * NT + 3 * t + 1)
            e.tensor_scalar(out=stc(t, 1), in0=stc(t, 0), scalar1=1.0 / D,
                            scalar2=None, op0=ALU.mult)
            e.tensor_scalar(out=stc(t, 3), in0=stc(t, 2), scalar1=1.0 / D,
                            scalar2=None, op0=ALU.mult)
            e.scalar_tensor_tensor(out=stc(t, 4), in0=stc(t, 0),
                                   scalar=-1.0 / (D * D), in1=stc(t, 0),
                                   op0=ALU.mult, op1=ALU.mult) \
                .then_inc(sem["bDV"], 1)
            e.wait_ge(sem["bDV"], 5 * NT + 3 * t + 2)
            e.tensor_add(out=stc(t, 5), in0=stc(t, 3), in1=stc(t, 4)) \
                .then_inc(sem["bVR"], 1)
            e.wait_ge(sem["bRS"], NT + t + 1)
            e.tensor_scalar(out=yt, in0=yt, scalar1=stc(t, 1),
                            scalar2=stc(t, 7), op0=ALU.subtract, op1=ALU.mult) \
                .then_inc(sem["bDV"], 1)
            e.wait_ge(sem["bDV"], 5 * NT + 3 * t + 3)
            Rt = R[:, 1024 * t:1024 * t + 1024]
            e.tensor_add(out=Rt, in0=Rt, in1=yt).then_inc(sem["bRES"], 1)

    def scalar_ops(e):
        gate(e)
        bdsts = [g_bc[:, 0:512], g_bc[:, 512:1024],
                 b_bc[:, 0:512], b_bc[:, 512:1024]]
        for k in range(4):
            e.wait_ge(sem["bHM"], k + 1)
            e.activation(out=bdsts[k], in_=ps[k % 2][:], func=AF.Copy) \
                .then_inc(sem["bHE"], 1)
        for t in range(NT):
            if not resident_in:
                e.wait_ge(sem["bXR"], 16 * (NW + t + 1))
            else:
                e.wait_ge(sem["bXR"], 16 * NW)
            Rt = R[:, 1024 * t:1024 * t + 1024]
            e.activation(out=sqscr[:], in_=Rt[:, 0:512], func=AF.Square,
                         accum_out=stc(t, 2))
            e.activation(out=sqscr[:], in_=Rt[:, 512:1024], func=AF.Square,
                         accum_out=stc(t, 9)).then_inc(sem["bS2"], 1)
            e.wait_ge(sem["bVR"], t + 1)
            e.activation(out=stc(t, 6), in_=stc(t, 5), func=AF.Ln,
                         bias=epsc[:]).then_inc(sem["bLX"], 1)
            e.wait_ge(sem["bLX"], t + 1)
            e.activation(out=stc(t, 7), in_=stc(t, 6), func=AF.Exp,
                         scale=-0.5).then_inc(sem["bRS"], 1)
            e.wait_ge(sem["bTR"], t + 1)
            dst = zT.rearrange("p (c s) -> p c s", c=8)[:, :, 128 * t:128 * t + 128]
            e.activation(out=dst, in_=psbf.rearrange("p (c s) -> p c s", c=8),
                         func=AF.Copy).then_inc(sem["bZT"], 1)
        tc.emit("scalar", e)
        for k in range(4):
            e.wait_ge(sem["bHM"], 4 + k + 1)
            e.activation(out=hT[:, 512 * k:512 * k + 512], in_=ps[k % 2][0:64, :],
                         func=AF.Relu, bias=bdT).then_inc(sem["bHE"], 1)
        for t in range(NT):
            e.wait_ge(sem["bYE"], 2 * t + 2)
            yt = xa[:, 1024 * (t % 2):1024 * (t % 2) + 1024]
            e.activation(out=sqscr[:], in_=yt[:, 0:512], func=AF.Square,
                         accum_out=stc(t, 2))
            e.activation(out=sqscr[:], in_=yt[:, 512:1024], func=AF.Square,
                         accum_out=stc(t, 9)).then_inc(sem["bS2"], 1)
            e.wait_ge(sem["bVR"], NT + t + 1)
            e.activation(out=stc(t, 6), in_=stc(t, 5), func=AF.Ln,
                         bias=epsc[:]).then_inc(sem["bLX"], 1)
            e.wait_ge(sem["bLX"], NT + t + 1)
            e.activation(out=stc(t, 7), in_=stc(t, 6), func=AF.Exp,
                         scale=-0.5).then_inc(sem["bRS"], 1)

    def tensor_ops(e):
        gate(e)
        e.wait_ge(sem["bXR"], 16 * NW)
        srcs = [g_row[:, 0:512], g_row[:, 512:1024],
                b_row[:, 0:512], b_row[:, 512:1024]]
        for k, srow in enumerate(srcs):
            if k >= 2:
                e.wait_ge(sem["bHE"], k - 1)
            e.matmul(ps[k % 2][:], ones1f, srow, start=True, stop=True) \
                .then_inc(sem["bHM"], 1)
        for t in range(NT):
            e.wait_ge(sem["bZ"], t + 1)
            if t >= 1:
                e.wait_ge(sem["bZT"], t)
            srcz = lnbf[:, 1024 * (t % 2):1024 * (t % 2) + 1024]
            last = None
            for dc in range(8):
                last = e.transpose(psbf[:, 128 * dc:128 * dc + 128],
                                   srcz[:, 128 * dc:128 * dc + 128], bufs.eye[:])
            last.then_inc(sem["bTR"], 1)
            for hh in range(2):
                last = e.matmul(ps[2 + hh][0:1, :], bufs.ones4[:, 0:1],
                                srcz[:, 512 * hh:512 * hh + 512],
                                start=(t == 0), stop=(t == NT - 1),
                                skip_group_check=True)
            if t == NT - 1:
                last.then_inc(sem["bCS"], 1)
        tc.emit("tensor", e)
        e.wait_ge(sem["bSS"], TCK)
        e.wait_ge(sem["bZT"], NT)
        for k in range(4):
            if k >= 2:
                e.wait_ge(sem["bHE"], 4 + k - 1)
            last = None
            for dc in range(8):
                last = e.matmul(ps[k % 2][0:64, :],
                                wd_sel_bf[:, 64 * dc:64 * dc + 64],
                                zT[:, 2048 * dc + 512 * k:2048 * dc + 512 * k + 512],
                                start=(dc == 0), stop=(dc == 7))
            last.then_inc(sem["bHM"], 1)
        e.wait_ge(sem["bHE"], 8)
        for t in range(NT):
            for dh in range(2):
                n = 2 * t + dh
                if n >= 2:
                    e.wait_ge(sem["bYE"], n - 1)
                pst = ps[2 + n % 2]
                e.matmul(pst[:], hT[:, 128 * t:128 * t + 128],
                         wu_sel_bf[:, 512 * dh:512 * dh + 512],
                         start=True, stop=False)
                e.matmul(pst[:], ones1_bf, bu_row_bf[:, 512 * dh:512 * dh + 512],
                         start=False, stop=True).then_inc(sem["bYM"], 1)

    def gpsimd_ops(e):
        gate(e)
        tc.emit("gpsimd", e)
        if store_out:
            e.wait_ge(sem["bRES"], NT)
            nout = 0
            xo = io["x_out"].rearrange("(q p) e -> q p e", p=128)
            for t in range(NT):
                e.dma_start(out=xo[t], in_=R[:, 1024 * t:1024 * t + 1024]) \
                    .then_inc(sem["bOUT"], 16)
                nout += 1
            if "af_d" in io:
                e.dma_start(out=io["af_d"], in_=bufs.A_bf[:]).then_inc(sem["bOUT"], 16)
                e.dma_start(out=io["cf_d"], in_=bufs.C_bf[:]).then_inc(sem["bOUT"], 16)
                e.dma_start(out=io["pt_d"], in_=pTb[:]).then_inc(sem["bOUT"], 16)
                e.dma_start(out=io["zt_d"], in_=zT[:]).then_inc(sem["bOUT"], 16)
                nout += 4
            e.wait_ge(sem["bOUT"], 16 * nout)

    return dict(sync=sync_ops2, vector=vector_ops, scalar=scalar_ops,
                tensor=tensor_ops, gpsimd=gpsimd_ops)


def alloc_common(nc, ctx):
    b = {}
    b["R_f32"] = ctx.enter_context(nc.sbuf_tensor("R_f32", [128, 16384], F32)).ap()
    b["A_bf"] = ctx.enter_context(nc.sbuf_tensor("A_bf", [128, 16384], BF16)).ap()
    b["B_bf"] = ctx.enter_context(nc.sbuf_tensor("B_bf", [128, 16384], BF16)).ap()
    b["C_bf"] = ctx.enter_context(nc.sbuf_tensor("C_bf", [128, 16384], BF16)).ap()
    b["W_bf"] = ctx.enter_context(nc.sbuf_tensor("W_bf", [128, 8192], BF16)).ap()
    b["pT"] = ctx.enter_context(nc.sbuf_tensor("pT", [128, 8192], BF16)).ap()
    b["lnbf"] = ctx.enter_context(nc.sbuf_tensor("lnbf", [128, 2048], BF16)).ap()
    b["xa"] = ctx.enter_context(nc.sbuf_tensor("xa", [128, 2048], F32)).ap()
    b["sqscr"] = ctx.enter_context(nc.sbuf_tensor("sqscr", [128, 512], F32)).ap()
    b["stats"] = ctx.enter_context(nc.sbuf_tensor("stats", [128, 32], F32)).ap()
    b["dn_sb"] = ctx.enter_context(nc.sbuf_tensor("dn_sb", [128, 64], F32)).ap()
    b["rd_all"] = ctx.enter_context(nc.sbuf_tensor("rd_all", [128, 16], F32)).ap()
    b["ones4"] = ctx.enter_context(nc.sbuf_tensor("ones4", [128, 4], BF16)).ap()
    b["epsc"] = ctx.enter_context(nc.sbuf_tensor("epsc", [128, 1], F32)).ap()
    b["eye"] = ctx.enter_context(nc.sbuf_tensor("eye", [128, 128], BF16)).ap()
    b["wbf"] = ctx.enter_context(nc.sbuf_tensor("wbf", [128, 256], BF16)).ap()
    b["ccs"] = ctx.enter_context(nc.sbuf_tensor("ccs", [128, 64], F32)).ap()
    b["mask_lt"] = ctx.enter_context(nc.sbuf_tensor("mask_lt", [128, 4], F32)).ap()
    b["mask_prev"] = ctx.enter_context(nc.sbuf_tensor("mask_prev", [128, 4], F32)).ap()
    b["first_m"] = ctx.enter_context(nc.sbuf_tensor("first_m", [128, 1], F32)).ap()
    b["ps"] = [ctx.enter_context(nc.psum_tensor(f"ps{i}", [128, 512], F32)).ap()
               for i in range(4)]
    b["psbf"] = ctx.enter_context(nc.psum_tensor("psbf", [128, 1024], BF16)).ap()
    b["ps64"] = ctx.enter_context(nc.psum_tensor("ps64", [128, 64], F32)).ap()
    return NS(b)


_NC_CACHE = {}


def build_launch_a(dbg=False):
    key = ("a", dbg)
    if key in _NC_CACHE:
        return _NC_CACHE[key]
    nc = bass.Bass("TRN2", target_bir_lowering=False, debug=False)
    io = {}
    io["x_own"] = nc.dram_tensor("x_own", [SC, D], F32, kind="ExternalInput").ap()
    io["kern"] = nc.dram_tensor("kern", [3, D, D], F32, kind="ExternalInput").ap()
    eye_in = nc.dram_tensor("eye_in", [128, 128], BF16, kind="ExternalInput").ap()
    io["kv_in"] = nc.dram_tensor("kv_in", [128, 32768], BF16, kind="Internal").ap()
    io["kv_g"] = nc.dram_tensor("kv_g", [512, 32768], BF16, kind="Internal").ap()
    io["x1_ext"] = nc.dram_tensor("x1", [SC, D], F32, kind="ExternalOutput").ap()
    if dbg:
        io["qt_d"] = nc.dram_tensor("qt_d", [128, 16384], BF16, kind="ExternalOutput").ap()
        io["kv_d"] = nc.dram_tensor("kv_d", [128, 32768], BF16, kind="ExternalOutput").ap()
        io["kvg_d"] = nc.dram_tensor("kvg_d", [512, 32768], BF16, kind="ExternalOutput").ap()
        io["dn_d"] = nc.dram_tensor("dn_d", [128, 64], F32, kind="ExternalOutput").ap()
    ctx = ExitStack()
    with ctx:
        bufs = alloc_common(nc, ctx)
        sem = make_sems(nc, ctx, A_SEMS)
        ops = build_stage_a(sem, bufs, io, store_out=True, dbg=dbg)
        block = ctx.enter_context(nc.Block())

        @block.sync
        def _(e):
            e.dma_start(out=bufs.eye, in_=eye_in).then_inc(sem["aEYE"], 16)
            ops["sync"](e)

        @block.vector
        def _(e):
            ops["vector"](e)

        @block.scalar
        def _(e):
            ops["scalar"](e)

        @block.tensor
        def _(e):
            ops["tensor"](e)

        @block.gpsimd
        def _(e):
            ops["gpsimd"](e)
    return nc


C_SEMS = ["cXR", "cS2", "cVR", "cRS", "cLX", "cDV", "cZ", "cTR", "cLT",
          "cWL", "cWC", "cWT", "cWE", "cGM", "cGE", "cSS", "cDM", "cBP",
          "cYM", "cYE", "cX2", "cRES", "cOUT"]

C2 = 0.02 / 32.0
C3 = 0.02 / 32.0


def build_stage_c(sem, bufs, io, store_out, resident_in, gate_in=None, trunc=None):
    R = bufs.R_f32
    ln2T = bufs.A_bf
    OT = bufs.B_bf
    GT = bufs.C_bf
    WQ = bufs.W_bf
    WK = bufs.pT
    xa = bufs.xa
    lnbf = bufs.lnbf
    sqscr = bufs.sqscr
    st = bufs.stats
    epsc = bufs.epsc
    eye = bufs.eye
    ps = bufs.ps
    psbf = bufs.psbf
    wbf = bufs.wbf

    w = io["w"]
    mask_lt = bufs.mask_lt
    mask_prev = bufs.mask_prev
    first_m = bufs.first_m

    pay = xa[:, 0:5]
    pay_g = bufs.ccs[:, 0:20].rearrange("p (j f) -> p j f", f=5)
    carryK = xa[:, 32:33]
    carryKV = xa[:, 33:34]
    vprev = xa[:, 34:35]
    qprev = xa[:, 35:36]
    gprev = xa[:, 36:37]
    t_a = xa[:, 38:39]
    t_b = xa[:, 39:40]
    oarprev = xa[:, 40:41]
    pay2 = xa[:, 42:43]
    pay2_g = bufs.ccs[:, 24:28]
    carry2 = xa[:, 48:49]
    bq1T = bufs.ccs[:, 24:32]
    bk1T = bufs.ccs[:, 32:40]
    bk2T = bufs.ccs[:, 40:48]
    bq1c = bufs.ccs[:, 48:56]
    bk2pc = bufs.ccs[:, 56:64]
    bprow = xa[0:1, 96:1120]
    bff = bufs.B_bf.bitcast(F32)
    wslab_pre = [bff[:, 4096:5120], bff[:, 5120:6144]]
    wslab_post = [xa[:, 0:1024], xa[:, 1024:2048]]
    wcast = [lnbf[:, 0:1024], lnbf[:, 1024:2048]]
    bprow_bf = ln2T[0:1, 0:1024]
    ones1_bf = ln2T[0:1, 1024:1152]

    def slot(i):
        return R[:, 2048 * i:2048 * i + 2048]

    def stc(t, i):
        o = 16 * (t % 2) + i
        return st[:, o:o + 1]

    def gate(e):
        if gate_in is not None:
            e.wait_ge(gate_in[0], gate_in[1])

    NWC = 6
    WMATS = ["Wk2", "Wq1", "Wk1", "Wproj"]
    WDST = [WQ, WQ, WK, WQ]

    tc = TokenChain(sem["cSS"], sem["cDM"])

    def _chain():
        tc.add("vector", lambda e: e.tensor_scalar(
            out=bq1c, in0=bq1T, scalar1=0.5, scalar2=None, op0=ALU.mult))
        tc.add("vector", lambda e: e.tensor_scalar(
            out=bk2pc, in0=bk2T, scalar1=C2 / 2.0, scalar2=None, op0=ALU.mult))
        sK, sKV, sQ, sKc, sKVc, sS, sOar, sE = [slot(i) for i in range(8)]
        for dc in range(8):
            for wmat, outslot, act in (
                    (WK, 0, "exp"), (WQ, 2, "tanh")):
                for s4 in range(4):
                    def mm(e, wmat=wmat, dc=dc, s4=s4):
                        last = None
                        for di in range(8):
                            last = e.matmul(
                                ps[s4 % 2][:],
                                wmat[:, di * 1024 + dc * 128:di * 1024 + dc * 128 + 128],
                                ln2T[:, di * 2048 + 512 * s4:di * 2048 + 512 * s4 + 512],
                                start=(di == 0), stop=(di == 7))
                        return last
                    tc.add("tensor", mm)
                    tc.add("vector", lambda e: e.nop())
                    if act == "exp":
                        tc.add("scalar", lambda e, dc=dc, s4=s4, o=outslot:
                               e.activation(out=slot(o)[:, 512 * s4:512 * s4 + 512],
                                            in_=ps[s4 % 2][:], func=AF.Exp,
                                            bias=bk1T[:, dc:dc + 1]))
                    else:
                        tc.add("scalar", lambda e, dc=dc, s4=s4, o=outslot:
                               e.activation(out=slot(o)[:, 512 * s4:512 * s4 + 512],
                                            in_=ps[s4 % 2][:], func=AF.Tanh,
                                            scale=0.5, bias=bq1c[:, dc:dc + 1]))
            tc.add("vector", lambda e: e.tensor_scalar(
                out=sQ, in0=sQ, scalar1=0.5, scalar2=0.5, op0=ALU.mult,
                op1=ALU.add))
            tc.add("vector", lambda e, dc=dc: e.tensor_mul(
                out=sKV, in0=sK, in1=ln2T[:, dc * 2048:dc * 2048 + 2048]))
            tc.add("vector", lambda e: e.tensor_tensor_scan(
                out=sKc, data0=sK, data1=sK, initial=0.0, op0=ALU.add,
                op1=ALU.bypass))
            tc.add("vector", lambda e: e.tensor_tensor_scan(
                out=sKVc, data0=sKV, data1=sKV, initial=0.0, op0=ALU.add,
                op1=ALU.bypass))
            tc.mark(f"scans{dc}")
            tc.add("gpsimd", lambda e, dc=dc: e.dma_start(
                out=io["cc1_in"][dc][:, 0:1], in_=sKc[:, 2047:2048],
                allow_slow_non_contiguous=True), dma=True)
            tc.add("gpsimd", lambda e, dc=dc: e.dma_start(
                out=io["cc1_in"][dc][:, 1:2], in_=sKVc[:, 2047:2048],
                allow_slow_non_contiguous=True), dma=True)
            tc.add("vector", lambda e, dc=dc: e.tensor_copy(
                out=pay[:, 2:3], in_=ln2T[:, dc * 2048 + 2047:dc * 2048 + 2048]))
            tc.add("vector", lambda e, dc=dc: e.tensor_copy(
                out=pay[:, 4:5], in_=GT[:, dc * 2048 + 2047:dc * 2048 + 2048]))
            tc.add("gpsimd", lambda e, dc=dc: e.dma_start(
                out=io["cc1_in"][dc][:, 3:4], in_=sQ[:, 2047:2048],
                allow_slow_non_contiguous=True), dma=True)
            tc.add("gpsimd", lambda e, dc=dc: e.dma_start(
                out=io["cc1_in"][dc][:, 2:3], in_=pay[:, 2:3],
                allow_slow_non_contiguous=True), dma=True)
            tc.add("gpsimd", lambda e, dc=dc: e.dma_start(
                out=io["cc1_in"][dc][:, 4:5], in_=pay[:, 4:5],
                allow_slow_non_contiguous=True), dma=True)
            tc.add("gpsimd", lambda e, dc=dc: e.collective_compute(
                "AllGather", ALU.bypass, replica_groups=RG,
                ins=[io["cc1_in"][dc].opt()], outs=[io["cc1_out"][dc].opt()]))
            tc.add("gpsimd", lambda e, dc=dc: e.dma_start(
                out=pay_g, in_=io["cc1_out"][dc].rearrange("(j p) f -> p j f",
                                                           p=128)), dma=True)
            for j in range(4):
                def acc1(e, j=j):
                    if j == 0:
                        e.tensor_scalar(out=carryK, in0=pay_g[:, 0, 0:1],
                                        scalar1=mask_lt[:, 0:1], scalar2=None,
                                        op0=ALU.mult)
                        e.tensor_scalar(out=carryKV, in0=pay_g[:, 0, 1:2],
                                        scalar1=mask_lt[:, 0:1], scalar2=None,
                                        op0=ALU.mult)
                        e.tensor_scalar(out=vprev, in0=pay_g[:, 0, 2:3],
                                        scalar1=mask_prev[:, 0:1], scalar2=None,
                                        op0=ALU.mult)
                        e.tensor_scalar(out=qprev, in0=pay_g[:, 0, 3:4],
                                        scalar1=mask_prev[:, 0:1], scalar2=None,
                                        op0=ALU.mult)
                        return e.tensor_scalar(out=gprev, in0=pay_g[:, 0, 4:5],
                                               scalar1=mask_prev[:, 0:1],
                                               scalar2=None, op0=ALU.mult)
                    e.scalar_tensor_tensor(out=carryK, in0=pay_g[:, j, 0:1],
                                           scalar=mask_lt[:, j:j + 1], in1=carryK,
                                           op0=ALU.mult, op1=ALU.add)
                    e.scalar_tensor_tensor(out=carryKV, in0=pay_g[:, j, 1:2],
                                           scalar=mask_lt[:, j:j + 1], in1=carryKV,
                                           op0=ALU.mult, op1=ALU.add)
                    e.scalar_tensor_tensor(out=vprev, in0=pay_g[:, j, 2:3],
                                           scalar=mask_prev[:, j:j + 1], in1=vprev,
                                           op0=ALU.mult, op1=ALU.add)
                    e.scalar_tensor_tensor(out=qprev, in0=pay_g[:, j, 3:4],
                                           scalar=mask_prev[:, j:j + 1], in1=qprev,
                                           op0=ALU.mult, op1=ALU.add)
                    return e.scalar_tensor_tensor(out=gprev, in0=pay_g[:, j, 4:5],
                                                  scalar=mask_prev[:, j:j + 1],
                                                  in1=gprev, op0=ALU.mult,
                                                  op1=ALU.add)
                tc.add("vector", acc1)
            tc.add("vector", lambda e: e.tensor_scalar(
                out=t_a, in0=carryK, scalar1=1e-6, scalar2=None, op0=ALU.add))
            tc.add("vector", lambda e: e.reciprocal(out=t_b, in_=t_a))
            tc.add("vector", lambda e: e.tensor_mul(out=t_a, in0=carryKV,
                                                    in1=t_b))
            tc.add("vector", lambda e: e.tensor_mul(out=oarprev, in0=t_a,
                                                    in1=qprev))
            tc.add("vector", lambda e: e.tensor_scalar(
                out=sKc, in0=sKc, scalar1=carryK, scalar2=1e-6, op0=ALU.add,
                op1=ALU.add))
            tc.add("vector", lambda e: e.tensor_scalar(
                out=sKVc, in0=sKVc, scalar1=carryKV, scalar2=None, op0=ALU.add))
            tc.add("vector", lambda e: e.reciprocal(out=sS, in_=sKc))
            tc.add("vector", lambda e: e.tensor_mul(out=sKV, in0=sKVc, in1=sS))
            tc.add("vector", lambda e: e.tensor_mul(out=sOar, in0=sKV, in1=sQ))
            tc.add("vector", lambda e, dc=dc: e.tensor_sub(
                out=sE[:, 0:1], in_=None, in0=ln2T[:, dc * 2048:dc * 2048 + 1],
                in1=oarprev) if False else e.tensor_sub(
                out=sE[:, 0:1], in0=ln2T[:, dc * 2048:dc * 2048 + 1],
                in1=oarprev))
            tc.add("vector", lambda e: e.tensor_scalar(
                out=sE[:, 0:1], in0=sE[:, 0:1], scalar1=first_m[:, 0:1],
                scalar2=None, op0=ALU.mult))
            tc.add("vector", lambda e, dc=dc: e.tensor_sub(
                out=sE[:, 1:2048], in0=ln2T[:, dc * 2048 + 1:dc * 2048 + 2048],
                in1=sOar[:, 0:2047]))
            tc.add("scalar", lambda e, dc=dc: e.activation(
                out=sS[:, 0:1], in_=gprev, func=AF.Tanh, scale=C2 / 2.0,
                bias=bk2pc[:, dc:dc + 1]))
            tc.add("scalar", lambda e, dc=dc: e.activation(
                out=sS[:, 1:2048], in_=GT[:, dc * 2048:dc * 2048 + 2047],
                func=AF.Tanh, scale=C2 / 2.0, bias=bk2pc[:, dc:dc + 1]))
            tc.add("vector", lambda e: e.tensor_scalar(
                out=sS, in0=sS, scalar1=0.5, scalar2=0.5, op0=ALU.mult,
                op1=ALU.add))
            tc.add("vector", lambda e: e.tensor_mul(out=sE, in0=sS, in1=sE))
            tc.add("vector", lambda e: e.tensor_tensor_scan(
                out=sK, data0=sE, data1=sE, initial=0.0, op0=ALU.add,
                op1=ALU.bypass))
            tc.add("gpsimd", lambda e, dc=dc: e.dma_start(
                out=io["cc2_in"][dc], in_=sK[:, 2047:2048],
                allow_slow_non_contiguous=True), dma=True)
            tc.add("gpsimd", lambda e, dc=dc: e.collective_compute(
                "AllGather", ALU.bypass, replica_groups=RG,
                ins=[io["cc2_in"][dc].opt()], outs=[io["cc2_out"][dc].opt()]))
            tc.add("gpsimd", lambda e, dc=dc: e.dma_start(
                out=pay2_g, in_=io["cc2_out"][dc].rearrange("(j p) f -> p (j f)",
                                                            p=128),
                allow_slow_non_contiguous=True), dma=True)
            for j in range(4):
                def acc3(e, j=j):
                    if j == 0:
                        return e.tensor_scalar(out=carry2, in0=pay2_g[:, 0:1],
                                               scalar1=mask_lt[:, 0:1],
                                               scalar2=None, op0=ALU.mult)
                    return e.scalar_tensor_tensor(out=carry2,
                                                  in0=pay2_g[:, j:j + 1],
                                                  scalar=mask_lt[:, j:j + 1],
                                                  in1=carry2, op0=ALU.mult,
                                                  op1=ALU.add)
                tc.add("vector", acc3)
            tc.add("vector", lambda e: e.tensor_scalar(
                out=sK, in0=sK, scalar1=carry2, scalar2=None, op0=ALU.add))
            tc.add("vector", lambda e: e.scalar_tensor_tensor(
                out=sS[:, 0:1], in0=sK[:, 0:1], scalar=C3, in1=qprev,
                op0=ALU.mult, op1=ALU.mult))
            tc.add("vector", lambda e: e.scalar_tensor_tensor(
                out=sS[:, 1:2048], in0=sK[:, 1:2048], scalar=C3,
                in1=sQ[:, 0:2047], op0=ALU.mult, op1=ALU.mult))
            tc.add("vector", lambda e, dc=dc: e.tensor_add(
                out=OT[:, dc * 2048:dc * 2048 + 2048], in0=sOar, in1=sS))
            tc.mark(f"done{dc}")
    _chain()
    TCK = tc.k

    def sync_ops(e):
        gate(e)
        e.dma_start(out=bq1T, in_=w["bq1"].rearrange("(c p) -> p c", p=128),
                    allow_slow_non_contiguous=True).then_inc(sem["cXR"], 16)
        e.wait_ge(sem["cXR"], 16)
        e.dma_start(out=bk1T, in_=w["bk1"].rearrange("(c p) -> p c", p=128),
                    allow_slow_non_contiguous=True).then_inc(sem["cXR"], 16)
        e.wait_ge(sem["cXR"], 32)
        e.dma_start(out=bk2T, in_=w["bk2"].rearrange("(c p) -> p c", p=128),
                    allow_slow_non_contiguous=True).then_inc(sem["cXR"], 16)
        e.wait_ge(sem["cXR"], 48)
        e.dma_start(out=mask_lt, in_=io["mask_lt"]).then_inc(sem["cXR"], 16)
        e.wait_ge(sem["cXR"], 64)
        e.dma_start(out=mask_prev, in_=io["mask_prev"]).then_inc(sem["cXR"], 16)
        e.wait_ge(sem["cXR"], 80)
        e.dma_start(out=first_m, in_=io["first_m"]).then_inc(sem["cXR"], 16)
        if not resident_in:
            xr = io["x_in"].rearrange("(t p) e -> t p e", p=128)
            for t in range(NT):
                e.wait_ge(sem["cXR"], 16 * (NWC + t))
                e.dma_start(out=R[:, 1024 * t:1024 * t + 1024], in_=xr[t]) \
                    .then_inc(sem["cXR"], 16)
        else:
            e.wait_ge(sem["cZ"], NT)
            e.wait_ge(sem["cS2"], NT)
            xo = io["x2_sp"].rearrange("(t p) e -> t p e", p=128)
            for t in range(NT):
                e.dma_start(out=xo[t], in_=R[:, 1024 * t:1024 * t + 1024]) \
                    .then_inc(sem["cX2"], 16)
            e.wait_ge(sem["cX2"], 16 * NT)
        if trunc == -1:
            return
        nmat = 4 if trunc is None else (1 if trunc == -4 else 3)
        for m in range(nmat):
            wm = w[WMATS[m]]
            for eo in range(8):
                n = m * 8 + eo
                if n == 24:
                    e.wait_ge(sem["cSS"], TCK)
                if n >= 1:
                    e.wait_ge(sem["cWL"], 16 * n)
                if n >= 2:
                    e.wait_ge(sem["cWC"], n - 1)
                dst = (wslab_post if n >= 24 else wslab_pre)[n % 2]
                e.dma_start(out=dst, in_=wm[128 * eo:128 * eo + 128, :]) \
                    .then_inc(sem["cWL"], 16)
        if trunc is not None:
            return
        e.wait_ge(sem["cWC"], 32)
        e.dma_start(out=bprow, in_=w["bproj"][None, :]).then_inc(sem["cXR"], 16)
        xsrc = io["x_in"] if not resident_in else io["x2_sp"]
        xr2 = xsrc.rearrange("(t p) e -> t p e", p=128)
        e.wait_ge(sem["cSS"], TCK)
        e.wait_ge(sem["cWC"], 32)
        e.wait_ge(sem["cBP"], 1)
        base = 16 * NT if resident_in else 0
        for t in range(NT):
            if t >= 1:
                e.wait_ge(sem["cX2"], base + 16 * t)
            if t >= 2:
                e.wait_ge(sem["cRES"], t - 1)
            e.dma_start(out=xa[:, 1024 * (t % 2):1024 * (t % 2) + 1024],
                        in_=xr2[t]).then_inc(sem["cX2"], 16)

    def vector_ops(e):
        gate(e)
        for t in range(NT):
            if not resident_in:
                e.wait_ge(sem["cXR"], 16 * (NWC + t + 1))
            else:
                e.wait_ge(sem["cXR"], 16 * NWC)
            Rt = R[:, 1024 * t:1024 * t + 1024]
            e.reduce_sum(out=stc(t, 0), in_=Rt, axis=AX)
            e.wait_ge(sem["cS2"], t + 1)
            e.tensor_add(out=stc(t, 2), in0=stc(t, 2), in1=stc(t, 9)) \
                .then_inc(sem["cDV"], 1)
            e.wait_ge(sem["cDV"], 2 * t + 1)
            e.tensor_scalar(out=stc(t, 1), in0=stc(t, 0), scalar1=1.0 / D,
                            scalar2=None, op0=ALU.mult)
            e.tensor_scalar(out=stc(t, 3), in0=stc(t, 2), scalar1=1.0 / D,
                            scalar2=None, op0=ALU.mult)
            e.scalar_tensor_tensor(out=stc(t, 4), in0=stc(t, 0),
                                   scalar=-1.0 / (D * D), in1=stc(t, 0),
                                   op0=ALU.mult, op1=ALU.mult) \
                .then_inc(sem["cDV"], 1)
            e.wait_ge(sem["cDV"], 2 * t + 2)
            e.tensor_add(out=stc(t, 5), in0=stc(t, 3), in1=stc(t, 4)) \
                .then_inc(sem["cVR"], 1)
            e.wait_ge(sem["cRS"], t + 1)
            e.tensor_scalar(out=lnbf[:, 1024 * (t % 2):1024 * (t % 2) + 1024],
                            in0=Rt, scalar1=stc(t, 1), scalar2=stc(t, 7),
                            op0=ALU.subtract, op1=ALU.mult) \
                .then_inc(sem["cZ"], 1)
        ncast_pre = {None: 24, -1: 0, -3: 32, -4: 8}.get(trunc, 24)
        for n in range(ncast_pre):
            if n == 0:
                e.wait_ge(sem["cLT"], NT)
            e.wait_ge(sem["cWL"], 16 * (n + 1))
            if n >= 2 and trunc != -3:
                e.wait_ge(sem["cWT"], 8 * (n - 1))
            e.tensor_copy(out=wcast[n % 2],
                          in_=wslab_pre[n % 2]).then_inc(sem["cWC"], 1)
        e.wait_ge(sem["cXR"], 16 * NWC)
        if trunc is None or trunc >= 0:
            e.wait_ge(sem["cGE"], 32)
        if resident_in:
            e.wait_ge(sem["cX2"], 16 * NT)
        tc.emit("vector", e, limit=max(trunc, 0) if trunc is not None else None)
        if trunc is not None:
            return
        for n in range(24, 32):
            e.wait_ge(sem["cWL"], 16 * (n + 1))
            e.wait_ge(sem["cWT"], 8 * (n - 1))
            e.tensor_copy(out=wcast[n % 2],
                          in_=wslab_post[n % 2]).then_inc(sem["cWC"], 1)
        nxr = NWC + (0 if resident_in else NT)
        e.wait_ge(sem["cXR"], 16 * (nxr + 1))
        e.wait_ge(sem["cWT"], 256)
        e.tensor_copy(out=bprow_bf, in_=bprow)
        e.memset(ones1_bf, 1.0).then_inc(sem["cBP"], 1)
        for t in range(NT):
            Rt = R[:, 1024 * t:1024 * t + 1024]
            e.wait_ge(sem["cYE"], 2 * t + 2)
            e.reduce_sum(out=stc(t, 0), in_=Rt, axis=AX)
            e.wait_ge(sem["cS2"], NT + t + 1)
            e.tensor_add(out=stc(t, 2), in0=stc(t, 2), in1=stc(t, 9)) \
                .then_inc(sem["cDV"], 1)
            e.wait_ge(sem["cDV"], 2 * NT + 3 * t + 1)
            e.tensor_scalar(out=stc(t, 1), in0=stc(t, 0), scalar1=1.0 / D,
                            scalar2=None, op0=ALU.mult)
            e.tensor_scalar(out=stc(t, 3), in0=stc(t, 2), scalar1=1.0 / D,
                            scalar2=None, op0=ALU.mult)
            e.scalar_tensor_tensor(out=stc(t, 4), in0=stc(t, 0),
                                   scalar=-1.0 / (D * D), in1=stc(t, 0),
                                   op0=ALU.mult, op1=ALU.mult) \
                .then_inc(sem["cDV"], 1)
            e.wait_ge(sem["cDV"], 2 * NT + 3 * t + 2)
            e.tensor_add(out=stc(t, 5), in0=stc(t, 3), in1=stc(t, 4)) \
                .then_inc(sem["cVR"], 1)
            e.wait_ge(sem["cRS"], NT + t + 1)
            e.tensor_scalar(out=Rt, in0=Rt, scalar1=stc(t, 1),
                            scalar2=stc(t, 7), op0=ALU.subtract, op1=ALU.mult) \
                .then_inc(sem["cDV"], 1)
            e.wait_ge(sem["cDV"], 2 * NT + 3 * t + 3)
            base = 16 * NT if resident_in else 0
            e.wait_ge(sem["cX2"], base + 16 * (t + 1))
            e.tensor_add(out=Rt, in0=Rt,
                         in1=xa[:, 1024 * (t % 2):1024 * (t % 2) + 1024]) \
                .then_inc(sem["cRES"], 1)

    def scalar_ops(e):
        gate(e)
        for t in range(NT):
            if not resident_in:
                e.wait_ge(sem["cXR"], 16 * (NWC + t + 1))
            else:
                e.wait_ge(sem["cXR"], 16 * NWC)
            Rt = R[:, 1024 * t:1024 * t + 1024]
            e.activation(out=sqscr[:], in_=Rt[:, 0:512], func=AF.Square,
                         accum_out=stc(t, 2))
            e.activation(out=sqscr[:], in_=Rt[:, 512:1024], func=AF.Square,
                         accum_out=stc(t, 9)).then_inc(sem["cS2"], 1)
            e.wait_ge(sem["cVR"], t + 1)
            e.activation(out=stc(t, 6), in_=stc(t, 5), func=AF.Ln,
                         bias=epsc[:]).then_inc(sem["cLX"], 1)
            e.wait_ge(sem["cLX"], t + 1)
            e.activation(out=stc(t, 7), in_=stc(t, 6), func=AF.Exp,
                         scale=-0.5).then_inc(sem["cRS"], 1)
            e.wait_ge(sem["cTR"], t + 1)
            dst = ln2T.rearrange("p (c s) -> p c s", c=8)[:, :, 128 * t:128 * t + 128]
            e.activation(out=dst, in_=psbf.rearrange("p (c s) -> p c s", c=8),
                         func=AF.Copy).then_inc(sem["cLT"], 1)
        for n in range(0 if trunc in (-1, -3) else 64):
            e.wait_ge(sem["cWT"], n + 1)
            eo, do = n // 8, n % 8
            e.activation(out=WQ[:, do * 1024 + eo * 128:do * 1024 + eo * 128 + 128],
                         in_=psbf[:, 0:128], func=AF.Copy).then_inc(sem["cWE"], 1)
        for k in range(0 if (trunc is not None and trunc < 0) else 32):
            e.wait_ge(sem["cGM"], k + 1)
            dc, s4 = k // 4, k % 4
            e.activation(out=GT[:, dc * 2048 + 512 * s4:dc * 2048 + 512 * s4 + 512],
                         in_=ps[k % 2][:], func=AF.Copy).then_inc(sem["cGE"], 1)
        for m in ([] if trunc in (-1, -3, -4) else range(1, 3)):
            for i in range(64):
                n = m * 64 + i
                eo, do = i // 8, i % 8
                if m == 1 and i == 0:
                    e.wait_ge(sem["cGM"], 32)
                e.wait_ge(sem["cWT"], n + 1)
                e.activation(out=WDST[m][:, do * 1024 + eo * 128:do * 1024 + eo * 128 + 128],
                             in_=psbf[:, 0:128], func=AF.Copy).then_inc(sem["cWE"], 1)
        if resident_in:
            e.wait_ge(sem["cX2"], 16 * NT)
        tc.emit("scalar", e, limit=max(trunc, 0) if trunc is not None else None)
        if trunc is not None:
            return
        for i in range(64):
            n = 192 + i
            eo, do = i // 8, i % 8
            e.wait_ge(sem["cWT"], n + 1)
            e.activation(out=WQ[:, do * 1024 + eo * 128:do * 1024 + eo * 128 + 128],
                         in_=psbf[:, 0:128], func=AF.Copy).then_inc(sem["cWE"], 1)
        for t in range(NT):
            for hh in range(2):
                n = 2 * t + hh
                e.wait_ge(sem["cYM"], n + 1)
                e.activation(out=R[:, 1024 * t + 512 * hh:1024 * t + 512 * hh + 512],
                             in_=ps[2 + n % 2][:], func=AF.Copy) \
                    .then_inc(sem["cYE"], 1)
            e.wait_ge(sem["cYE"], 2 * t + 2)
            Rt = R[:, 1024 * t:1024 * t + 1024]
            e.activation(out=sqscr[:], in_=Rt[:, 0:512], func=AF.Square,
                         accum_out=stc(t, 2))
            e.activation(out=sqscr[:], in_=Rt[:, 512:1024], func=AF.Square,
                         accum_out=stc(t, 9)).then_inc(sem["cS2"], 1)
            e.wait_ge(sem["cVR"], NT + t + 1)
            e.activation(out=stc(t, 6), in_=stc(t, 5), func=AF.Ln,
                         bias=epsc[:]).then_inc(sem["cLX"], 1)
            e.wait_ge(sem["cLX"], NT + t + 1)
            e.activation(out=stc(t, 7), in_=stc(t, 6), func=AF.Exp,
                         scale=-0.5).then_inc(sem["cRS"], 1)

    def tensor_ops(e):
        gate(e)
        for t in range(NT):
            e.wait_ge(sem["cZ"], t + 1)
            if t >= 1:
                e.wait_ge(sem["cLT"], t)
            srcz = lnbf[:, 1024 * (t % 2):1024 * (t % 2) + 1024]
            last = None
            for dc in range(8):
                last = e.transpose(psbf[:, 128 * dc:128 * dc + 128],
                                   srcz[:, 128 * dc:128 * dc + 128], eye[:])
            last.then_inc(sem["cTR"], 1)
        e.wait_ge(sem["cLT"], NT)
        for n in range(0 if trunc in (-1, -3) else 64):
            slab, do = n // 8, n % 8
            e.wait_ge(sem["cWC"], slab + 1)
            if n >= 1:
                e.wait_ge(sem["cWE"], n)
            e.transpose(psbf[:, 0:128],
                        wcast[slab % 2][:, 128 * do:128 * do + 128],
                        eye[:]).then_inc(sem["cWT"], 1)
        if trunc is None or trunc >= 0:
            e.wait_ge(sem["cWE"], 64)
        for k in range(0 if (trunc is not None and trunc < 0) else 32):
            dc, s4 = k // 4, k % 4
            if k >= 2:
                e.wait_ge(sem["cGE"], k - 1)
            last = None
            for di in range(8):
                last = e.matmul(
                    ps[k % 2][:],
                    WQ[:, di * 1024 + dc * 128:di * 1024 + dc * 128 + 128],
                    ln2T[:, di * 2048 + 512 * s4:di * 2048 + 512 * s4 + 512],
                    start=(di == 0), stop=(di == 7))
            last.then_inc(sem["cGM"], 1)
        for n in ([] if trunc in (-1, -3, -4) else range(64, 192)):
            slab, do = n // 8, n % 8
            e.wait_ge(sem["cWC"], slab + 1)
            e.wait_ge(sem["cWE"], n)
            e.transpose(psbf[:, 0:128],
                        wcast[slab % 2][:, 128 * do:128 * do + 128],
                        eye[:]).then_inc(sem["cWT"], 1)
        if trunc is None or trunc > 0:
            e.wait_ge(sem["cWE"], 192)
        tc.emit("tensor", e, limit=max(trunc, 0) if trunc is not None else None)
        if trunc is not None:
            return
        for n in range(192, 256):
            slab, do = n // 8, n % 8
            e.wait_ge(sem["cWC"], slab + 1)
            e.wait_ge(sem["cWE"], n)
            e.transpose(psbf[:, 0:128],
                        wcast[slab % 2][:, 128 * do:128 * do + 128],
                        eye[:]).then_inc(sem["cWT"], 1)
        e.wait_ge(sem["cSS"], TCK)
        e.wait_ge(sem["cWE"], 256)
        e.wait_ge(sem["cBP"], 1)
        for t in range(NT):
            for hh in range(2):
                n = 2 * t + hh
                if n >= 2:
                    e.wait_ge(sem["cYE"], n - 1)
                pst = ps[2 + n % 2]
                last = None
                for di in range(8):
                    last = e.matmul(
                        pst[:], OT[:, di * 2048 + 128 * t:di * 2048 + 128 * t + 128],
                        WQ[:, di * 1024 + 512 * hh:di * 1024 + 512 * hh + 512],
                        start=(di == 0), stop=False)
                e.matmul(pst[:], ones1_bf,
                         bprow_bf[:, 512 * hh:512 * hh + 512],
                         start=False, stop=True).then_inc(sem["cYM"], 1)

    def gpsimd_ops(e):
        gate(e)
        tc.emit("gpsimd", e, limit=max(trunc, 0) if trunc is not None else None)
        if trunc is not None:
            kk, mm, lcc = tc.counts_at(max(trunc, 0))
            if kk > 0:
                e.wait_ge(sem["cSS"], kk)
            if mm > 0:
                e.wait_ge(sem["cDM"], 16 * mm)
            if lcc is not None:
                e.wait_ge(lcc, 1)
            e.dma_start(out=io["r_d"], in_=R[:]).then_inc(sem["cOUT"], 16)
            e.dma_start(out=io["ot_d"], in_=OT[:]).then_inc(sem["cOUT"], 16)
            e.dma_start(out=io["xa_d"], in_=xa[:]).then_inc(sem["cOUT"], 16)
            for dcq in range(8):
                e.dma_start(out=io["cc_d"][:, 5 * dcq:5 * dcq + 5],
                            in_=io["cc1_out"][dcq]).then_inc(sem["cOUT"], 16)
                e.dma_start(out=io["cc_d"][:, 40 + dcq:40 + dcq + 1],
                            in_=io["cc2_out"][dcq],
                            allow_slow_non_contiguous=True).then_inc(sem["cOUT"], 16)
            e.wait_ge(sem["cOUT"], 48 + 256)
            return
        if store_out:
            e.wait_ge(sem["cRES"], NT)
            xo = io["x_out"].rearrange("(q p) e -> q p e", p=128)
            for t in range(NT):
                e.dma_start(out=xo[t], in_=R[:, 1024 * t:1024 * t + 1024]) \
                    .then_inc(sem["cOUT"], 16)
            e.wait_ge(sem["cOUT"], 16 * NT)

    return dict(sync=sync_ops, vector=vector_ops, scalar=scalar_ops,
                tensor=tensor_ops, gpsimd=gpsimd_ops, TCK=TCK, tc=tc,
                marks=tc.marks)


D_SEMS = ["dXR", "dWL", "dWC", "dS2", "dVR", "dRS", "dLX", "dDV", "dZ",
          "dTR", "dLT", "dPM", "dPEv", "dHM",
          "dHE", "dRM", "dRE", "dYM", "dYE", "dRES", "dOUT", "dONE"]

GELU_C = float(np.sqrt(2.0 / np.pi))


def build_stage_d(sem, bufs, io, store_out, resident_in, gate_in=None):
    R = bufs.R_f32
    ln3T = bufs.A_bf
    HT = bufs.B_bf
    u2s = bufs.C_bf
    v1s = bufs.W_bf
    pTb = bufs.pT
    xa = bufs.xa
    lnbf = bufs.lnbf
    sqscr = bufs.sqscr
    st = bufs.stats
    epsc = bufs.epsc
    eye = bufs.eye
    ps = bufs.ps
    psbf = bufs.psbf

    u1s = pTb[:, 0:2048]
    v2s = pTb[:, 2048:4096]
    b2row = pTb[0:1, 4096:5120]
    PTs = [pTb[:, 5120:6144], pTb[:, 6144:7168]]
    RTs = pTb[:, 7168:8192]
    ones1_bf = lnbf[0:1, 0:128]
    b1T = bufs.dn_sb[:, 0:32]

    w = io["w"]

    def stc(t, i):
        o = 16 * (t % 2) + i
        return st[:, o:o + 1]

    def gate(e):
        if gate_in is not None:
            e.wait_ge(gate_in[0], gate_in[1])

    def sync_ops(e):
        gate(e)
        e.dma_start(out=b1T, in_=w["b1"].rearrange("(c p) -> p c", p=128),
                    allow_slow_non_contiguous=True).then_inc(sem["dXR"], 16)
        if not resident_in:
            xr = io["x_in"].rearrange("(t p) e -> t p e", p=128)
            for t in range(NT):
                e.wait_ge(sem["dXR"], 16 * (t + 1))
                e.dma_start(out=R[:, 1024 * t:1024 * t + 1024], in_=xr[t]) \
                    .then_inc(sem["dXR"], 16)
        loads = []
        loads.append(w["U1"].rearrange("(c p) r -> p c r", p=128))
        for rc in range(2):
            for hh in range(2):
                loads.append(w["V1"][128 * rc:128 * rc + 128,
                                     2048 * hh:2048 * hh + 2048])
        for c0 in range(4):
            loads.append(w["U2"][1024 * c0:1024 * (c0 + 1)]
                         .rearrange("(c p) r -> p c r", p=128))
        loads.append(w["V2"].rearrange("(c p) e -> p c e", p=128))
        loads.append(w["b2"][None, :])
        for k, srcap in enumerate(loads):
            if k >= 1:
                e.wait_ge(sem["dWC"], k)
            if k == 10:
                dst = xa[0:1, 0:1024]
            elif len(srcap.shape) == 3:
                dst = xa[:, 0:srcap.shape[1] * srcap.shape[2]] \
                    .rearrange("p (c r) -> p c r", c=srcap.shape[1])
            else:
                dst = xa[:, 0:srcap.shape[1]]
            e.dma_start(out=dst, in_=srcap).then_inc(sem["dWL"], 16)

    def vector_ops(e):
        gate(e)
        for t in range(NT):
            if not resident_in:
                e.wait_ge(sem["dXR"], 16 * (t + 2))
            else:
                e.wait_ge(sem["dXR"], 16)
            Rt = R[:, 1024 * t:1024 * t + 1024]
            e.reduce_sum(out=stc(t, 0), in_=Rt, axis=AX)
            e.wait_ge(sem["dS2"], t + 1)
            e.tensor_add(out=stc(t, 2), in0=stc(t, 2), in1=stc(t, 9)) \
                .then_inc(sem["dDV"], 1)
            e.wait_ge(sem["dDV"], 2 * t + 1)
            e.tensor_scalar(out=stc(t, 1), in0=stc(t, 0), scalar1=1.0 / D,
                            scalar2=None, op0=ALU.mult)
            e.tensor_scalar(out=stc(t, 3), in0=stc(t, 2), scalar1=1.0 / D,
                            scalar2=None, op0=ALU.mult)
            e.scalar_tensor_tensor(out=stc(t, 4), in0=stc(t, 0),
                                   scalar=-1.0 / (D * D), in1=stc(t, 0),
                                   op0=ALU.mult, op1=ALU.mult) \
                .then_inc(sem["dDV"], 1)
            e.wait_ge(sem["dDV"], 2 * t + 2)
            e.tensor_add(out=stc(t, 5), in0=stc(t, 3), in1=stc(t, 4)) \
                .then_inc(sem["dVR"], 1)
            e.wait_ge(sem["dRS"], t + 1)
            e.tensor_scalar(out=lnbf[:, 1024 * (t % 2):1024 * (t % 2) + 1024],
                            in0=Rt, scalar1=stc(t, 1), scalar2=stc(t, 7),
                            op0=ALU.subtract, op1=ALU.mult) \
                .then_inc(sem["dZ"], 1)
        cdsts = [u1s]
        for rc in range(2):
            for hh in range(2):
                cdsts.append(v1s[:, 4096 * rc + 2048 * hh:4096 * rc + 2048 * hh + 2048])
        for c0 in range(4):
            cdsts.append(u2s[:, 2048 * c0:2048 * (c0 + 1)])
        cdsts.append(v2s)
        cdsts.append(b2row)
        for k, dst in enumerate(cdsts):
            e.wait_ge(sem["dWL"], 16 * (k + 1))
            nsrc = 1024 if k == 10 else 2048
            srcv = xa[0:1, 0:1024] if k == 10 else xa[:, 0:2048]
            e.tensor_copy(out=dst, in_=srcv).then_inc(sem["dWC"], 1)
        e.wait_ge(sem["dTR"], NT)
        e.memset(ones1_bf, 1.0).then_inc(sem["dONE"], 1)
        for s4 in range(4):
            for sc in range(4):
                t = s4 * 4 + sc
                for dh in range(2):
                    n = 2 * t + dh
                    e.wait_ge(sem["dYM"], n + 1)
                    ya = xa[:, 1024 * (sc % 2) + 512 * dh:
                            1024 * (sc % 2) + 512 * dh + 512]
                    e.tensor_copy(out=ya, in_=ps[n % 2][:]).then_inc(sem["dYE"], 1)
                yt = xa[:, 1024 * (sc % 2):1024 * (sc % 2) + 1024]
                e.wait_ge(sem["dYE"], 2 * t + 2)
                e.reduce_sum(out=stc(t, 0), in_=yt, axis=AX)
                e.wait_ge(sem["dS2"], NT + t + 1)
                e.tensor_add(out=stc(t, 2), in0=stc(t, 2), in1=stc(t, 9)) \
                    .then_inc(sem["dDV"], 1)
                e.wait_ge(sem["dDV"], 2 * NT + 3 * t + 1)
                e.tensor_scalar(out=stc(t, 1), in0=stc(t, 0), scalar1=1.0 / D,
                                scalar2=None, op0=ALU.mult)
                e.tensor_scalar(out=stc(t, 3), in0=stc(t, 2), scalar1=1.0 / D,
                                scalar2=None, op0=ALU.mult)
                e.scalar_tensor_tensor(out=stc(t, 4), in0=stc(t, 0),
                                       scalar=-1.0 / (D * D), in1=stc(t, 0),
                                       op0=ALU.mult, op1=ALU.mult) \
                    .then_inc(sem["dDV"], 1)
                e.wait_ge(sem["dDV"], 2 * NT + 3 * t + 2)
                e.tensor_add(out=stc(t, 5), in0=stc(t, 3), in1=stc(t, 4)) \
                    .then_inc(sem["dVR"], 1)
                e.wait_ge(sem["dRS"], NT + t + 1)
                e.tensor_scalar(out=yt, in0=yt, scalar1=stc(t, 1),
                                scalar2=stc(t, 7), op0=ALU.subtract,
                                op1=ALU.mult).then_inc(sem["dDV"], 1)
                e.wait_ge(sem["dDV"], 2 * NT + 3 * t + 3)
                Rt = R[:, 1024 * t:1024 * t + 1024]
                e.tensor_add(out=Rt, in0=Rt, in1=yt).then_inc(sem["dRES"], 1)

    def scalar_ops(e):
        gate(e)
        for t in range(NT):
            if not resident_in:
                e.wait_ge(sem["dXR"], 16 * (t + 2))
            else:
                e.wait_ge(sem["dXR"], 16)
            Rt = R[:, 1024 * t:1024 * t + 1024]
            e.activation(out=sqscr[:], in_=Rt[:, 0:512], func=AF.Square,
                         accum_out=stc(t, 2))
            e.activation(out=sqscr[:], in_=Rt[:, 512:1024], func=AF.Square,
                         accum_out=stc(t, 9)).then_inc(sem["dS2"], 1)
            e.wait_ge(sem["dVR"], t + 1)
            e.activation(out=stc(t, 6), in_=stc(t, 5), func=AF.Ln,
                         bias=epsc[:]).then_inc(sem["dLX"], 1)
            e.wait_ge(sem["dLX"], t + 1)
            e.activation(out=stc(t, 7), in_=stc(t, 6), func=AF.Exp,
                         scale=-0.5).then_inc(sem["dRS"], 1)
            e.wait_ge(sem["dTR"], t + 1)
            dst = ln3T.rearrange("p (c s) -> p c s", c=8)[:, :, 128 * t:128 * t + 128]
            e.activation(out=dst, in_=psbf.rearrange("p (c s) -> p c s", c=8),
                         func=AF.Copy).then_inc(sem["dLT"], 1)
        for s4 in range(4):
            for rc in range(2):
                n = 2 * s4 + rc
                e.wait_ge(sem["dPM"], n + 1)
                e.activation(out=PTs[s4 % 2][:, 512 * rc:512 * rc + 512],
                             in_=ps[rc][:], func=AF.Copy).then_inc(sem["dPEv"], 1)
            for fc in range(32):
                k = s4 * 32 + fc
                e.wait_ge(sem["dHM"], k + 1)
                e.activation(out=HT[:, 512 * fc:512 * fc + 512],
                             in_=ps[fc % 2][:], func=AF.Gelu_apprx_tanh,
                             bias=b1T[:, fc:fc + 1]).then_inc(sem["dHE"], 1)
            for rc in range(2):
                n = 2 * s4 + rc
                e.wait_ge(sem["dRM"], n + 1)
                e.activation(out=RTs[:, 512 * rc:512 * rc + 512],
                             in_=ps[2 + rc][:], func=AF.Copy) \
                    .then_inc(sem["dRE"], 1)
            for sc in range(4):
                t = s4 * 4 + sc
                yt = xa[:, 1024 * (sc % 2):1024 * (sc % 2) + 1024]
                e.wait_ge(sem["dYE"], 2 * t + 2)
                e.activation(out=sqscr[:], in_=yt[:, 0:512], func=AF.Square,
                             accum_out=stc(t, 2))
                e.activation(out=sqscr[:], in_=yt[:, 512:1024], func=AF.Square,
                             accum_out=stc(t, 9)).then_inc(sem["dS2"], 1)
                e.wait_ge(sem["dVR"], NT + t + 1)
                e.activation(out=stc(t, 6), in_=stc(t, 5), func=AF.Ln,
                             bias=epsc[:]).then_inc(sem["dLX"], 1)
                e.wait_ge(sem["dLX"], NT + t + 1)
                e.activation(out=stc(t, 7), in_=stc(t, 6), func=AF.Exp,
                             scale=-0.5).then_inc(sem["dRS"], 1)

    def tensor_ops(e):
        gate(e)
        for t in range(NT):
            e.wait_ge(sem["dZ"], t + 1)
            if t >= 1:
                e.wait_ge(sem["dLT"], t)
            srcz = lnbf[:, 1024 * (t % 2):1024 * (t % 2) + 1024]
            last = None
            for dc in range(8):
                last = e.transpose(psbf[:, 128 * dc:128 * dc + 128],
                                   srcz[:, 128 * dc:128 * dc + 128], eye[:])
            last.then_inc(sem["dTR"], 1)
        e.wait_ge(sem["dLT"], NT)
        e.wait_ge(sem["dONE"], 1)
        for s4 in range(4):
            e.wait_ge(sem["dWC"], 1)
            if s4 >= 1:
                e.wait_ge(sem["dYE"], 8 * s4)
                e.wait_ge(sem["dPEv"], 2 * s4)
            for rc in range(2):
                last = None
                for dcc in range(8):
                    last = e.matmul(
                        ps[rc][:],
                        u1s[:, dcc * 256 + rc * 128:dcc * 256 + rc * 128 + 128],
                        ln3T[:, dcc * 2048 + 512 * s4:dcc * 2048 + 512 * s4 + 512],
                        start=(dcc == 0), stop=(dcc == 7))
                last.then_inc(sem["dPM"], 1)
            e.wait_ge(sem["dWC"], 5)
            e.wait_ge(sem["dPEv"], 2 * (s4 + 1))
            for fc in range(32):
                k = s4 * 32 + fc
                if k >= 2:
                    e.wait_ge(sem["dHE"], k - 1)
                last = None
                for rc in range(2):
                    last = e.matmul(
                        ps[fc % 2][:],
                        v1s[:, rc * 4096 + fc * 128:rc * 4096 + fc * 128 + 128],
                        PTs[s4 % 2][:, rc * 512:rc * 512 + 512],
                        start=(rc == 0), stop=(rc == 1))
                last.then_inc(sem["dHM"], 1)
            e.wait_ge(sem["dWC"], 9)
            e.wait_ge(sem["dHE"], 32 * (s4 + 1))
            if s4 >= 1:
                e.wait_ge(sem["dRE"], 2 * s4)
            for rc in range(2):
                last = None
                for fc in range(32):
                    last = e.matmul(
                        ps[2 + rc][:],
                        u2s[:, fc * 256 + rc * 128:fc * 256 + rc * 128 + 128],
                        HT[:, fc * 512:fc * 512 + 512],
                        start=(fc == 0), stop=(fc == 31))
                last.then_inc(sem["dRM"], 1)
            e.wait_ge(sem["dWC"], 11)
            e.wait_ge(sem["dRE"], 2 * (s4 + 1))
            for sc in range(4):
                t = s4 * 4 + sc
                for dh in range(2):
                    n = 2 * t + dh
                    if n >= 2:
                        e.wait_ge(sem["dYE"], n - 1)
                    pst = ps[n % 2]
                    for rc in range(2):
                        e.matmul(pst[:],
                                 RTs[:, rc * 512 + sc * 128:rc * 512 + sc * 128 + 128],
                                 v2s[:, rc * 1024 + dh * 512:rc * 1024 + dh * 512 + 512],
                                 start=(rc == 0), stop=False)
                    e.matmul(pst[:], ones1_bf, b2row[:, 512 * dh:512 * dh + 512],
                             start=False, stop=True).then_inc(sem["dYM"], 1)

    def gpsimd_ops(e):
        gate(e)
        if store_out:
            xo = io["y_out"].rearrange("(q p) e -> q p e", p=128)
            for t in range(NT):
                e.wait_ge(sem["dRES"], t + 1)
                e.dma_start(out=xo[t], in_=R[:, 1024 * t:1024 * t + 1024]) \
                    .then_inc(sem["dOUT"], 16)
            e.wait_ge(sem["dOUT"], 16 * NT)

    return dict(sync=sync_ops, vector=vector_ops, scalar=scalar_ops,
                tensor=tensor_ops, gpsimd=gpsimd_ops)


FFN1_KEYS = ["U1", "V1", "b1", "U2", "V2", "b2"]


def build_launch_d():
    key = ("d",)
    if key in _NC_CACHE:
        return _NC_CACHE[key]
    nc = bass.Bass("TRN2", target_bir_lowering=False, debug=False)
    io = {}
    io["x_in"] = nc.dram_tensor("x_in", [SC, D], F32, kind="ExternalInput").ap()
    w = {}
    shapes = {"U1": [D, 256], "V1": [256, 4096], "b1": [4096],
              "U2": [4096, 256], "V2": [256, D], "b2": [D]}
    for k in FFN1_KEYS:
        w[k] = nc.dram_tensor("w_" + k, shapes[k], F32, kind="ExternalInput").ap()
    io["w"] = w
    eye_in = nc.dram_tensor("eye_in", [128, 128], BF16, kind="ExternalInput").ap()
    io["y_out"] = nc.dram_tensor("y_out", [SC, D], F32, kind="ExternalOutput").ap()
    ctx = ExitStack()
    with ctx:
        bufs = alloc_common(nc, ctx)
        sem = make_sems(nc, ctx, D_SEMS + ["dEYE", "dONE2"])
        ops = build_stage_d(sem, bufs, io, store_out=True, resident_in=False)
        block = ctx.enter_context(nc.Block())

        @block.sync
        def _(e):
            e.dma_start(out=bufs.eye, in_=eye_in).then_inc(sem["dEYE"], 16)
            ops["sync"](e)

        @block.vector
        def _(e):
            e.memset(bufs.epsc[:], float(EPS)).then_inc(sem["dONE2"], 1)
            ops["vector"](e)

        @block.scalar
        def _(e):
            e.wait_ge(sem["dONE2"], 1)
            ops["scalar"](e)

        @block.tensor
        def _(e):
            e.wait_ge(sem["dEYE"], 16)
            ops["tensor"](e)

        @block.gpsimd
        def _(e):
            ops["gpsimd"](e)
    _NC_CACHE[key] = nc
    return nc


def run_stage_d(x3_full, weights):
    nc = build_launch_d()
    eye = np.eye(128, dtype=np.float16)
    in_maps = []
    for c in range(NCORES):
        b, r = c // GROUP, c % GROUP
        m = {"x_in": np.ascontiguousarray(x3_full[b, r * SC:(r + 1) * SC]),
             "eye_in": eye}
        for k in FFN1_KEYS:
            m["w_" + k] = np.ascontiguousarray(weights[k])
        in_maps.append(m)
    res = run_bass_kernel_spmd(nc, in_maps, core_ids=list(range(NCORES)))
    out = np.zeros((B, S, D), np.float32)
    for c in range(NCORES):
        b, r = c // GROUP, c % GROUP
        out[b, r * SC:(r + 1) * SC] = res.results[c]["y_out"]
    return out


FFN0_KEYS = ["ln_g", "ln_b", "bn_g", "bn_b", "bn_mean", "bn_var", "Wr", "br",
             "W_down", "b_down", "W_up", "b_up"]


def build_launch_b():
    key = ("b",)
    if key in _NC_CACHE:
        return _NC_CACHE[key]
    nc = bass.Bass("TRN2", target_bir_lowering=False, debug=False)
    io = {}
    io["x_in"] = nc.dram_tensor("x_in", [SC, D], F32, kind="ExternalInput").ap()
    w = {}
    shapes = {"ln_g": [D], "ln_b": [D], "bn_g": [D], "bn_b": [D],
              "bn_mean": [D], "bn_var": [D], "Wr": [D, 8], "br": [8],
              "W_down": [8, D, 64], "b_down": [8, 64], "W_up": [8, 64, D],
              "b_up": [8, D]}
    for k in FFN0_KEYS:
        w[k] = nc.dram_tensor("w_" + k, shapes[k], F32, kind="ExternalInput").ap()
    io["w"] = w
    eye_in = nc.dram_tensor("eye_in", [128, 128], BF16, kind="ExternalInput").ap()
    io["zs_in"] = nc.dram_tensor("zs_in", [1, D], F32, kind="Internal").ap()
    io["zs_out"] = nc.dram_tensor("zs_out", [1, D], F32, kind="Internal").ap()
    io["mrow_d"] = nc.dram_tensor("mrow_d", [1, 8], F32, kind="Internal").ap()
    io["x_out"] = nc.dram_tensor("x_out", [SC, D], F32, kind="ExternalOutput").ap()
    io["af_d"] = nc.dram_tensor("af_d", [128, 16384], BF16, kind="ExternalOutput").ap()
    io["cf_d"] = nc.dram_tensor("cf_d", [128, 16384], BF16, kind="ExternalOutput").ap()
    io["pt_d"] = nc.dram_tensor("pt_d", [128, 8192], BF16, kind="ExternalOutput").ap()
    io["zt_d"] = nc.dram_tensor("zt_d", [128, 16384], BF16, kind="ExternalOutput").ap()
    ctx = ExitStack()
    with ctx:
        bufs = alloc_common(nc, ctx)
        sem = make_sems(nc, ctx, B_SEMS + ["bEYE", "bONE"])
        ops = build_stage_b(sem, bufs, io, store_out=True, resident_in=False)
        block = ctx.enter_context(nc.Block())

        @block.sync
        def _(e):
            e.dma_start(out=bufs.eye, in_=eye_in).then_inc(sem["bEYE"], 16)
            ops["sync"](e)

        @block.vector
        def _(e):
            e.memset(bufs.ones4[:], 1.0)
            e.memset(bufs.epsc[:], float(EPS)).then_inc(sem["bONE"], 1)
            ops["vector"](e)

        @block.scalar
        def _(e):
            e.wait_ge(sem["bONE"], 1)
            ops["scalar"](e)

        @block.tensor
        def _(e):
            e.wait_ge(sem["bEYE"], 16)
            e.wait_ge(sem["bONE"], 1)
            ops["tensor"](e)

        @block.gpsimd
        def _(e):
            ops["gpsimd"](e)
    _NC_CACHE[key] = nc
    return nc


def run_stage_b(x1_full, weights):
    nc = build_launch_b()
    eye = np.eye(128, dtype=np.float16)
    in_maps = []
    for c in range(NCORES):
        b, r = c // GROUP, c % GROUP
        m = {"x_in": np.ascontiguousarray(x1_full[b, r * SC:(r + 1) * SC]),
             "eye_in": eye}
        for k in FFN0_KEYS:
            m["w_" + k] = np.ascontiguousarray(weights[k])
        in_maps.append(m)
    res = run_bass_kernel_spmd(nc, in_maps, core_ids=list(range(NCORES)))
    out = np.zeros((B, S, D), np.float32)
    for c in range(NCORES):
        b, r = c // GROUP, c % GROUP
        out[b, r * SC:(r + 1) * SC] = res.results[c]["x_out"]
    return out


def run_stage_a(x_full, kernel_w):
    nc = build_launch_a()
    eye = np.eye(128, dtype=np.float16)
    in_maps = []
    for c in range(NCORES):
        b, r = c // GROUP, c % GROUP
        in_maps.append({
            "x_own": np.ascontiguousarray(x_full[b, r * SC:(r + 1) * SC]),
            "kern": np.ascontiguousarray(kernel_w),
            "eye_in": eye,
        })
    res = run_bass_kernel_spmd(nc, in_maps, core_ids=list(range(NCORES)))
    out = np.zeros((B, S, D), np.float32)
    for c in range(NCORES):
        b, r = c // GROUP, c % GROUP
        out[b, r * SC:(r + 1) * SC] = res.results[c]["x1"]
    return out


def kernel(**inputs):
    return kernel_fused(**inputs)


L1_KEYS = ["Wq1", "bq1", "Wk1", "bk1", "Wk2", "bk2", "Wproj", "bproj"]


def make_masks(core):
    r = core % GROUP
    mask_lt = np.zeros((128, 4), np.float32)
    mask_prev = np.zeros((128, 4), np.float32)
    first_m = np.zeros((128, 1), np.float32)
    mask_lt[:, :r] = 1.0
    if r > 0:
        mask_prev[:, r - 1] = 1.0
        first_m[:] = 1.0
    return mask_lt, mask_prev, first_m


def build_launch_c(trunc=None):
    key = ("c", trunc)
    if key in _NC_CACHE:
        return _NC_CACHE[key]
    nc = bass.Bass("TRN2", target_bir_lowering=False, debug=False)
    io = {}
    io["x_in"] = nc.dram_tensor("x_in", [SC, D], F32, kind="ExternalInput").ap()
    w = {}
    shapes = {"Wq1": [D, D], "bq1": [D], "Wk1": [D, D], "bk1": [D],
              "Wk2": [D, D], "bk2": [D], "Wproj": [D, D], "bproj": [D]}
    for k in L1_KEYS:
        w[k] = nc.dram_tensor("w_" + k, shapes[k], F32, kind="ExternalInput").ap()
    io["w"] = w
    eye_in = nc.dram_tensor("eye_in", [128, 128], BF16, kind="ExternalInput").ap()
    io["mask_lt"] = nc.dram_tensor("mask_lt_in", [128, 4], F32, kind="ExternalInput").ap()
    io["mask_prev"] = nc.dram_tensor("mask_prev_in", [128, 4], F32, kind="ExternalInput").ap()
    io["first_m"] = nc.dram_tensor("first_m_in", [128, 1], F32, kind="ExternalInput").ap()
    io["cc1_in"] = [nc.dram_tensor(f"cc1_in{i}", [128, 5], F32, kind="Internal").ap()
                    for i in range(8)]
    io["cc1_out"] = [nc.dram_tensor(f"cc1_out{i}", [512, 5], F32, kind="Internal").ap()
                     for i in range(8)]
    io["cc2_in"] = [nc.dram_tensor(f"cc2_in{i}", [128, 1], F32, kind="Internal").ap()
                    for i in range(8)]
    io["cc2_out"] = [nc.dram_tensor(f"cc2_out{i}", [512, 1], F32, kind="Internal").ap()
                     for i in range(8)]
    io["x_out"] = nc.dram_tensor("x_out", [SC, D], F32, kind="ExternalOutput").ap()
    if trunc is not None:
        io["r_d"] = nc.dram_tensor("r_d", [128, 16384], F32, kind="ExternalOutput").ap()
        io["ot_d"] = nc.dram_tensor("ot_d", [128, 16384], BF16, kind="ExternalOutput").ap()
        io["xa_d"] = nc.dram_tensor("xa_d", [128, 2048], F32, kind="ExternalOutput").ap()
        io["cc_d"] = nc.dram_tensor("cc_d", [512, 48], F32, kind="ExternalOutput").ap()
    ctx = ExitStack()
    with ctx:
        bufs = alloc_common(nc, ctx)
        sem = make_sems(nc, ctx, C_SEMS + ["cEYE", "cONE"])
        ops = build_stage_c(sem, bufs, io, store_out=True, resident_in=False,
                            trunc=trunc)
        block = ctx.enter_context(nc.Block())

        @block.sync
        def _(e):
            e.dma_start(out=bufs.eye, in_=eye_in).then_inc(sem["cEYE"], 16)
            ops["sync"](e)

        @block.vector
        def _(e):
            e.memset(bufs.epsc[:], float(EPS)).then_inc(sem["cONE"], 1)
            ops["vector"](e)

        @block.scalar
        def _(e):
            e.wait_ge(sem["cONE"], 1)
            ops["scalar"](e)

        @block.tensor
        def _(e):
            e.wait_ge(sem["cEYE"], 16)
            ops["tensor"](e)

        @block.gpsimd
        def _(e):
            ops["gpsimd"](e)
    _NC_CACHE[key] = nc
    return nc


def run_stage_c(x2_full, weights):
    nc = build_launch_c()
    eye = np.eye(128, dtype=np.float16)
    in_maps = []
    for c in range(NCORES):
        b, r = c // GROUP, c % GROUP
        ml, mp, fm = make_masks(c)
        m = {"x_in": np.ascontiguousarray(x2_full[b, r * SC:(r + 1) * SC]),
             "eye_in": eye, "mask_lt_in": ml, "mask_prev_in": mp,
             "first_m_in": fm}
        for k in L1_KEYS:
            m["w_" + k] = np.ascontiguousarray(weights[k])
        in_maps.append(m)
    res = run_bass_kernel_spmd(nc, in_maps, core_ids=list(range(NCORES)))
    out = np.zeros((B, S, D), np.float32)
    for c in range(NCORES):
        b, r = c // GROUP, c % GROUP
        out[b, r * SC:(r + 1) * SC] = res.results[c]["x_out"]
    return out


def build_fused():
    key = ("fused",)
    if key in _NC_CACHE:
        return _NC_CACHE[key]
    nc = bass.Bass("TRN2", target_bir_lowering=False, debug=False)
    ioa = {}
    ioa["x_own"] = nc.dram_tensor("x_own", [SC, D], F32, kind="ExternalInput").ap()
    ioa["kern"] = nc.dram_tensor("kern", [3, D, D], F32, kind="ExternalInput").ap()
    eye_in = nc.dram_tensor("eye_in", [128, 128], BF16, kind="ExternalInput").ap()
    ioa["kv_in"] = nc.dram_tensor("kv_in", [128, 32768], BF16, kind="Internal").ap()
    ioa["kv_g"] = nc.dram_tensor("kv_g", [512, 32768], BF16, kind="Internal").ap()

    iob = {}
    wb = {}
    shapes_b = {"ln_g": [D], "ln_b": [D], "bn_g": [D], "bn_b": [D],
                "bn_mean": [D], "bn_var": [D], "Wr": [D, 8], "br": [8],
                "W_down": [8, D, 64], "b_down": [8, 64], "W_up": [8, 64, D],
                "b_up": [8, D]}
    for k in FFN0_KEYS:
        wb[k] = nc.dram_tensor("wb_" + k, shapes_b[k], F32, kind="ExternalInput").ap()
    iob["w"] = wb
    iob["zs_in"] = nc.dram_tensor("zs_in", [1, D], F32, kind="Internal").ap()
    iob["zs_out"] = nc.dram_tensor("zs_out", [1, D], F32, kind="Internal").ap()
    iob["mrow_d"] = nc.dram_tensor("mrow_d", [1, 8], F32, kind="Internal").ap()

    ioc = {}
    wc = {}
    shapes_c = {"Wq1": [D, D], "bq1": [D], "Wk1": [D, D], "bk1": [D],
                "Wk2": [D, D], "bk2": [D], "Wproj": [D, D], "bproj": [D]}
    for k in L1_KEYS:
        wc[k] = nc.dram_tensor("wc_" + k, shapes_c[k], F32, kind="ExternalInput").ap()
    ioc["w"] = wc
    ioc["mask_lt"] = nc.dram_tensor("mask_lt_in", [128, 4], F32, kind="ExternalInput").ap()
    ioc["mask_prev"] = nc.dram_tensor("mask_prev_in", [128, 4], F32, kind="ExternalInput").ap()
    ioc["first_m"] = nc.dram_tensor("first_m_in", [128, 1], F32, kind="ExternalInput").ap()
    ioc["cc1_in"] = [nc.dram_tensor(f"cc1_in{i}", [128, 5], F32, kind="Internal").ap()
                     for i in range(8)]
    ioc["cc1_out"] = [nc.dram_tensor(f"cc1_out{i}", [512, 5], F32, kind="Internal").ap()
                      for i in range(8)]
    ioc["cc2_in"] = [nc.dram_tensor(f"cc2_in{i}", [128, 1], F32, kind="Internal").ap()
                     for i in range(8)]
    ioc["cc2_out"] = [nc.dram_tensor(f"cc2_out{i}", [512, 1], F32, kind="Internal").ap()
                      for i in range(8)]
    ioc["x2_sp"] = nc.dram_tensor("x2_sp", [SC, D], F32, kind="Internal").ap()

    iod = {}
    wd = {}
    shapes_d = {"U1": [D, 256], "V1": [256, 4096], "b1": [4096],
                "U2": [4096, 256], "V2": [256, D], "b2": [D]}
    for k in FFN1_KEYS:
        wd[k] = nc.dram_tensor("wd_" + k, shapes_d[k], F32, kind="ExternalInput").ap()
    iod["w"] = wd
    iod["y_out"] = nc.dram_tensor("y_out", [SC, D], F32, kind="ExternalOutput").ap()

    ctx = ExitStack()
    with ctx:
        bufs = alloc_common(nc, ctx)
        fa = [s for s in A_SEMS if s not in ("aEYE", "aOUT", "aQG")]
        fb = [s for s in B_SEMS if s not in ("bOUT", "bCC")]
        fc = [s for s in C_SEMS if s not in ("cOUT",)]
        fd = list(D_SEMS)
        sem = make_sems(nc, ctx, fa + fb + fc + fd + ["gEYE", "gONE"])
        opsa = build_stage_a(sem, bufs, ioa, store_out=False)
        opsb = build_stage_b(sem, bufs, iob, store_out=False, resident_in=True,
                             gate_in=(sem["aRES"], NT))
        opsc = build_stage_c(sem, bufs, ioc, store_out=False, resident_in=True,
                             gate_in=(sem["bRES"], NT))
        opsd = build_stage_d(sem, bufs, iod, store_out=True, resident_in=True,
                             gate_in=(sem["cRES"], NT))
        block = ctx.enter_context(nc.Block())

        @block.sync
        def _(e):
            e.dma_start(out=bufs.eye, in_=eye_in).then_inc(sem["gEYE"], 16)
            for name in ("aEYE",):
                pass
            opsa["sync"](e)
            opsb["sync"](e)
            opsc["sync"](e)
            opsd["sync"](e)

        @block.vector
        def _(e):
            e.memset(bufs.epsc[:], float(EPS)).then_inc(sem["gONE"], 1)
            opsa["vector"](e)
            opsb["vector"](e)
            opsc["vector"](e)
            opsd["vector"](e)

        @block.scalar
        def _(e):
            e.wait_ge(sem["gONE"], 1)
            opsa["scalar"](e)
            opsb["scalar"](e)
            opsc["scalar"](e)
            opsd["scalar"](e)

        @block.tensor
        def _(e):
            e.wait_ge(sem["gEYE"], 16)
            opsa["tensor"](e)
            opsb["tensor"](e)
            opsc["tensor"](e)
            opsd["tensor"](e)

        @block.gpsimd
        def _(e):
            opsa["gpsimd"](e)
            opsb["gpsimd"](e)
            opsc["gpsimd"](e)
            opsd["gpsimd"](e)
    _NC_CACHE[key] = nc
    return nc


def kernel_fused(**inputs):
    ins = {k: np.asarray(v) for k, v in inputs.items()}
    nc = build_fused()
    eye = np.eye(128, dtype=np.float16)
    x = ins["L0_attn_x"]
    in_maps = []
    for c in range(NCORES):
        b, r = c // GROUP, c % GROUP
        ml_, mp, fm = make_masks(c)
        m = {"x_own": np.ascontiguousarray(x[b, r * SC:(r + 1) * SC]),
             "kern": np.ascontiguousarray(ins["L0_attn_kernel"]),
             "eye_in": eye, "mask_lt_in": ml_, "mask_prev_in": mp,
             "first_m_in": fm}
        for k in FFN0_KEYS:
            m["wb_" + k] = np.ascontiguousarray(ins["L0_ffn_" + k])
        for k in L1_KEYS:
            m["wc_" + k] = np.ascontiguousarray(ins["L1_attn_" + k])
        for k in FFN1_KEYS:
            m["wd_" + k] = np.ascontiguousarray(ins["L1_ffn_" + k])
        in_maps.append(m)
    res = run_bass_kernel_spmd(nc, in_maps, core_ids=list(range(NCORES)))
    out = np.zeros((B, S, D), np.float32)
    for c in range(NCORES):
        b, r = c // GROUP, c % GROUP
        out[b, r * SC:(r + 1) * SC] = res.results[c]["y_out"]
    return out


# version 07 (final)
# speedup vs baseline: 1.2418x; 1.0054x over previous
import numpy as np
import ml_dtypes
from contextlib import ExitStack

import concourse.bass as bass
import concourse.mybir as mybir
from concourse.bass_utils import run_bass_kernel_spmd

F32 = mybir.dt.float32
BF16 = mybir.dt.float16
AF = mybir.ActivationFunctionType
ALU = mybir.AluOpType
AX = mybir.AxisListType.X

NCORES = 8
GROUP = 4
B = 2
S = 8192
D = 1024
SC = S // GROUP
NT = SC // 128
EPS = 1e-5

RG = [[0, 1, 2, 3], [4, 5, 6, 7]]

A_SEMS = ["aX", "aX2", "aS2", "aVR", "aRS", "aST", "aTR", "aLT", "aKL", "aKC",
          "aMM", "aEVQ", "aEVK", "aEVV", "aSP", "aSPV", "aCC", "aKV", "aSC", "aEX",
          "aPV", "aPVE", "aDN", "aDNE", "aQG", "aRD", "aOS", "aRES", "aEYE",
          "aONE", "aOUT", "aLX", "aDV"]


def make_sems(nc, ctx, names):
    return {n: ctx.enter_context(nc.semaphore(n)) for n in names}


class NS:
    def __init__(self, d):
        self.__dict__.update(d)


def build_stage_a(sem, bufs, io, store_out, dbg=False):
    qt = bufs.A_bf
    lnT = bufs.C_bf
    ktq = bufs.B_bf
    vq = bufs.C_bf
    pT = bufs.pT
    ker = bufs.W_bf
    R = bufs.R_f32
    xa = bufs.xa
    lnbf = bufs.lnbf
    sqscr = bufs.sqscr
    st = bufs.stats
    dn_sb = bufs.dn_sb
    rd_all = bufs.rd_all
    ones4 = bufs.ones4
    epsc = bufs.epsc
    eye = bufs.eye
    ps = bufs.ps
    psbf = bufs.psbf
    ps64 = bufs.ps64

    x_own, kern, kv_in, kv_g = io["x_own"], io["kern"], io["kv_in"], io["kv_g"]

    def stc(t, i):
        o = 16 * (t % 2) + i
        return st[:, o:o + 1]

    def sync_ops(e):
        xr = x_own.rearrange("(t p) e -> t p e", p=128)
        for t in range(NT):
            if t >= 1:
                e.wait_ge(sem["aX"], 16 * t)
            if t >= 2:
                e.wait_ge(sem["aST"], t - 1)
                e.wait_ge(sem["aS2"], t - 1)
            e.dma_start(out=xa[:, 1024 * (t % 2):1024 * (t % 2) + 1024],
                        in_=xr[t]).then_inc(sem["aX"], 16)
        kr = kern.rearrange("k (c p) e -> k p c e", p=128)
        Rst = R.rearrange("p (a e) -> p a e", e=1024)
        for i, ki in enumerate((1, 2, 0)):
            e.wait_ge(sem["aKC"], i)
            e.dma_start(out=Rst[:, 0:8, :], in_=kr[ki]).then_inc(sem["aKL"], 16)
        e.wait_ge(sem["aEVK"], 32)
        e.dma_start(out=kv_in[:, 0:16384], in_=ktq[:]).then_inc(sem["aSP"], 16)
        kvg_kt = kv_g.rearrange("(j p) n -> j p n", p=128)
        for kt8 in range(8):
            j, h = kt8 // 2, kt8 % 2
            if kt8 == 0:
                e.wait_ge(sem["aCC"], 1)
            if kt8 >= 2:
                e.wait_ge(sem["aSC"], 32 * (kt8 - 1))
                e.wait_ge(sem["aPV"], 32 * (kt8 - 1))
                e.wait_ge(sem["aDN"], 4 * (kt8 - 1))
            src = kvg_kt[j][:, 0:16384].rearrange("p (c s) -> p c s", c=8)
            if kt8 >= 1:
                e.wait_ge(sem["aKV"], 32 * kt8)
            e.dma_start(out=ktq[:, 8192 * (kt8 % 2):8192 * (kt8 % 2) + 8192]
                        .rearrange("p (c s) -> p c s", c=8),
                        in_=src[:, :, 1024 * h:1024 * h + 1024]
                        ).then_inc(sem["aKV"], 16)
            e.wait_ge(sem["aKV"], 32 * kt8 + 16)
            e.dma_start(out=vq[:, 8192 * (kt8 % 2):8192 * (kt8 % 2) + 8192],
                        in_=kvg_kt[j][:, 16384 + 8192 * h:16384 + 8192 * (h + 1)]
                        ).then_inc(sem["aKV"], 16)
        for t in range(NT):
            if t >= 1:
                e.wait_ge(sem["aX2"], 16 * t)
            if t >= 2:
                e.wait_ge(sem["aRES"], t - 1)
            e.dma_start(out=xa[:, 1024 * (t % 2):1024 * (t % 2) + 1024],
                        in_=xr[t]).then_inc(sem["aX2"], 16)

    def vector_ops(e):
        e.memset(ones4[:], 1.0)
        e.memset(epsc[:], float(EPS)).then_inc(sem["aONE"], 1)
        for t in range(NT):
            e.wait_ge(sem["aX"], 16 * (t + 1))
            xv = xa[:, 1024 * (t % 2):1024 * (t % 2) + 1024]
            e.reduce_sum(out=stc(t, 0), in_=xv, axis=AX)
            e.wait_ge(sem["aS2"], t + 1)
            e.tensor_add(out=stc(t, 2), in0=stc(t, 2), in1=stc(t, 9)) \
                .then_inc(sem["aDV"], 1)
            e.wait_ge(sem["aDV"], 2 * t + 1)
            e.tensor_scalar(out=stc(t, 1), in0=stc(t, 0), scalar1=1.0 / D,
                            scalar2=None, op0=ALU.mult)
            e.tensor_scalar(out=stc(t, 3), in0=stc(t, 2), scalar1=1.0 / D,
                            scalar2=None, op0=ALU.mult)
            e.scalar_tensor_tensor(out=stc(t, 4), in0=stc(t, 0),
                                   scalar=-1.0 / (D * D), in1=stc(t, 0),
                                   op0=ALU.mult, op1=ALU.mult) \
                .then_inc(sem["aDV"], 1)
            e.wait_ge(sem["aDV"], 2 * t + 2)
            e.tensor_add(out=stc(t, 5), in0=stc(t, 3), in1=stc(t, 4)) \
                .then_inc(sem["aVR"], 1)
            e.wait_ge(sem["aRS"], t + 1)
            e.tensor_scalar(out=lnbf[:, 1024 * (t % 2):1024 * (t % 2) + 1024],
                            in0=xv, scalar1=stc(t, 1), scalar2=stc(t, 7),
                            op0=ALU.subtract, op1=ALU.mult) \
                .then_inc(sem["aST"], 1)
        for i in range(3):
            e.wait_ge(sem["aKL"], 16 * (i + 1))
            if i >= 1:
                e.wait_ge(sem["aMM"], 32 * i)
            e.tensor_copy(out=ker[:, 0:8192], in_=R[:, 0:8192]) \
                .then_inc(sem["aKC"], 1)
        for kt8 in range(8):
            for qg in range(4):
                for qc in range(4):
                    for dh in range(2):
                        n = ((kt8 * 4 + qg) * 4 + qc) * 2 + dh
                        e.wait_ge(sem["aPV"], n + 1)
                        dst = R[:, (qg * 4 + qc) * 1024 + dh * 512:
                                (qg * 4 + qc) * 1024 + dh * 512 + 512]
                        pvp = ps[2 + (qc * 2 + dh) % 2]
                        if kt8 == 0:
                            e.tensor_copy(out=dst, in_=pvp[:]) \
                                .then_inc(sem["aPVE"], 1)
                        else:
                            e.tensor_add(out=dst, in0=dst, in1=pvp[:]) \
                                .then_inc(sem["aPVE"], 1)
                nqg = kt8 * 4 + qg
                e.wait_ge(sem["aDN"], nqg + 1)
                if kt8 == 0:
                    e.tensor_copy(out=dn_sb[:, 16 * qg:16 * qg + 16],
                                  in_=ps64[:, 0:16]).then_inc(sem["aDNE"], 1)
                else:
                    e.tensor_add(out=dn_sb[:, 16 * qg:16 * qg + 16],
                                 in0=dn_sb[:, 16 * qg:16 * qg + 16],
                                 in1=ps64[:, 0:16]).then_inc(sem["aDNE"], 1)
        dnv = dn_sb.rearrange("p (q f) -> p q f", f=4)
        e.wait_ge(sem["aDNE"], 32)
        e.reciprocal(out=rd_all[:], in_=dnv[:, :, 0]).then_inc(sem["aRD"], 1)
        e.wait_ge(sem["aRD"], 1)
        for t in range(NT):
            Ot = R[:, 1024 * t:1024 * t + 1024]
            e.tensor_scalar(out=Ot, in0=Ot, scalar1=rd_all[:, t:t + 1],
                            scalar2=None, op0=ALU.mult).then_inc(sem["aOS"], 1)
            e.wait_ge(sem["aOS"], t + 1)
            e.reduce_sum(out=stc(t, 0), in_=Ot, axis=AX)
            e.wait_ge(sem["aS2"], NT + t + 1)
            e.tensor_add(out=stc(t, 2), in0=stc(t, 2), in1=stc(t, 9)) \
                .then_inc(sem["aDV"], 1)
            e.wait_ge(sem["aDV"], 2 * NT + 3 * t + 1)
            e.tensor_scalar(out=stc(t, 1), in0=stc(t, 0), scalar1=1.0 / D,
                            scalar2=None, op0=ALU.mult)
            e.tensor_scalar(out=stc(t, 3), in0=stc(t, 2), scalar1=1.0 / D,
                            scalar2=None, op0=ALU.mult)
            e.scalar_tensor_tensor(out=stc(t, 4), in0=stc(t, 0),
                                   scalar=-1.0 / (D * D), in1=stc(t, 0),
                                   op0=ALU.mult, op1=ALU.mult) \
                .then_inc(sem["aDV"], 1)
            e.wait_ge(sem["aDV"], 2 * NT + 3 * t + 2)
            e.tensor_add(out=stc(t, 5), in0=stc(t, 3), in1=stc(t, 4)) \
                .then_inc(sem["aVR"], 1)
            e.wait_ge(sem["aRS"], NT + t + 1)
            e.tensor_scalar(out=Ot, in0=Ot, scalar1=stc(t, 1),
                            scalar2=stc(t, 7), op0=ALU.subtract, op1=ALU.mult) \
                .then_inc(sem["aDV"], 1)
            e.wait_ge(sem["aDV"], 2 * NT + 3 * t + 3)
            e.wait_ge(sem["aX2"], 16 * (t + 1))
            e.tensor_add(out=Ot, in0=Ot,
                         in1=xa[:, 1024 * (t % 2):1024 * (t % 2) + 1024]) \
                .then_inc(sem["aRES"], 1)

    def scalar_ops(e):
        e.wait_ge(sem["aONE"], 1)
        for t in range(NT):
            e.wait_ge(sem["aX"], 16 * (t + 1))
            xv = xa[:, 1024 * (t % 2):1024 * (t % 2) + 1024]
            e.activation(out=sqscr[:], in_=xv[:, 0:512], func=AF.Square,
                         accum_out=stc(t, 2))
            e.activation(out=sqscr[:], in_=xv[:, 512:1024], func=AF.Square,
                         accum_out=stc(t, 9)).then_inc(sem["aS2"], 1)
            e.wait_ge(sem["aVR"], t + 1)
            e.activation(out=stc(t, 6), in_=stc(t, 5), func=AF.Ln,
                         bias=epsc[:]).then_inc(sem["aLX"], 1)
            e.wait_ge(sem["aLX"], t + 1)
            e.activation(out=stc(t, 7), in_=stc(t, 6), func=AF.Exp,
                         scale=-0.5).then_inc(sem["aRS"], 1)
            e.wait_ge(sem["aTR"], t + 1)
            dst = lnT.rearrange("p (c s) -> p c s", c=8)[:, :, 128 * t:128 * t + 128]
            e.activation(out=dst, in_=psbf.rearrange("p (c s) -> p c s", c=8),
                         func=AF.Copy).then_inc(sem["aLT"], 1)
        for sp in range(2):
            for ec in range(8):
                for m in range(2):
                    sc4 = 2 * sp + m
                    e.wait_ge(sem["aMM"], 16 * sp + 2 * ec + m + 1)
                    e.activation(out=ktq[:, ec * 2048 + 512 * sc4:
                                         ec * 2048 + 512 * sc4 + 512],
                                 in_=ps[m + 2 * (ec % 2)][:], func=AF.Copy) \
                        .then_inc(sem["aEVK"], 1)
        for stt in range(16):
            for eh in range(2):
                k = stt * 2 + eh
                e.wait_ge(sem["aMM"], 32 + 2 * stt + eh + 1)
                if k >= 4:
                    e.wait_ge(sem["aSPV"], 16 * (k - 3))
                e.activation(out=lnbf[:, (k % 4) * 512:(k % 4) * 512 + 512],
                             in_=ps[eh + 2 * (stt % 2)][:], func=AF.Copy) \
                    .then_inc(sem["aEVV"], 1)
        for sp in range(2):
            for ec in range(8):
                for m in range(2):
                    sc4 = 2 * sp + m
                    e.wait_ge(sem["aMM"], 64 + 16 * sp + 2 * ec + m + 1)
                    e.activation(out=qt[:, ec * 2048 + 512 * sc4:
                                        ec * 2048 + 512 * sc4 + 512],
                                 in_=ps[m + 2 * (ec % 2)][:], func=AF.Copy) \
                        .then_inc(sem["aEVQ"], 1)
        for kt8 in range(8):
            for p in range(2):
                base = 32 * kt8 + 16 * p
                for jc in range(8):
                    for m in range(2):
                        qg = 2 * p + m
                        e.wait_ge(sem["aSC"], base + 2 * jc + m + 1)
                        pt_dst = pT[:, 4096 * (qg % 2) + 512 * jc:
                                    4096 * (qg % 2) + 512 * jc + 512]
                        e.activation(out=pt_dst, in_=ps[m + 2 * (jc % 2)][:],
                                     func=AF.Exp, scale=1.0 / 32.0) \
                            .then_inc(sem["aEX"], 1)
        for t in range(NT):
            e.wait_ge(sem["aOS"], t + 1)
            Ot = R[:, 1024 * t:1024 * t + 1024]
            e.activation(out=sqscr[:], in_=Ot[:, 0:512], func=AF.Square,
                         accum_out=stc(t, 2))
            e.activation(out=sqscr[:], in_=Ot[:, 512:1024], func=AF.Square,
                         accum_out=stc(t, 9)).then_inc(sem["aS2"], 1)
            e.wait_ge(sem["aVR"], NT + t + 1)
            e.activation(out=stc(t, 6), in_=stc(t, 5), func=AF.Ln,
                         bias=epsc[:]).then_inc(sem["aLX"], 1)
            e.wait_ge(sem["aLX"], NT + t + 1)
            e.activation(out=stc(t, 7), in_=stc(t, 6), func=AF.Exp,
                         scale=-0.5).then_inc(sem["aRS"], 1)

    def tensor_ops(e):
        e.wait_ge(sem["aEYE"] if "aEYE" in sem else sem["gEYE"], 16)
        for t in range(NT):
            e.wait_ge(sem["aST"], t + 1)
            if t >= 1:
                e.wait_ge(sem["aLT"], t)
            src = lnbf[:, 1024 * (t % 2):1024 * (t % 2) + 1024]
            last = None
            for dc in range(8):
                last = e.transpose(psbf[:, 128 * dc:128 * dc + 128],
                                   src[:, 128 * dc:128 * dc + 128], eye[:])
            last.then_inc(sem["aTR"], 1)
        e.wait_ge(sem["aLT"], NT)
        kslice = ker[:, 0:8192]
        for kb, evs in ((0, "aEVK"), (2, "aEVQ")):
            e.wait_ge(sem["aKC"], 1 if kb == 0 else 3)
            for sp in range(2):
                for ec in range(8):
                    if kb == 2 and sp == 0 and ec == 0:
                        e.wait_ge(sem["aEVV"], 32)
                    if ec >= 2:
                        e.wait_ge(sem[evs], 16 * sp + 2 * (ec - 1))
                    elif sp >= 1:
                        e.wait_ge(sem[evs], 16 * sp)
                    lasts = [None, None]
                    for dc in range(8):
                        kst = kslice[:, dc * 1024 + ec * 128:dc * 1024 + ec * 128 + 128]
                        for m in range(2):
                            sc4 = 2 * sp + m
                            lasts[m] = e.matmul(
                                ps[m + 2 * (ec % 2)][:], kst,
                                lnT[:, dc * 2048 + 512 * sc4:dc * 2048 + 512 * sc4 + 512],
                                start=(dc == 0), stop=(dc == 7),
                                skip_group_check=True)
                    lasts[0].then_inc(sem["aMM"], 1)
                    lasts[1].then_inc(sem["aMM"], 1)
            if kb == 0:
                e.wait_ge(sem["aKC"], 2)
                for stt in range(16):
                    if stt == 0:
                        e.wait_ge(sem["aEVK"], 32)
                    if stt >= 2:
                        e.wait_ge(sem["aEVV"], 2 * (stt - 1))
                    lasts = [None, None]
                    for dc in range(8):
                        kst = lnT[:, dc * 2048 + 128 * stt:dc * 2048 + 128 * stt + 128]
                        for eh in range(2):
                            lasts[eh] = e.matmul(
                                ps[eh + 2 * (stt % 2)][:], kst,
                                ker[:, dc * 1024 + eh * 512:dc * 1024 + eh * 512 + 512],
                                start=(dc == 0), stop=(dc == 7),
                                skip_group_check=True)
                    lasts[0].then_inc(sem["aMM"], 1)
                    lasts[1].then_inc(sem["aMM"], 1)
        e.wait_ge(sem["aONE"], 1)
        e.wait_ge(sem["aEVQ"], 32)
        for kt8 in range(8):
            e.wait_ge(sem["aKV"], 32 * (kt8 + 1))
            ktb = ktq[:, 8192 * (kt8 % 2):8192 * (kt8 % 2) + 8192]
            vb = vq[:, 8192 * (kt8 % 2):8192 * (kt8 % 2) + 8192]
            for p in range(2):
                base = 32 * kt8 + 16 * p
                for jc in range(8):
                    if jc >= 2:
                        e.wait_ge(sem["aEX"], base + 2 * jc - 2)
                    elif jc == 0 and base >= 2:
                        e.wait_ge(sem["aEX"], base)
                    elif jc == 1:
                        if 4 * kt8 + 2 * p >= 1:
                            e.wait_ge(sem["aPVE"], 8 * (4 * kt8 + 2 * p))
                    lasts = [None, None]
                    for ec in range(8):
                        kst = ktb[:, ec * 1024 + jc * 128:ec * 1024 + jc * 128 + 128]
                        for m in range(2):
                            qg = 2 * p + m
                            lasts[m] = e.matmul(
                                ps[m + 2 * (jc % 2)][:], kst,
                                qt[:, ec * 2048 + qg * 512:ec * 2048 + qg * 512 + 512],
                                start=(ec == 0), stop=(ec == 7),
                                skip_group_check=True)
                    lasts[0].then_inc(sem["aSC"], 1)
                    lasts[1].then_inc(sem["aSC"], 1)
                e.wait_ge(sem["aEX"], base + 16)
                for m in range(2):
                    qg = 2 * p + m
                    nqg = kt8 * 4 + qg
                    pt_b = pT[:, 4096 * (qg % 2):4096 * (qg % 2) + 4096]
                    for qc in range(4):
                        n0 = (nqg * 4 + qc) * 2
                        if n0 >= 1:
                            e.wait_ge(sem["aPVE"], n0)
                        if nqg >= 1:
                            e.wait_ge(sem["aDNE"], nqg)
                        lasts = [None, None, None]
                        for jc in range(8):
                            pst = pt_b[:, jc * 512 + qc * 128:jc * 512 + qc * 128 + 128]
                            for dh in range(2):
                                lasts[dh] = e.matmul(
                                    ps[2 + dh][:], pst,
                                    vb[:, jc * 1024 + dh * 512:jc * 1024 + dh * 512 + 512],
                                    start=(jc == 0), stop=(jc == 7),
                                    skip_group_check=True)
                            lasts[2] = e.matmul(
                                ps64[:, 4 * qc:4 * qc + 4], pst,
                                ones4[:], start=(jc == 0), stop=(jc == 7),
                                skip_group_check=True)
                        lasts[0].then_inc(sem["aPV"], 1)
                        lasts[1].then_inc(sem["aPV"], 1)
                        if qc == 3:
                            lasts[2].then_inc(sem["aDN"], 1)

    def gpsimd_ops(e):
        for k in range(32):
            e.wait_ge(sem["aEVV"], k + 1)
            e.dma_start(out=kv_in[:, 16384 + 512 * k:16384 + 512 * k + 512],
                        in_=lnbf[:, (k % 4) * 512:(k % 4) * 512 + 512]) \
                .then_inc(sem["aSPV"], 16)
        e.wait_ge(sem["aSP"], 16)
        e.wait_ge(sem["aSPV"], 512)
        e.collective_compute("AllGather", ALU.bypass, replica_groups=RG,
                             ins=[kv_in.opt()], outs=[kv_g.opt()]) \
            .then_inc(sem["aCC"])
        if store_out:
            e.wait_ge(sem["aRES"], NT)
            nout = 0
            xo = io["x1_ext"].rearrange("(q p) e -> q p e", p=128)
            for t in range(NT):
                e.dma_start(out=xo[t], in_=R[:, 1024 * t:1024 * t + 1024]) \
                    .then_inc(sem["aOUT"], 16)
                nout += 1
            if dbg:
                e.dma_start(out=io["qt_d"], in_=qt[:]).then_inc(sem["aOUT"], 16)
                e.dma_start(out=io["kv_d"], in_=kv_in[:]).then_inc(sem["aOUT"], 16)
                e.dma_start(out=io["kvg_d"], in_=kv_g[:]).then_inc(sem["aOUT"], 16)
                e.dma_start(out=io["dn_d"], in_=dn_sb[:]).then_inc(sem["aOUT"], 16)
                nout += 4
            e.wait_ge(sem["aOUT"], 16 * nout)

    return dict(sync=sync_ops, vector=vector_ops, scalar=scalar_ops,
                tensor=tensor_ops, gpsimd=gpsimd_ops)


B_SEMS = ["bXR", "bS2", "bVR", "bRS", "bLX", "bDV", "bZ", "bTR", "bZT", "bCS",
          "bCC", "bSS", "bDM", "bHM", "bHE", "bYM", "bYE", "bRES", "bOUT"]


class TokenChain:

    def __init__(self, ss, dm):
        self.ss, self.dm = ss, dm
        self.k = 0
        self.m = 0
        self.steps = []
        self._last_cc = None
        self.marks = {}

    def mark(self, label):
        self.marks[label] = len(self.steps)

    def add(self, eng, fn, dma=False, cc=None):
        self.steps.append((eng, fn, self.k, self.m, dma, cc, self._last_cc))
        if cc is not None:
            self._last_cc = cc
        elif dma:
            self.m += 1
        else:
            self.k += 1

    def counts_at(self, pos):
        k = m = 0
        last_cc = None
        for eng, fn, k0, m0, dma, cc, pc in self.steps[:pos]:
            if cc is not None:
                last_cc = cc
            elif dma:
                m += 1
            else:
                k += 1
        return k, m, last_cc

    def emit(self, eng_name, e, limit=None):
        for i, (eng, fn, k0, m0, dma, cc, prior_cc) in enumerate(self.steps):
            if limit is not None and i >= limit:
                break
            if eng != eng_name:
                continue
            if k0 > 0:
                e.wait_ge(self.ss, k0)
            if m0 > 0:
                e.wait_ge(self.dm, 16 * m0)
            if prior_cc is not None:
                e.wait_ge(prior_cc, 1)
            ins = fn(e)
            if cc is not None:
                ins.then_inc(cc)
            elif dma:
                ins.then_inc(self.dm, 16)
            else:
                ins.then_inc(self.ss, 1)


def build_stage_b(sem, bufs, io, store_out, resident_in, gate_in=None):
    R = bufs.R_f32
    zT = bufs.B_bf
    af = bufs.A_bf.bitcast(F32)
    cf = bufs.C_bf.bitcast(F32)
    pTb = bufs.pT
    xa = bufs.xa
    lnbf = bufs.lnbf
    sqscr = bufs.sqscr
    st = bufs.stats
    epsc = bufs.epsc
    ps = bufs.ps
    psbf = bufs.psbf
    ps64 = bufs.ps64

    g_bc = af[:, 0:1024]
    b_bc = af[:, 1024:2048]
    wd_all = af[:, 2048:6144]
    wd_sel = af[:, 6144:6656]
    maskP = af[:, 6656:6664]
    gate_bc = af[:, 6664:6665]
    zbnT = af[:, 6672:6680]
    wr_sb = af[:, 6688:6752].rearrange("p (c e) -> p c e", c=8)
    ones1f = af[0:1, 6752:6880]
    onec = af[0:1, 6880:6881]
    mask_col = cf[32:40, 4160:4161]
    bdT = af[0:64, 6888:6889]
    wu_stage = cf[64:128, 1024:2048]
    wu_sel = cf[64:128, 2048:3072]
    bdown_sb = cf[32:40, 3072:3136]
    bup_sb = cf[32:40, 3136:4160]
    bnm = cf[0:1, 0:1024]
    bnv = cf[0:1, 1024:2048]
    bng = cf[0:1, 2048:3072]
    bnb = cf[0:1, 3072:4096]
    g_row = cf[0:1, 4096:5120]
    b_row = cf[0:1, 5120:6144]
    r0 = cf[0:1, 6144:7168]
    zsr = cf[0:1, 7168:8192]
    lvq = af[0:1, 6992:8016]
    br_row = af[0:1, 6912:6920]
    lg = af[0:1, 6928:6936]
    mx = af[0:1, 6936:6937]
    nmx = af[0:1, 6937:6938]
    el = af[0:1, 6944:6952]
    smr = af[0:1, 6952:6953]
    rsr = af[0:1, 6953:6954]
    probs = af[0:1, 6960:6968]
    pmax = af[0:1, 6968:6969]
    mask_row = af[0:1, 6976:6984]
    wd_sel_bf = pTb[:, 0:512]
    wu_sel_bf = pTb[0:64, 512:1536]
    hT = pTb[0:64, 1536:3584]
    bu_row_bf = pTb[0:1, 3584:4608]
    ones1_bf = pTb[0:1, 4608:4736]

    w = io["w"]

    def stc(t, i):
        o = 16 * (t % 2) + i
        return st[:, o:o + 1]

    tc = TokenChain(sem["bSS"], sem["bDM"])

    def _chain():
        tc.add("scalar", lambda e: e.activation(out=r0[:, 0:512], in_=ps[2][0:1, :],
                                                func=AF.Copy))
        tc.add("scalar", lambda e: e.activation(out=r0[:, 512:1024], in_=ps[3][0:1, :],
                                                func=AF.Copy))
        tc.add("gpsimd", lambda e: e.dma_start(out=io["zs_in"], in_=r0), dma=True)
        tc.add("gpsimd", lambda e: e.collective_compute(
            "AllReduce", ALU.add, replica_groups=RG,
            ins=[io["zs_in"].opt()], outs=[io["zs_out"].opt()]))
        tc.add("gpsimd", lambda e: e.dma_start(out=zsr, in_=io["zs_out"]), dma=True)
        tc.add("vector", lambda e: e.tensor_scalar(out=zsr, in0=zsr, scalar1=1.0 / S,
                                                   scalar2=None, op0=ALU.mult))
        tc.add("vector", lambda e: e.tensor_sub(out=r0, in0=zsr, in1=bnm))
        tc.add("scalar", lambda e: e.activation(out=lvq, in_=bnv, func=AF.Ln,
                                                bias=epsc[0:1, :]))
        tc.add("scalar", lambda e: e.activation(out=lvq, in_=lvq, func=AF.Exp,
                                                scale=-0.5))
        tc.add("vector", lambda e: e.tensor_mul(out=r0, in0=r0, in1=lvq))
        tc.add("vector", lambda e: e.tensor_mul(out=r0, in0=r0, in1=bng))
        tc.add("vector", lambda e: e.tensor_add(out=r0, in0=r0, in1=bnb))

        def zbnt_mms(e):
            for dc in range(8):
                last = e.matmul(ps64[:, dc:dc + 1], r0[:, 128 * dc:128 * dc + 128],
                                onec, start=True, stop=True)
            return last
        tc.add("tensor", zbnt_mms)
        tc.add("scalar", lambda e: e.activation(out=zbnT, in_=ps64[:, 0:8],
                                                func=AF.Copy))

        def logit_mms(e):
            for dc in range(8):
                last = e.matmul(ps[0][0:1, 0:8], zbnT[:, dc:dc + 1],
                                wr_sb[:, dc, :], start=(dc == 0), stop=(dc == 7))
            return last
        tc.add("tensor", logit_mms)
        tc.add("scalar", lambda e: e.activation(out=lg, in_=ps[0][0:1, 0:8],
                                                func=AF.Copy))
        tc.add("vector", lambda e: e.tensor_add(out=lg, in0=lg, in1=br_row))
        tc.add("vector", lambda e: e.reduce_max(out=mx, in_=lg, axis=AX))
        tc.add("vector", lambda e: e.tensor_scalar(out=nmx, in0=mx, scalar1=-1.0,
                                                   scalar2=None, op0=ALU.mult))
        tc.add("scalar", lambda e: e.activation(out=el, in_=lg, func=AF.Exp,
                                                bias=nmx))
        tc.add("vector", lambda e: e.reduce_sum(out=smr, in_=el, axis=AX))
        tc.add("vector", lambda e: e.reciprocal(out=rsr, in_=smr))
        tc.add("vector", lambda e: e.tensor_scalar(out=probs, in0=el, scalar1=rsr,
                                                   scalar2=None, op0=ALU.mult))
        tc.add("vector", lambda e: e.reduce_max(out=pmax, in_=probs, axis=AX))
        tc.add("vector", lambda e: e.tensor_scalar(out=mask_row, in0=probs,
                                                   scalar1=pmax, scalar2=None,
                                                   op0=ALU.is_ge))
        tc.add("gpsimd", lambda e: e.dma_start(out=io["mrow_d"], in_=mask_row),
               dma=True)
        tc.add("gpsimd", lambda e: e.dma_start(
            out=mask_col, in_=io["mrow_d"].rearrange("o e -> e o")), dma=True)
        tc.add("tensor", lambda e: e.matmul(ps[1][:, 0:8], ones1f, mask_row,
                                            start=True, stop=True))
        tc.add("tensor", lambda e: e.matmul(ps[1][:, 8:9], ones1f, pmax,
                                            start=True, stop=True, skip_group_check=True))
        tc.add("scalar", lambda e: e.activation(out=maskP, in_=ps[1][:, 0:8],
                                                func=AF.Copy))
        tc.add("scalar", lambda e: e.activation(out=gate_bc, in_=ps[1][:, 8:9],
                                                func=AF.Copy))
        for ee in range(8):
            we = wd_all[:, 512 * ee:512 * ee + 512]
            if ee == 0:
                tc.add("vector", lambda e, we=we: e.tensor_scalar(
                    out=wd_sel, in0=we, scalar1=maskP[:, 0:1], scalar2=None,
                    op0=ALU.mult))
            else:
                tc.add("vector", lambda e, we=we, ee=ee: e.scalar_tensor_tensor(
                    out=wd_sel, in0=we, scalar=maskP[:, ee:ee + 1], in1=wd_sel,
                    op0=ALU.mult, op1=ALU.add))
        tc.add("vector", lambda e: e.tensor_copy(out=wd_sel_bf, in_=wd_sel))
        for ee in range(8):
            tc.add("gpsimd", lambda e, ee=ee: e.dma_start(
                out=wu_stage, in_=w["W_up"][ee]), dma=True)
            if ee == 0:
                tc.add("vector", lambda e: e.tensor_scalar(
                    out=wu_sel, in0=wu_stage, scalar1=maskP[64:128, 0:1],
                    scalar2=None, op0=ALU.mult))
            else:
                tc.add("vector", lambda e, ee=ee: e.scalar_tensor_tensor(
                    out=wu_sel, in0=wu_stage, scalar=maskP[64:128, ee:ee + 1],
                    in1=wu_sel, op0=ALU.mult, op1=ALU.add))
        tc.add("vector", lambda e: e.tensor_copy(out=wu_sel_bf, in_=wu_sel))
        tc.add("tensor", lambda e: e.matmul(ps[0][0:64, 0:1], bdown_sb, mask_col,
                                            start=True, stop=True))
        tc.add("scalar", lambda e: e.activation(out=bdT, in_=ps[0][0:64, 0:1],
                                                func=AF.Copy))
        tc.add("tensor", lambda e: e.matmul(ps[2][0:1, :], mask_col,
                                            bup_sb[:, 0:512], start=True, stop=True))
        tc.add("tensor", lambda e: e.matmul(ps[3][0:1, :], mask_col,
                                            bup_sb[:, 512:1024], start=True,
                                            stop=True))
        tc.add("scalar", lambda e: e.activation(out=bu_row_bf[:, 0:512],
                                                in_=ps[2][0:1, :], func=AF.Copy))
        tc.add("scalar", lambda e: e.activation(out=bu_row_bf[:, 512:1024],
                                                in_=ps[3][0:1, :], func=AF.Copy))
    _chain()
    TCK = tc.k

    def gate(e):
        if gate_in is not None:
            e.wait_ge(gate_in[0], gate_in[1])

    NW = 11

    def sync_ops2(e):
        gate(e)
        loads = [
            (g_row, w["ln_g"][None, :]),
            (b_row, w["ln_b"][None, :]),
            (bnm, w["bn_mean"][None, :]),
            (bnv, w["bn_var"][None, :]),
            (bng, w["bn_g"][None, :]),
            (bnb, w["bn_b"][None, :]),
            (br_row, w["br"][None, :]),
            (wr_sb, w["Wr"].rearrange("(c p) e -> p c e", p=128)),
            (bdown_sb, w["b_down"]),
            (bup_sb, w["b_up"]),
        ]
        nb = 0
        for dst, srcap in loads:
            if nb >= 1:
                e.wait_ge(sem["bXR"], 16 * nb)
            e.dma_start(out=dst, in_=srcap).then_inc(sem["bXR"], 16)
            nb += 1
        for ee in range(8):
            e.wait_ge(sem["bXR"], 16 * nb)
            e.dma_start(out=wd_all[:, 512 * ee:512 * ee + 512]
                        .rearrange("p (c h) -> p c h", c=8),
                        in_=w["W_down"][ee].rearrange("(c p) h -> p c h", p=128)) \
                .then_inc(sem["bXR"], 16)
            nb += 1
        if not resident_in:
            xr = io["x_in"].rearrange("(t p) e -> t p e", p=128)
            for t in range(NT):
                e.wait_ge(sem["bXR"], 16 * nb)
                e.dma_start(out=R[:, 1024 * t:1024 * t + 1024], in_=xr[t]) \
                    .then_inc(sem["bXR"], 16)
                nb += 1

    NW = 18

    def vector_ops(e):
        gate(e)
        e.memset(ones1f, 1.0)
        e.memset(onec, 1.0)
        e.memset(ones1_bf, 1.0)
        for t in range(NT):
            if not resident_in:
                e.wait_ge(sem["bXR"], 16 * (NW + t + 1))
            else:
                e.wait_ge(sem["bXR"], 16 * NW)
            Rt = R[:, 1024 * t:1024 * t + 1024]
            e.reduce_sum(out=stc(t, 0), in_=Rt, axis=AX)
            e.wait_ge(sem["bS2"], t + 1)
            e.tensor_add(out=stc(t, 2), in0=stc(t, 2), in1=stc(t, 9)) \
                .then_inc(sem["bDV"], 1)
            e.wait_ge(sem["bDV"], 5 * t + 1)
            e.tensor_scalar(out=stc(t, 1), in0=stc(t, 0), scalar1=1.0 / D,
                            scalar2=None, op0=ALU.mult)
            e.tensor_scalar(out=stc(t, 3), in0=stc(t, 2), scalar1=1.0 / D,
                            scalar2=None, op0=ALU.mult)
            e.scalar_tensor_tensor(out=stc(t, 4), in0=stc(t, 0),
                                   scalar=-1.0 / (D * D), in1=stc(t, 0),
                                   op0=ALU.mult, op1=ALU.mult) \
                .then_inc(sem["bDV"], 1)
            e.wait_ge(sem["bDV"], 5 * t + 2)
            e.tensor_add(out=stc(t, 5), in0=stc(t, 3), in1=stc(t, 4)) \
                .then_inc(sem["bVR"], 1)
            e.wait_ge(sem["bRS"], t + 1)
            za = xa[:, 1024 * (t % 2):1024 * (t % 2) + 1024]
            e.tensor_scalar(out=za, in0=Rt, scalar1=stc(t, 1),
                            scalar2=stc(t, 7), op0=ALU.subtract, op1=ALU.mult) \
                .then_inc(sem["bDV"], 1)
            e.wait_ge(sem["bDV"], 5 * t + 3)
            e.wait_ge(sem["bHE"], 4)
            e.tensor_mul(out=za, in0=za, in1=g_bc).then_inc(sem["bDV"], 1)
            e.wait_ge(sem["bDV"], 5 * t + 4)
            e.tensor_add(out=lnbf[:, 1024 * (t % 2):1024 * (t % 2) + 1024],
                         in0=za, in1=b_bc).then_inc(sem["bZ"], 1)
            e.nop().then_inc(sem["bDV"], 1)
        tc.emit("vector", e)
        for t in range(NT):
            for dh in range(2):
                n = 2 * t + dh
                e.wait_ge(sem["bYM"], n + 1)
                ya = xa[:, 1024 * (t % 2) + 512 * dh:
                        1024 * (t % 2) + 512 * dh + 512]
                e.tensor_scalar(out=ya, in0=ps[2 + n % 2][:], scalar1=gate_bc,
                                scalar2=None, op0=ALU.mult).then_inc(sem["bYE"], 1)
            yt = xa[:, 1024 * (t % 2):1024 * (t % 2) + 1024]
            e.wait_ge(sem["bYE"], 2 * t + 2)
            e.reduce_sum(out=stc(t, 0), in_=yt, axis=AX)
            e.wait_ge(sem["bS2"], NT + t + 1)
            e.tensor_add(out=stc(t, 2), in0=stc(t, 2), in1=stc(t, 9)) \
                .then_inc(sem["bDV"], 1)
            e.wait_ge(sem["bDV"], 5 * NT + 3 * t + 1)
            e.tensor_scalar(out=stc(t, 1), in0=stc(t, 0), scalar1=1.0 / D,
                            scalar2=None, op0=ALU.mult)
            e.tensor_scalar(out=stc(t, 3), in0=stc(t, 2), scalar1=1.0 / D,
                            scalar2=None, op0=ALU.mult)
            e.scalar_tensor_tensor(out=stc(t, 4), in0=stc(t, 0),
                                   scalar=-1.0 / (D * D), in1=stc(t, 0),
                                   op0=ALU.mult, op1=ALU.mult) \
                .then_inc(sem["bDV"], 1)
            e.wait_ge(sem["bDV"], 5 * NT + 3 * t + 2)
            e.tensor_add(out=stc(t, 5), in0=stc(t, 3), in1=stc(t, 4)) \
                .then_inc(sem["bVR"], 1)
            e.wait_ge(sem["bRS"], NT + t + 1)
            e.tensor_scalar(out=yt, in0=yt, scalar1=stc(t, 1),
                            scalar2=stc(t, 7), op0=ALU.subtract, op1=ALU.mult) \
                .then_inc(sem["bDV"], 1)
            e.wait_ge(sem["bDV"], 5 * NT + 3 * t + 3)
            Rt = R[:, 1024 * t:1024 * t + 1024]
            e.tensor_add(out=Rt, in0=Rt, in1=yt).then_inc(sem["bRES"], 1)

    def scalar_ops(e):
        gate(e)
        bdsts = [g_bc[:, 0:512], g_bc[:, 512:1024],
                 b_bc[:, 0:512], b_bc[:, 512:1024]]
        for k in range(4):
            e.wait_ge(sem["bHM"], k + 1)
            e.activation(out=bdsts[k], in_=ps[k % 2][:], func=AF.Copy) \
                .then_inc(sem["bHE"], 1)
        for t in range(NT):
            if not resident_in:
                e.wait_ge(sem["bXR"], 16 * (NW + t + 1))
            else:
                e.wait_ge(sem["bXR"], 16 * NW)
            Rt = R[:, 1024 * t:1024 * t + 1024]
            e.activation(out=sqscr[:], in_=Rt[:, 0:512], func=AF.Square,
                         accum_out=stc(t, 2))
            e.activation(out=sqscr[:], in_=Rt[:, 512:1024], func=AF.Square,
                         accum_out=stc(t, 9)).then_inc(sem["bS2"], 1)
            e.wait_ge(sem["bVR"], t + 1)
            e.activation(out=stc(t, 6), in_=stc(t, 5), func=AF.Ln,
                         bias=epsc[:]).then_inc(sem["bLX"], 1)
            e.wait_ge(sem["bLX"], t + 1)
            e.activation(out=stc(t, 7), in_=stc(t, 6), func=AF.Exp,
                         scale=-0.5).then_inc(sem["bRS"], 1)
            e.wait_ge(sem["bTR"], t + 1)
            dst = zT.rearrange("p (c s) -> p c s", c=8)[:, :, 128 * t:128 * t + 128]
            e.activation(out=dst, in_=psbf.rearrange("p (c s) -> p c s", c=8),
                         func=AF.Copy).then_inc(sem["bZT"], 1)
        tc.emit("scalar", e)
        for k in range(4):
            e.wait_ge(sem["bHM"], 4 + k + 1)
            e.activation(out=hT[:, 512 * k:512 * k + 512], in_=ps[k % 2][0:64, :],
                         func=AF.Relu, bias=bdT).then_inc(sem["bHE"], 1)
        for t in range(NT):
            e.wait_ge(sem["bYE"], 2 * t + 2)
            yt = xa[:, 1024 * (t % 2):1024 * (t % 2) + 1024]
            e.activation(out=sqscr[:], in_=yt[:, 0:512], func=AF.Square,
                         accum_out=stc(t, 2))
            e.activation(out=sqscr[:], in_=yt[:, 512:1024], func=AF.Square,
                         accum_out=stc(t, 9)).then_inc(sem["bS2"], 1)
            e.wait_ge(sem["bVR"], NT + t + 1)
            e.activation(out=stc(t, 6), in_=stc(t, 5), func=AF.Ln,
                         bias=epsc[:]).then_inc(sem["bLX"], 1)
            e.wait_ge(sem["bLX"], NT + t + 1)
            e.activation(out=stc(t, 7), in_=stc(t, 6), func=AF.Exp,
                         scale=-0.5).then_inc(sem["bRS"], 1)

    def tensor_ops(e):
        gate(e)
        e.wait_ge(sem["bXR"], 16 * NW)
        srcs = [g_row[:, 0:512], g_row[:, 512:1024],
                b_row[:, 0:512], b_row[:, 512:1024]]
        for k, srow in enumerate(srcs):
            if k >= 2:
                e.wait_ge(sem["bHE"], k - 1)
            e.matmul(ps[k % 2][:], ones1f, srow, start=True, stop=True) \
                .then_inc(sem["bHM"], 1)
        for t in range(NT):
            e.wait_ge(sem["bZ"], t + 1)
            if t >= 1:
                e.wait_ge(sem["bZT"], t)
            srcz = lnbf[:, 1024 * (t % 2):1024 * (t % 2) + 1024]
            last = None
            for dc in range(8):
                last = e.transpose(psbf[:, 128 * dc:128 * dc + 128],
                                   srcz[:, 128 * dc:128 * dc + 128], bufs.eye[:])
            last.then_inc(sem["bTR"], 1)
            for hh in range(2):
                last = e.matmul(ps[2 + hh][0:1, :], bufs.ones4[:, 0:1],
                                srcz[:, 512 * hh:512 * hh + 512],
                                start=(t == 0), stop=(t == NT - 1),
                                skip_group_check=True)
            if t == NT - 1:
                last.then_inc(sem["bCS"], 1)
        tc.emit("tensor", e)
        e.wait_ge(sem["bSS"], TCK)
        e.wait_ge(sem["bZT"], NT)
        for k in range(4):
            if k >= 2:
                e.wait_ge(sem["bHE"], 4 + k - 1)
            last = None
            for dc in range(8):
                last = e.matmul(ps[k % 2][0:64, :],
                                wd_sel_bf[:, 64 * dc:64 * dc + 64],
                                zT[:, 2048 * dc + 512 * k:2048 * dc + 512 * k + 512],
                                start=(dc == 0), stop=(dc == 7))
            last.then_inc(sem["bHM"], 1)
        e.wait_ge(sem["bHE"], 8)
        for t in range(NT):
            for dh in range(2):
                n = 2 * t + dh
                if n >= 2:
                    e.wait_ge(sem["bYE"], n - 1)
                pst = ps[2 + n % 2]
                e.matmul(pst[:], hT[:, 128 * t:128 * t + 128],
                         wu_sel_bf[:, 512 * dh:512 * dh + 512],
                         start=True, stop=False)
                e.matmul(pst[:], ones1_bf, bu_row_bf[:, 512 * dh:512 * dh + 512],
                         start=False, stop=True).then_inc(sem["bYM"], 1)

    def gpsimd_ops(e):
        gate(e)
        tc.emit("gpsimd", e)
        if store_out:
            e.wait_ge(sem["bRES"], NT)
            nout = 0
            xo = io["x_out"].rearrange("(q p) e -> q p e", p=128)
            for t in range(NT):
                e.dma_start(out=xo[t], in_=R[:, 1024 * t:1024 * t + 1024]) \
                    .then_inc(sem["bOUT"], 16)
                nout += 1
            if "af_d" in io:
                e.dma_start(out=io["af_d"], in_=bufs.A_bf[:]).then_inc(sem["bOUT"], 16)
                e.dma_start(out=io["cf_d"], in_=bufs.C_bf[:]).then_inc(sem["bOUT"], 16)
                e.dma_start(out=io["pt_d"], in_=pTb[:]).then_inc(sem["bOUT"], 16)
                e.dma_start(out=io["zt_d"], in_=zT[:]).then_inc(sem["bOUT"], 16)
                nout += 4
            e.wait_ge(sem["bOUT"], 16 * nout)

    return dict(sync=sync_ops2, vector=vector_ops, scalar=scalar_ops,
                tensor=tensor_ops, gpsimd=gpsimd_ops)


def alloc_common(nc, ctx):
    b = {}
    b["R_f32"] = ctx.enter_context(nc.sbuf_tensor("R_f32", [128, 16384], F32)).ap()
    b["A_bf"] = ctx.enter_context(nc.sbuf_tensor("A_bf", [128, 16384], BF16)).ap()
    b["B_bf"] = ctx.enter_context(nc.sbuf_tensor("B_bf", [128, 16384], BF16)).ap()
    b["C_bf"] = ctx.enter_context(nc.sbuf_tensor("C_bf", [128, 16384], BF16)).ap()
    b["W_bf"] = ctx.enter_context(nc.sbuf_tensor("W_bf", [128, 8192], BF16)).ap()
    b["pT"] = ctx.enter_context(nc.sbuf_tensor("pT", [128, 8192], BF16)).ap()
    b["lnbf"] = ctx.enter_context(nc.sbuf_tensor("lnbf", [128, 2048], BF16)).ap()
    b["xa"] = ctx.enter_context(nc.sbuf_tensor("xa", [128, 2048], F32)).ap()
    b["sqscr"] = ctx.enter_context(nc.sbuf_tensor("sqscr", [128, 512], F32)).ap()
    b["stats"] = ctx.enter_context(nc.sbuf_tensor("stats", [128, 32], F32)).ap()
    b["dn_sb"] = ctx.enter_context(nc.sbuf_tensor("dn_sb", [128, 64], F32)).ap()
    b["rd_all"] = ctx.enter_context(nc.sbuf_tensor("rd_all", [128, 16], F32)).ap()
    b["ones4"] = ctx.enter_context(nc.sbuf_tensor("ones4", [128, 4], BF16)).ap()
    b["epsc"] = ctx.enter_context(nc.sbuf_tensor("epsc", [128, 1], F32)).ap()
    b["eye"] = ctx.enter_context(nc.sbuf_tensor("eye", [128, 128], BF16)).ap()
    b["wbf"] = ctx.enter_context(nc.sbuf_tensor("wbf", [128, 256], BF16)).ap()
    b["ccs"] = ctx.enter_context(nc.sbuf_tensor("ccs", [128, 64], F32)).ap()
    b["mask_lt"] = ctx.enter_context(nc.sbuf_tensor("mask_lt", [128, 4], F32)).ap()
    b["mask_prev"] = ctx.enter_context(nc.sbuf_tensor("mask_prev", [128, 4], F32)).ap()
    b["first_m"] = ctx.enter_context(nc.sbuf_tensor("first_m", [128, 1], F32)).ap()
    b["ps"] = [ctx.enter_context(nc.psum_tensor(f"ps{i}", [128, 512], F32)).ap()
               for i in range(4)]
    b["psbf"] = ctx.enter_context(nc.psum_tensor("psbf", [128, 1024], BF16)).ap()
    b["ps64"] = ctx.enter_context(nc.psum_tensor("ps64", [128, 64], F32)).ap()
    return NS(b)


_NC_CACHE = {}


def build_launch_a(dbg=False):
    key = ("a", dbg)
    if key in _NC_CACHE:
        return _NC_CACHE[key]
    nc = bass.Bass("TRN2", target_bir_lowering=False, debug=False)
    io = {}
    io["x_own"] = nc.dram_tensor("x_own", [SC, D], F32, kind="ExternalInput").ap()
    io["kern"] = nc.dram_tensor("kern", [3, D, D], F32, kind="ExternalInput").ap()
    eye_in = nc.dram_tensor("eye_in", [128, 128], BF16, kind="ExternalInput").ap()
    io["kv_in"] = nc.dram_tensor("kv_in", [128, 32768], BF16, kind="Internal").ap()
    io["kv_g"] = nc.dram_tensor("kv_g", [512, 32768], BF16, kind="Internal").ap()
    io["x1_ext"] = nc.dram_tensor("x1", [SC, D], F32, kind="ExternalOutput").ap()
    if dbg:
        io["qt_d"] = nc.dram_tensor("qt_d", [128, 16384], BF16, kind="ExternalOutput").ap()
        io["kv_d"] = nc.dram_tensor("kv_d", [128, 32768], BF16, kind="ExternalOutput").ap()
        io["kvg_d"] = nc.dram_tensor("kvg_d", [512, 32768], BF16, kind="ExternalOutput").ap()
        io["dn_d"] = nc.dram_tensor("dn_d", [128, 64], F32, kind="ExternalOutput").ap()
    ctx = ExitStack()
    with ctx:
        bufs = alloc_common(nc, ctx)
        sem = make_sems(nc, ctx, A_SEMS)
        ops = build_stage_a(sem, bufs, io, store_out=True, dbg=dbg)
        block = ctx.enter_context(nc.Block())

        @block.sync
        def _(e):
            e.dma_start(out=bufs.eye, in_=eye_in).then_inc(sem["aEYE"], 16)
            ops["sync"](e)

        @block.vector
        def _(e):
            ops["vector"](e)

        @block.scalar
        def _(e):
            ops["scalar"](e)

        @block.tensor
        def _(e):
            ops["tensor"](e)

        @block.gpsimd
        def _(e):
            ops["gpsimd"](e)
    return nc


C_SEMS = ["cXR", "cS2", "cVR", "cRS", "cLX", "cDV", "cZ", "cTR", "cLT",
          "cWL", "cWC", "cWT", "cWE", "cGM", "cGE", "cSS", "cDM", "cBP",
          "cYM", "cYE", "cX2", "cRES", "cOUT"]

C2 = 0.02 / 32.0
C3 = 0.02 / 32.0


def build_stage_c(sem, bufs, io, store_out, resident_in, gate_in=None, trunc=None):
    R = bufs.R_f32
    ln2T = bufs.A_bf
    OT = bufs.B_bf
    GT = bufs.C_bf
    WQ = bufs.W_bf
    WK = bufs.pT
    xa = bufs.xa
    lnbf = bufs.lnbf
    sqscr = bufs.sqscr
    st = bufs.stats
    epsc = bufs.epsc
    eye = bufs.eye
    ps = bufs.ps
    psbf = bufs.psbf
    wbf = bufs.wbf

    w = io["w"]
    mask_lt = bufs.mask_lt
    mask_prev = bufs.mask_prev
    first_m = bufs.first_m

    pay = xa[:, 0:5]
    pay_g = bufs.ccs[:, 0:20].rearrange("p (j f) -> p j f", f=5)
    carryK = xa[:, 32:33]
    carryKV = xa[:, 33:34]
    vprev = xa[:, 34:35]
    qprev = xa[:, 35:36]
    gprev = xa[:, 36:37]
    t_a = xa[:, 38:39]
    t_b = xa[:, 39:40]
    oarprev = xa[:, 40:41]
    pay2 = xa[:, 42:43]
    pay2_g = bufs.ccs[:, 24:28]
    carry2 = xa[:, 48:49]
    bq1T = bufs.ccs[:, 24:32]
    bk1T = bufs.ccs[:, 32:40]
    bk2T = bufs.ccs[:, 40:48]
    bq1c = bufs.ccs[:, 48:56]
    bk2pc = bufs.ccs[:, 56:64]
    bprow = xa[0:1, 96:1120]
    bff = bufs.B_bf.bitcast(F32)
    wslab_pre = [bff[:, 4096:5120], bff[:, 5120:6144]]
    wslab_post = [xa[:, 0:1024], xa[:, 1024:2048]]
    wcast = [lnbf[:, 0:1024], lnbf[:, 1024:2048]]
    bprow_bf = ln2T[0:1, 0:1024]
    ones1_bf = ln2T[0:1, 1024:1152]

    def slot(i):
        return R[:, 2048 * i:2048 * i + 2048]

    def stc(t, i):
        o = 16 * (t % 2) + i
        return st[:, o:o + 1]

    def gate(e):
        if gate_in is not None:
            e.wait_ge(gate_in[0], gate_in[1])

    NWC = 6
    WMATS = ["Wk2", "Wq1", "Wk1", "Wproj"]
    WDST = [WQ, WQ, WK, WQ]

    tc = TokenChain(sem["cSS"], sem["cDM"])

    def _chain():
        tc.add("vector", lambda e: e.tensor_scalar(
            out=bq1c, in0=bq1T, scalar1=0.5, scalar2=None, op0=ALU.mult))
        tc.add("vector", lambda e: e.tensor_scalar(
            out=bk2pc, in0=bk2T, scalar1=C2 / 2.0, scalar2=None, op0=ALU.mult))
        sK, sKV, sQ, sKc, sKVc, sS, sOar, sE = [slot(i) for i in range(8)]
        for dc in range(8):
            def mmp(e, wmat, dc):
                last = None
                for di in range(8):
                    kst = wmat[:, di * 1024 + dc * 128:di * 1024 + dc * 128 + 128]
                    for s4 in range(4):
                        last = e.matmul(
                            ps[s4][:], kst,
                            ln2T[:, di * 2048 + 512 * s4:di * 2048 + 512 * s4 + 512],
                            start=(di == 0), stop=(di == 7),
                            skip_group_check=True)
                return last
            tc.add("tensor", lambda e, dc=dc: mmp(e, WK, dc))

            def evk(e, dc=dc):
                last = None
                for s4 in range(4):
                    last = e.activation(out=slot(0)[:, 512 * s4:512 * s4 + 512],
                                        in_=ps[s4][:], func=AF.Exp,
                                        bias=bk1T[:, dc:dc + 1])
                return last
            tc.add("scalar", evk)
            tc.add("tensor", lambda e, dc=dc: mmp(e, WQ, dc))

            def evq(e, dc=dc):
                last = None
                for s4 in range(4):
                    last = e.activation(out=slot(2)[:, 512 * s4:512 * s4 + 512],
                                        in_=ps[s4][:], func=AF.Tanh,
                                        scale=0.5, bias=bq1c[:, dc:dc + 1])
                return last
            tc.add("scalar", evq)

            def v1(e, dc=dc):
                e.tensor_scalar(out=sQ, in0=sQ, scalar1=0.5, scalar2=0.5,
                                op0=ALU.mult, op1=ALU.add)
                e.tensor_mul(out=sKV, in0=sK,
                             in1=ln2T[:, dc * 2048:dc * 2048 + 2048])
                e.tensor_tensor_scan(out=sKc, data0=sK, data1=sK, initial=0.0,
                                     op0=ALU.add, op1=ALU.bypass)
                return e.tensor_tensor_scan(out=sKVc, data0=sKV, data1=sKV,
                                            initial=0.0, op0=ALU.add,
                                            op1=ALU.bypass)
            tc.add("vector", v1)

            def v2(e, dc=dc):
                e.tensor_copy(out=pay[:, 0:1], in_=sKc[:, 2047:2048])
                e.tensor_copy(out=pay[:, 2:3],
                              in_=ln2T[:, dc * 2048 + 2047:dc * 2048 + 2048])
                e.tensor_copy(out=pay[:, 3:4], in_=sQ[:, 2047:2048])
                e.tensor_copy(out=pay[:, 4:5],
                              in_=GT[:, dc * 2048 + 2047:dc * 2048 + 2048])
                return e.tensor_copy(out=pay[:, 1:2], in_=sKVc[:, 2047:2048])
            tc.add("vector", v2)
            tc.mark(f"scans{dc}")
            tc.add("gpsimd", lambda e, dc=dc: e.dma_start(
                out=io["cc1_in"][dc], in_=pay,
                allow_slow_non_contiguous=True), dma=True)
            tc.add("gpsimd", lambda e, dc=dc: e.collective_compute(
                "AllGather", ALU.bypass, replica_groups=RG,
                ins=[io["cc1_in"][dc].opt()], outs=[io["cc1_out"][dc].opt()]))
            tc.add("gpsimd", lambda e, dc=dc: e.dma_start(
                out=pay_g, in_=io["cc1_out"][dc].rearrange("(j p) f -> p j f",
                                                           p=128)), dma=True)

            def b1(e, dc=dc):
                e.tensor_scalar(out=carryK, in0=pay_g[:, 0, 0:1],
                                scalar1=mask_lt[:, 0:1], scalar2=None,
                                op0=ALU.mult)
                e.tensor_scalar(out=carryKV, in0=pay_g[:, 0, 1:2],
                                scalar1=mask_lt[:, 0:1], scalar2=None,
                                op0=ALU.mult)
                e.tensor_scalar(out=vprev, in0=pay_g[:, 0, 2:3],
                                scalar1=mask_prev[:, 0:1], scalar2=None,
                                op0=ALU.mult)
                e.tensor_scalar(out=qprev, in0=pay_g[:, 0, 3:4],
                                scalar1=mask_prev[:, 0:1], scalar2=None,
                                op0=ALU.mult)
                e.tensor_scalar(out=gprev, in0=pay_g[:, 0, 4:5],
                                scalar1=mask_prev[:, 0:1], scalar2=None,
                                op0=ALU.mult)
                for j in range(1, 4):
                    e.scalar_tensor_tensor(out=carryK, in0=pay_g[:, j, 0:1],
                                           scalar=mask_lt[:, j:j + 1],
                                           in1=carryK, op0=ALU.mult,
                                           op1=ALU.add)
                    e.scalar_tensor_tensor(out=carryKV, in0=pay_g[:, j, 1:2],
                                           scalar=mask_lt[:, j:j + 1],
                                           in1=carryKV, op0=ALU.mult,
                                           op1=ALU.add)
                    e.scalar_tensor_tensor(out=vprev, in0=pay_g[:, j, 2:3],
                                           scalar=mask_prev[:, j:j + 1],
                                           in1=vprev, op0=ALU.mult, op1=ALU.add)
                    e.scalar_tensor_tensor(out=qprev, in0=pay_g[:, j, 3:4],
                                           scalar=mask_prev[:, j:j + 1],
                                           in1=qprev, op0=ALU.mult, op1=ALU.add)
                    e.scalar_tensor_tensor(out=gprev, in0=pay_g[:, j, 4:5],
                                           scalar=mask_prev[:, j:j + 1],
                                           in1=gprev, op0=ALU.mult, op1=ALU.add)
                e.tensor_scalar(out=t_a, in0=carryK, scalar1=1e-6,
                                scalar2=None, op0=ALU.add)
                e.tensor_scalar(out=sKc, in0=sKc, scalar1=carryK, scalar2=1e-6,
                                op0=ALU.add, op1=ALU.add)
                e.tensor_scalar(out=sKVc, in0=sKVc, scalar1=carryKV,
                                scalar2=None, op0=ALU.add)
                e.reciprocal(out=t_b, in_=t_a)
                e.reciprocal(out=sK, in_=sKc)
                e.tensor_mul(out=t_a, in0=carryKV, in1=t_b)
                e.tensor_mul(out=sKVc, in0=sKVc, in1=sK)
                e.tensor_mul(out=oarprev, in0=t_a, in1=qprev)
                e.tensor_mul(out=sKV, in0=sKVc, in1=sQ)
                e.tensor_scalar(out=sKc[:, 0:1],
                                in0=ln2T[:, dc * 2048:dc * 2048 + 1],
                                scalar1=oarprev, scalar2=first_m[:, 0:1],
                                op0=ALU.subtract, op1=ALU.mult)
                return e.tensor_sub(out=sKc[:, 1:2048],
                                    in0=ln2T[:, dc * 2048 + 1:dc * 2048 + 2048],
                                    in1=sKV[:, 0:2047])
            tc.add("vector", b1)

            def b2(e, dc=dc):
                e.activation(out=sS[:, 1:2048],
                             in_=GT[:, dc * 2048:dc * 2048 + 2047],
                             func=AF.Tanh, scale=C2 / 2.0,
                             bias=bk2pc[:, dc:dc + 1])
                return e.activation(out=sS[:, 0:1], in_=gprev, func=AF.Tanh,
                                    scale=C2 / 2.0, bias=bk2pc[:, dc:dc + 1])
            tc.add("scalar", b2)

            def b3(e):
                e.tensor_scalar(out=sS, in0=sS, scalar1=0.5, scalar2=0.5,
                                op0=ALU.mult, op1=ALU.add)
                e.tensor_mul(out=sKc, in0=sS, in1=sKc)
                return e.tensor_tensor_scan(out=sK, data0=sKc, data1=sKc,
                                            initial=0.0, op0=ALU.add,
                                            op1=ALU.bypass)
            tc.add("vector", b3)
            tc.add("gpsimd", lambda e, dc=dc: e.dma_start(
                out=io["cc2_in"][dc], in_=sK[:, 2047:2048],
                allow_slow_non_contiguous=True), dma=True)
            tc.add("gpsimd", lambda e, dc=dc: e.collective_compute(
                "AllGather", ALU.bypass, replica_groups=RG,
                ins=[io["cc2_in"][dc].opt()], outs=[io["cc2_out"][dc].opt()]))
            tc.add("gpsimd", lambda e, dc=dc: e.dma_start(
                out=pay2_g, in_=io["cc2_out"][dc].rearrange("(j p) f -> p (j f)",
                                                            p=128),
                allow_slow_non_contiguous=True), dma=True)
            for j in range(4):
                def acc3(e, j=j):
                    if j == 0:
                        return e.tensor_scalar(out=carry2, in0=pay2_g[:, 0:1],
                                               scalar1=mask_lt[:, 0:1],
                                               scalar2=None, op0=ALU.mult)
                    return e.scalar_tensor_tensor(out=carry2,
                                                  in0=pay2_g[:, j:j + 1],
                                                  scalar=mask_lt[:, j:j + 1],
                                                  in1=carry2, op0=ALU.mult,
                                                  op1=ALU.add)
                tc.add("vector", acc3)

            def b9(e, dc=dc):
                e.tensor_scalar(out=sK, in0=sK, scalar1=carry2, scalar2=None,
                                op0=ALU.add)
                e.scalar_tensor_tensor(out=sS[:, 0:1], in0=sK[:, 0:1],
                                       scalar=C3, in1=qprev, op0=ALU.mult,
                                       op1=ALU.mult)
                e.scalar_tensor_tensor(out=sS[:, 1:2048], in0=sK[:, 1:2048],
                                       scalar=C3, in1=sQ[:, 0:2047],
                                       op0=ALU.mult, op1=ALU.mult)
                return e.tensor_add(out=OT[:, dc * 2048:dc * 2048 + 2048],
                                    in0=sKV, in1=sS)
            tc.add("vector", b9)
            tc.mark(f"done{dc}")
    _chain()
    TCK = tc.k

    def sync_ops(e):
        gate(e)
        e.dma_start(out=bq1T, in_=w["bq1"].rearrange("(c p) -> p c", p=128),
                    allow_slow_non_contiguous=True).then_inc(sem["cXR"], 16)
        e.wait_ge(sem["cXR"], 16)
        e.dma_start(out=bk1T, in_=w["bk1"].rearrange("(c p) -> p c", p=128),
                    allow_slow_non_contiguous=True).then_inc(sem["cXR"], 16)
        e.wait_ge(sem["cXR"], 32)
        e.dma_start(out=bk2T, in_=w["bk2"].rearrange("(c p) -> p c", p=128),
                    allow_slow_non_contiguous=True).then_inc(sem["cXR"], 16)
        e.wait_ge(sem["cXR"], 48)
        e.dma_start(out=mask_lt, in_=io["mask_lt"]).then_inc(sem["cXR"], 16)
        e.wait_ge(sem["cXR"], 64)
        e.dma_start(out=mask_prev, in_=io["mask_prev"]).then_inc(sem["cXR"], 16)
        e.wait_ge(sem["cXR"], 80)
        e.dma_start(out=first_m, in_=io["first_m"]).then_inc(sem["cXR"], 16)
        if not resident_in:
            xr = io["x_in"].rearrange("(t p) e -> t p e", p=128)
            for t in range(NT):
                e.wait_ge(sem["cXR"], 16 * (NWC + t))
                e.dma_start(out=R[:, 1024 * t:1024 * t + 1024], in_=xr[t]) \
                    .then_inc(sem["cXR"], 16)
        else:
            e.wait_ge(sem["cZ"], NT)
            e.wait_ge(sem["cS2"], NT)
            xo = io["x2_sp"].rearrange("(t p) e -> t p e", p=128)
            for t in range(NT):
                e.dma_start(out=xo[t], in_=R[:, 1024 * t:1024 * t + 1024]) \
                    .then_inc(sem["cX2"], 16)
            e.wait_ge(sem["cX2"], 16 * NT)
        if trunc == -1:
            return
        nmat = 4 if trunc is None else (1 if trunc == -4 else 3)
        for m in range(nmat):
            wm = w[WMATS[m]]
            for eo in range(8):
                n = m * 8 + eo
                if n == 24:
                    e.wait_ge(sem["cSS"], TCK)
                if n >= 1:
                    e.wait_ge(sem["cWL"], 16 * n)
                if n >= 2:
                    e.wait_ge(sem["cWC"], n - 1)
                dst = (wslab_post if n >= 24 else wslab_pre)[n % 2]
                e.dma_start(out=dst, in_=wm[128 * eo:128 * eo + 128, :]) \
                    .then_inc(sem["cWL"], 16)
        if trunc is not None:
            return
        e.wait_ge(sem["cWC"], 32)
        e.dma_start(out=bprow, in_=w["bproj"][None, :]).then_inc(sem["cXR"], 16)
        xsrc = io["x_in"] if not resident_in else io["x2_sp"]
        xr2 = xsrc.rearrange("(t p) e -> t p e", p=128)
        e.wait_ge(sem["cSS"], TCK)
        e.wait_ge(sem["cWC"], 32)
        e.wait_ge(sem["cBP"], 1)
        base = 16 * NT if resident_in else 0
        for t in range(NT):
            if t >= 1:
                e.wait_ge(sem["cX2"], base + 16 * t)
            if t >= 2:
                e.wait_ge(sem["cRES"], t - 1)
            e.dma_start(out=xa[:, 1024 * (t % 2):1024 * (t % 2) + 1024],
                        in_=xr2[t]).then_inc(sem["cX2"], 16)

    def vector_ops(e):
        gate(e)
        for t in range(NT):
            if not resident_in:
                e.wait_ge(sem["cXR"], 16 * (NWC + t + 1))
            else:
                e.wait_ge(sem["cXR"], 16 * NWC)
            Rt = R[:, 1024 * t:1024 * t + 1024]
            e.reduce_sum(out=stc(t, 0), in_=Rt, axis=AX)
            e.wait_ge(sem["cS2"], t + 1)
            e.tensor_add(out=stc(t, 2), in0=stc(t, 2), in1=stc(t, 9)) \
                .then_inc(sem["cDV"], 1)
            e.wait_ge(sem["cDV"], 2 * t + 1)
            e.tensor_scalar(out=stc(t, 1), in0=stc(t, 0), scalar1=1.0 / D,
                            scalar2=None, op0=ALU.mult)
            e.tensor_scalar(out=stc(t, 3), in0=stc(t, 2), scalar1=1.0 / D,
                            scalar2=None, op0=ALU.mult)
            e.scalar_tensor_tensor(out=stc(t, 4), in0=stc(t, 0),
                                   scalar=-1.0 / (D * D), in1=stc(t, 0),
                                   op0=ALU.mult, op1=ALU.mult) \
                .then_inc(sem["cDV"], 1)
            e.wait_ge(sem["cDV"], 2 * t + 2)
            e.tensor_add(out=stc(t, 5), in0=stc(t, 3), in1=stc(t, 4)) \
                .then_inc(sem["cVR"], 1)
            e.wait_ge(sem["cRS"], t + 1)
            e.tensor_scalar(out=lnbf[:, 1024 * (t % 2):1024 * (t % 2) + 1024],
                            in0=Rt, scalar1=stc(t, 1), scalar2=stc(t, 7),
                            op0=ALU.subtract, op1=ALU.mult) \
                .then_inc(sem["cZ"], 1)
        ncast_pre = {None: 24, -1: 0, -3: 32, -4: 8}.get(trunc, 24)
        for n in range(ncast_pre):
            if n == 0:
                e.wait_ge(sem["cLT"], NT)
            e.wait_ge(sem["cWL"], 16 * (n + 1))
            if n >= 2 and trunc != -3:
                e.wait_ge(sem["cWT"], 8 * (n - 1))
            e.tensor_copy(out=wcast[n % 2],
                          in_=wslab_pre[n % 2]).then_inc(sem["cWC"], 1)
        e.wait_ge(sem["cXR"], 16 * NWC)
        if trunc is None or trunc >= 0:
            e.wait_ge(sem["cGE"], 32)
        if resident_in:
            e.wait_ge(sem["cX2"], 16 * NT)
        tc.emit("vector", e, limit=max(trunc, 0) if trunc is not None else None)
        if trunc is not None:
            return
        for n in range(24, 32):
            e.wait_ge(sem["cWL"], 16 * (n + 1))
            e.wait_ge(sem["cWT"], 8 * (n - 1))
            e.tensor_copy(out=wcast[n % 2],
                          in_=wslab_post[n % 2]).then_inc(sem["cWC"], 1)
        nxr = NWC + (0 if resident_in else NT)
        e.wait_ge(sem["cXR"], 16 * (nxr + 1))
        e.wait_ge(sem["cWT"], 256)
        e.tensor_copy(out=bprow_bf, in_=bprow)
        e.memset(ones1_bf, 1.0).then_inc(sem["cBP"], 1)
        for t in range(NT):
            Rt = R[:, 1024 * t:1024 * t + 1024]
            e.wait_ge(sem["cYE"], 2 * t + 2)
            e.reduce_sum(out=stc(t, 0), in_=Rt, axis=AX)
            e.wait_ge(sem["cS2"], NT + t + 1)
            e.tensor_add(out=stc(t, 2), in0=stc(t, 2), in1=stc(t, 9)) \
                .then_inc(sem["cDV"], 1)
            e.wait_ge(sem["cDV"], 2 * NT + 3 * t + 1)
            e.tensor_scalar(out=stc(t, 1), in0=stc(t, 0), scalar1=1.0 / D,
                            scalar2=None, op0=ALU.mult)
            e.tensor_scalar(out=stc(t, 3), in0=stc(t, 2), scalar1=1.0 / D,
                            scalar2=None, op0=ALU.mult)
            e.scalar_tensor_tensor(out=stc(t, 4), in0=stc(t, 0),
                                   scalar=-1.0 / (D * D), in1=stc(t, 0),
                                   op0=ALU.mult, op1=ALU.mult) \
                .then_inc(sem["cDV"], 1)
            e.wait_ge(sem["cDV"], 2 * NT + 3 * t + 2)
            e.tensor_add(out=stc(t, 5), in0=stc(t, 3), in1=stc(t, 4)) \
                .then_inc(sem["cVR"], 1)
            e.wait_ge(sem["cRS"], NT + t + 1)
            e.tensor_scalar(out=Rt, in0=Rt, scalar1=stc(t, 1),
                            scalar2=stc(t, 7), op0=ALU.subtract, op1=ALU.mult) \
                .then_inc(sem["cDV"], 1)
            e.wait_ge(sem["cDV"], 2 * NT + 3 * t + 3)
            base = 16 * NT if resident_in else 0
            e.wait_ge(sem["cX2"], base + 16 * (t + 1))
            e.tensor_add(out=Rt, in0=Rt,
                         in1=xa[:, 1024 * (t % 2):1024 * (t % 2) + 1024]) \
                .then_inc(sem["cRES"], 1)

    def scalar_ops(e):
        gate(e)
        for t in range(NT):
            if not resident_in:
                e.wait_ge(sem["cXR"], 16 * (NWC + t + 1))
            else:
                e.wait_ge(sem["cXR"], 16 * NWC)
            Rt = R[:, 1024 * t:1024 * t + 1024]
            e.activation(out=sqscr[:], in_=Rt[:, 0:512], func=AF.Square,
                         accum_out=stc(t, 2))
            e.activation(out=sqscr[:], in_=Rt[:, 512:1024], func=AF.Square,
                         accum_out=stc(t, 9)).then_inc(sem["cS2"], 1)
            e.wait_ge(sem["cVR"], t + 1)
            e.activation(out=stc(t, 6), in_=stc(t, 5), func=AF.Ln,
                         bias=epsc[:]).then_inc(sem["cLX"], 1)
            e.wait_ge(sem["cLX"], t + 1)
            e.activation(out=stc(t, 7), in_=stc(t, 6), func=AF.Exp,
                         scale=-0.5).then_inc(sem["cRS"], 1)
            e.wait_ge(sem["cTR"], t + 1)
            dst = ln2T.rearrange("p (c s) -> p c s", c=8)[:, :, 128 * t:128 * t + 128]
            e.activation(out=dst, in_=psbf.rearrange("p (c s) -> p c s", c=8),
                         func=AF.Copy).then_inc(sem["cLT"], 1)
        for n in range(0 if trunc in (-1, -3) else 64):
            e.wait_ge(sem["cWT"], n + 1)
            eo, do = n // 8, n % 8
            e.activation(out=WQ[:, do * 1024 + eo * 128:do * 1024 + eo * 128 + 128],
                         in_=psbf[:, 0:128], func=AF.Copy).then_inc(sem["cWE"], 1)
        for k in range(0 if (trunc is not None and trunc < 0) else 32):
            e.wait_ge(sem["cGM"], k + 1)
            dc, s4 = k // 4, k % 4
            e.activation(out=GT[:, dc * 2048 + 512 * s4:dc * 2048 + 512 * s4 + 512],
                         in_=ps[k % 2][:], func=AF.Copy).then_inc(sem["cGE"], 1)
        for m in ([] if trunc in (-1, -3, -4) else range(1, 3)):
            for i in range(64):
                n = m * 64 + i
                eo, do = i // 8, i % 8
                if m == 1 and i == 0:
                    e.wait_ge(sem["cGM"], 32)
                e.wait_ge(sem["cWT"], n + 1)
                e.activation(out=WDST[m][:, do * 1024 + eo * 128:do * 1024 + eo * 128 + 128],
                             in_=psbf[:, 0:128], func=AF.Copy).then_inc(sem["cWE"], 1)
        if resident_in:
            e.wait_ge(sem["cX2"], 16 * NT)
        tc.emit("scalar", e, limit=max(trunc, 0) if trunc is not None else None)
        if trunc is not None:
            return
        for i in range(64):
            n = 192 + i
            eo, do = i // 8, i % 8
            e.wait_ge(sem["cWT"], n + 1)
            e.activation(out=WQ[:, do * 1024 + eo * 128:do * 1024 + eo * 128 + 128],
                         in_=psbf[:, 0:128], func=AF.Copy).then_inc(sem["cWE"], 1)
        for t in range(NT):
            for hh in range(2):
                n = 2 * t + hh
                e.wait_ge(sem["cYM"], n + 1)
                e.activation(out=R[:, 1024 * t + 512 * hh:1024 * t + 512 * hh + 512],
                             in_=ps[2 + n % 2][:], func=AF.Copy) \
                    .then_inc(sem["cYE"], 1)
            e.wait_ge(sem["cYE"], 2 * t + 2)
            Rt = R[:, 1024 * t:1024 * t + 1024]
            e.activation(out=sqscr[:], in_=Rt[:, 0:512], func=AF.Square,
                         accum_out=stc(t, 2))
            e.activation(out=sqscr[:], in_=Rt[:, 512:1024], func=AF.Square,
                         accum_out=stc(t, 9)).then_inc(sem["cS2"], 1)
            e.wait_ge(sem["cVR"], NT + t + 1)
            e.activation(out=stc(t, 6), in_=stc(t, 5), func=AF.Ln,
                         bias=epsc[:]).then_inc(sem["cLX"], 1)
            e.wait_ge(sem["cLX"], NT + t + 1)
            e.activation(out=stc(t, 7), in_=stc(t, 6), func=AF.Exp,
                         scale=-0.5).then_inc(sem["cRS"], 1)

    def tensor_ops(e):
        gate(e)
        for t in range(NT):
            e.wait_ge(sem["cZ"], t + 1)
            if t >= 1:
                e.wait_ge(sem["cLT"], t)
            srcz = lnbf[:, 1024 * (t % 2):1024 * (t % 2) + 1024]
            last = None
            for dc in range(8):
                last = e.transpose(psbf[:, 128 * dc:128 * dc + 128],
                                   srcz[:, 128 * dc:128 * dc + 128], eye[:])
            last.then_inc(sem["cTR"], 1)
        e.wait_ge(sem["cLT"], NT)
        for n in range(0 if trunc in (-1, -3) else 64):
            slab, do = n // 8, n % 8
            e.wait_ge(sem["cWC"], slab + 1)
            if n >= 1:
                e.wait_ge(sem["cWE"], n)
            e.transpose(psbf[:, 0:128],
                        wcast[slab % 2][:, 128 * do:128 * do + 128],
                        eye[:]).then_inc(sem["cWT"], 1)
        if trunc is None or trunc >= 0:
            e.wait_ge(sem["cWE"], 64)
        for k in range(0 if (trunc is not None and trunc < 0) else 32):
            dc, s4 = k // 4, k % 4
            if k >= 2:
                e.wait_ge(sem["cGE"], k - 1)
            last = None
            for di in range(8):
                last = e.matmul(
                    ps[k % 2][:],
                    WQ[:, di * 1024 + dc * 128:di * 1024 + dc * 128 + 128],
                    ln2T[:, di * 2048 + 512 * s4:di * 2048 + 512 * s4 + 512],
                    start=(di == 0), stop=(di == 7))
            last.then_inc(sem["cGM"], 1)
        for n in ([] if trunc in (-1, -3, -4) else range(64, 192)):
            slab, do = n // 8, n % 8
            e.wait_ge(sem["cWC"], slab + 1)
            e.wait_ge(sem["cWE"], n)
            e.transpose(psbf[:, 0:128],
                        wcast[slab % 2][:, 128 * do:128 * do + 128],
                        eye[:]).then_inc(sem["cWT"], 1)
        if trunc is None or trunc > 0:
            e.wait_ge(sem["cWE"], 192)
        tc.emit("tensor", e, limit=max(trunc, 0) if trunc is not None else None)
        if trunc is not None:
            return
        for n in range(192, 256):
            slab, do = n // 8, n % 8
            e.wait_ge(sem["cWC"], slab + 1)
            e.wait_ge(sem["cWE"], n)
            e.transpose(psbf[:, 0:128],
                        wcast[slab % 2][:, 128 * do:128 * do + 128],
                        eye[:]).then_inc(sem["cWT"], 1)
        e.wait_ge(sem["cSS"], TCK)
        e.wait_ge(sem["cWE"], 256)
        e.wait_ge(sem["cBP"], 1)
        for t in range(NT):
            for hh in range(2):
                n = 2 * t + hh
                if n >= 2:
                    e.wait_ge(sem["cYE"], n - 1)
                pst = ps[2 + n % 2]
                last = None
                for di in range(8):
                    last = e.matmul(
                        pst[:], OT[:, di * 2048 + 128 * t:di * 2048 + 128 * t + 128],
                        WQ[:, di * 1024 + 512 * hh:di * 1024 + 512 * hh + 512],
                        start=(di == 0), stop=False)
                e.matmul(pst[:], ones1_bf,
                         bprow_bf[:, 512 * hh:512 * hh + 512],
                         start=False, stop=True).then_inc(sem["cYM"], 1)

    def gpsimd_ops(e):
        gate(e)
        tc.emit("gpsimd", e, limit=max(trunc, 0) if trunc is not None else None)
        if trunc is not None:
            kk, mm, lcc = tc.counts_at(max(trunc, 0))
            if kk > 0:
                e.wait_ge(sem["cSS"], kk)
            if mm > 0:
                e.wait_ge(sem["cDM"], 16 * mm)
            if lcc is not None:
                e.wait_ge(lcc, 1)
            e.dma_start(out=io["r_d"], in_=R[:]).then_inc(sem["cOUT"], 16)
            e.dma_start(out=io["ot_d"], in_=OT[:]).then_inc(sem["cOUT"], 16)
            e.dma_start(out=io["xa_d"], in_=xa[:]).then_inc(sem["cOUT"], 16)
            for dcq in range(8):
                e.dma_start(out=io["cc_d"][:, 5 * dcq:5 * dcq + 5],
                            in_=io["cc1_out"][dcq]).then_inc(sem["cOUT"], 16)
                e.dma_start(out=io["cc_d"][:, 40 + dcq:40 + dcq + 1],
                            in_=io["cc2_out"][dcq],
                            allow_slow_non_contiguous=True).then_inc(sem["cOUT"], 16)
            e.wait_ge(sem["cOUT"], 48 + 256)
            return
        if store_out:
            e.wait_ge(sem["cRES"], NT)
            xo = io["x_out"].rearrange("(q p) e -> q p e", p=128)
            for t in range(NT):
                e.dma_start(out=xo[t], in_=R[:, 1024 * t:1024 * t + 1024]) \
                    .then_inc(sem["cOUT"], 16)
            e.wait_ge(sem["cOUT"], 16 * NT)

    return dict(sync=sync_ops, vector=vector_ops, scalar=scalar_ops,
                tensor=tensor_ops, gpsimd=gpsimd_ops, TCK=TCK, tc=tc,
                marks=tc.marks)


D_SEMS = ["dXR", "dWL", "dWC", "dS2", "dVR", "dRS", "dLX", "dDV", "dZ",
          "dTR", "dLT", "dPM", "dPEv", "dHM",
          "dHE", "dRM", "dRE", "dYM", "dYE", "dRES", "dOUT", "dONE"]

GELU_C = float(np.sqrt(2.0 / np.pi))


def build_stage_d(sem, bufs, io, store_out, resident_in, gate_in=None):
    R = bufs.R_f32
    ln3T = bufs.A_bf
    HT = bufs.B_bf
    u2s = bufs.C_bf
    v1s = bufs.W_bf
    pTb = bufs.pT
    xa = bufs.xa
    lnbf = bufs.lnbf
    sqscr = bufs.sqscr
    st = bufs.stats
    epsc = bufs.epsc
    eye = bufs.eye
    ps = bufs.ps
    psbf = bufs.psbf

    u1s = pTb[:, 0:2048]
    v2s = pTb[:, 2048:4096]
    b2row = pTb[0:1, 4096:5120]
    PTs = [pTb[:, 5120:6144], pTb[:, 6144:7168]]
    RTs = pTb[:, 7168:8192]
    ones1_bf = lnbf[0:1, 0:128]
    b1T = bufs.dn_sb[:, 0:32]

    w = io["w"]

    def stc(t, i):
        o = 16 * (t % 2) + i
        return st[:, o:o + 1]

    def gate(e):
        if gate_in is not None:
            e.wait_ge(gate_in[0], gate_in[1])

    def sync_ops(e):
        gate(e)
        e.dma_start(out=b1T, in_=w["b1"].rearrange("(c p) -> p c", p=128),
                    allow_slow_non_contiguous=True).then_inc(sem["dXR"], 16)
        if not resident_in:
            xr = io["x_in"].rearrange("(t p) e -> t p e", p=128)
            for t in range(NT):
                e.wait_ge(sem["dXR"], 16 * (t + 1))
                e.dma_start(out=R[:, 1024 * t:1024 * t + 1024], in_=xr[t]) \
                    .then_inc(sem["dXR"], 16)
        loads = []
        loads.append(w["U1"].rearrange("(c p) r -> p c r", p=128))
        for rc in range(2):
            for hh in range(2):
                loads.append(w["V1"][128 * rc:128 * rc + 128,
                                     2048 * hh:2048 * hh + 2048])
        for c0 in range(4):
            loads.append(w["U2"][1024 * c0:1024 * (c0 + 1)]
                         .rearrange("(c p) r -> p c r", p=128))
        loads.append(w["V2"].rearrange("(c p) e -> p c e", p=128))
        loads.append(w["b2"][None, :])
        for k, srcap in enumerate(loads):
            if k >= 1:
                e.wait_ge(sem["dWC"], k)
            if k == 10:
                dst = xa[0:1, 0:1024]
            elif len(srcap.shape) == 3:
                dst = xa[:, 0:srcap.shape[1] * srcap.shape[2]] \
                    .rearrange("p (c r) -> p c r", c=srcap.shape[1])
            else:
                dst = xa[:, 0:srcap.shape[1]]
            e.dma_start(out=dst, in_=srcap).then_inc(sem["dWL"], 16)

    def vector_ops(e):
        gate(e)
        for t in range(NT):
            if not resident_in:
                e.wait_ge(sem["dXR"], 16 * (t + 2))
            else:
                e.wait_ge(sem["dXR"], 16)
            Rt = R[:, 1024 * t:1024 * t + 1024]
            e.reduce_sum(out=stc(t, 0), in_=Rt, axis=AX)
            e.wait_ge(sem["dS2"], t + 1)
            e.tensor_add(out=stc(t, 2), in0=stc(t, 2), in1=stc(t, 9)) \
                .then_inc(sem["dDV"], 1)
            e.wait_ge(sem["dDV"], 2 * t + 1)
            e.tensor_scalar(out=stc(t, 1), in0=stc(t, 0), scalar1=1.0 / D,
                            scalar2=None, op0=ALU.mult)
            e.tensor_scalar(out=stc(t, 3), in0=stc(t, 2), scalar1=1.0 / D,
                            scalar2=None, op0=ALU.mult)
            e.scalar_tensor_tensor(out=stc(t, 4), in0=stc(t, 0),
                                   scalar=-1.0 / (D * D), in1=stc(t, 0),
                                   op0=ALU.mult, op1=ALU.mult) \
                .then_inc(sem["dDV"], 1)
            e.wait_ge(sem["dDV"], 2 * t + 2)
            e.tensor_add(out=stc(t, 5), in0=stc(t, 3), in1=stc(t, 4)) \
                .then_inc(sem["dVR"], 1)
            e.wait_ge(sem["dRS"], t + 1)
            e.tensor_scalar(out=lnbf[:, 1024 * (t % 2):1024 * (t % 2) + 1024],
                            in0=Rt, scalar1=stc(t, 1), scalar2=stc(t, 7),
                            op0=ALU.subtract, op1=ALU.mult) \
                .then_inc(sem["dZ"], 1)
        cdsts = [u1s]
        for rc in range(2):
            for hh in range(2):
                cdsts.append(v1s[:, 4096 * rc + 2048 * hh:4096 * rc + 2048 * hh + 2048])
        for c0 in range(4):
            cdsts.append(u2s[:, 2048 * c0:2048 * (c0 + 1)])
        cdsts.append(v2s)
        cdsts.append(b2row)
        for k, dst in enumerate(cdsts):
            e.wait_ge(sem["dWL"], 16 * (k + 1))
            nsrc = 1024 if k == 10 else 2048
            srcv = xa[0:1, 0:1024] if k == 10 else xa[:, 0:2048]
            e.tensor_copy(out=dst, in_=srcv).then_inc(sem["dWC"], 1)
        e.wait_ge(sem["dTR"], NT)
        e.memset(ones1_bf, 1.0).then_inc(sem["dONE"], 1)
        for s4 in range(4):
            for sc in range(4):
                t = s4 * 4 + sc
                for dh in range(2):
                    n = 2 * t + dh
                    e.wait_ge(sem["dYM"], n + 1)
                    ya = xa[:, 1024 * (sc % 2) + 512 * dh:
                            1024 * (sc % 2) + 512 * dh + 512]
                    e.tensor_copy(out=ya, in_=ps[n % 2][:]).then_inc(sem["dYE"], 1)
                yt = xa[:, 1024 * (sc % 2):1024 * (sc % 2) + 1024]
                e.wait_ge(sem["dYE"], 2 * t + 2)
                e.reduce_sum(out=stc(t, 0), in_=yt, axis=AX)
                e.wait_ge(sem["dS2"], NT + t + 1)
                e.tensor_add(out=stc(t, 2), in0=stc(t, 2), in1=stc(t, 9)) \
                    .then_inc(sem["dDV"], 1)
                e.wait_ge(sem["dDV"], 2 * NT + 3 * t + 1)
                e.tensor_scalar(out=stc(t, 1), in0=stc(t, 0), scalar1=1.0 / D,
                                scalar2=None, op0=ALU.mult)
                e.tensor_scalar(out=stc(t, 3), in0=stc(t, 2), scalar1=1.0 / D,
                                scalar2=None, op0=ALU.mult)
                e.scalar_tensor_tensor(out=stc(t, 4), in0=stc(t, 0),
                                       scalar=-1.0 / (D * D), in1=stc(t, 0),
                                       op0=ALU.mult, op1=ALU.mult) \
                    .then_inc(sem["dDV"], 1)
                e.wait_ge(sem["dDV"], 2 * NT + 3 * t + 2)
                e.tensor_add(out=stc(t, 5), in0=stc(t, 3), in1=stc(t, 4)) \
                    .then_inc(sem["dVR"], 1)
                e.wait_ge(sem["dRS"], NT + t + 1)
                e.tensor_scalar(out=yt, in0=yt, scalar1=stc(t, 1),
                                scalar2=stc(t, 7), op0=ALU.subtract,
                                op1=ALU.mult).then_inc(sem["dDV"], 1)
                e.wait_ge(sem["dDV"], 2 * NT + 3 * t + 3)
                Rt = R[:, 1024 * t:1024 * t + 1024]
                e.tensor_add(out=Rt, in0=Rt, in1=yt).then_inc(sem["dRES"], 1)

    def scalar_ops(e):
        gate(e)
        for t in range(NT):
            if not resident_in:
                e.wait_ge(sem["dXR"], 16 * (t + 2))
            else:
                e.wait_ge(sem["dXR"], 16)
            Rt = R[:, 1024 * t:1024 * t + 1024]
            e.activation(out=sqscr[:], in_=Rt[:, 0:512], func=AF.Square,
                         accum_out=stc(t, 2))
            e.activation(out=sqscr[:], in_=Rt[:, 512:1024], func=AF.Square,
                         accum_out=stc(t, 9)).then_inc(sem["dS2"], 1)
            e.wait_ge(sem["dVR"], t + 1)
            e.activation(out=stc(t, 6), in_=stc(t, 5), func=AF.Ln,
                         bias=epsc[:]).then_inc(sem["dLX"], 1)
            e.wait_ge(sem["dLX"], t + 1)
            e.activation(out=stc(t, 7), in_=stc(t, 6), func=AF.Exp,
                         scale=-0.5).then_inc(sem["dRS"], 1)
            e.wait_ge(sem["dTR"], t + 1)
            dst = ln3T.rearrange("p (c s) -> p c s", c=8)[:, :, 128 * t:128 * t + 128]
            e.activation(out=dst, in_=psbf.rearrange("p (c s) -> p c s", c=8),
                         func=AF.Copy).then_inc(sem["dLT"], 1)
        for s4 in range(4):
            for rc in range(2):
                n = 2 * s4 + rc
                e.wait_ge(sem["dPM"], n + 1)
                e.activation(out=PTs[s4 % 2][:, 512 * rc:512 * rc + 512],
                             in_=ps[rc][:], func=AF.Copy).then_inc(sem["dPEv"], 1)
            for fc in range(32):
                k = s4 * 32 + fc
                e.wait_ge(sem["dHM"], k + 1)
                e.activation(out=HT[:, 512 * fc:512 * fc + 512],
                             in_=ps[fc % 2][:], func=AF.Gelu_apprx_tanh,
                             bias=b1T[:, fc:fc + 1]).then_inc(sem["dHE"], 1)
            for rc in range(2):
                n = 2 * s4 + rc
                e.wait_ge(sem["dRM"], n + 1)
                e.activation(out=RTs[:, 512 * rc:512 * rc + 512],
                             in_=ps[2 + rc][:], func=AF.Copy) \
                    .then_inc(sem["dRE"], 1)
            for sc in range(4):
                t = s4 * 4 + sc
                yt = xa[:, 1024 * (sc % 2):1024 * (sc % 2) + 1024]
                e.wait_ge(sem["dYE"], 2 * t + 2)
                e.activation(out=sqscr[:], in_=yt[:, 0:512], func=AF.Square,
                             accum_out=stc(t, 2))
                e.activation(out=sqscr[:], in_=yt[:, 512:1024], func=AF.Square,
                             accum_out=stc(t, 9)).then_inc(sem["dS2"], 1)
                e.wait_ge(sem["dVR"], NT + t + 1)
                e.activation(out=stc(t, 6), in_=stc(t, 5), func=AF.Ln,
                             bias=epsc[:]).then_inc(sem["dLX"], 1)
                e.wait_ge(sem["dLX"], NT + t + 1)
                e.activation(out=stc(t, 7), in_=stc(t, 6), func=AF.Exp,
                             scale=-0.5).then_inc(sem["dRS"], 1)

    def tensor_ops(e):
        gate(e)
        for t in range(NT):
            e.wait_ge(sem["dZ"], t + 1)
            if t >= 1:
                e.wait_ge(sem["dLT"], t)
            srcz = lnbf[:, 1024 * (t % 2):1024 * (t % 2) + 1024]
            last = None
            for dc in range(8):
                last = e.transpose(psbf[:, 128 * dc:128 * dc + 128],
                                   srcz[:, 128 * dc:128 * dc + 128], eye[:])
            last.then_inc(sem["dTR"], 1)
        e.wait_ge(sem["dLT"], NT)
        e.wait_ge(sem["dONE"], 1)
        for s4 in range(4):
            e.wait_ge(sem["dWC"], 1)
            if s4 >= 1:
                e.wait_ge(sem["dYE"], 8 * s4)
                e.wait_ge(sem["dPEv"], 2 * s4)
            for rc in range(2):
                last = None
                for dcc in range(8):
                    last = e.matmul(
                        ps[rc][:],
                        u1s[:, dcc * 256 + rc * 128:dcc * 256 + rc * 128 + 128],
                        ln3T[:, dcc * 2048 + 512 * s4:dcc * 2048 + 512 * s4 + 512],
                        start=(dcc == 0), stop=(dcc == 7))
                last.then_inc(sem["dPM"], 1)
            e.wait_ge(sem["dWC"], 5)
            e.wait_ge(sem["dPEv"], 2 * (s4 + 1))
            for fc in range(32):
                k = s4 * 32 + fc
                if k >= 2:
                    e.wait_ge(sem["dHE"], k - 1)
                last = None
                for rc in range(2):
                    last = e.matmul(
                        ps[fc % 2][:],
                        v1s[:, rc * 4096 + fc * 128:rc * 4096 + fc * 128 + 128],
                        PTs[s4 % 2][:, rc * 512:rc * 512 + 512],
                        start=(rc == 0), stop=(rc == 1))
                last.then_inc(sem["dHM"], 1)
            e.wait_ge(sem["dWC"], 9)
            e.wait_ge(sem["dHE"], 32 * (s4 + 1))
            if s4 >= 1:
                e.wait_ge(sem["dRE"], 2 * s4)
            for rc in range(2):
                last = None
                for fc in range(32):
                    last = e.matmul(
                        ps[2 + rc][:],
                        u2s[:, fc * 256 + rc * 128:fc * 256 + rc * 128 + 128],
                        HT[:, fc * 512:fc * 512 + 512],
                        start=(fc == 0), stop=(fc == 31))
                last.then_inc(sem["dRM"], 1)
            e.wait_ge(sem["dWC"], 11)
            e.wait_ge(sem["dRE"], 2 * (s4 + 1))
            for sc in range(4):
                t = s4 * 4 + sc
                for dh in range(2):
                    n = 2 * t + dh
                    if n >= 2:
                        e.wait_ge(sem["dYE"], n - 1)
                    pst = ps[n % 2]
                    for rc in range(2):
                        e.matmul(pst[:],
                                 RTs[:, rc * 512 + sc * 128:rc * 512 + sc * 128 + 128],
                                 v2s[:, rc * 1024 + dh * 512:rc * 1024 + dh * 512 + 512],
                                 start=(rc == 0), stop=False)
                    e.matmul(pst[:], ones1_bf, b2row[:, 512 * dh:512 * dh + 512],
                             start=False, stop=True).then_inc(sem["dYM"], 1)

    def gpsimd_ops(e):
        gate(e)
        if store_out:
            xo = io["y_out"].rearrange("(q p) e -> q p e", p=128)
            for t in range(NT):
                e.wait_ge(sem["dRES"], t + 1)
                e.dma_start(out=xo[t], in_=R[:, 1024 * t:1024 * t + 1024]) \
                    .then_inc(sem["dOUT"], 16)
            e.wait_ge(sem["dOUT"], 16 * NT)

    return dict(sync=sync_ops, vector=vector_ops, scalar=scalar_ops,
                tensor=tensor_ops, gpsimd=gpsimd_ops)


FFN1_KEYS = ["U1", "V1", "b1", "U2", "V2", "b2"]


def build_launch_d():
    key = ("d",)
    if key in _NC_CACHE:
        return _NC_CACHE[key]
    nc = bass.Bass("TRN2", target_bir_lowering=False, debug=False)
    io = {}
    io["x_in"] = nc.dram_tensor("x_in", [SC, D], F32, kind="ExternalInput").ap()
    w = {}
    shapes = {"U1": [D, 256], "V1": [256, 4096], "b1": [4096],
              "U2": [4096, 256], "V2": [256, D], "b2": [D]}
    for k in FFN1_KEYS:
        w[k] = nc.dram_tensor("w_" + k, shapes[k], F32, kind="ExternalInput").ap()
    io["w"] = w
    eye_in = nc.dram_tensor("eye_in", [128, 128], BF16, kind="ExternalInput").ap()
    io["y_out"] = nc.dram_tensor("y_out", [SC, D], F32, kind="ExternalOutput").ap()
    ctx = ExitStack()
    with ctx:
        bufs = alloc_common(nc, ctx)
        sem = make_sems(nc, ctx, D_SEMS + ["dEYE", "dONE2"])
        ops = build_stage_d(sem, bufs, io, store_out=True, resident_in=False)
        block = ctx.enter_context(nc.Block())

        @block.sync
        def _(e):
            e.dma_start(out=bufs.eye, in_=eye_in).then_inc(sem["dEYE"], 16)
            ops["sync"](e)

        @block.vector
        def _(e):
            e.memset(bufs.epsc[:], float(EPS)).then_inc(sem["dONE2"], 1)
            ops["vector"](e)

        @block.scalar
        def _(e):
            e.wait_ge(sem["dONE2"], 1)
            ops["scalar"](e)

        @block.tensor
        def _(e):
            e.wait_ge(sem["dEYE"], 16)
            ops["tensor"](e)

        @block.gpsimd
        def _(e):
            ops["gpsimd"](e)
    _NC_CACHE[key] = nc
    return nc


def run_stage_d(x3_full, weights):
    nc = build_launch_d()
    eye = np.eye(128, dtype=np.float16)
    in_maps = []
    for c in range(NCORES):
        b, r = c // GROUP, c % GROUP
        m = {"x_in": np.ascontiguousarray(x3_full[b, r * SC:(r + 1) * SC]),
             "eye_in": eye}
        for k in FFN1_KEYS:
            m["w_" + k] = np.ascontiguousarray(weights[k])
        in_maps.append(m)
    res = run_bass_kernel_spmd(nc, in_maps, core_ids=list(range(NCORES)))
    out = np.zeros((B, S, D), np.float32)
    for c in range(NCORES):
        b, r = c // GROUP, c % GROUP
        out[b, r * SC:(r + 1) * SC] = res.results[c]["y_out"]
    return out


FFN0_KEYS = ["ln_g", "ln_b", "bn_g", "bn_b", "bn_mean", "bn_var", "Wr", "br",
             "W_down", "b_down", "W_up", "b_up"]


def build_launch_b():
    key = ("b",)
    if key in _NC_CACHE:
        return _NC_CACHE[key]
    nc = bass.Bass("TRN2", target_bir_lowering=False, debug=False)
    io = {}
    io["x_in"] = nc.dram_tensor("x_in", [SC, D], F32, kind="ExternalInput").ap()
    w = {}
    shapes = {"ln_g": [D], "ln_b": [D], "bn_g": [D], "bn_b": [D],
              "bn_mean": [D], "bn_var": [D], "Wr": [D, 8], "br": [8],
              "W_down": [8, D, 64], "b_down": [8, 64], "W_up": [8, 64, D],
              "b_up": [8, D]}
    for k in FFN0_KEYS:
        w[k] = nc.dram_tensor("w_" + k, shapes[k], F32, kind="ExternalInput").ap()
    io["w"] = w
    eye_in = nc.dram_tensor("eye_in", [128, 128], BF16, kind="ExternalInput").ap()
    io["zs_in"] = nc.dram_tensor("zs_in", [1, D], F32, kind="Internal").ap()
    io["zs_out"] = nc.dram_tensor("zs_out", [1, D], F32, kind="Internal").ap()
    io["mrow_d"] = nc.dram_tensor("mrow_d", [1, 8], F32, kind="Internal").ap()
    io["x_out"] = nc.dram_tensor("x_out", [SC, D], F32, kind="ExternalOutput").ap()
    io["af_d"] = nc.dram_tensor("af_d", [128, 16384], BF16, kind="ExternalOutput").ap()
    io["cf_d"] = nc.dram_tensor("cf_d", [128, 16384], BF16, kind="ExternalOutput").ap()
    io["pt_d"] = nc.dram_tensor("pt_d", [128, 8192], BF16, kind="ExternalOutput").ap()
    io["zt_d"] = nc.dram_tensor("zt_d", [128, 16384], BF16, kind="ExternalOutput").ap()
    ctx = ExitStack()
    with ctx:
        bufs = alloc_common(nc, ctx)
        sem = make_sems(nc, ctx, B_SEMS + ["bEYE", "bONE"])
        ops = build_stage_b(sem, bufs, io, store_out=True, resident_in=False)
        block = ctx.enter_context(nc.Block())

        @block.sync
        def _(e):
            e.dma_start(out=bufs.eye, in_=eye_in).then_inc(sem["bEYE"], 16)
            ops["sync"](e)

        @block.vector
        def _(e):
            e.memset(bufs.ones4[:], 1.0)
            e.memset(bufs.epsc[:], float(EPS)).then_inc(sem["bONE"], 1)
            ops["vector"](e)

        @block.scalar
        def _(e):
            e.wait_ge(sem["bONE"], 1)
            ops["scalar"](e)

        @block.tensor
        def _(e):
            e.wait_ge(sem["bEYE"], 16)
            e.wait_ge(sem["bONE"], 1)
            ops["tensor"](e)

        @block.gpsimd
        def _(e):
            ops["gpsimd"](e)
    _NC_CACHE[key] = nc
    return nc


def run_stage_b(x1_full, weights):
    nc = build_launch_b()
    eye = np.eye(128, dtype=np.float16)
    in_maps = []
    for c in range(NCORES):
        b, r = c // GROUP, c % GROUP
        m = {"x_in": np.ascontiguousarray(x1_full[b, r * SC:(r + 1) * SC]),
             "eye_in": eye}
        for k in FFN0_KEYS:
            m["w_" + k] = np.ascontiguousarray(weights[k])
        in_maps.append(m)
    res = run_bass_kernel_spmd(nc, in_maps, core_ids=list(range(NCORES)))
    out = np.zeros((B, S, D), np.float32)
    for c in range(NCORES):
        b, r = c // GROUP, c % GROUP
        out[b, r * SC:(r + 1) * SC] = res.results[c]["x_out"]
    return out


def run_stage_a(x_full, kernel_w):
    nc = build_launch_a()
    eye = np.eye(128, dtype=np.float16)
    in_maps = []
    for c in range(NCORES):
        b, r = c // GROUP, c % GROUP
        in_maps.append({
            "x_own": np.ascontiguousarray(x_full[b, r * SC:(r + 1) * SC]),
            "kern": np.ascontiguousarray(kernel_w),
            "eye_in": eye,
        })
    res = run_bass_kernel_spmd(nc, in_maps, core_ids=list(range(NCORES)))
    out = np.zeros((B, S, D), np.float32)
    for c in range(NCORES):
        b, r = c // GROUP, c % GROUP
        out[b, r * SC:(r + 1) * SC] = res.results[c]["x1"]
    return out


def kernel(**inputs):
    return kernel_fused(**inputs)


L1_KEYS = ["Wq1", "bq1", "Wk1", "bk1", "Wk2", "bk2", "Wproj", "bproj"]


def make_masks(core):
    r = core % GROUP
    mask_lt = np.zeros((128, 4), np.float32)
    mask_prev = np.zeros((128, 4), np.float32)
    first_m = np.zeros((128, 1), np.float32)
    mask_lt[:, :r] = 1.0
    if r > 0:
        mask_prev[:, r - 1] = 1.0
        first_m[:] = 1.0
    return mask_lt, mask_prev, first_m


def build_launch_c(trunc=None):
    key = ("c", trunc)
    if key in _NC_CACHE:
        return _NC_CACHE[key]
    nc = bass.Bass("TRN2", target_bir_lowering=False, debug=False)
    io = {}
    io["x_in"] = nc.dram_tensor("x_in", [SC, D], F32, kind="ExternalInput").ap()
    w = {}
    shapes = {"Wq1": [D, D], "bq1": [D], "Wk1": [D, D], "bk1": [D],
              "Wk2": [D, D], "bk2": [D], "Wproj": [D, D], "bproj": [D]}
    for k in L1_KEYS:
        w[k] = nc.dram_tensor("w_" + k, shapes[k], F32, kind="ExternalInput").ap()
    io["w"] = w
    eye_in = nc.dram_tensor("eye_in", [128, 128], BF16, kind="ExternalInput").ap()
    io["mask_lt"] = nc.dram_tensor("mask_lt_in", [128, 4], F32, kind="ExternalInput").ap()
    io["mask_prev"] = nc.dram_tensor("mask_prev_in", [128, 4], F32, kind="ExternalInput").ap()
    io["first_m"] = nc.dram_tensor("first_m_in", [128, 1], F32, kind="ExternalInput").ap()
    io["cc1_in"] = [nc.dram_tensor(f"cc1_in{i}", [128, 5], F32, kind="Internal").ap()
                    for i in range(8)]
    io["cc1_out"] = [nc.dram_tensor(f"cc1_out{i}", [512, 5], F32, kind="Internal").ap()
                     for i in range(8)]
    io["cc2_in"] = [nc.dram_tensor(f"cc2_in{i}", [128, 1], F32, kind="Internal").ap()
                    for i in range(8)]
    io["cc2_out"] = [nc.dram_tensor(f"cc2_out{i}", [512, 1], F32, kind="Internal").ap()
                     for i in range(8)]
    io["x_out"] = nc.dram_tensor("x_out", [SC, D], F32, kind="ExternalOutput").ap()
    if trunc is not None:
        io["r_d"] = nc.dram_tensor("r_d", [128, 16384], F32, kind="ExternalOutput").ap()
        io["ot_d"] = nc.dram_tensor("ot_d", [128, 16384], BF16, kind="ExternalOutput").ap()
        io["xa_d"] = nc.dram_tensor("xa_d", [128, 2048], F32, kind="ExternalOutput").ap()
        io["cc_d"] = nc.dram_tensor("cc_d", [512, 48], F32, kind="ExternalOutput").ap()
    ctx = ExitStack()
    with ctx:
        bufs = alloc_common(nc, ctx)
        sem = make_sems(nc, ctx, C_SEMS + ["cEYE", "cONE"])
        ops = build_stage_c(sem, bufs, io, store_out=True, resident_in=False,
                            trunc=trunc)
        block = ctx.enter_context(nc.Block())

        @block.sync
        def _(e):
            e.dma_start(out=bufs.eye, in_=eye_in).then_inc(sem["cEYE"], 16)
            ops["sync"](e)

        @block.vector
        def _(e):
            e.memset(bufs.epsc[:], float(EPS)).then_inc(sem["cONE"], 1)
            ops["vector"](e)

        @block.scalar
        def _(e):
            e.wait_ge(sem["cONE"], 1)
            ops["scalar"](e)

        @block.tensor
        def _(e):
            e.wait_ge(sem["cEYE"], 16)
            ops["tensor"](e)

        @block.gpsimd
        def _(e):
            ops["gpsimd"](e)
    _NC_CACHE[key] = nc
    return nc


def run_stage_c(x2_full, weights):
    nc = build_launch_c()
    eye = np.eye(128, dtype=np.float16)
    in_maps = []
    for c in range(NCORES):
        b, r = c // GROUP, c % GROUP
        ml, mp, fm = make_masks(c)
        m = {"x_in": np.ascontiguousarray(x2_full[b, r * SC:(r + 1) * SC]),
             "eye_in": eye, "mask_lt_in": ml, "mask_prev_in": mp,
             "first_m_in": fm}
        for k in L1_KEYS:
            m["w_" + k] = np.ascontiguousarray(weights[k])
        in_maps.append(m)
    res = run_bass_kernel_spmd(nc, in_maps, core_ids=list(range(NCORES)))
    out = np.zeros((B, S, D), np.float32)
    for c in range(NCORES):
        b, r = c // GROUP, c % GROUP
        out[b, r * SC:(r + 1) * SC] = res.results[c]["x_out"]
    return out


def build_fused():
    key = ("fused",)
    if key in _NC_CACHE:
        return _NC_CACHE[key]
    nc = bass.Bass("TRN2", target_bir_lowering=False, debug=False)
    ioa = {}
    ioa["x_own"] = nc.dram_tensor("x_own", [SC, D], F32, kind="ExternalInput").ap()
    ioa["kern"] = nc.dram_tensor("kern", [3, D, D], F32, kind="ExternalInput").ap()
    eye_in = nc.dram_tensor("eye_in", [128, 128], BF16, kind="ExternalInput").ap()
    ioa["kv_in"] = nc.dram_tensor("kv_in", [128, 32768], BF16, kind="Internal").ap()
    ioa["kv_g"] = nc.dram_tensor("kv_g", [512, 32768], BF16, kind="Internal").ap()

    iob = {}
    wb = {}
    shapes_b = {"ln_g": [D], "ln_b": [D], "bn_g": [D], "bn_b": [D],
                "bn_mean": [D], "bn_var": [D], "Wr": [D, 8], "br": [8],
                "W_down": [8, D, 64], "b_down": [8, 64], "W_up": [8, 64, D],
                "b_up": [8, D]}
    for k in FFN0_KEYS:
        wb[k] = nc.dram_tensor("wb_" + k, shapes_b[k], F32, kind="ExternalInput").ap()
    iob["w"] = wb
    iob["zs_in"] = nc.dram_tensor("zs_in", [1, D], F32, kind="Internal").ap()
    iob["zs_out"] = nc.dram_tensor("zs_out", [1, D], F32, kind="Internal").ap()
    iob["mrow_d"] = nc.dram_tensor("mrow_d", [1, 8], F32, kind="Internal").ap()

    ioc = {}
    wc = {}
    shapes_c = {"Wq1": [D, D], "bq1": [D], "Wk1": [D, D], "bk1": [D],
                "Wk2": [D, D], "bk2": [D], "Wproj": [D, D], "bproj": [D]}
    for k in L1_KEYS:
        wc[k] = nc.dram_tensor("wc_" + k, shapes_c[k], F32, kind="ExternalInput").ap()
    ioc["w"] = wc
    ioc["mask_lt"] = nc.dram_tensor("mask_lt_in", [128, 4], F32, kind="ExternalInput").ap()
    ioc["mask_prev"] = nc.dram_tensor("mask_prev_in", [128, 4], F32, kind="ExternalInput").ap()
    ioc["first_m"] = nc.dram_tensor("first_m_in", [128, 1], F32, kind="ExternalInput").ap()
    ioc["cc1_in"] = [nc.dram_tensor(f"cc1_in{i}", [128, 5], F32, kind="Internal").ap()
                     for i in range(8)]
    ioc["cc1_out"] = [nc.dram_tensor(f"cc1_out{i}", [512, 5], F32, kind="Internal").ap()
                      for i in range(8)]
    ioc["cc2_in"] = [nc.dram_tensor(f"cc2_in{i}", [128, 1], F32, kind="Internal").ap()
                     for i in range(8)]
    ioc["cc2_out"] = [nc.dram_tensor(f"cc2_out{i}", [512, 1], F32, kind="Internal").ap()
                      for i in range(8)]
    ioc["x2_sp"] = nc.dram_tensor("x2_sp", [SC, D], F32, kind="Internal").ap()

    iod = {}
    wd = {}
    shapes_d = {"U1": [D, 256], "V1": [256, 4096], "b1": [4096],
                "U2": [4096, 256], "V2": [256, D], "b2": [D]}
    for k in FFN1_KEYS:
        wd[k] = nc.dram_tensor("wd_" + k, shapes_d[k], F32, kind="ExternalInput").ap()
    iod["w"] = wd
    iod["y_out"] = nc.dram_tensor("y_out", [SC, D], F32, kind="ExternalOutput").ap()

    ctx = ExitStack()
    with ctx:
        bufs = alloc_common(nc, ctx)
        fa = [s for s in A_SEMS if s not in ("aEYE", "aOUT", "aQG")]
        fb = [s for s in B_SEMS if s not in ("bOUT", "bCC")]
        fc = [s for s in C_SEMS if s not in ("cOUT",)]
        fd = list(D_SEMS)
        sem = make_sems(nc, ctx, fa + fb + fc + fd + ["gEYE", "gONE"])
        opsa = build_stage_a(sem, bufs, ioa, store_out=False)
        opsb = build_stage_b(sem, bufs, iob, store_out=False, resident_in=True,
                             gate_in=(sem["aRES"], NT))
        opsc = build_stage_c(sem, bufs, ioc, store_out=False, resident_in=True,
                             gate_in=(sem["bRES"], NT))
        opsd = build_stage_d(sem, bufs, iod, store_out=True, resident_in=True,
                             gate_in=(sem["cRES"], NT))
        block = ctx.enter_context(nc.Block())

        @block.sync
        def _(e):
            e.dma_start(out=bufs.eye, in_=eye_in).then_inc(sem["gEYE"], 16)
            for name in ("aEYE",):
                pass
            opsa["sync"](e)
            opsb["sync"](e)
            opsc["sync"](e)
            opsd["sync"](e)

        @block.vector
        def _(e):
            e.memset(bufs.epsc[:], float(EPS)).then_inc(sem["gONE"], 1)
            opsa["vector"](e)
            opsb["vector"](e)
            opsc["vector"](e)
            opsd["vector"](e)

        @block.scalar
        def _(e):
            e.wait_ge(sem["gONE"], 1)
            opsa["scalar"](e)
            opsb["scalar"](e)
            opsc["scalar"](e)
            opsd["scalar"](e)

        @block.tensor
        def _(e):
            e.wait_ge(sem["gEYE"], 16)
            opsa["tensor"](e)
            opsb["tensor"](e)
            opsc["tensor"](e)
            opsd["tensor"](e)

        @block.gpsimd
        def _(e):
            opsa["gpsimd"](e)
            opsb["gpsimd"](e)
            opsc["gpsimd"](e)
            opsd["gpsimd"](e)
    _NC_CACHE[key] = nc
    return nc


def kernel_fused(**inputs):
    ins = {k: np.asarray(v) for k, v in inputs.items()}
    nc = build_fused()
    eye = np.eye(128, dtype=np.float16)
    x = ins["L0_attn_x"]
    in_maps = []
    for c in range(NCORES):
        b, r = c // GROUP, c % GROUP
        ml_, mp, fm = make_masks(c)
        m = {"x_own": np.ascontiguousarray(x[b, r * SC:(r + 1) * SC]),
             "kern": np.ascontiguousarray(ins["L0_attn_kernel"]),
             "eye_in": eye, "mask_lt_in": ml_, "mask_prev_in": mp,
             "first_m_in": fm}
        for k in FFN0_KEYS:
            m["wb_" + k] = np.ascontiguousarray(ins["L0_ffn_" + k])
        for k in L1_KEYS:
            m["wc_" + k] = np.ascontiguousarray(ins["L1_attn_" + k])
        for k in FFN1_KEYS:
            m["wd_" + k] = np.ascontiguousarray(ins["L1_ffn_" + k])
        in_maps.append(m)
    res = run_bass_kernel_spmd(nc, in_maps, core_ids=list(range(NCORES)))
    out = np.zeros((B, S, D), np.float32)
    for c in range(NCORES):
        b, r = c // GROUP, c % GROUP
        out[b, r * SC:(r + 1) * SC] = res.results[c]["y_out"]
    return out
